# version 11
# baseline (speedup 1.0000x reference)
import numpy as np
import ml_dtypes
import concourse.bass as bass
import concourse.mybir as mybir
from concourse.bass_utils import run_bass_kernel_spmd

F32 = mybir.dt.float32
BF16 = mybir.dt.bfloat16
AO = mybir.AluOpType
AF = mybir.ActivationFunctionType
AX = mybir.AxisListType

MAGIC = 12582912.0
KILL = -100000.0
EPS = 1e-5

B, S, D, H, DH = 4, 4096, 1024, 16, 4096
NOWN, NBANK = 2048, 16
NQ = NOWN + NBANK
G8, GN = 8, 256
KBANK = 29
NKILL = 2 * G8
KB0 = 96
KC = KB0 + NKILL

BPOS = [511, 512, 1023, 1024, 1535, 1536, 2047, 2048,
        2559, 2560, 3071, 3072, 3583, 3584, 3584, 3584]

NEED_A = [(256 * (g + 1) + 127) // 128 for g in range(G8)]
NEED_B = [(2048 + 256 * (g + 1) + 127) // 128 for g in range(G8)]
KSLOT = [max(a, b) for a, b in zip(NEED_A, NEED_B)]

FSLAB = 512
NSLAB = NOWN // FSLAB
SLABW = FSLAB + 2
SW = SLABW

ENGS = ("sy", "te", "ve", "se", "ge")
SEMS = {}
VJUNK = None


class Sch:

    def __init__(self):
        self.ops = {e: [] for e in ENGS}
        self.cnt = {e: 0 for e in ENGS}
        self.dfam = {}
        self.mkdsem = None
        self.bufs = {}
        self.lastw = {e: {} for e in ENGS}
        self.bar = None

    def barrier(self):
        self.bar = (dict(self.cnt), dict(self.dfam), set())

    def _wait(self, eng, semkey, val):
        if val <= self.lastw[eng].get(semkey, 0):
            return
        self.lastw[eng][semkey] = val
        self.ops[eng].append(lambda h, semkey=semkey, val=val: h.wait_ge(SEMS[semkey], val))

    def _resolve(self, sk, v):
        if sk.startswith("D:") and v < 0:
            return 16 * self.dfam[sk[2:]]
        return v

    def emit(self, eng, fn, reads=(), writes=(), dma=False, inc=True, lazy=False):
        if self.bar is not None and eng not in self.bar[2]:
            sc, sd, applied = self.bar
            for e2, v in sc.items():
                if e2 != eng and v > 0:
                    self._wait(eng, e2, v)
            for f, c in sd.items():
                if c > 0:
                    self._wait(eng, "D:" + f, 16 * c)
            applied.add(eng)
            if len(applied) == len(ENGS):
                self.bar = None
        need = {}
        for b in reads:
            rec = self.bufs.get(b)
            if rec and rec["w"]:
                sk, v = rec["w"]
                if sk != eng:
                    need[sk] = max(need.get(sk, 0), self._resolve(sk, v))
        for b in writes:
            rec = self.bufs.get(b)
            if rec:
                if rec["w"]:
                    sk, v = rec["w"]
                    if sk != eng:
                        need[sk] = max(need.get(sk, 0), self._resolve(sk, v))
                for sk, v in rec["r"].items():
                    if sk != eng:
                        need[sk] = max(need.get(sk, 0), self._resolve(sk, v))
        for sk, v in sorted(need.items()):
            self._wait(eng, sk, v)
        if dma:
            fam = writes[0]
            if fam not in self.dfam:
                self.dfam[fam] = 0
                self.mkdsem(fam)
            self.dfam[fam] += 1
            semkey, val = "D:" + fam, -1
            self.ops[eng].append(lambda h, fn=fn, semkey=semkey: fn(h).then_inc(SEMS[semkey], 16))
        elif inc:
            self.cnt[eng] += 1
            semkey, val = eng, self.cnt[eng]
            self.ops[eng].append(lambda h, fn=fn, semkey=semkey: fn(h).then_inc(SEMS[semkey], 1))
            if lazy:
                for _ in range(4):
                    self.cnt[eng] += 1
                    self.ops[eng].append(lambda h, semkey=semkey: h.memset(
                        VJUNK[:, :], 0.25).then_inc(SEMS[semkey], 1))
                val = self.cnt[eng]
        else:
            semkey, val = eng, self.cnt[eng] + 1
            self.ops[eng].append(fn)
        for b in writes:
            self.bufs[b] = {"w": (semkey, val), "r": {}}
        for b in reads:
            rec = self.bufs.setdefault(b, {"w": None, "r": {}})
            if semkey.startswith("D:"):
                rec["r"][semkey] = -1
            else:
                rec["r"][semkey] = max(rec["r"].get(semkey, 0), val)


def build_program():
    import contextlib
    nc = bass.Bass("TRN2", target_bir_lowering=False, detect_race_conditions=False)
    sch = Sch()
    E = sch.emit
    root = contextlib.ExitStack()

    din = lambda name, shape, dt=F32: nc.dram_tensor(name, shape, dt, kind="ExternalInput")
    xful = din("xful", [128, 8 * S])
    xloc = din("xloc", [128, 8 * NQ])
    wq_d = din("wq", [128, 8 * D]); wk_d = din("wk", [128, 8 * D])
    wv_d = din("wv", [128, 8 * D]); wo_d = din("wo", [128, 8 * D])
    bq_d = din("bq", [128, 8]); bk_d = din("bk", [128, 8])
    bvr_d = din("bvr", [1, D]); bo_d = din("bo", [128, 8])
    w1t_d = din("w1t", [128, 8 * DH])
    w2t_d = din("w2t", [128, 32 * D])
    cw_d = din("cw", [128, 3 * 32]); cb_d = din("cb", [128, 32])
    c2q_d = din("c2q", [128, NQ]); nsq_d = din("nsq", [128, NQ])
    c2k_d = din("c2k", [128, S]); nsk_d = din("nsk", [128, S])
    wkill_d = din("wkill", [NKILL, S], BF16)
    qkill_d = din("qkill", [NKILL, (G8 + 1) * GN], BF16)
    m1_d = din("m1", [128, 512], BF16); m2_d = din("m2", [128, 512], BF16)
    bmask_d = din("bmask", [128, KBANK * NBANK], BF16)
    selL_d = din("selL", [128, NSLAB * 512])
    selR_d = din("selR", [128, NSLAB * 512])
    cact_d = din("cact", [128, 2 * NSLAB])
    idf_d = din("idf", [128, 128])
    yout = nc.dram_tensor("y", [128, 8 * NOWN], F32, kind="ExternalOutput")

    import os as _osd
    _k = dict(kind="ExternalOutput") if _osd.environ.get("K_DBG_OUT") == "1" else {}
    ln1f_h = nc.dram_tensor("ln1f_h", [128, 8 * S], BF16, **_k)
    ln1o_h = nc.dram_tensor("ln1o_h", [128, 8 * NQ], BF16, **_k)
    xatt_h = nc.dram_tensor("xatt_h", [128, 8 * NQ], F32, **_k)
    obf_h = nc.dram_tensor("obf_h", [128, 8 * NQ], BF16, **_k)
    w1q_h = nc.dram_tensor("w1q_h", [128, 8 * DH], BF16, **_k)
    w2q_h = nc.dram_tensor("w2q_h", [128, 32 * D], BF16, **_k)
    dbg_h = nc.dram_tensor("dbg_h", [128, 2304], F32, **_k) if _k else None
    dbg2_h = nc.dram_tensor("dbg2_h", [128, 4 * KBANK * NBANK + 64], BF16, **_k) if _k else None

    sb = lambda st, name, shape, dt=F32: st.enter_context(nc.sbuf_tensor(name, shape, dt))
    SEMS.clear()
    for e in ENGS:
        SEMS[e] = root.enter_context(nc.semaphore(f"sem_{e}"))

    def _mkdsem(fam):
        SEMS["D:" + fam] = root.enter_context(nc.semaphore(f"d_{len(SEMS)}"))
    sch.mkdsem = _mkdsem

    onesbf = sb(root, "onesbf", [128, 128], BF16)
    onesf = sb(root, "onesf", [1, 128])
    onesw = sb(root, "onesw", [1, 512])
    idf_s = sb(root, "idf_s", [128, 128])
    selq = [sb(root, f"selq{j}", [128, 128]) for j in range(2)]
    vjunk = sb(root, "vjunk", [32, 32])
    global VJUNK
    VJUNK = vjunk
    hs4 = [sb(root, f"hs4{j}", [128, 4], BF16) for j in range(2)]
    biases = sb(root, "biases", [128, 24])
    cw_s = sb(root, "cw_s", [128, 3 * 32]); cb_s = sb(root, "cb_s", [128, 32])
    cact_s = sb(root, "cact_s", [128, 2 * NSLAB])
    swinv = sb(root, "swinv", [128, 2])
    m1_s = sb(root, "m1_s", [128, 512], BF16); m2_s = sb(root, "m2_s", [128, 512], BF16)
    bmask_s = sb(root, "bmask_s", [128, KBANK * NBANK], BF16)
    PS = [root.enter_context(nc.psum_tensor(f"ps{i}", [128, 512], F32)) for i in range(8)]
    PSN = [f"PS{i}" for i in range(8)]

    E("ve", lambda v: v.memset(onesbf[:, :], 1.0), writes=("onesbf",))

    def vguard():
        for _ in range(4):
            E("ve", lambda v: v.memset(vjunk[:, :], 0.25), reads=("vjunk",), writes=("vjunk",))
    E("ve", lambda v: v.memset(onesf[:, :], 1.0), writes=("onesf",))
    E("ve", lambda v: v.memset(onesw[:, :], 1.0), writes=("onesw",))
    E("sy", lambda s: s.dma_start(out=idf_s[:, :], in_=idf_d.ap()), writes=("idf_s",), dma=True)
    E("sy", lambda s: s.dma_start(out=m1_s[:, :], in_=m1_d.ap()), writes=("m1_s",), dma=True)
    E("sy", lambda s: s.dma_start(out=m2_s[:, :], in_=m2_d.ap()), writes=("m2_s",), dma=True)
    E("sy", lambda s: s.dma_start(out=bmask_s[:, :], in_=bmask_d.ap()), writes=("bmask_s",), dma=True)
    E("sy", lambda s: s.dma_start(out=biases[:, 0:8], in_=bq_d.ap()), writes=("bias_q",), dma=True)
    E("sy", lambda s: s.dma_start(out=biases[:, 8:16], in_=bk_d.ap()), writes=("bias_k",), dma=True)
    E("sy", lambda s: s.dma_start(out=biases[:, 16:24], in_=bo_d.ap()), writes=("bias_o",), dma=True)
    E("sy", lambda s: s.dma_start(out=cw_s[:, :], in_=cw_d.ap()), writes=("cw_s",), dma=True)
    E("sy", lambda s: s.dma_start(out=cb_s[:, :], in_=cb_d.ap()), writes=("cb_s",), dma=True)
    E("sy", lambda s: s.dma_start(out=cact_s[:, :], in_=cact_d.ap()), writes=("cact_s",), dma=True)
    for j in range(2):
        h_ = hs4[j]
        E("ve", lambda v, h_=h_: v.memset(h_[:, :], 0.0), writes=(f"hs4{j}",))
        E("ve", lambda v, h_=h_, j=j: v.memset(h_[0:64, 2 * j:2 * j + 1], 1.0),
          reads=(f"hs4{j}",), writes=(f"hs4{j}",))
        E("ve", lambda v, h_=h_, j=j: v.memset(h_[64:128, 2 * j + 1:2 * j + 2], 1.0),
          reads=(f"hs4{j}",), writes=(f"hs4{j}",))
        q_ = selq[j]
        E("ve", lambda v, q_=q_: v.memset(q_[:, :], 0.0), writes=(f"selq{j}",))
        E("ve", lambda v, q_=q_, j=j: v.memset(q_[64 * j:64 * j + 1, 0:64], 1.0),
          reads=(f"selq{j}",), writes=(f"selq{j}",))
        E("ve", lambda v, q_=q_, j=j: v.memset(q_[64 * j + 32:64 * j + 33, 64:128], 1.0),
          reads=(f"selq{j}",), writes=(f"selq{j}",))

    _ln_n = [0]

    LNW = 544

    def make_ln(st):
        i = _ln_n[0]; _ln_n[0] += 1
        lt = {"id": str(i),
              "xbf": sb(st, f"lnxbf{i}", [128, 2 * LNW], BF16),
              "sq": sb(st, f"lnsq{i}", [128, 2 * LNW], BF16),
              "tmp": sb(st, f"lntmp{i}", [128, 4 * LNW])}
        if dbg_h is not None and i == 0:
            for gap0 in (512, LNW + 512, 2 * LNW + 512):
                E("ve", lambda v, lt=lt, gap0=gap0: v.memset(lt["tmp"][:, gap0:gap0 + LNW - 512], 0.0),
                  writes=(f"lnm{i}",))
        return lt

    def emit_ln(lt, n, src, dst_bf=None, dst_f=None, resid=None, res_out=None):
        li = lt["id"]
        for ic in range(8):
            ap, rd = src(ic)
            sl = ic % 2
            xb = lt["xbf"][:, LNW * sl:LNW * sl + n]
            sq = lt["sq"][:, LNW * sl:LNW * sl + n]
            bxb, bsq2 = f"lnxb{sl}_{li}", f"lnsq{sl}_{li}"
            E("se", lambda s, ap=ap, xb=xb: s.copy(xb, ap), reads=rd + (bxb,), writes=(bxb,))
            E("se", lambda s, ap=ap, sq=sq: s.activation(sq, ap, AF.Square), reads=rd + (bsq2,), writes=(bsq2,))
            E("te", lambda t_, xb=xb, ic=ic, n=n: t_.matmul(PS[6][:, 0:n], onesbf[:, :], xb,
              start=(ic == 0), stop=(ic == 7)),
              reads=(bxb, "onesbf"),
              writes=("PS6",) if ic in (0, 7) else (), inc=True)
            E("te", lambda t_, sq=sq, ic=ic, n=n: t_.matmul(PS[7][:, 0:n], onesbf[:, :], sq,
              start=(ic == 0), stop=(ic == 7)),
              reads=(bsq2, "onesbf"),
              writes=("PS7",) if ic in (0, 7) else (), inc=True)
        tmp = lt["tmp"]
        mb = tmp[:, 0:n]; wv_ = tmp[:, LNW:LNW + n]; rb = tmp[:, 2 * LNW:2 * LNW + n]; dd = tmp[:, 3 * LNW:3 * LNW + n]
        E("ve", lambda v: v.tensor_scalar(mb, PS[6][:, 0:n], 1.0 / D, None, AO.mult),
          reads=("PS6", f"lnm{li}"), writes=(f"lnm{li}",))
        E("ve", lambda v: v.tensor_mul(wv_, mb, mb), reads=(f"lnm{li}", f"lnw{li}"), writes=(f"lnw{li}",))
        E("ve", lambda v: v.scalar_tensor_tensor(wv_, PS[7][:, 0:n], 1.0 / D, wv_, AO.mult, AO.subtract),
          reads=("PS7", f"lnw{li}"), writes=(f"lnw{li}",))
        E("ve", lambda v: v.tensor_scalar_add(wv_, wv_, EPS), reads=(f"lnw{li}",), writes=(f"lnw{li}",))
        E("se", lambda s: s.activation(rb, wv_, AF.Ln), reads=(f"lnw{li}", f"lnr{li}"), writes=(f"lnr{li}",))
        E("se", lambda s: s.activation(rb, rb, AF.Exp, scale=-0.5), reads=(f"lnr{li}",), writes=(f"lnr{li}",))
        for ic in range(8):
            ap, rd = src(ic)
            dn = f"lnd{li}"
            E("ve", lambda v, ap=ap: v.tensor_sub(dd, ap, mb), reads=rd + (f"lnm{li}", dn), writes=(dn,))
            if dst_bf is not None:
                oap, ob = dst_bf(ic)
                E("ve", lambda v, oap=oap: v.tensor_mul(oap, dd, rb), reads=(dn, f"lnr{li}"), writes=(ob,))
            if dst_f is not None:
                oap, ob = dst_f(ic)
                E("ve", lambda v, oap=oap: v.tensor_mul(oap, dd, rb), reads=(dn, f"lnr{li}"), writes=(ob,))
            if resid is not None:
                rap, rrd = resid(ic)
                oap, ob = res_out(ic)
                E("ve", lambda v: v.tensor_mul(dd, dd, rb), reads=(dn, f"lnr{li}"), writes=(dn,))
                E("ve", lambda v, oap=oap, rap=rap: v.tensor_add(oap, dd, rap),
                  reads=(dn,) + rrd, writes=(ob,))

    with contextlib.ExitStack() as p1:
        lt1 = make_ln(p1)
        xin = sb(p1, "xin", [128, 8 * 512])
        lnout = sb(p1, "lnout", [128, 8 * 512], BF16)

        def p1_pass(src_d, dst_d, total, tag0):
            for g in range((total + 511) // 512):
                c0 = min(512 * g, total - 512)
                n = 512
                for ic in range(8):
                    E("sy", lambda s, ic=ic, c0=c0, n=n: s.dma_start(
                        out=xin[:, 512 * ic:512 * ic + n],
                        in_=src_d.ap()[:, total * ic + c0: total * ic + c0 + n]),
                        reads=(f"xin{ic}",), writes=(f"xin{ic}",), dma=True)
                emit_ln(lt1, n,
                        src=lambda ic: (xin[:, 512 * ic:512 * ic + n], (f"xin{ic}",)),
                        dst_bf=lambda ic: (lnout[:, 512 * ic:512 * ic + n], f"lno{ic}"))
                for ic in range(8):
                    E("ge", lambda gg, ic=ic, c0=c0, n=n: gg.dma_start(
                        out=dst_d.ap()[:, total * ic + c0: total * ic + c0 + n],
                        in_=lnout[:, 512 * ic:512 * ic + n]),
                        reads=(f"lno{ic}",), writes=(f"{tag0}done",), dma=True)

        p1_pass(xful, ln1f_h, S, "lf")
        p1_pass(xloc, ln1o_h, NQ, "lo")
        if dbg_h is not None:
            for _s in range(2):
                E("ve", lambda v, _s=_s: v.tensor_copy(lt1["tmp"][:, 1900 + 16 * _s:1916 + 16 * _s],
                  lt1["sq"][:, 544 * _s:544 * _s + 16]),
                  reads=(f"lnsq{_s}_0", "lnm0"), writes=("lnm0",), inc=True)
                E("ve", lambda v, _s=_s: v.tensor_copy(lt1["tmp"][:, 1932 + 16 * _s:1948 + 16 * _s],
                  lt1["xbf"][:, 544 * _s:544 * _s + 16]),
                  reads=(f"lnxb{_s}_0", "lnm0"), writes=("lnm0",), inc=True)
            E("ge", lambda g: g.dma_start(out=dbg_h.ap()[:, 0:2048], in_=lt1["tmp"][:, 0:2048]),
              reads=("lnm0", "lnw0", "lnr0", "lnd0"), writes=("dbgdone",), dma=True)
            E("ge", lambda g: g.dma_start(out=dbg_h.ap()[:, 2048:2064], in_=xin[:, 0:16]),
              reads=("xin0",), writes=("dbgdone",), dma=True)
    sch.barrier()

    import os as _os0
    _DBG_STOP = _os0.environ.get("K_DBG_STOP") == "1"
    with contextlib.ExitStack() as pw_:
        pw = root if _DBG_STOP else pw_
        wbuf = sb(pw, "wbuf", [128, 1024])
        wqb = sb(pw, "wqb", [128, 1024], BF16)
        wstat = sb(pw, "wstat", [128, 48])
        wtr = sb(pw, "wtr", [128, 32])
        wtrT = sb(pw, "wtrT", [32, 128])

        def quant_weights(src_d, dst_d, ncols, tag, sw_col):
            nt = ncols // 1024
            E("ve", lambda v: v.memset(wstat[:, 0:48], 0.0), reads=("wstat",), writes=("wstat",))
            E("ve", lambda v: v.memset(wtr[:, 0:32], 0.0), reads=("wtr",), writes=("wtr",))
            for it in range(nt):
                E("sy", lambda s, it=it: s.dma_start(out=wbuf[:, :], in_=src_d.ap()[:, 1024 * it:1024 * (it + 1)]),
                  reads=("wbuf",), writes=("wbuf",), dma=True)
                E("ve", lambda v, it=it: v.tensor_reduce(wstat[:, 20:21], wbuf[:, :], AX.X, AO.add,
                  apply_absolute_value=True), reads=("wbuf", "wstat"), writes=("wstat",), lazy=True)
                E("ve", lambda v: v.tensor_add(wstat[:, 16:17], wstat[:, 16:17], wstat[:, 20:21]),
                  reads=("wstat",), writes=("wstat",), lazy=True)
            E("ve", lambda v: v.tensor_copy(wtr[:, 0:1], wstat[:, 16:17]),
              reads=("wstat", "wtr"), writes=("wtr",), lazy=True)
            vguard()
            for kb_ in range(4):
                E("ve", lambda v, kb_=kb_: v.transpose(wtrT[0:32, 32 * kb_:32 * kb_ + 32],
                  wtr[32 * kb_:32 * kb_ + 32, 0:32]),
                  reads=("wtr", "wtrT"), writes=("wtrT",), lazy=True)
            vguard()
            E("ve", lambda v: v.tensor_reduce(wstat[0:1, 17:18], wtrT[0:1, 0:128], AX.X, AO.add),
              reads=("wtrT", "wstat"), writes=("wstat",), lazy=True)
            E("ve", lambda v: v.tensor_scalar(wstat[0:1, 17:18], wstat[0:1, 17:18], 1.0 / (ncols * 128.0), EPS,
              AO.mult, AO.max), reads=("wstat",), writes=("wstat",), lazy=True)
            E("te", lambda t: t.matmul(PS[6][:, 0:1], onesf[0:1, :], wstat[0:1, 17:18], start=True, stop=True),
              reads=("wstat", "onesf", "PS6"), writes=("PS6",))
            E("ve", lambda v: v.tensor_copy(swinv[:, sw_col:sw_col + 1], PS[6][:, 0:1]),
              reads=("PS6",), writes=(f"swinv{sw_col}",), lazy=True)
            E("ve", lambda v: v.reciprocal(wstat[0:1, 18:19], wstat[0:1, 17:18]), reads=("wstat",), writes=("wstat",), lazy=True)
            E("te", lambda t: t.matmul(PS[6][:, 0:1], onesf[0:1, :], wstat[0:1, 18:19], start=True, stop=True),
              reads=("wstat", "onesf", "PS6"), writes=("PS6",))
            E("ve", lambda v: v.tensor_copy(wstat[:, 19:20], PS[6][:, 0:1]), reads=("PS6", "wstat"), writes=("wsbc" + tag,), lazy=True)
            for it in range(nt):
                E("sy", lambda s, it=it: s.dma_start(out=wbuf[:, :], in_=src_d.ap()[:, 1024 * it:1024 * (it + 1)]),
                  reads=("wbuf",), writes=("wbuf",), dma=True)
                E("ve", lambda v: v.tensor_scalar(wbuf[:, :], wbuf[:, :], wstat[:, 19:20], MAGIC, AO.mult, AO.add),
                  reads=("wbuf", "wsbc" + tag), writes=("wbuf",))
                E("ve", lambda v: v.tensor_scalar(wbuf[:, :], wbuf[:, :], MAGIC, 1.0, AO.subtract, AO.min),
                  reads=("wbuf",), writes=("wbuf",))
                E("ve", lambda v: v.tensor_scalar(wqb[:, :], wbuf[:, :], -1.0, None, AO.max),
                  reads=("wbuf", "wqb"), writes=("wqb",))
                E("ge", lambda g, it=it: g.dma_start(out=dst_d.ap()[:, 1024 * it:1024 * (it + 1)], in_=wqb[:, :]),
                  reads=("wqb",), writes=(f"{tag}done",), dma=True)

        quant_weights(w1t_d, w1q_h, 8 * DH, "w1", 0)
        if dbg_h is not None:
            E("ge", lambda g: g.dma_start(out=dbg_h.ap()[:, 2064:2112], in_=wstat[:, 0:48]),
              reads=("wstat", "wsbcw1"), writes=("dbg2",), dma=True)
            E("ge", lambda g: g.dma_start(out=dbg_h.ap()[0:32, 2112:2240], in_=wtrT[:, :]),
              reads=("wtrT",), writes=("dbg2",), dma=True)
        quant_weights(w2t_d, w2q_h, 32 * D, "w2", 1)

        bvhl = sb(pw, "bvhl", [1, 2 * D], BF16)
        bvf = sb(pw, "bvf", [1, D])
        kaug = [sb(pw, f"kaug{h}", [KC, S], BF16) for h in range(4)]
        vqt = sb(pw, "vqt", [128, 32 * 260], BF16)
        qaug = [[sb(pw, f"qa{h}_{g}", [KC, GN if g < G8 else NBANK], BF16)
                 for g in range(G8 + 1)] for h in range(4)]
        lnfw = [sb(pw, f"lnfw{s_}", [128, 8 * 512], BF16) for s_ in range(2)]
        lnow = [sb(pw, f"lnow{s_}", [128, 8 * GN], BF16) for s_ in range(2)]
        rkw = [sb(pw, f"rkw{s_}", [128, 2 * 512]) for s_ in range(2)]
        rqw = [sb(pw, f"rqw{s_}", [128, 2 * GN]) for s_ in range(2)]
        wqkv = sb(pw, "wqkv", [128, 256])
        wqkv_b = sb(pw, "wqkv_b", [128, 8 * 256 * 3], BF16)
        xsb = sb(pw, "xsb", [128, 3 * 512])
        sqb = sb(pw, "sqb", [128, 512], BF16)
        p4t = sb(pw, "p4t", [128, 3 * 4 * GN], BF16)
        pbank = sb(pw, "pbank", [128, 4 * KBANK * NBANK], BF16)
        oraw = sb(pw, "oraw", [128, 2 * NQ])
        drawt = sb(pw, "drawt", [128, NQ])
        dinvb = sb(pw, "dinvb", [128, NQ])
        obf = sb(pw, "obf", [128, 2 * NQ], BF16)
        nstat = sb(pw, "nstat", [128, 40])
        ntr = sb(pw, "ntr", [32, 32])
        ntrT = sb(pw, "ntrT", [32, 32])
        ropeT = sb(pw, "ropeT", [128, 512]); ropeT2 = sb(pw, "ropeT2", [128, 512])
        valr = sb(pw, "valr", [1, 8])

        E("ve", lambda v: v.memset(bass.AP(vqt, 64, [[260 * 32, 128], [65, 128], [1, 1]]), 1.0),
          writes=("vq_ones",))
        E("sy", lambda s: s.dma_start(out=bvf[:, :], in_=bvr_d.ap()), writes=("bvf",), dma=True)
        E("ve", lambda v: v.tensor_copy(bvhl[0:1, 0:D], bvf[0:1, :]), reads=("bvf",), writes=("bvhl_h",))
        E("ve", lambda v: v.tensor_sub(bvhl[0:1, D:2 * D], bvf[0:1, :], bvhl[0:1, 0:D]),
          reads=("bvf", "bvhl_h"), writes=("bvhl_l",))

        def emit_rope(xs, xs_c0, ctab, cc0, stab, sc0, n, dsts, rd):
            E("ve", lambda v: v.tensor_mul(ropeT[:, 0:n], xs[:, xs_c0:xs_c0 + n], ctab[:, cc0:cc0 + n]),
              reads=rd + ("ropeT",), writes=("ropeT",))
            for b_ in (0, 64):
                E("ve", lambda v, b_=b_: v.tensor_mul(ropeT2[b_:b_ + 32, 0:n],
                  xs[b_ + 32:b_ + 64, xs_c0:xs_c0 + n], stab[b_ + 32:b_ + 64, sc0:sc0 + n]),
                  reads=rd + ("ropeT2",), writes=("ropeT2",))
                E("ve", lambda v, b_=b_: v.tensor_mul(ropeT2[b_ + 32:b_ + 64, 0:n],
                  xs[b_:b_ + 32, xs_c0:xs_c0 + n], stab[b_:b_ + 32, sc0:sc0 + n]),
                  reads=rd + ("ropeT2",), writes=("ropeT2",))
            for i, (dt_, p0, c0, dbuf) in enumerate(dsts):
                b_ = 64 * i
                E("ve", lambda v, dt_=dt_, p0=p0, c0=c0, b_=b_: v.tensor_add(
                    dt_[p0:p0 + 64, c0:c0 + n], ropeT[b_:b_ + 64, 0:n], ropeT2[b_:b_ + 64, 0:n]),
                    reads=("ropeT", "ropeT2", dbuf), writes=(dbuf,))

        import os as _os
        _QORD = [int(c) for c in _os.environ.get("K_QORDER", "0123")]
        _QPH = int(_os.environ.get("K_QPHASES", "9"))
        _QW = _os.environ.get("K_QW", "1") == "1"
        _Q64 = _os.environ.get("K_Q64", "1") == "1"
        _QMS = _os.environ.get("K_QMS", "1") == "1"
        for _Qi, Q in enumerate(_QORD):
            for ic in (range(8) if (_Qi == 0 or _QW) else ()):
                for wi, wd in enumerate((wq_d, wk_d, wv_d)):
                    E("sy", lambda s, ic=ic, wd=wd, Q=Q: s.dma_start(
                        out=wqkv[:, 0:256],
                        in_=wd.ap()[:, D * ic + 256 * Q: D * ic + 256 * Q + 256]),
                        reads=("wqkv",), writes=("wqkv",), dma=True)
                    E("ve", lambda v, ic=ic, wi=wi: v.tensor_copy(
                        wqkv_b[:, 768 * ic + 256 * wi: 768 * ic + 256 * wi + 256], wqkv[:, 0:256]),
                        reads=("wqkv", f"wb{wi}_{ic}"), writes=(f"wb{wi}_{ic}",))
            WBap = lambda wi, ic, j, w=128: wqkv_b[:, 768 * ic + 256 * wi + 128 * j: 768 * ic + 256 * wi + 128 * j + w]
            WBrd = lambda wi, ic: (f"wb{wi}_{ic}",)

            if _Qi == 0:
                for h in range(4):
                    E("ve", lambda v, h=h: v.memset(kaug[h][64:96, :], 0.0),
                      reads=(f"ka{h}", f"kc{h}"), writes=(f"kc{h}",))
                    E("sy", lambda s, h=h: s.dma_start(out=kaug[h][KB0:KB0 + NKILL, :], in_=wkill_d.ap()),
                      reads=(f"kkill{h}",), writes=(f"kkill{h}",), dma=True)
            if _Qi == 0 or _QMS:
                E("ve", lambda v: v.memset(nstat[:, 0:40], 0.0), reads=("nstat",), writes=("nstat",))
                E("ve", lambda v: v.memset(ntr[:, :], 0.0), reads=("ntr",), writes=("ntr",))
                E("ve", lambda v: v.memset(drawt[:, :], 1.0), reads=("draw",), writes=("draw",))

            for tg in (range(8) if (_Qi == 0 or _QPH >= 1) else ()):
                sl = tg % 2
                lf = lnfw[sl]
                for ic in range(8):
                    E("sy", lambda s, ic=ic, tg=tg, lf=lf: s.dma_start(
                        out=lf[:, 512 * ic:512 * (ic + 1)],
                        in_=ln1f_h.ap()[:, S * ic + 512 * tg: S * ic + 512 * (tg + 1)]),
                        reads=("lfdone", f"lnfw{sl}"), writes=(f"lnfw{sl}",), dma=True)
                rk = rkw[sl]
                E("sy", lambda s, tg=tg, rk=rk: s.dma_start(out=rk[:, 0:512],
                  in_=c2k_d.ap()[:, 512 * tg:512 * (tg + 1)]), reads=(f"rkw{sl}",), writes=(f"rkw{sl}",), dma=True)
                E("sy", lambda s, tg=tg, rk=rk: s.dma_start(out=rk[:, 512:1024],
                  in_=nsk_d.ap()[:, 512 * tg:512 * (tg + 1)]), reads=(f"rkw{sl}",), writes=(f"rkw{sl}",), dma=True)
                for j in range(2):
                    pj = PS[4 + sl]; pb = PSN[4 + sl]
                    for ic in range(8):
                        E("te", lambda t, ic=ic, j=j, lf=lf, pj=pj: t.matmul(
                            pj[:, 0:512], WBap(1, ic, j), lf[:, 512 * ic:512 * (ic + 1)],
                            start=(ic == 0), stop=(ic == 7)),
                            reads=(f"lnfw{sl}",) + WBrd(1, ic),
                            writes=(pb,) if ic in (0, 7) else (), inc=(ic == 7))
                    xcol = 512 * sl
                    E("se", lambda s, pj=pj, j=j, xcol=xcol, Q=Q: s.activation(
                        xsb[:, xcol:xcol + 512], pj[:, 0:512], AF.Identity,
                        bias=biases[:, 8 + 2 * Q + j:9 + 2 * Q + j]),
                        reads=(pb, "bias_k", f"xsb{sl}"), writes=(f"xsb{sl}",))
                    E("ve", lambda v, xcol=xcol: v.tensor_mul(sqb[:, :], xsb[:, xcol:xcol + 512],
                      xsb[:, xcol:xcol + 512]), reads=(f"xsb{sl}", "sqb"), writes=("sqb",))
                    E("te", lambda t, j=j: t.matmul(PS[6][0:4, 0:512], hs4[j][:, :], sqb[:, :],
                      start=True, stop=True), reads=("sqb", f"hs4{j}", "PS6"), writes=("PS6",))
                    E("ve", lambda v: v.tensor_reduce(nstat[0:4, 4:5], PS[6][0:4, 0:512], AX.X, AO.max),
                      reads=("PS6", "nstat"), writes=("nstat",), lazy=True)
                    E("ve", lambda v: v.tensor_max(nstat[0:4, 0:1], nstat[0:4, 0:1], nstat[0:4, 4:5]),
                      reads=("nstat",), writes=("nstat",), lazy=True)
                    emit_rope(xsb, xcol, rk, 0, rk, 512, 512,
                              [(kaug[2 * j], 0, 512 * tg, f"ka{2 * j}"),
                               (kaug[2 * j + 1], 0, 512 * tg, f"ka{2 * j + 1}")],
                              (f"xsb{sl}", f"rkw{sl}"))
                for q4 in range(4):
                    tt = 4 * tg + q4
                    pv_ = PS[4 + (1 - sl)]; pvb = PSN[4 + (1 - sl)]
                    for ic in range(8):
                        E("te", lambda t, ic=ic, q4=q4, lf=lf, pv_=pv_: t.matmul(
                            pv_[:, 0:256], lf[:, 512 * ic + 128 * q4:512 * ic + 128 * q4 + 128],
                            WBap(2, ic, 0, 256), start=(ic == 0), stop=False),
                            reads=(f"lnfw{sl}",) + WBrd(2, ic),
                            writes=(pvb,) if ic == 0 else (), inc=False)
                    E("te", lambda t, pv_=pv_, Q=Q: t.matmul(pv_[:, 0:256], onesbf[0:1, :],
                      bvhl[0:1, 256 * Q:256 * Q + 256], start=False, stop=False),
                      reads=("bvhl_h", "onesbf"), inc=False)
                    E("te", lambda t, pv_=pv_, Q=Q: t.matmul(pv_[:, 0:256], onesbf[0:1, :],
                      bvhl[0:1, D + 256 * Q:D + 256 * Q + 256], start=False, stop=True),
                      reads=("bvhl_l", "onesbf"), writes=(pvb,))
                    E("se", lambda s, tt=tt, pv_=pv_: s.copy(
                        bass.AP(vqt, 260 * tt, [[260 * 32, 128], [65, 4], [1, 64]]), pv_[:, 0:256]),
                        reads=(pvb, "vq_ones", f"vq{tt}"), writes=(f"vq{tt}",))

            for g in (range(G8 + 1) if (_Qi == 0 or _QPH >= 2) else ()):
                n = GN if g < G8 else NBANK
                sl = g % 2
                lo = lnow[sl]
                for ic in range(8):
                    E("sy", lambda s, ic=ic, g=g, n=n, lo=lo: s.dma_start(
                        out=lo[:, GN * ic:GN * ic + n],
                        in_=ln1o_h.ap()[:, NQ * ic + GN * g: NQ * ic + GN * g + n]),
                        reads=("lodone", f"lnow{sl}"), writes=(f"lnow{sl}",), dma=True)
                rq = rqw[sl]
                E("sy", lambda s, g=g, n=n, rq=rq: s.dma_start(out=rq[:, 0:n],
                  in_=c2q_d.ap()[:, GN * g:GN * g + n]), reads=(f"rqw{sl}",), writes=(f"rqw{sl}",), dma=True)
                E("sy", lambda s, g=g, n=n, rq=rq: s.dma_start(out=rq[:, GN:GN + n],
                  in_=nsq_d.ap()[:, GN * g:GN * g + n]), reads=(f"rqw{sl}",), writes=(f"rqw{sl}",), dma=True)
                for j in range(2):
                    pj = PS[4 + sl]; pb = PSN[4 + sl]
                    for ic in range(8):
                        E("te", lambda t, ic=ic, j=j, n=n, lo=lo, pj=pj: t.matmul(
                            pj[:, 0:n], WBap(0, ic, j), lo[:, GN * ic:GN * ic + n],
                            start=(ic == 0), stop=(ic == 7)),
                            reads=(f"lnow{sl}",) + WBrd(0, ic),
                            writes=(pb,) if ic in (0, 7) else (), inc=(ic == 7))
                    E("se", lambda s, pj=pj, j=j, n=n, Q=Q: s.activation(xsb[:, 1024:1024 + n], pj[:, 0:n], AF.Identity,
                      bias=biases[:, 2 * Q + j:2 * Q + j + 1]),
                      reads=(pb, "bias_q", "xsbq"), writes=("xsbq",))
                    E("ve", lambda v, n=n: v.tensor_mul(sqb[:, 0:n], xsb[:, 1024:1024 + n], xsb[:, 1024:1024 + n]),
                      reads=("xsbq", "sqb"), writes=("sqb",))
                    E("te", lambda t, j=j, n=n: t.matmul(PS[6][0:4, 0:n], hs4[j][:, :], sqb[:, 0:n],
                      start=True, stop=True), reads=("sqb", f"hs4{j}", "PS6"), writes=("PS6",))
                    E("ve", lambda v, n=n: v.tensor_reduce(nstat[0:4, 5:6], PS[6][0:4, 0:n], AX.X, AO.max),
                      reads=("PS6", "nstat"), writes=("nstat",), lazy=True)
                    E("ve", lambda v: v.tensor_max(nstat[0:4, 1:2], nstat[0:4, 1:2], nstat[0:4, 5:6]),
                      reads=("nstat",), writes=("nstat",), lazy=True)
                    emit_rope(xsb, 1024, rq, 0, rq, GN, n,
                              [(qaug[2 * j][g], 0, 0, f"qa{2 * j}_{g}"),
                               (qaug[2 * j + 1][g], 0, 0, f"qa{2 * j + 1}_{g}")],
                              ("xsbq", f"rqw{sl}"))
            if _Qi == 0:
                for h in range(4):
                    for g in range(G8 + 1):
                        n = GN if g < G8 else NBANK
                        qa = qaug[h][g]
                        E("ve", lambda v, qa=qa, n=n: v.memset(qa[64:96, 0:n], 0.0), writes=(f"qc1_{h}_{g}",))
                        E("ve", lambda v, qa=qa, n=n: v.memset(qa[64:65, 0:n], 1.0),
                          reads=(f"qc1_{h}_{g}",), writes=(f"qc1_{h}_{g}",))
                        E("sy", lambda s, qa=qa, g=g, n=n: s.dma_start(
                            out=qa[KB0:KB0 + NKILL, 0:n],
                            in_=qkill_d.ap()[:, GN * g:GN * g + n]),
                            writes=(f"qc2_{h}_{g}",), dma=True)

            if _Qi > 0 and not _Q64:
                continue
            E("ve", lambda v: v.tensor_add(nstat[0:4, 6:7], nstat[0:4, 0:1], nstat[0:4, 1:2]),
              reads=("nstat",), writes=("nstat",), lazy=True)
            E("ve", lambda v: v.tensor_scalar(nstat[0:4, 6:7], nstat[0:4, 6:7], -1.0 / 16.0, None, AO.mult),
              reads=("nstat",), writes=("nstat",), lazy=True)
            E("ve", lambda v: v.tensor_copy(ntr[0:32, 0:1], nstat[0:32, 6:7]),
              reads=("nstat", "ntr"), writes=("ntr",), lazy=True)
            vguard()
            E("ve", lambda v: v.transpose(ntrT[0:32, 0:32], ntr[0:32, 0:32]),
              reads=("ntr", "ntrT"), writes=("ntrT",), lazy=True)
            vguard()
            E("ve", lambda v: v.tensor_copy(valr[0:1, 0:4], ntrT[0:1, 0:4]), reads=("ntrT", "valr"), writes=("valr",), lazy=True)
            for h in range(4):
                for kc8 in range(8):
                    E("ve", lambda v, h=h, kc8=kc8: v.tensor_scalar(
                        kaug[h][64:65, 512 * kc8:512 * (kc8 + 1)], onesw[0:1, :],
                        valr[0:1, h:h + 1], None, AO.mult),
                        reads=("valr", "onesw", f"kc{h}"), writes=(f"kc{h}",))

            for g in (range(G8 + 1) if (_Qi == 0 or _QPH >= 3) else ()):
                n = GN if g < G8 else NBANK
                K = KSLOT[g] if g < G8 else KBANK
                is_bank = (g == G8)

                def emit_score(i, g=g, n=n, is_bank=is_bank):
                    rot = i % 3
                    bs = 2 * (i % 2)
                    for h in range(4):
                        psc = PS[(0 if bs == 0 else 4) + h // 2]
                        pscb = PSN[(0 if bs == 0 else 4) + h // 2]
                        coff = 256 * (h % 2)
                        oap = psc[:, coff:coff + n]
                        kap = kaug[h][0:KC, 128 * i:128 * (i + 1)]
                        qap = qaug[h][g][0:KC, 0:n]
                        E("te", lambda t, oap=oap, kap=kap, qap=qap: t.matmul(
                            oap, kap, qap, start=True, stop=True),
                            reads=(f"ka{h}", f"kc{h}", f"kkill{h}", f"qa{h}_{g}",
                                   f"qc1_{h}_{g}", f"qc2_{h}_{g}", pscb),
                            writes=(pscb,))
                    for h in range(4):
                        psc = PS[(0 if bs == 0 else 4) + h // 2]
                        pscb = PSN[(0 if bs == 0 else 4) + h // 2]
                        coff = 256 * (h % 2)
                        iap = psc[:, coff:coff + n]
                        if not is_bank:
                            pcol = (4 * rot + h) * GN
                            oap = p4t[:, pcol:pcol + n]
                            E("se", lambda s, oap=oap, iap=iap: s.activation(oap, iap, AF.Exp),
                              reads=(pscb, f"p4_{rot}_{h}"), writes=(f"p4_{rot}_{h}",))
                        else:
                            pcol = h * KBANK * NBANK + i * NBANK
                            oap = pbank[:, pcol:pcol + n]
                            E("se", lambda s, oap=oap, iap=iap: s.activation(oap, iap, AF.Exp),
                              reads=(pscb, f"pbk{h}"), writes=(f"pbk{h}",))
                    if not is_bank:
                        for needs, mask, mbuf in ((NEED_A[g], m1_s, "m1_s"), (NEED_B[g], m2_s, "m2_s")):
                            for l in range(2):
                                if i == needs - 2 + l:
                                    pap = bass.AP(p4t, (4 * rot) * GN + 128 * l,
                                                  [[3 * 4 * GN, 128], [GN, 4], [1, 128]])
                                    map_ = bass.AP(mask, 0, [[512, 128], [128, 4], [1, 128]])
                                    E("ve", lambda v, pap=pap, map_=map_: v.tensor_tensor(
                                        pap, pap, map_, AO.mult),
                                        reads=(f"p4_{rot}_0", f"p4_{rot}_1", f"p4_{rot}_2", f"p4_{rot}_3", mbuf),
                                        writes=(f"p4_{rot}_0", f"p4_{rot}_1", f"p4_{rot}_2", f"p4_{rot}_3"))

                def emit_pv(i, g=g, n=n, K=K, is_bank=is_bank):
                    rot = i % 3
                    for h in range(4):
                        obank = (2, 3, 6, 7)[h]
                        po = PS[obank]; pob = PSN[obank]
                        coff = 0
                        if not is_bank:
                            pcol = (4 * rot + h) * GN
                            prd = f"p4_{rot}_{h}"
                            src = p4t
                        else:
                            pcol = h * KBANK * NBANK + i * NBANK
                            prd = f"pbm{h}"
                            src = pbank
                        oap = po[0:65, coff:coff + n]
                        vap = bass.AP(vqt, 260 * i + 65 * h, [[260 * 32, 128], [1, 65]])
                        pap = src[:, pcol:pcol + n]
                        E("te", lambda t, oap=oap, vap=vap, pap=pap, i=i, K=K: t.matmul(
                            oap, vap, pap, start=(i == 0), stop=(i == K - 1)),
                            reads=(f"vq{i}", prd, "vq_ones"),
                            writes=(pob,) if i in (0, K - 1) else (), inc=(i == K - 1))

                if not is_bank:
                    for i in range(K):
                        emit_score(i)
                        if i >= 1:
                            emit_pv(i - 1)
                    emit_pv(K - 1)
                else:
                    for i in range(K):
                        emit_score(i)
                    for h in range(4):
                        E("ve", lambda v, h=h: v.tensor_tensor(
                            pbank[:, h * KBANK * NBANK:(h + 1) * KBANK * NBANK],
                            pbank[:, h * KBANK * NBANK:(h + 1) * KBANK * NBANK],
                            bmask_s[:, :], AO.mult),
                            reads=(f"pbk{h}", "bmask_s"), writes=(f"pbm{h}", f"pbk{h}"))
                    for i in range(K):
                        emit_pv(i)

                gc0 = GN * g
                for h in range(4):
                    obank = (2, 3, 6, 7)[h]
                    po = PS[obank]; pob = PSN[obank]
                    coff = 0
                    E("se", lambda s, h=h, po=po, coff=coff, gc0=gc0, n=n: s.copy(
                        oraw[64 * (h % 2):64 * (h % 2) + 64, NQ * (h // 2) + gc0: NQ * (h // 2) + gc0 + n],
                        po[0:64, coff:coff + n]), reads=(pob, "oraw"), writes=("oraw",))
                    E("se", lambda s, h=h, po=po, coff=coff, gc0=gc0, n=n: s.copy(
                        drawt[32 * h:32 * h + 1, gc0:gc0 + n], po[64:65, coff:coff + n]),
                        reads=(pob, "draw"), writes=("draw",))
                if dbg2_h is not None and is_bank:
                    E("ge", lambda g_: g_.dma_start(out=dbg2_h.ap()[:, 0:4 * KBANK * NBANK],
                      in_=pbank[:, :]), reads=tuple(f"pbm{h}" for h in range(4)),
                      writes=("dbg4",), dma=True)
                    E("ge", lambda g_: g_.dma_start(
                      out=dbg2_h.ap()[:, 4 * KBANK * NBANK:4 * KBANK * NBANK + 16],
                      in_=drawt[:, 2048:2064]), reads=("draw",), writes=("dbg4",), dma=True)
                    E("ge", lambda g_: g_.dma_start(
                      out=dbg2_h.ap()[:, 4 * KBANK * NBANK + 16:4 * KBANK * NBANK + 32],
                      in_=oraw[:, 2048:2064]), reads=("oraw",), writes=("dbg4",), dma=True)
                    E("ge", lambda g_: g_.dma_start(
                      out=dbg2_h.ap()[:, 4 * KBANK * NBANK + 32:4 * KBANK * NBANK + 48],
                      in_=oraw[:, NQ + 2048:NQ + 2064]), reads=("oraw",), writes=("dbg4",), dma=True)

            if _Qi > 0 and _QPH < 4:
                continue
            E("ve", lambda v: v.reciprocal(dinvb[:, :], drawt[:, :]), reads=("draw", "dinv"), writes=("dinv",), lazy=True)
            for jc in range(2):
                for t5 in range(5):
                    c5 = min(512 * t5, NQ - 512)
                    n5 = 512
                    E("te", lambda t, jc=jc, c5=c5, n5=n5: t.matmul(PS[4][:, 0:n5], selq[jc][:, :],
                      dinvb[:, c5:c5 + n5], start=True, stop=True),
                      reads=("dinv", f"selq{jc}", "PS4"), writes=("PS4",))
                    E("ve", lambda v, c5=c5, n5=n5: v.tensor_copy(drawt[:, c5:c5 + n5], PS[4][:, 0:n5]),
                      reads=("PS4", "draw"), writes=("draw2",))
                    E("ve", lambda v, jc=jc, c5=c5, n5=n5: v.tensor_mul(
                        obf[:, NQ * jc + c5: NQ * jc + c5 + n5],
                        oraw[:, NQ * jc + c5: NQ * jc + c5 + n5], drawt[:, c5:c5 + n5]),
                        reads=("draw2", "oraw", f"obf{jc}_{t5}"), writes=(f"obf{jc}_{t5}",))
                E("ge", lambda g_, jc=jc, Q=Q: g_.dma_start(
                    out=obf_h.ap()[:, NQ * (2 * Q + jc): NQ * (2 * Q + jc + 1)],
                    in_=obf[:, NQ * jc: NQ * (jc + 1)]),
                    reads=tuple(f"obf{jc}_{t5}" for t5 in range(5)), writes=(f"obfh{2 * Q + jc}",), dma=True)

    sch.barrier()
    if _DBG_STOP:
        sch.ops["ge"].append(lambda h: h.wait_ge(SEMS["D:obfh1"], 16))
        return _finish(nc, sch, root)
    with contextlib.ExitStack() as p3:
        lt3 = make_ln(p3)
        wo_b = sb(p3, "wo_b", [128, 8 * D], BF16)
        wo_f = sb(p3, "wo_f", [128, 2048])
        obw = sb(p3, "obw", [128, 8 * 512], BF16)
        zbuf = sb(p3, "zbuf", [128, 8 * 512])
        xlw = sb(p3, "xlw", [128, 8 * 512])
        xaw = sb(p3, "xaw", [128, 8 * 512])
        for half in range(4):
            E("sy", lambda s, half=half: s.dma_start(out=wo_f[:, :],
              in_=wo_d.ap()[:, 2048 * half:2048 * (half + 1)]), reads=("wo_f",), writes=("wo_f",), dma=True)
            E("ve", lambda v, half=half: v.tensor_copy(wo_b[:, 2048 * half:2048 * (half + 1)], wo_f[:, :]),
              reads=("wo_f",), writes=(f"wo_b{half}",))
        WOrd = tuple(f"wo_b{half}" for half in range(4))
        for g in range(5):
            n = 512
            c0 = min(512 * g, NQ - 512)
            for ic in range(8):
                E("sy", lambda s, ic=ic, c0=c0, n=n: s.dma_start(out=obw[:, 512 * ic:512 * ic + n],
                  in_=obf_h.ap()[:, NQ * ic + c0: NQ * ic + c0 + n]),
                  reads=(f"obfh{ic}", "obw"), writes=("obw",), dma=True)
                E("sy", lambda s, ic=ic, c0=c0, n=n: s.dma_start(out=xlw[:, 512 * ic:512 * ic + n],
                  in_=xloc.ap()[:, NQ * ic + c0: NQ * ic + c0 + n]),
                  reads=("xlw",), writes=("xlw",), dma=True)
            for oc in range(8):
                pj = PS[4 + oc % 2]; pb = PSN[4 + oc % 2]
                for ic in range(8):
                    E("te", lambda t, oc=oc, ic=ic, n=n, pj=pj: t.matmul(
                        pj[:, 0:n], wo_b[:, D * ic + 128 * oc: D * ic + 128 * oc + 128],
                        obw[:, 512 * ic:512 * ic + n], start=(ic == 0), stop=(ic == 7)),
                        reads=("obw",) + WOrd,
                        writes=(pb,) if ic in (0, 7) else (), inc=(ic == 7))
                E("se", lambda s, oc=oc, n=n, pj=pj: s.activation(zbuf[:, 512 * oc:512 * oc + n],
                  pj[:, 0:n], AF.Identity, bias=biases[:, 16 + oc:17 + oc]),
                  reads=(pb, "bias_o", "zbuf"), writes=("zbuf",))
            emit_ln(lt3, n,
                    src=lambda ic: (zbuf[:, 512 * ic:512 * ic + n], ("zbuf",)),
                    resid=lambda ic: (xlw[:, 512 * ic:512 * ic + n], ("xlw",)),
                    res_out=lambda ic: (xaw[:, 512 * ic:512 * ic + n], f"xaw{ic}"))
            for ic in range(8):
                E("ge", lambda g_, ic=ic, c0=c0, n=n: g_.dma_start(
                    out=xatt_h.ap()[:, NQ * ic + c0: NQ * ic + c0 + n],
                    in_=xaw[:, 512 * ic:512 * ic + n]),
                    reads=(f"xaw{ic}",), writes=("xattdone",), dma=True)

    sch.barrier()
    with contextlib.ExitStack() as p4s:
        lt4 = make_ln(p4s)
        hslab = sb(p4s, "hslab", [128, 32 * SW])
        hbank = sb(p4s, "hbank", [128, 32 * NBANK])
        xattw = sb(p4s, "xattw", [128, 8 * SLABW])
        ln2f = sb(p4s, "ln2f", [128, 8 * SLABW])
        xqb = sb(p4s, "xqb", [128, 8 * SLABW], BF16)
        w1w = [sb(p4s, f"w1w{i}", [128, 8 * 128], BF16) for i in range(2)]
        w2w = [sb(p4s, f"w2w{i}", [128, 1024], BF16) for i in range(2)]
        tmpc = sb(p4s, "tmpc", [128, FSLAB])
        amtr = sb(p4s, "amtr", [128, 544])
        amtc = sb(p4s, "amtc", [128, 544])
        selw = sb(p4s, "selw", [128, 512])
        st32 = sb(p4s, "st32", [32, 544])
        amT = sb(p4s, "amT", [32, 544])
        amC = sb(p4s, "amC", [32, 544])
        srow = sb(p4s, "srow", [1, 1100])
        sbc = sb(p4s, "sbc", [128, SLABW]); sxinv = sb(p4s, "sxinv", [128, SLABW])
        s2bc = sb(p4s, "s2bc", [128, FSLAB]); s2inv = sb(p4s, "s2inv", [128, FSLAB])
        h2qb = [sb(p4s, f"h2qb{i}", [128, 512], BF16) for i in range(2)]
        zb2 = sb(p4s, "zb2", [128, 8 * FSLAB])
        ybuf = sb(p4s, "ybuf", [128, 8 * FSLAB])
        ecol = sb(p4s, "ecol", [128, 2 * 32])
        E("ve", lambda v: v.memset(amC[:, :], 0.0), writes=("amC",))
        E("ve", lambda v: v.memset(amT[:, :], 0.0), writes=("amT",))
        E("ve", lambda v: v.memset(amtr[:, :], 0.0), writes=("amtr",))

        def absmax_rows(tag, src_fn, nsrc, n, dst_c0):
            for i in range(nsrc):
                api, rdi = src_fn(i)
                if i == 0:
                    E("se", lambda s, api=api: s.activation(amtr[:, 0:n], api, AF.Abs),
                      reads=rdi + ("amtr",), writes=("amtr",))
                else:
                    E("se", lambda s, api=api: s.activation(amtc[:, 0:n], api, AF.Abs),
                      reads=rdi + ("amtc",), writes=("amtc",))
                    E("ve", lambda v: v.tensor_max(amtr[:, 0:n], amtr[:, 0:n], amtc[:, 0:n]),
                      reads=("amtc", "amtr"), writes=("amtr",))
            E("ve", lambda v: v.tensor_copy(amtc[0:64, 0:n], amtr[64:128, 0:n]),
              reads=("amtr", "amtc"), writes=("amtc",))
            E("ve", lambda v: v.tensor_max(amtr[0:64, 0:n], amtr[0:64, 0:n], amtc[0:64, 0:n]),
              reads=("amtc", "amtr"), writes=("amtr",))
            E("ve", lambda v: v.tensor_copy(amtc[0:32, 0:n], amtr[32:64, 0:n]),
              reads=("amtr", "amtc"), writes=("amtc",))
            E("ve", lambda v: v.tensor_max(amtr[0:32, 0:n], amtr[0:32, 0:n], amtc[0:32, 0:n]),
              reads=("amtc", "amtr"), writes=("amtr",))
            nb = (n + 31) // 32
            vguard()
            for j in range(nb):
                E("ve", lambda v, j=j: v.transpose(amT[0:32, 32 * j:32 * j + 32],
                  amtr[0:32, 32 * j:32 * j + 32]), reads=("amtr", "amT"), writes=("amT",), lazy=True)
            vguard()
            for j in range(nb):
                E("ve", lambda v, j=j: v.tensor_reduce(amC[0:32, 32 * j:32 * j + 1],
                  amT[0:32, 32 * j:32 * j + 32], AX.X, AO.max), reads=("amT", "amC"), writes=("amC",), lazy=True)
            vguard()
            for j in range(nb):
                E("ve", lambda v, j=j: v.transpose(st32[0:32, 32 * j:32 * j + 32],
                  amC[0:32, 32 * j:32 * j + 32]), reads=("amC", "st32"), writes=("st32",), lazy=True)
            vguard()

        def bcast_row(src_c0, n, dst, dstbuf):
            for it in range((n + 511) // 512):
                c0 = min(512 * it, max(0, n - 512))
                w = min(512, n)
                E("te", lambda t, c0=c0, w=w: t.matmul(PS[5][:, 0:w], onesf[0:1, :],
                  srow[0:1, src_c0 + c0: src_c0 + c0 + w], start=True, stop=True),
                  reads=("srow", "onesf", "PS5"), writes=("PS5",))
                E("ve", lambda v, c0=c0, w=w: v.tensor_copy(dst[:, c0:c0 + w], PS[5][:, 0:w]),
                  reads=("PS5", dstbuf), writes=(dstbuf,))

        def quant_cols(srcT, src_rd_w, n, sb_, sbuf_n, dstT, dst_w, stride):
            for ic in range(8):
                E("ve", lambda v, ic=ic: v.tensor_tensor(srcT[:, stride * ic:stride * ic + n],
                  srcT[:, stride * ic:stride * ic + n], sb_[:, 0:n], AO.mult),
                  reads=(src_rd_w(ic), sbuf_n), writes=(src_rd_w(ic),))
                E("ve", lambda v, ic=ic: v.tensor_scalar(srcT[:, stride * ic:stride * ic + n],
                  srcT[:, stride * ic:stride * ic + n], MAGIC, MAGIC, AO.add, AO.subtract),
                  reads=(src_rd_w(ic),), writes=(src_rd_w(ic),))
                E("ve", lambda v, ic=ic: v.tensor_copy(dstT[:, stride * ic:stride * ic + n],
                  srcT[:, stride * ic:stride * ic + n]), reads=(src_rd_w(ic), dst_w(ic)), writes=(dst_w(ic),))

        BW = 512
        bc0 = NQ - BW
        for ic in range(8):
            E("sy", lambda s, ic=ic: s.dma_start(out=xattw[:, SLABW * ic:SLABW * ic + BW],
              in_=xatt_h.ap()[:, NQ * ic + bc0: NQ * ic + bc0 + BW]),
              reads=("xattdone", "xattw"), writes=("xattw",), dma=True)
        emit_ln(lt4, BW,
                src=lambda ic: (xattw[:, SLABW * ic:SLABW * ic + BW], ("xattw",)),
                dst_f=lambda ic: (ln2f[:, SLABW * ic:SLABW * ic + BW], f"ln2f{ic}"))
        absmax_rows("bk", lambda ic: (ln2f[:, SLABW * ic:SLABW * ic + BW],
                                      tuple(f"ln2f{i}" for i in range(8))), 8, BW, 0)
        E("ve", lambda v: v.tensor_scalar(srow[0:1, 550:550 + BW], st32[0:1, 0:BW], EPS, None, AO.max),
          reads=("srow", "st32"), writes=("srow",))
        E("ve", lambda v: v.reciprocal(srow[0:1, 0:BW], srow[0:1, 550:550 + BW]),
          reads=("srow",), writes=("srow",), lazy=True)
        E("ve", lambda v: v.tensor_scalar(srow[0:1, 0:BW], srow[0:1, 0:BW], 127.0, None, AO.mult),
          reads=("srow",), writes=("srow",))
        E("ve", lambda v: v.tensor_scalar(srow[0:1, 550:550 + BW], srow[0:1, 550:550 + BW],
          1.0 / 127.0, None, AO.mult), reads=("srow",), writes=("srow",))
        bcast_row(0, BW, sbc[:, 0:BW], "sbc")
        bcast_row(550, BW, sxinv[:, 0:BW], "sxinv")
        quant_cols(ln2f, lambda ic: f"ln2f{ic}", BW, sbc[:, 0:BW], "sbc",
                   xqb, lambda ic: f"xqb{ic}", SLABW)
        XQKRD = tuple(f"xqb{ic}" for ic in range(8))
        for oc in range(32):
            wsl = oc % 2
            E("sy", lambda s, oc=oc, wsl=wsl: s.dma_start(out=w1w[wsl][:, :],
              in_=bass.AP(w1q_h, 128 * oc, [[8 * DH, 128], [DH, 8], [1, 128]])),
              reads=("w1done", f"w1w{wsl}"), writes=(f"w1w{wsl}",), dma=True)
            pj = PS[4 + oc % 2]; pb = PSN[4 + oc % 2]
            for ic in range(8):
                E("te", lambda t, ic=ic, wsl=wsl, pj=pj: t.matmul(
                    pj[:, 0:BW], w1w[wsl][:, 128 * ic:128 * (ic + 1)],
                    xqb[:, SLABW * ic: SLABW * ic + BW], start=(ic == 0), stop=(ic == 7)),
                    reads=XQKRD + (f"w1w{wsl}",),
                    writes=(pb,) if ic in (0, 7) else (), inc=(ic == 7))
            E("ve", lambda v, oc=oc, pj=pj: v.scalar_tensor_tensor(hbank[:, NBANK * oc:NBANK * (oc + 1)],
              pj[:, BW - NBANK:BW], swinv[:, 0:1], sxinv[:, BW - NBANK:BW], AO.mult, AO.mult),
              reads=(pb, "swinv0", "sxinv", "hbank"), writes=("hbank",))

        for sl_i in range(NSLAB):
            t0 = FSLAB * sl_i
            lo = max(0, t0 - 1); hi = min(NQ, t0 + FSLAB + 1)
            hi = min(hi, NOWN + 1) if sl_i == NSLAB - 1 else hi
            off = lo - (t0 - 1)
            nw = hi - lo
            for ic in range(8):
                E("sy", lambda s, ic=ic, lo=lo, nw=nw, off=off: s.dma_start(
                    out=xattw[:, SLABW * ic + off: SLABW * ic + off + nw],
                    in_=xatt_h.ap()[:, NQ * ic + lo: NQ * ic + lo + nw]),
                    reads=("xattdone", "xattw"), writes=("xattw",), dma=True)
            if off:
                E("ve", lambda v: v.memset(bass.AP(xattw, 0, [[8 * SLABW, 128], [SLABW, 8], [1, 1]]), 0.0),
                  reads=("xattw",), writes=("xattw",))
            for (lc0, lw) in ((0, 512), (SLABW - 512, 512)):
                emit_ln(lt4, lw,
                        src=lambda ic, lc0=lc0, lw=lw: (
                            xattw[:, SLABW * ic + lc0:SLABW * ic + lc0 + lw], ("xattw",)),
                        dst_f=lambda ic, lc0=lc0, lw=lw: (
                            ln2f[:, SLABW * ic + lc0:SLABW * ic + lc0 + lw], f"ln2f{ic}"))
            LN2RD = tuple(f"ln2f{ic}" for ic in range(8))
            absmax_rows("xq", lambda ic: (ln2f[:, SLABW * ic:SLABW * (ic + 1)], LN2RD), 8, SLABW, 0)
            E("ve", lambda v: v.tensor_scalar(srow[0:1, 550:550 + SLABW], st32[0:1, 0:SLABW], EPS, None, AO.max),
              reads=("srow", "st32"), writes=("srow",))
            E("ve", lambda v: v.reciprocal(srow[0:1, 0:SLABW], srow[0:1, 550:550 + SLABW]),
              reads=("srow",), writes=("srow",), lazy=True)
            E("ve", lambda v: v.tensor_scalar(srow[0:1, 0:SLABW], srow[0:1, 0:SLABW], 127.0, None, AO.mult),
              reads=("srow",), writes=("srow",))
            E("ve", lambda v: v.tensor_scalar(srow[0:1, 550:550 + SLABW], srow[0:1, 550:550 + SLABW],
              1.0 / 127.0, None, AO.mult), reads=("srow",), writes=("srow",))
            bcast_row(0, SLABW, sbc, "sbc")
            bcast_row(550, SLABW, sxinv, "sxinv")
            quant_cols(ln2f, lambda ic: f"ln2f{ic}", SLABW, sbc[:, 0:SLABW], "sbc",
                       xqb, lambda ic: f"xqb{ic}", SLABW)
            XQRD = tuple(f"xqb{ic}" for ic in range(8))
            for oc in range(32):
                wsl = oc % 2
                E("sy", lambda s, oc=oc, wsl=wsl: s.dma_start(out=w1w[wsl][:, :],
                  in_=bass.AP(w1q_h, 128 * oc, [[8 * DH, 128], [DH, 8], [1, 128]])),
                  reads=("w1done", f"w1w{wsl}"), writes=(f"w1w{wsl}",), dma=True)
                for (c0, nn) in ((0, 512), (SLABW - 128, 128)):
                    pj = PS[4 + oc % 2]; pb = PSN[4 + oc % 2]
                    for ic in range(8):
                        E("te", lambda t, ic=ic, c0=c0, nn=nn, wsl=wsl, pj=pj: t.matmul(
                            pj[:, 0:nn], w1w[wsl][:, 128 * ic:128 * (ic + 1)],
                            xqb[:, SLABW * ic + c0: SLABW * ic + c0 + nn],
                            start=(ic == 0), stop=(ic == 7)),
                            reads=XQRD + (f"w1w{wsl}",),
                            writes=(pb,) if ic in (0, 7) else (), inc=(ic == 7))
                    E("ve", lambda v, oc=oc, c0=c0, nn=nn, pj=pj: v.scalar_tensor_tensor(
                        hslab[:, SW * oc + c0: SW * oc + c0 + nn], pj[:, 0:nn], swinv[:, 0:1],
                        sxinv[:, c0:c0 + nn], AO.mult, AO.mult),
                        reads=(pb, "swinv0", "sxinv", f"hs{oc}", f"h2_{oc}"), writes=(f"hs{oc}", f"h2_{oc}"))
            HSRD = tuple(f"hs{oc}" for oc in range(32))
            if dbg_h is not None and sl_i == 0:
                E("ge", lambda g_: g_.dma_start(out=dbg_h.ap()[:, 1548:1558], in_=hslab[:, 504:514]),
                  reads=HSRD, writes=("dbg3",), dma=True)
            DBGPRE = ("dbg3",) if dbg_h is not None and sl_i == 0 else ()
            for side, extc, selc in ((0, 0, 0), (1, SLABW - 1, 1)):
                E("sy", lambda s, sl_i=sl_i, selc=selc: s.dma_start(
                    out=selw[:, 0:512],
                    in_=(selL_d if selc == 0 else selR_d).ap()[:, 512 * sl_i:512 * (sl_i + 1)]),
                    reads=("selw",), writes=("selw",), dma=True)
                E("ve", lambda v, selc=selc: v.tensor_tensor(
                    amtc[:, 0:512],
                    hbank[:, 0:32 * NBANK], selw[:, 0:512], AO.mult),
                    reads=("hbank", "selw", "amtc"), writes=("amtc",))
                E("ve", lambda v, selc=selc: v.tensor_reduce(
                    ecol[:, 32 * selc:32 * (selc + 1)],
                    bass.AP(amtc, 0, [[544, 128], [NBANK, 32], [1, NBANK]]),
                    AX.X, AO.add), reads=("amtc", "ecol"), writes=("ecol",), lazy=True)
                hcols = bass.AP(hslab, extc, [[32 * SW, 128], [SW, 32], [1, 1]])
                E("ve", lambda v, selc=selc, hcols=hcols: v.tensor_sub(
                    amtc[:, 0:32], ecol[:, 32 * selc:32 * (selc + 1)], hcols),
                    reads=("ecol", "amtc") + HSRD + DBGPRE, writes=("amtc",), lazy=True)
                E("ve", lambda v, sl_i=sl_i, selc=selc: v.tensor_scalar(
                    amtc[:, 0:32], amtc[:, 0:32], cact_s[:, 2 * sl_i + selc:2 * sl_i + selc + 1],
                    None, AO.mult), reads=("amtc", "cact_s"), writes=("amtc",), lazy=True)
                E("ve", lambda v, hcols=hcols: v.tensor_add(hcols, hcols, amtc[:, 0:32]),
                  reads=("amtc",) + HSRD, writes=HSRD, lazy=True)
            for oc in range(32):
                E("ve", lambda v, oc=oc: v.tensor_scalar(tmpc[:, :],
                  hslab[:, SW * oc: SW * oc + FSLAB], cw_s[:, oc:oc + 1], None, AO.mult),
                  reads=(f"hs{oc}", "cw_s", "tmpc"), writes=("tmpc",))
                E("ve", lambda v, oc=oc: v.scalar_tensor_tensor(tmpc[:, :],
                  hslab[:, SW * oc + 1: SW * oc + 1 + FSLAB], cw_s[:, 32 + oc:33 + oc], tmpc[:, :],
                  AO.mult, AO.add), reads=(f"hs{oc}", "cw_s", "tmpc"), writes=("tmpc",))
                E("ve", lambda v, oc=oc: v.scalar_tensor_tensor(tmpc[:, :],
                  hslab[:, SW * oc + 2: SW * oc + 2 + FSLAB], cw_s[:, 64 + oc:65 + oc], tmpc[:, :],
                  AO.mult, AO.add), reads=(f"hs{oc}", "cw_s", "tmpc"), writes=("tmpc",))
                E("se", lambda s, oc=oc: s.activation(hslab[:, SW * oc + 1: SW * oc + 1 + FSLAB],
                  tmpc[:, :], AF.Silu, bias=cb_s[:, oc:oc + 1]),
                  reads=("tmpc", "cb_s", f"hs{oc}"), writes=(f"h2_{oc}",))
            H2RD = tuple(f"h2_{oc}" for oc in range(32))
            absmax_rows("h2", lambda oc: (hslab[:, SW * oc + 1: SW * oc + 1 + FSLAB], H2RD), 32, FSLAB, 0)
            E("ve", lambda v: v.tensor_scalar(srow[0:1, 550:550 + FSLAB], st32[0:1, 0:FSLAB], EPS, None, AO.max),
              reads=("srow", "st32"), writes=("srow",))
            E("ve", lambda v: v.reciprocal(srow[0:1, 0:FSLAB], srow[0:1, 550:550 + FSLAB]),
              reads=("srow",), writes=("srow",), lazy=True)
            E("ve", lambda v: v.tensor_scalar(srow[0:1, 0:FSLAB], srow[0:1, 0:FSLAB], 127.0, None, AO.mult),
              reads=("srow",), writes=("srow",))
            E("ve", lambda v: v.tensor_scalar(srow[0:1, 550:550 + FSLAB], srow[0:1, 550:550 + FSLAB],
              1.0 / 127.0, None, AO.mult), reads=("srow",), writes=("srow",))
            bcast_row(0, FSLAB, s2bc, "s2bc")
            bcast_row(550, FSLAB, s2inv, "s2inv")
            for ch in range(32):
                wsl = ch % 2
                E("sy", lambda s, ch=ch, wsl=wsl: s.dma_start(out=w2w[wsl][:, :],
                  in_=w2q_h.ap()[:, 1024 * ch:1024 * (ch + 1)]),
                  reads=("w2done", f"w2w{wsl}"), writes=(f"w2w{wsl}",), dma=True)
                hq = h2qb[ch % 2]
                E("ve", lambda v, ch=ch: v.tensor_tensor(tmpc[:, :],
                  hslab[:, SW * ch + 1: SW * ch + 1 + FSLAB], s2bc[:, 0:FSLAB], AO.mult),
                  reads=(f"h2_{ch}", "s2bc", "tmpc"), writes=("tmpc",))
                E("ve", lambda v: v.tensor_scalar(tmpc[:, :], tmpc[:, :], MAGIC, MAGIC, AO.add, AO.subtract),
                  reads=("tmpc",), writes=("tmpc",))
                E("ve", lambda v, hq=hq, ch=ch: v.tensor_copy(hq[:, :], tmpc[:, :]),
                  reads=("tmpc", f"hq{ch % 2}"), writes=(f"hq{ch % 2}",))
                for oc in range(8):
                    pj = PS[oc]; pb = PSN[oc]
                    E("te", lambda t, ch=ch, oc=oc, wsl=wsl, hq=hq, pj=pj: t.matmul(
                        pj[:, 0:FSLAB], w2w[wsl][:, 128 * oc:128 * (oc + 1)], hq[:, :],
                        start=(ch == 0), stop=(ch == 31)),
                        reads=(f"hq{ch % 2}", f"w2w{wsl}"),
                        writes=(pb,) if ch in (0, 31) else (), inc=(ch == 31 or oc == 7))
            for oc in range(8):
                pj = PS[oc]; pb = PSN[oc]
                E("ve", lambda v, oc=oc, pj=pj: v.scalar_tensor_tensor(zb2[:, FSLAB * oc: FSLAB * (oc + 1)],
                  pj[:, 0:FSLAB], swinv[:, 1:2], s2inv[:, 0:FSLAB], AO.mult, AO.mult),
                  reads=(pb, "swinv1", "s2inv", "zb2"), writes=("zb2",))
            emit_ln(lt4, FSLAB,
                    src=lambda ic: (zb2[:, FSLAB * ic:FSLAB * (ic + 1)], ("zb2",)),
                    resid=lambda ic: (xattw[:, SLABW * ic + 1: SLABW * ic + 1 + FSLAB], ("xattw",)),
                    res_out=lambda ic: (ybuf[:, FSLAB * ic:FSLAB * (ic + 1)], f"yb{ic}"))
            for ic in range(8):
                E("ge", lambda g_, ic=ic, sl_i=sl_i: g_.dma_start(
                    out=yout.ap()[:, NOWN * ic + FSLAB * sl_i: NOWN * ic + FSLAB * (sl_i + 1)],
                    in_=ybuf[:, FSLAB * ic:FSLAB * (ic + 1)]),
                    reads=(f"yb{ic}",), writes=("ydone",), dma=True)
            if dbg_h is not None and sl_i == 0:
                E("ge", lambda g_: g_.dma_start(out=dbg_h.ap()[:, 0:514], in_=xqb[:, 0:SLABW]),
                  reads=tuple(f"xqb{i}" for i in range(8)), writes=("dbg3",), dma=True)
                E("ge", lambda g_: g_.dma_start(out=dbg_h.ap()[:, 514:1028], in_=hslab[:, 0:SLABW]),
                  reads=tuple(f"h2_{o}" for o in range(32)), writes=("dbg3",), dma=True)
                E("ge", lambda g_: g_.dma_start(out=dbg_h.ap()[:, 1028:1540], in_=zb2[:, 0:512]),
                  reads=("zb2",), writes=("dbg3",), dma=True)
                E("ge", lambda g_: g_.dma_start(out=dbg_h.ap()[:, 1540:1544], in_=s2bc[:, 4:8]),
                  reads=("s2bc",), writes=("dbg3",), dma=True)
                E("ge", lambda g_: g_.dma_start(out=dbg_h.ap()[:, 1544:1548], in_=sbc[:, 4:8]),
                  reads=("sbc",), writes=("dbg3",), dma=True)
                E("ge", lambda g_: g_.dma_start(out=dbg_h.ap()[:, 1558:1622], in_=ecol[:, 0:64]),
                  reads=("ecol",), writes=("dbg3",), dma=True)
                E("ge", lambda g_: g_.dma_start(out=dbg_h.ap()[:, 1622:1630], in_=cact_s[:, 0:8]),
                  reads=("cact_s",), writes=("dbg3",), dma=True)
                E("ge", lambda g_: g_.dma_start(out=dbg_h.ap()[:, 1630:1640], in_=sxinv[:, 504:514]),
                  reads=("sxinv",), writes=("dbg3",), dma=True)
                E("ge", lambda g_: g_.dma_start(out=dbg_h.ap()[:, 1640:1672], in_=selw[:, 0:32]),
                  reads=("selw",), writes=("dbg3",), dma=True)
                E("ge", lambda g_: g_.dma_start(out=dbg_h.ap()[:, 1672:1704], in_=hbank[:, 0:32]),
                  reads=("hbank",), writes=("dbg3",), dma=True)
                E("ge", lambda g_: g_.dma_start(out=dbg_h.ap()[:, 1704:1714], in_=sbc[:, 504:514]),
                  reads=("sbc",), writes=("dbg3",), dma=True)

    ytot = 16 * sch.dfam["ydone"]
    sch.ops["ge"].append(lambda h: h.wait_ge(SEMS["D:ydone"], ytot))
    return _finish(nc, sch, root)


def _finish(nc, sch, root):
    with nc.Block() as block:
        @block.sync
        def _(eng):
            for f in sch.ops["sy"]:
                f(eng)

        @block.tensor
        def _(eng):
            for f in sch.ops["te"]:
                f(eng)

        @block.vector
        def _(eng):
            for f in sch.ops["ve"]:
                f(eng)

        @block.scalar
        def _(eng):
            for f in sch.ops["se"]:
                f(eng)

        @block.gpsimd
        def _(eng):
            for f in sch.ops["ge"]:
                f(eng)

    root.close()
    return nc


SPLIT = np.concatenate([np.arange(0, 64, 2), np.arange(1, 64, 2)])


def _chunkmaj(a):
    rows, cols = a.shape
    nch = rows // 128
    return np.ascontiguousarray(a.reshape(nch, 128, cols).transpose(1, 0, 2).reshape(128, nch * cols))


def _feat(a):
    return _chunkmaj(np.ascontiguousarray(a.T))


_CACHE = {}


def kernel(L0_attn_x, L0_attn_wq, L0_attn_bq, L0_attn_wk, L0_attn_bk,
           L0_attn_wv, L0_attn_bv, L0_attn_wo, L0_attn_bo,
           L0_ffn_w1, L0_ffn_conv_w, L0_ffn_conv_b, L0_ffn_w2):
    x = np.asarray(L0_attn_x, np.float32)
    wq = np.asarray(L0_attn_wq, np.float32); wk = np.asarray(L0_attn_wk, np.float32)
    wv = np.asarray(L0_attn_wv, np.float32); wo = np.asarray(L0_attn_wo, np.float32)
    bq = np.asarray(L0_attn_bq, np.float32); bk = np.asarray(L0_attn_bk, np.float32)
    bv = np.asarray(L0_attn_bv, np.float32); bo = np.asarray(L0_attn_bo, np.float32)
    w1 = np.asarray(L0_ffn_w1, np.float32); w2 = np.asarray(L0_ffn_w2, np.float32)
    cwt = np.asarray(L0_ffn_conv_w, np.float32); cbt = np.asarray(L0_ffn_conv_b, np.float32)

    if "nc" not in _CACHE:
        _CACHE["nc"] = build_program()
    nc = _CACHE["nc"]

    perm = np.concatenate([64 * h + SPLIT for h in range(H)])
    wq_p = wq[:, perm]; wk_p = wk[:, perm]
    bq_p = bq[perm]; bk_p = bk[perm]
    d_wq = _chunkmaj(wq_p); d_wk = _chunkmaj(wk_p)
    d_wv = _chunkmaj(wv); d_wo = _chunkmaj(wo)
    d_bq = np.ascontiguousarray(bq_p.reshape(8, 128).T)
    d_bk = np.ascontiguousarray(bk_p.reshape(8, 128).T)
    d_bo = np.ascontiguousarray(bo.reshape(8, 128).T)
    d_bvr = bv.reshape(1, D)
    d_w1t = _chunkmaj(np.ascontiguousarray(w1.T))
    d_w2t = _chunkmaj(np.ascontiguousarray(w2.T))
    d_cw = np.concatenate([cwt[:, 0, t].reshape(32, 128).T for t in range(3)], axis=1)
    d_cw = np.ascontiguousarray(d_cw)
    d_cb = np.ascontiguousarray(cbt.reshape(32, 128).T)
    d_idf = np.eye(128, dtype=np.float32)

    s8 = 1.0 / np.sqrt(np.sqrt(64.0))
    s8 = 8.0 ** -0.5
    invf = (10000.0 ** (-np.arange(0, 64, 2, dtype=np.float64) / 64.0))

    def rope_tables(pos):
        ang = pos[None, :].astype(np.float64) * invf[:, None]
        c = np.cos(ang) * s8; s = np.sin(ang) * s8
        slot_c = np.concatenate([c, c], axis=0)
        slot_s = np.concatenate([s, -s], axis=0)
        C2 = np.concatenate([slot_c, slot_c], axis=0).astype(np.float32)
        NS = np.concatenate([slot_s, slot_s], axis=0).astype(np.float32)
        return np.ascontiguousarray(C2), np.ascontiguousarray(NS)

    kpos = np.arange(S)
    d_c2k, d_nsk = rope_tables(kpos)

    tri = (np.arange(128)[:, None] <= np.arange(128)[None, :]).astype(ml_dtypes.bfloat16)
    ones128 = np.ones((128, 128), ml_dtypes.bfloat16)
    tri4 = np.ascontiguousarray(np.tile(tri, (1, 4)))
    ones4 = np.ascontiguousarray(np.tile(ones128, (1, 4)))

    bm = np.zeros((128, KBANK * NBANK), ml_dtypes.bfloat16)
    for i in range(KBANK):
        for c in range(NBANK):
            p = BPOS[c]
            valid = (128 * i + np.arange(128)) <= p
            bm[:, NBANK * i + c] = valid.astype(ml_dtypes.bfloat16)
    bm = np.ascontiguousarray(bm)

    per_side = {}
    for side in range(2):
        qpos = np.concatenate([np.arange(side * 2048, side * 2048 + 2048), np.array(BPOS)])
        c2q, nsq = rope_tables(qpos)
        wkill = np.zeros((NKILL, S), ml_dtypes.bfloat16)
        off = 16 * side
        for g in range(G8):
            for l in range(2):
                thr = 128 * (off + 2 * g + l + 1)
                wkill[2 * g + l, thr:] = ml_dtypes.bfloat16(1.0)
        qkill = np.zeros((NKILL, (G8 + 1) * GN), ml_dtypes.bfloat16)
        for g in range(G8):
            for l in range(2):
                qkill[2 * g + l, GN * g + 128 * l: GN * g + 128 * (l + 1)] = ml_dtypes.bfloat16(KILL)
        m1 = tri4 if side == 0 else ones4
        m2 = ones4 if side == 0 else tri4
        selL = np.zeros((128, NSLAB * 512), np.float32)
        selR = np.zeros((128, NSLAB * 512), np.float32)
        cact = np.zeros((128, 2 * NSLAB), np.float32)
        for sl_i in range(NSLAB):
            tL = side * 2048 + FSLAB * sl_i
            tR = tL + FSLAB - 1
            if sl_i == 0:
                if side == 1:
                    pat = np.zeros(NBANK, np.float32); pat[6] = 1.0
                    selL[:, 0:512] = np.tile(pat, 32)[None, :].repeat(128, 0)
                    cact[:, 0] = 1.0
            if sl_i == NSLAB - 1:
                if side == 0:
                    pat = np.zeros(NBANK, np.float32); pat[7] = 1.0
                    selR[:, 512 * sl_i:512 * (sl_i + 1)] = np.tile(pat, 32)[None, :].repeat(128, 0)
                    cact[:, 2 * sl_i + 1] = 1.0
                else:
                    cact[:, 2 * sl_i + 1] = 1.0
        per_side[side] = dict(c2q=np.ascontiguousarray(c2q), nsq=np.ascontiguousarray(nsq),
                              wkill=np.ascontiguousarray(wkill), qkill=np.ascontiguousarray(qkill),
                              m1=np.ascontiguousarray(m1),
                              m2=np.ascontiguousarray(m2), selL=selL, selR=selR, cact=cact)

    in_maps = []
    for core in range(8):
        b = core // 2; side = core % 2
        ps = per_side[side]
        xb = x[b]
        own = xb[side * 2048: side * 2048 + 2048]
        bank = xb[BPOS]
        xloc = np.concatenate([own, bank], axis=0)
        in_maps.append({
            "xful": _feat(xb), "xloc": _feat(xloc),
            "wq": d_wq, "wk": d_wk, "wv": d_wv, "wo": d_wo,
            "bq": d_bq, "bk": d_bk, "bvr": d_bvr, "bo": d_bo,
            "w1t": d_w1t, "w2t": d_w2t, "cw": d_cw, "cb": d_cb,
            "c2q": ps["c2q"], "nsq": ps["nsq"], "c2k": d_c2k, "nsk": d_nsk,
            "wkill": ps["wkill"], "qkill": ps["qkill"], "m1": ps["m1"], "m2": ps["m2"], "bmask": bm,
            "selL": ps["selL"], "selR": ps["selR"], "cact": ps["cact"], "idf": d_idf,
        })

    res = run_bass_kernel_spmd(nc, in_maps, core_ids=list(range(8)))

    out = np.empty((B, S, D), np.float32)
    for core in range(8):
        b = core // 2; side = core % 2
        y = res.results[core]["y"]
        yt = y.reshape(128, 8, NOWN).transpose(1, 0, 2).reshape(D, NOWN)
        out[b, side * 2048: side * 2048 + 2048] = yt.T
    return out


# version 12
# speedup vs baseline: 1.0387x; 1.0387x over previous
import numpy as np
import ml_dtypes
import concourse.bass as bass
import concourse.mybir as mybir
from concourse.bass_utils import run_bass_kernel_spmd

F32 = mybir.dt.float32
BF16 = mybir.dt.bfloat16
AO = mybir.AluOpType
AF = mybir.ActivationFunctionType
AX = mybir.AxisListType

MAGIC = 12582912.0
KILL = -100000.0
EPS = 1e-5

B, S, D, H, DH = 4, 4096, 1024, 16, 4096
NOWN, NBANK = 2048, 16
NQ = NOWN + NBANK
G8, GN = 8, 256
KBANK = 29
NKILL = 2 * G8
KB0 = 96
KC = KB0 + NKILL

BPOS = [511, 512, 1023, 1024, 1535, 1536, 2047, 2048,
        2559, 2560, 3071, 3072, 3583, 3584, 3584, 3584]

NEED_A = [(256 * (g + 1) + 127) // 128 for g in range(G8)]
NEED_B = [(2048 + 256 * (g + 1) + 127) // 128 for g in range(G8)]
KSLOT = [max(a, b) for a, b in zip(NEED_A, NEED_B)]

FSLAB = 512
NSLAB = NOWN // FSLAB
SLABW = FSLAB + 2
SW = SLABW

ENGS = ("sy", "te", "ve", "se", "ge")
SEMS = {}
VJUNK = None


class Sch:

    def __init__(self):
        self.ops = {e: [] for e in ENGS}
        self.cnt = {e: 0 for e in ENGS}
        self.dfam = {}
        self.mkdsem = None
        self.bufs = {}
        self.lastw = {e: {} for e in ENGS}
        self.bar = None

    def barrier(self):
        self.bar = (dict(self.cnt), dict(self.dfam), set())

    def _wait(self, eng, semkey, val):
        if val <= self.lastw[eng].get(semkey, 0):
            return
        self.lastw[eng][semkey] = val
        self.ops[eng].append(lambda h, semkey=semkey, val=val: h.wait_ge(SEMS[semkey], val))

    def _resolve(self, sk, v):
        if sk.startswith("D:") and v < 0:
            return 16 * self.dfam[sk[2:]]
        return v

    def emit(self, eng, fn, reads=(), writes=(), dma=False, inc=True, lazy=False):
        if self.bar is not None and eng not in self.bar[2]:
            sc, sd, applied = self.bar
            for e2, v in sc.items():
                if e2 != eng and v > 0:
                    self._wait(eng, e2, v)
            for f, c in sd.items():
                if c > 0:
                    self._wait(eng, "D:" + f, 16 * c)
            applied.add(eng)
            if len(applied) == len(ENGS):
                self.bar = None
        need = {}
        for b in reads:
            rec = self.bufs.get(b)
            if rec and rec["w"]:
                sk, v = rec["w"]
                if sk != eng:
                    need[sk] = max(need.get(sk, 0), self._resolve(sk, v))
        for b in writes:
            rec = self.bufs.get(b)
            if rec:
                if rec["w"]:
                    sk, v = rec["w"]
                    if sk != eng:
                        need[sk] = max(need.get(sk, 0), self._resolve(sk, v))
                for sk, v in rec["r"].items():
                    if sk != eng:
                        need[sk] = max(need.get(sk, 0), self._resolve(sk, v))
        for sk, v in sorted(need.items()):
            self._wait(eng, sk, v)
        if dma:
            fam = writes[0]
            if fam not in self.dfam:
                self.dfam[fam] = 0
                self.mkdsem(fam)
            self.dfam[fam] += 1
            semkey, val = "D:" + fam, -1
            self.ops[eng].append(lambda h, fn=fn, semkey=semkey: fn(h).then_inc(SEMS[semkey], 16))
        elif inc:
            self.cnt[eng] += 1
            semkey, val = eng, self.cnt[eng]
            self.ops[eng].append(lambda h, fn=fn, semkey=semkey: fn(h).then_inc(SEMS[semkey], 1))
            if lazy:
                for _ in range(4):
                    self.cnt[eng] += 1
                    self.ops[eng].append(lambda h, semkey=semkey: h.memset(
                        VJUNK[:, :], 0.25).then_inc(SEMS[semkey], 1))
                val = self.cnt[eng]
        else:
            semkey, val = eng, self.cnt[eng] + 1
            self.ops[eng].append(fn)
        for b in writes:
            self.bufs[b] = {"w": (semkey, val), "r": {}}
        for b in reads:
            rec = self.bufs.setdefault(b, {"w": None, "r": {}})
            if semkey.startswith("D:"):
                rec["r"][semkey] = -1
            else:
                rec["r"][semkey] = max(rec["r"].get(semkey, 0), val)


def build_program():
    import contextlib
    nc = bass.Bass("TRN2", target_bir_lowering=False, detect_race_conditions=False)
    sch = Sch()
    E = sch.emit
    root = contextlib.ExitStack()

    din = lambda name, shape, dt=F32: nc.dram_tensor(name, shape, dt, kind="ExternalInput")
    xful = din("xful", [128, 8 * S])
    xloc = din("xloc", [128, 8 * NQ])
    wq_d = din("wq", [128, 8 * D]); wk_d = din("wk", [128, 8 * D])
    wv_d = din("wv", [128, 8 * D]); wo_d = din("wo", [128, 8 * D])
    bq_d = din("bq", [128, 8]); bk_d = din("bk", [128, 8])
    bvr_d = din("bvr", [1, D]); bo_d = din("bo", [128, 8])
    w1t_d = din("w1t", [128, 8 * DH])
    w2t_d = din("w2t", [128, 32 * D])
    cw_d = din("cw", [128, 3 * 32]); cb_d = din("cb", [128, 32])
    c2q_d = din("c2q", [128, NQ]); nsq_d = din("nsq", [128, NQ])
    c2k_d = din("c2k", [128, S]); nsk_d = din("nsk", [128, S])
    wkill_d = din("wkill", [NKILL, S], BF16)
    qkill_d = din("qkill", [NKILL, (G8 + 1) * GN], BF16)
    m1_d = din("m1", [128, 512], BF16); m2_d = din("m2", [128, 512], BF16)
    bmask_d = din("bmask", [128, KBANK * NBANK], BF16)
    selL_d = din("selL", [128, NSLAB * 512])
    selR_d = din("selR", [128, NSLAB * 512])
    cact_d = din("cact", [128, 2 * NSLAB])
    idf_d = din("idf", [128, 128])
    yout = nc.dram_tensor("y", [128, 8 * NOWN], F32, kind="ExternalOutput")

    import os as _osd
    _k = dict(kind="ExternalOutput") if _osd.environ.get("K_DBG_OUT") == "1" else {}
    ln1f_h = nc.dram_tensor("ln1f_h", [128, 8 * S], BF16, **_k)
    ln1o_h = nc.dram_tensor("ln1o_h", [128, 8 * NQ], BF16, **_k)
    xatt_h = nc.dram_tensor("xatt_h", [128, 8 * NQ], F32, **_k)
    obf_h = nc.dram_tensor("obf_h", [128, 8 * NQ], BF16, **_k)
    w1q_h = nc.dram_tensor("w1q_h", [128, 8 * DH], BF16, **_k)
    w2q_h = nc.dram_tensor("w2q_h", [128, 32 * D], BF16, **_k)
    dbg_h = nc.dram_tensor("dbg_h", [128, 2304], F32, **_k) if _k else None
    dbg2_h = nc.dram_tensor("dbg2_h", [128, 4 * KBANK * NBANK + 64], BF16, **_k) if _k else None

    sb = lambda st, name, shape, dt=F32: st.enter_context(nc.sbuf_tensor(name, shape, dt))
    SEMS.clear()
    for e in ENGS:
        SEMS[e] = root.enter_context(nc.semaphore(f"sem_{e}"))

    def _mkdsem(fam):
        SEMS["D:" + fam] = root.enter_context(nc.semaphore(f"d_{len(SEMS)}"))
    sch.mkdsem = _mkdsem

    onesbf = sb(root, "onesbf", [128, 128], BF16)
    onesf = sb(root, "onesf", [1, 128])
    onesw = sb(root, "onesw", [1, 512])
    idf_s = sb(root, "idf_s", [128, 128])
    selq = [sb(root, f"selq{j}", [128, 128]) for j in range(2)]
    vjunk = sb(root, "vjunk", [32, 32])
    global VJUNK
    VJUNK = vjunk
    hs4 = [sb(root, f"hs4{j}", [128, 4], BF16) for j in range(2)]
    biases = sb(root, "biases", [128, 24])
    cw_s = sb(root, "cw_s", [128, 3 * 32]); cb_s = sb(root, "cb_s", [128, 32])
    cact_s = sb(root, "cact_s", [128, 2 * NSLAB])
    swinv = sb(root, "swinv", [128, 2])
    m1_s = sb(root, "m1_s", [128, 512], BF16); m2_s = sb(root, "m2_s", [128, 512], BF16)
    bmask_s = sb(root, "bmask_s", [128, KBANK * NBANK], BF16)
    PS = [root.enter_context(nc.psum_tensor(f"ps{i}", [128, 512], F32)) for i in range(8)]
    PSN = [f"PS{i}" for i in range(8)]

    E("ve", lambda v: v.memset(onesbf[:, :], 1.0), writes=("onesbf",))

    def vguard():
        for _ in range(4):
            E("ve", lambda v: v.memset(vjunk[:, :], 0.25), reads=("vjunk",), writes=("vjunk",))
    E("ve", lambda v: v.memset(onesf[:, :], 1.0), writes=("onesf",))
    E("ve", lambda v: v.memset(onesw[:, :], 1.0), writes=("onesw",))
    E("sy", lambda s: s.dma_start(out=idf_s[:, :], in_=idf_d.ap()), writes=("idf_s",), dma=True)
    E("sy", lambda s: s.dma_start(out=m1_s[:, :], in_=m1_d.ap()), writes=("m1_s",), dma=True)
    E("sy", lambda s: s.dma_start(out=m2_s[:, :], in_=m2_d.ap()), writes=("m2_s",), dma=True)
    E("sy", lambda s: s.dma_start(out=bmask_s[:, :], in_=bmask_d.ap()), writes=("bmask_s",), dma=True)
    E("sy", lambda s: s.dma_start(out=biases[:, 0:8], in_=bq_d.ap()), writes=("bias_q",), dma=True)
    E("sy", lambda s: s.dma_start(out=biases[:, 8:16], in_=bk_d.ap()), writes=("bias_k",), dma=True)
    E("sy", lambda s: s.dma_start(out=biases[:, 16:24], in_=bo_d.ap()), writes=("bias_o",), dma=True)
    E("sy", lambda s: s.dma_start(out=cw_s[:, :], in_=cw_d.ap()), writes=("cw_s",), dma=True)
    E("sy", lambda s: s.dma_start(out=cb_s[:, :], in_=cb_d.ap()), writes=("cb_s",), dma=True)
    E("sy", lambda s: s.dma_start(out=cact_s[:, :], in_=cact_d.ap()), writes=("cact_s",), dma=True)
    for j in range(2):
        h_ = hs4[j]
        E("ve", lambda v, h_=h_: v.memset(h_[:, :], 0.0), writes=(f"hs4{j}",))
        E("ve", lambda v, h_=h_, j=j: v.memset(h_[0:64, 2 * j:2 * j + 1], 1.0),
          reads=(f"hs4{j}",), writes=(f"hs4{j}",))
        E("ve", lambda v, h_=h_, j=j: v.memset(h_[64:128, 2 * j + 1:2 * j + 2], 1.0),
          reads=(f"hs4{j}",), writes=(f"hs4{j}",))
        q_ = selq[j]
        E("ve", lambda v, q_=q_: v.memset(q_[:, :], 0.0), writes=(f"selq{j}",))
        E("ve", lambda v, q_=q_, j=j: v.memset(q_[64 * j:64 * j + 1, 0:64], 1.0),
          reads=(f"selq{j}",), writes=(f"selq{j}",))
        E("ve", lambda v, q_=q_, j=j: v.memset(q_[64 * j + 32:64 * j + 33, 64:128], 1.0),
          reads=(f"selq{j}",), writes=(f"selq{j}",))

    _ln_n = [0]

    LNW = 544

    def make_ln(st):
        i = _ln_n[0]; _ln_n[0] += 1
        lt = {"id": str(i),
              "xbf": sb(st, f"lnxbf{i}", [128, 2 * LNW], BF16),
              "sq": sb(st, f"lnsq{i}", [128, 2 * LNW], BF16),
              "tmp": sb(st, f"lntmp{i}", [128, 4 * LNW])}
        if dbg_h is not None and i == 0:
            for gap0 in (512, LNW + 512, 2 * LNW + 512):
                E("ve", lambda v, lt=lt, gap0=gap0: v.memset(lt["tmp"][:, gap0:gap0 + LNW - 512], 0.0),
                  writes=(f"lnm{i}",))
        return lt

    def emit_ln(lt, n, src, dst_bf=None, dst_f=None, resid=None, res_out=None, banks=(6, 7)):
        li = lt["id"]
        psA, psB = PS[banks[0]], PS[banks[1]]
        fA, fB = f"PS{banks[0]}", f"PS{banks[1]}"
        for ic in range(8):
            ap, rd = src(ic)
            sl = ic % 2
            xb = lt["xbf"][:, LNW * sl:LNW * sl + n]
            sq = lt["sq"][:, LNW * sl:LNW * sl + n]
            bxb, bsq2 = f"lnxb{sl}_{li}", f"lnsq{sl}_{li}"
            E("se", lambda s, ap=ap, xb=xb: s.copy(xb, ap), reads=rd + (bxb,), writes=(bxb,))
            E("se", lambda s, ap=ap, sq=sq: s.activation(sq, ap, AF.Square), reads=rd + (bsq2,), writes=(bsq2,))
            E("te", lambda t_, xb=xb, ic=ic, n=n: t_.matmul(psA[:, 0:n], onesbf[:, :], xb,
              start=(ic == 0), stop=(ic == 7)),
              reads=(bxb, "onesbf"),
              writes=(fA,) if ic in (0, 7) else (), inc=True)
            E("te", lambda t_, sq=sq, ic=ic, n=n: t_.matmul(psB[:, 0:n], onesbf[:, :], sq,
              start=(ic == 0), stop=(ic == 7)),
              reads=(bsq2, "onesbf"),
              writes=(fB,) if ic in (0, 7) else (), inc=True)
        tmp = lt["tmp"]
        mb = tmp[:, 0:n]; wv_ = tmp[:, LNW:LNW + n]; rb = tmp[:, 2 * LNW:2 * LNW + n]; dd = tmp[:, 3 * LNW:3 * LNW + n]
        E("ve", lambda v: v.tensor_scalar(mb, psA[:, 0:n], 1.0 / D, None, AO.mult),
          reads=(fA, f"lnm{li}"), writes=(f"lnm{li}",))
        E("ve", lambda v: v.tensor_mul(wv_, mb, mb), reads=(f"lnm{li}", f"lnw{li}"), writes=(f"lnw{li}",))
        E("ve", lambda v: v.scalar_tensor_tensor(wv_, psB[:, 0:n], 1.0 / D, wv_, AO.mult, AO.subtract),
          reads=(fB, f"lnw{li}"), writes=(f"lnw{li}",))
        E("ve", lambda v: v.tensor_scalar_add(wv_, wv_, EPS), reads=(f"lnw{li}",), writes=(f"lnw{li}",))
        E("se", lambda s: s.activation(rb, wv_, AF.Ln), reads=(f"lnw{li}", f"lnr{li}"), writes=(f"lnr{li}",))
        E("se", lambda s: s.activation(rb, rb, AF.Exp, scale=-0.5), reads=(f"lnr{li}",), writes=(f"lnr{li}",))
        for ic in range(8):
            ap, rd = src(ic)
            dn = f"lnd{li}"
            E("ve", lambda v, ap=ap: v.tensor_sub(dd, ap, mb), reads=rd + (f"lnm{li}", dn), writes=(dn,))
            if dst_bf is not None:
                oap, ob = dst_bf(ic)
                E("ve", lambda v, oap=oap: v.tensor_mul(oap, dd, rb), reads=(dn, f"lnr{li}"), writes=(ob,))
            if dst_f is not None:
                oap, ob = dst_f(ic)
                E("ve", lambda v, oap=oap: v.tensor_mul(oap, dd, rb), reads=(dn, f"lnr{li}"), writes=(ob,))
            if resid is not None:
                rap, rrd = resid(ic)
                oap, ob = res_out(ic)
                E("ve", lambda v: v.tensor_mul(dd, dd, rb), reads=(dn, f"lnr{li}"), writes=(dn,))
                E("ve", lambda v, oap=oap, rap=rap: v.tensor_add(oap, dd, rap),
                  reads=(dn,) + rrd, writes=(ob,))

    with contextlib.ExitStack() as p1:
        lt1 = [make_ln(p1), make_ln(p1)]
        xin = [sb(p1, f"xinb{i}", [128, 8 * 512]) for i in range(2)]
        lnout = [sb(p1, f"lnoutb{i}", [128, 8 * 512], BF16) for i in range(2)]
        _p1w = [0]

        def p1_pass(src_d, dst_d, total, tag0):
            for g in range((total + 511) // 512):
                c0 = min(512 * g, total - 512)
                n = 512
                sl = _p1w[0] % 2; _p1w[0] += 1
                xi = xin[sl]; lo = lnout[sl]
                for ic in range(8):
                    E("sy", lambda s, ic=ic, c0=c0, n=n, xi=xi, sl=sl: s.dma_start(
                        out=xi[:, 512 * ic:512 * ic + n],
                        in_=src_d.ap()[:, total * ic + c0: total * ic + c0 + n]),
                        reads=(f"xinw{sl}",), writes=(f"xinw{sl}",), dma=True)
                emit_ln(lt1[sl], n,
                        src=lambda ic, xi=xi, sl=sl: (xi[:, 512 * ic:512 * ic + n], (f"xinw{sl}",)),
                        dst_bf=lambda ic, lo=lo, sl=sl: (lo[:, 512 * ic:512 * ic + n], f"lno{ic}_{sl}"),
                        banks=(6, 7) if sl == 0 else (4, 5))
                for ic in range(8):
                    E("ge", lambda gg, ic=ic, c0=c0, n=n, lo=lo, sl=sl: gg.dma_start(
                        out=dst_d.ap()[:, total * ic + c0: total * ic + c0 + n],
                        in_=lo[:, 512 * ic:512 * ic + n]),
                        reads=(f"lno{ic}_{sl}",), writes=(f"{tag0}done",), dma=True)

        p1_pass(xful, ln1f_h, S, "lf")
        p1_pass(xloc, ln1o_h, NQ, "lo")
        if dbg_h is not None:
            for _s in range(2):
                E("ve", lambda v, _s=_s: v.tensor_copy(lt1[0]["tmp"][:, 1900 + 16 * _s:1916 + 16 * _s],
                  lt1[0]["sq"][:, 544 * _s:544 * _s + 16]),
                  reads=(f"lnsq{_s}_0", "lnm0"), writes=("lnm0",), inc=True)
                E("ve", lambda v, _s=_s: v.tensor_copy(lt1[0]["tmp"][:, 1932 + 16 * _s:1948 + 16 * _s],
                  lt1[0]["xbf"][:, 544 * _s:544 * _s + 16]),
                  reads=(f"lnxb{_s}_0", "lnm0"), writes=("lnm0",), inc=True)
            E("ge", lambda g: g.dma_start(out=dbg_h.ap()[:, 0:2048], in_=lt1[0]["tmp"][:, 0:2048]),
              reads=("lnm0", "lnw0", "lnr0", "lnd0"), writes=("dbgdone",), dma=True)
            E("ge", lambda g: g.dma_start(out=dbg_h.ap()[:, 2048:2064], in_=xin[0][:, 0:16]),
              reads=("xinw0",), writes=("dbgdone",), dma=True)
    sch.barrier()

    import os as _os0
    _DBG_STOP = _os0.environ.get("K_DBG_STOP") == "1"
    with contextlib.ExitStack() as pw_:
        pw = root if _DBG_STOP else pw_
        WTW = 1024
        wbuf = [sb(pw, f"wbufs{i}", [128, WTW]) for i in range(2)]
        wqb = [sb(pw, f"wqbs{i}", [128, WTW], BF16) for i in range(2)]
        wstat = sb(pw, "wstat", [128, 56])
        wtr = sb(pw, "wtr", [128, 32])
        wtrT = sb(pw, "wtrT", [32, 128])

        def quant_weights(src_d, dst_d, ncols, tag, sw_col):
            nt = ncols // WTW
            pieces = []

            def init_p():
                E("ve", lambda v: v.memset(wstat[:, 0:56], 0.0), reads=("wstat",), writes=("wstat",))
                E("ve", lambda v: v.memset(wtr[:, 0:32], 0.0), reads=("wtr",), writes=("wtr",))
            pieces.append(init_p)

            def mean_tile(it):
                sl = it % 2
                E("sy", lambda s, it=it, sl=sl: s.dma_start(out=wbuf[sl][:, :],
                  in_=src_d.ap()[:, WTW * it:WTW * (it + 1)]),
                  reads=(f"wbufs{sl}",), writes=(f"wbufs{sl}",), dma=True)
                E("ve", lambda v, it=it, sl=sl: v.tensor_reduce(wstat[:, 20 + it:21 + it], wbuf[sl][:, :],
                  AX.X, AO.add, apply_absolute_value=True),
                  reads=(f"wbufs{sl}", "wstat"), writes=("wstat",), lazy=True)
            for it in range(nt):
                pieces.append(lambda it=it: mean_tile(it))

            def finisher():
                E("ve", lambda v: v.tensor_reduce(wstat[:, 16:17], wstat[:, 20:20 + nt], AX.X, AO.add),
                  reads=("wstat",), writes=("wstat",), lazy=True)
                E("ve", lambda v: v.tensor_copy(wtr[:, 0:1], wstat[:, 16:17]),
                  reads=("wstat", "wtr"), writes=("wtr",), lazy=True)
                vguard()
                for kb_ in range(4):
                    E("ve", lambda v, kb_=kb_: v.transpose(wtrT[0:32, 32 * kb_:32 * kb_ + 32],
                      wtr[32 * kb_:32 * kb_ + 32, 0:32]),
                      reads=("wtr", "wtrT"), writes=("wtrT",), lazy=True)
                vguard()
                E("ve", lambda v: v.tensor_reduce(wstat[0:1, 17:18], wtrT[0:1, 0:128], AX.X, AO.add),
                  reads=("wtrT", "wstat"), writes=("wstat",), lazy=True)
                E("ve", lambda v: v.tensor_scalar(wstat[0:1, 17:18], wstat[0:1, 17:18], 1.0 / (ncols * 128.0), EPS,
                  AO.mult, AO.max), reads=("wstat",), writes=("wstat",), lazy=True)
                E("te", lambda t: t.matmul(PS[6][:, 0:1], onesf[0:1, :], wstat[0:1, 17:18], start=True, stop=True),
                  reads=("wstat", "onesf", "PS6"), writes=("PS6",))
                E("ve", lambda v: v.tensor_copy(swinv[:, sw_col:sw_col + 1], PS[6][:, 0:1]),
                  reads=("PS6",), writes=(f"swinv{sw_col}",), lazy=True)
                E("ve", lambda v: v.reciprocal(wstat[0:1, 18:19], wstat[0:1, 17:18]), reads=("wstat",), writes=("wstat",), lazy=True)
                E("te", lambda t: t.matmul(PS[6][:, 0:1], onesf[0:1, :], wstat[0:1, 18:19], start=True, stop=True),
                  reads=("wstat", "onesf", "PS6"), writes=("PS6",))
                E("ve", lambda v: v.tensor_copy(wstat[:, 19:20], PS[6][:, 0:1]), reads=("PS6", "wstat"), writes=("wsbc" + tag,), lazy=True)
            pieces.append(finisher)

            def quant_tile(it):
                sl = it % 2
                E("sy", lambda s, it=it, sl=sl: s.dma_start(out=wbuf[sl][:, :],
                  in_=src_d.ap()[:, WTW * it:WTW * (it + 1)]),
                  reads=(f"wbufs{sl}",), writes=(f"wbufs{sl}",), dma=True)
                E("ve", lambda v, sl=sl: v.tensor_scalar(wbuf[sl][:, :], wbuf[sl][:, :], wstat[:, 19:20], MAGIC, AO.mult, AO.add),
                  reads=(f"wbufs{sl}", "wsbc" + tag), writes=(f"wbufs{sl}",))
                E("ve", lambda v, sl=sl: v.tensor_scalar(wbuf[sl][:, :], wbuf[sl][:, :], MAGIC, 1.0, AO.subtract, AO.min),
                  reads=(f"wbufs{sl}",), writes=(f"wbufs{sl}",))
                E("ve", lambda v, sl=sl: v.tensor_scalar(wqb[sl][:, :], wbuf[sl][:, :], -1.0, None, AO.max),
                  reads=(f"wbufs{sl}", f"wqbs{sl}"), writes=(f"wqbs{sl}",))
                E("ge", lambda g, it=it, sl=sl: g.dma_start(out=dst_d.ap()[:, WTW * it:WTW * (it + 1)], in_=wqb[sl][:, :]),
                  reads=(f"wqbs{sl}",), writes=(f"{tag}done",), dma=True)
            for it in range(nt):
                pieces.append(lambda it=it: quant_tile(it))
            return pieces

        wq_pieces = quant_weights(w1t_d, w1q_h, 8 * DH, "w1", 0)
        if dbg_h is not None:
            def dbg_wq():
                E("ge", lambda g: g.dma_start(out=dbg_h.ap()[:, 2064:2112], in_=wstat[:, 0:48]),
                  reads=("wstat", "wsbcw1"), writes=("dbg2",), dma=True)
                E("ge", lambda g: g.dma_start(out=dbg_h.ap()[0:32, 2112:2240], in_=wtrT[:, :]),
                  reads=("wtrT",), writes=("dbg2",), dma=True)
            wq_pieces.append(dbg_wq)
        wq_pieces += quant_weights(w2t_d, w2q_h, 32 * D, "w2", 1)
        _wq_cursor = [0]

        def emit_wq(npiece):
            hi = min(_wq_cursor[0] + npiece, len(wq_pieces))
            for pi in range(_wq_cursor[0], hi):
                wq_pieces[pi]()
            _wq_cursor[0] = hi

        bvhl = sb(pw, "bvhl", [1, 2 * D], BF16)
        bvf = sb(pw, "bvf", [1, D])
        kaug = [sb(pw, f"kaug{h}", [KC, S], BF16) for h in range(4)]
        vqt = sb(pw, "vqt", [128, 32 * 260], BF16)
        qaug = [[sb(pw, f"qa{h}_{g}", [KC, GN if g < G8 else NBANK], BF16)
                 for g in range(G8 + 1)] for h in range(4)]
        lnfw = [sb(pw, f"lnfw{s_}", [128, 8 * 512], BF16) for s_ in range(2)]
        lnow = [sb(pw, f"lnow{s_}", [128, 8 * GN], BF16) for s_ in range(2)]
        rkw = [sb(pw, f"rkw{s_}", [128, 2 * 512]) for s_ in range(2)]
        rqw = [sb(pw, f"rqw{s_}", [128, 2 * GN]) for s_ in range(2)]
        wqkv = sb(pw, "wqkv", [128, 256])
        wqkv_b = sb(pw, "wqkv_b", [128, 8 * 256 * 3], BF16)
        xsb = sb(pw, "xsb", [128, 3 * 512])
        sqb = sb(pw, "sqb", [128, 512], BF16)
        p4t = sb(pw, "p4t", [128, 3 * 4 * GN], BF16)
        pbank = sb(pw, "pbank", [128, 4 * KBANK * NBANK], BF16)
        oraw = sb(pw, "oraw", [128, 2 * NQ])
        drawt = sb(pw, "drawt", [128, NQ])
        dinvb = sb(pw, "dinvb", [128, NQ])
        obf = sb(pw, "obf", [128, 2 * NQ], BF16)
        nstat = sb(pw, "nstat", [128, 40])
        ntr = sb(pw, "ntr", [32, 32])
        ntrT = sb(pw, "ntrT", [32, 32])
        ropeT = sb(pw, "ropeT", [128, 512]); ropeT2 = sb(pw, "ropeT2", [128, 512])
        valr = sb(pw, "valr", [1, 8])

        E("ve", lambda v: v.memset(bass.AP(vqt, 64, [[260 * 32, 128], [65, 128], [1, 1]]), 1.0),
          writes=("vq_ones",))
        E("sy", lambda s: s.dma_start(out=bvf[:, :], in_=bvr_d.ap()), writes=("bvf",), dma=True)
        E("ve", lambda v: v.tensor_copy(bvhl[0:1, 0:D], bvf[0:1, :]), reads=("bvf",), writes=("bvhl_h",))
        E("ve", lambda v: v.tensor_sub(bvhl[0:1, D:2 * D], bvf[0:1, :], bvhl[0:1, 0:D]),
          reads=("bvf", "bvhl_h"), writes=("bvhl_l",))

        def emit_rope(xs, xs_c0, ctab, cc0, stab, sc0, n, dsts, rd):
            E("ve", lambda v: v.tensor_mul(ropeT[:, 0:n], xs[:, xs_c0:xs_c0 + n], ctab[:, cc0:cc0 + n]),
              reads=rd + ("ropeT",), writes=("ropeT",))
            for b_ in (0, 64):
                E("ve", lambda v, b_=b_: v.tensor_mul(ropeT2[b_:b_ + 32, 0:n],
                  xs[b_ + 32:b_ + 64, xs_c0:xs_c0 + n], stab[b_ + 32:b_ + 64, sc0:sc0 + n]),
                  reads=rd + ("ropeT2",), writes=("ropeT2",))
                E("ve", lambda v, b_=b_: v.tensor_mul(ropeT2[b_ + 32:b_ + 64, 0:n],
                  xs[b_:b_ + 32, xs_c0:xs_c0 + n], stab[b_:b_ + 32, sc0:sc0 + n]),
                  reads=rd + ("ropeT2",), writes=("ropeT2",))
            for i, (dt_, p0, c0, dbuf) in enumerate(dsts):
                b_ = 64 * i
                E("ve", lambda v, dt_=dt_, p0=p0, c0=c0, b_=b_: v.tensor_add(
                    dt_[p0:p0 + 64, c0:c0 + n], ropeT[b_:b_ + 64, 0:n], ropeT2[b_:b_ + 64, 0:n]),
                    reads=("ropeT", "ropeT2", dbuf), writes=(dbuf,))

        import os as _os
        _QORD = [int(c) for c in _os.environ.get("K_QORDER", "0123")]
        _QPH = int(_os.environ.get("K_QPHASES", "9"))
        _QW = _os.environ.get("K_QW", "1") == "1"
        _Q64 = _os.environ.get("K_Q64", "1") == "1"
        _QMS = _os.environ.get("K_QMS", "1") == "1"
        for _Qi, Q in enumerate(_QORD):
            emit_wq(34 if _Qi < 3 else len(wq_pieces))
            for ic in (range(8) if (_Qi == 0 or _QW) else ()):
                for wi, wd in enumerate((wq_d, wk_d, wv_d)):
                    E("sy", lambda s, ic=ic, wd=wd, Q=Q: s.dma_start(
                        out=wqkv[:, 0:256],
                        in_=wd.ap()[:, D * ic + 256 * Q: D * ic + 256 * Q + 256]),
                        reads=("wqkv",), writes=("wqkv",), dma=True)
                    E("ve", lambda v, ic=ic, wi=wi: v.tensor_copy(
                        wqkv_b[:, 768 * ic + 256 * wi: 768 * ic + 256 * wi + 256], wqkv[:, 0:256]),
                        reads=("wqkv", f"wb{wi}_{ic}"), writes=(f"wb{wi}_{ic}",))
            WBap = lambda wi, ic, j, w=128: wqkv_b[:, 768 * ic + 256 * wi + 128 * j: 768 * ic + 256 * wi + 128 * j + w]
            WBrd = lambda wi, ic: (f"wb{wi}_{ic}",)

            if _Qi == 0:
                for h in range(4):
                    E("ve", lambda v, h=h: v.memset(kaug[h][64:96, :], 0.0),
                      reads=(f"ka{h}", f"kc{h}"), writes=(f"kc{h}",))
                    E("sy", lambda s, h=h: s.dma_start(out=kaug[h][KB0:KB0 + NKILL, :], in_=wkill_d.ap()),
                      reads=(f"kkill{h}",), writes=(f"kkill{h}",), dma=True)
            if _Qi == 0 or _QMS:
                E("ve", lambda v: v.memset(nstat[:, 0:40], 0.0), reads=("nstat",), writes=("nstat",))
                E("ve", lambda v: v.memset(ntr[:, :], 0.0), reads=("ntr",), writes=("ntr",))
                E("ve", lambda v: v.memset(drawt[:, :], 1.0), reads=("draw",), writes=("draw",))

            for tg in (range(8) if (_Qi == 0 or _QPH >= 1) else ()):
                sl = tg % 2
                lf = lnfw[sl]
                for ic in range(8):
                    E("sy", lambda s, ic=ic, tg=tg, lf=lf: s.dma_start(
                        out=lf[:, 512 * ic:512 * (ic + 1)],
                        in_=ln1f_h.ap()[:, S * ic + 512 * tg: S * ic + 512 * (tg + 1)]),
                        reads=("lfdone", f"lnfw{sl}"), writes=(f"lnfw{sl}",), dma=True)
                rk = rkw[sl]
                E("sy", lambda s, tg=tg, rk=rk: s.dma_start(out=rk[:, 0:512],
                  in_=c2k_d.ap()[:, 512 * tg:512 * (tg + 1)]), reads=(f"rkw{sl}",), writes=(f"rkw{sl}",), dma=True)
                E("sy", lambda s, tg=tg, rk=rk: s.dma_start(out=rk[:, 512:1024],
                  in_=nsk_d.ap()[:, 512 * tg:512 * (tg + 1)]), reads=(f"rkw{sl}",), writes=(f"rkw{sl}",), dma=True)
                for j in range(2):
                    pj = PS[4 + sl]; pb = PSN[4 + sl]
                    for ic in range(8):
                        E("te", lambda t, ic=ic, j=j, lf=lf, pj=pj: t.matmul(
                            pj[:, 0:512], WBap(1, ic, j), lf[:, 512 * ic:512 * (ic + 1)],
                            start=(ic == 0), stop=(ic == 7)),
                            reads=(f"lnfw{sl}",) + WBrd(1, ic),
                            writes=(pb,) if ic in (0, 7) else (), inc=(ic == 7))
                    xcol = 512 * sl
                    E("se", lambda s, pj=pj, j=j, xcol=xcol, Q=Q: s.activation(
                        xsb[:, xcol:xcol + 512], pj[:, 0:512], AF.Identity,
                        bias=biases[:, 8 + 2 * Q + j:9 + 2 * Q + j]),
                        reads=(pb, "bias_k", f"xsb{sl}"), writes=(f"xsb{sl}",))
                    E("ve", lambda v, xcol=xcol: v.tensor_mul(sqb[:, :], xsb[:, xcol:xcol + 512],
                      xsb[:, xcol:xcol + 512]), reads=(f"xsb{sl}", "sqb"), writes=("sqb",))
                    E("te", lambda t, j=j: t.matmul(PS[6][0:4, 0:512], hs4[j][:, :], sqb[:, :],
                      start=True, stop=True), reads=("sqb", f"hs4{j}", "PS6"), writes=("PS6",))
                    E("ve", lambda v: v.tensor_reduce(nstat[0:4, 4:5], PS[6][0:4, 0:512], AX.X, AO.max),
                      reads=("PS6", "nstat"), writes=("nstat",), lazy=True)
                    E("ve", lambda v: v.tensor_max(nstat[0:4, 0:1], nstat[0:4, 0:1], nstat[0:4, 4:5]),
                      reads=("nstat",), writes=("nstat",), lazy=True)
                    emit_rope(xsb, xcol, rk, 0, rk, 512, 512,
                              [(kaug[2 * j], 0, 512 * tg, f"ka{2 * j}"),
                               (kaug[2 * j + 1], 0, 512 * tg, f"ka{2 * j + 1}")],
                              (f"xsb{sl}", f"rkw{sl}"))
                for q4 in range(4):
                    tt = 4 * tg + q4
                    pv_ = PS[4 + (1 - sl)]; pvb = PSN[4 + (1 - sl)]
                    for ic in range(8):
                        E("te", lambda t, ic=ic, q4=q4, lf=lf, pv_=pv_: t.matmul(
                            pv_[:, 0:256], lf[:, 512 * ic + 128 * q4:512 * ic + 128 * q4 + 128],
                            WBap(2, ic, 0, 256), start=(ic == 0), stop=False),
                            reads=(f"lnfw{sl}",) + WBrd(2, ic),
                            writes=(pvb,) if ic == 0 else (), inc=False)
                    E("te", lambda t, pv_=pv_, Q=Q: t.matmul(pv_[:, 0:256], onesbf[0:1, :],
                      bvhl[0:1, 256 * Q:256 * Q + 256], start=False, stop=False),
                      reads=("bvhl_h", "onesbf"), inc=False)
                    E("te", lambda t, pv_=pv_, Q=Q: t.matmul(pv_[:, 0:256], onesbf[0:1, :],
                      bvhl[0:1, D + 256 * Q:D + 256 * Q + 256], start=False, stop=True),
                      reads=("bvhl_l", "onesbf"), writes=(pvb,))
                    E("se", lambda s, tt=tt, pv_=pv_: s.copy(
                        bass.AP(vqt, 260 * tt, [[260 * 32, 128], [65, 4], [1, 64]]), pv_[:, 0:256]),
                        reads=(pvb, "vq_ones", f"vq{tt}"), writes=(f"vq{tt}",))

            for g in (range(G8 + 1) if (_Qi == 0 or _QPH >= 2) else ()):
                n = GN if g < G8 else NBANK
                sl = g % 2
                lo = lnow[sl]
                for ic in range(8):
                    E("sy", lambda s, ic=ic, g=g, n=n, lo=lo: s.dma_start(
                        out=lo[:, GN * ic:GN * ic + n],
                        in_=ln1o_h.ap()[:, NQ * ic + GN * g: NQ * ic + GN * g + n]),
                        reads=("lodone", f"lnow{sl}"), writes=(f"lnow{sl}",), dma=True)
                rq = rqw[sl]
                E("sy", lambda s, g=g, n=n, rq=rq: s.dma_start(out=rq[:, 0:n],
                  in_=c2q_d.ap()[:, GN * g:GN * g + n]), reads=(f"rqw{sl}",), writes=(f"rqw{sl}",), dma=True)
                E("sy", lambda s, g=g, n=n, rq=rq: s.dma_start(out=rq[:, GN:GN + n],
                  in_=nsq_d.ap()[:, GN * g:GN * g + n]), reads=(f"rqw{sl}",), writes=(f"rqw{sl}",), dma=True)
                for j in range(2):
                    pj = PS[4 + sl]; pb = PSN[4 + sl]
                    for ic in range(8):
                        E("te", lambda t, ic=ic, j=j, n=n, lo=lo, pj=pj: t.matmul(
                            pj[:, 0:n], WBap(0, ic, j), lo[:, GN * ic:GN * ic + n],
                            start=(ic == 0), stop=(ic == 7)),
                            reads=(f"lnow{sl}",) + WBrd(0, ic),
                            writes=(pb,) if ic in (0, 7) else (), inc=(ic == 7))
                    E("se", lambda s, pj=pj, j=j, n=n, Q=Q: s.activation(xsb[:, 1024:1024 + n], pj[:, 0:n], AF.Identity,
                      bias=biases[:, 2 * Q + j:2 * Q + j + 1]),
                      reads=(pb, "bias_q", "xsbq"), writes=("xsbq",))
                    E("ve", lambda v, n=n: v.tensor_mul(sqb[:, 0:n], xsb[:, 1024:1024 + n], xsb[:, 1024:1024 + n]),
                      reads=("xsbq", "sqb"), writes=("sqb",))
                    E("te", lambda t, j=j, n=n: t.matmul(PS[6][0:4, 0:n], hs4[j][:, :], sqb[:, 0:n],
                      start=True, stop=True), reads=("sqb", f"hs4{j}", "PS6"), writes=("PS6",))
                    E("ve", lambda v, n=n: v.tensor_reduce(nstat[0:4, 5:6], PS[6][0:4, 0:n], AX.X, AO.max),
                      reads=("PS6", "nstat"), writes=("nstat",), lazy=True)
                    E("ve", lambda v: v.tensor_max(nstat[0:4, 1:2], nstat[0:4, 1:2], nstat[0:4, 5:6]),
                      reads=("nstat",), writes=("nstat",), lazy=True)
                    emit_rope(xsb, 1024, rq, 0, rq, GN, n,
                              [(qaug[2 * j][g], 0, 0, f"qa{2 * j}_{g}"),
                               (qaug[2 * j + 1][g], 0, 0, f"qa{2 * j + 1}_{g}")],
                              ("xsbq", f"rqw{sl}"))
            if _Qi == 0:
                for h in range(4):
                    for g in range(G8 + 1):
                        n = GN if g < G8 else NBANK
                        qa = qaug[h][g]
                        E("ve", lambda v, qa=qa, n=n: v.memset(qa[64:96, 0:n], 0.0), writes=(f"qc1_{h}_{g}",))
                        E("ve", lambda v, qa=qa, n=n: v.memset(qa[64:65, 0:n], 1.0),
                          reads=(f"qc1_{h}_{g}",), writes=(f"qc1_{h}_{g}",))
                        E("sy", lambda s, qa=qa, g=g, n=n: s.dma_start(
                            out=qa[KB0:KB0 + NKILL, 0:n],
                            in_=qkill_d.ap()[:, GN * g:GN * g + n]),
                            writes=(f"qc2_{h}_{g}",), dma=True)

            if _Qi > 0 and not _Q64:
                continue
            E("ve", lambda v: v.tensor_add(nstat[0:4, 6:7], nstat[0:4, 0:1], nstat[0:4, 1:2]),
              reads=("nstat",), writes=("nstat",), lazy=True)
            E("ve", lambda v: v.tensor_scalar(nstat[0:4, 6:7], nstat[0:4, 6:7], -1.0 / 16.0, None, AO.mult),
              reads=("nstat",), writes=("nstat",), lazy=True)
            E("ve", lambda v: v.tensor_copy(ntr[0:32, 0:1], nstat[0:32, 6:7]),
              reads=("nstat", "ntr"), writes=("ntr",), lazy=True)
            vguard()
            E("ve", lambda v: v.transpose(ntrT[0:32, 0:32], ntr[0:32, 0:32]),
              reads=("ntr", "ntrT"), writes=("ntrT",), lazy=True)
            vguard()
            E("ve", lambda v: v.tensor_copy(valr[0:1, 0:4], ntrT[0:1, 0:4]), reads=("ntrT", "valr"), writes=("valr",), lazy=True)
            for h in range(4):
                for kc8 in range(8):
                    E("ve", lambda v, h=h, kc8=kc8: v.tensor_scalar(
                        kaug[h][64:65, 512 * kc8:512 * (kc8 + 1)], onesw[0:1, :],
                        valr[0:1, h:h + 1], None, AO.mult),
                        reads=("valr", "onesw", f"kc{h}"), writes=(f"kc{h}",))

            for g in (range(G8 + 1) if (_Qi == 0 or _QPH >= 3) else ()):
                n = GN if g < G8 else NBANK
                K = KSLOT[g] if g < G8 else KBANK
                is_bank = (g == G8)

                def emit_score(i, g=g, n=n, is_bank=is_bank):
                    rot = i % 3
                    bs = 2 * (i % 2)
                    for h in range(4):
                        psc = PS[(0 if bs == 0 else 4) + h // 2]
                        pscb = PSN[(0 if bs == 0 else 4) + h // 2]
                        coff = 256 * (h % 2)
                        oap = psc[:, coff:coff + n]
                        kap = kaug[h][0:KC, 128 * i:128 * (i + 1)]
                        qap = qaug[h][g][0:KC, 0:n]
                        E("te", lambda t, oap=oap, kap=kap, qap=qap: t.matmul(
                            oap, kap, qap, start=True, stop=True),
                            reads=(f"ka{h}", f"kc{h}", f"kkill{h}", f"qa{h}_{g}",
                                   f"qc1_{h}_{g}", f"qc2_{h}_{g}", pscb),
                            writes=(pscb,))
                    for h in range(4):
                        psc = PS[(0 if bs == 0 else 4) + h // 2]
                        pscb = PSN[(0 if bs == 0 else 4) + h // 2]
                        coff = 256 * (h % 2)
                        iap = psc[:, coff:coff + n]
                        if not is_bank:
                            pcol = (4 * rot + h) * GN
                            oap = p4t[:, pcol:pcol + n]
                            E("se", lambda s, oap=oap, iap=iap: s.activation(oap, iap, AF.Exp),
                              reads=(pscb, f"p4_{rot}_{h}"), writes=(f"p4_{rot}_{h}",))
                        else:
                            pcol = h * KBANK * NBANK + i * NBANK
                            oap = pbank[:, pcol:pcol + n]
                            E("se", lambda s, oap=oap, iap=iap: s.activation(oap, iap, AF.Exp),
                              reads=(pscb, f"pbk{h}"), writes=(f"pbk{h}",))
                    if not is_bank:
                        for needs, mask, mbuf in ((NEED_A[g], m1_s, "m1_s"), (NEED_B[g], m2_s, "m2_s")):
                            for l in range(2):
                                if i == needs - 2 + l:
                                    pap = bass.AP(p4t, (4 * rot) * GN + 128 * l,
                                                  [[3 * 4 * GN, 128], [GN, 4], [1, 128]])
                                    map_ = bass.AP(mask, 0, [[512, 128], [128, 4], [1, 128]])
                                    E("ve", lambda v, pap=pap, map_=map_: v.tensor_tensor(
                                        pap, pap, map_, AO.mult),
                                        reads=(f"p4_{rot}_0", f"p4_{rot}_1", f"p4_{rot}_2", f"p4_{rot}_3", mbuf),
                                        writes=(f"p4_{rot}_0", f"p4_{rot}_1", f"p4_{rot}_2", f"p4_{rot}_3"))

                def emit_pv(i, g=g, n=n, K=K, is_bank=is_bank):
                    rot = i % 3
                    for h in range(4):
                        obank = (2, 3, 6, 7)[h]
                        po = PS[obank]; pob = PSN[obank]
                        coff = 0
                        if not is_bank:
                            pcol = (4 * rot + h) * GN
                            prd = f"p4_{rot}_{h}"
                            src = p4t
                        else:
                            pcol = h * KBANK * NBANK + i * NBANK
                            prd = f"pbm{h}"
                            src = pbank
                        oap = po[0:65, coff:coff + n]
                        vap = bass.AP(vqt, 260 * i + 65 * h, [[260 * 32, 128], [1, 65]])
                        pap = src[:, pcol:pcol + n]
                        E("te", lambda t, oap=oap, vap=vap, pap=pap, i=i, K=K: t.matmul(
                            oap, vap, pap, start=(i == 0), stop=(i == K - 1)),
                            reads=(f"vq{i}", prd, "vq_ones"),
                            writes=(pob,) if i in (0, K - 1) else (), inc=(i == K - 1))

                if not is_bank:
                    for i in range(K):
                        emit_score(i)
                        if i >= 1:
                            emit_pv(i - 1)
                    emit_pv(K - 1)
                else:
                    for i in range(K):
                        emit_score(i)
                    for h in range(4):
                        E("ve", lambda v, h=h: v.tensor_tensor(
                            pbank[:, h * KBANK * NBANK:(h + 1) * KBANK * NBANK],
                            pbank[:, h * KBANK * NBANK:(h + 1) * KBANK * NBANK],
                            bmask_s[:, :], AO.mult),
                            reads=(f"pbk{h}", "bmask_s"), writes=(f"pbm{h}", f"pbk{h}"))
                    for i in range(K):
                        emit_pv(i)

                gc0 = GN * g
                for h in range(4):
                    obank = (2, 3, 6, 7)[h]
                    po = PS[obank]; pob = PSN[obank]
                    coff = 0
                    E("se", lambda s, h=h, po=po, coff=coff, gc0=gc0, n=n: s.copy(
                        oraw[64 * (h % 2):64 * (h % 2) + 64, NQ * (h // 2) + gc0: NQ * (h // 2) + gc0 + n],
                        po[0:64, coff:coff + n]), reads=(pob, "oraw"), writes=("oraw",))
                    E("se", lambda s, h=h, po=po, coff=coff, gc0=gc0, n=n: s.copy(
                        drawt[32 * h:32 * h + 1, gc0:gc0 + n], po[64:65, coff:coff + n]),
                        reads=(pob, "draw"), writes=("draw",))
                if dbg2_h is not None and is_bank:
                    E("ge", lambda g_: g_.dma_start(out=dbg2_h.ap()[:, 0:4 * KBANK * NBANK],
                      in_=pbank[:, :]), reads=tuple(f"pbm{h}" for h in range(4)),
                      writes=("dbg4",), dma=True)
                    E("ge", lambda g_: g_.dma_start(
                      out=dbg2_h.ap()[:, 4 * KBANK * NBANK:4 * KBANK * NBANK + 16],
                      in_=drawt[:, 2048:2064]), reads=("draw",), writes=("dbg4",), dma=True)
                    E("ge", lambda g_: g_.dma_start(
                      out=dbg2_h.ap()[:, 4 * KBANK * NBANK + 16:4 * KBANK * NBANK + 32],
                      in_=oraw[:, 2048:2064]), reads=("oraw",), writes=("dbg4",), dma=True)
                    E("ge", lambda g_: g_.dma_start(
                      out=dbg2_h.ap()[:, 4 * KBANK * NBANK + 32:4 * KBANK * NBANK + 48],
                      in_=oraw[:, NQ + 2048:NQ + 2064]), reads=("oraw",), writes=("dbg4",), dma=True)

            if _Qi > 0 and _QPH < 4:
                continue
            E("ve", lambda v: v.reciprocal(dinvb[:, :], drawt[:, :]), reads=("draw", "dinv"), writes=("dinv",), lazy=True)
            for jc in range(2):
                for t5 in range(5):
                    c5 = min(512 * t5, NQ - 512)
                    n5 = 512
                    E("te", lambda t, jc=jc, c5=c5, n5=n5: t.matmul(PS[4][:, 0:n5], selq[jc][:, :],
                      dinvb[:, c5:c5 + n5], start=True, stop=True),
                      reads=("dinv", f"selq{jc}", "PS4"), writes=("PS4",))
                    E("ve", lambda v, c5=c5, n5=n5: v.tensor_copy(drawt[:, c5:c5 + n5], PS[4][:, 0:n5]),
                      reads=("PS4", "draw"), writes=("draw2",))
                    E("ve", lambda v, jc=jc, c5=c5, n5=n5: v.tensor_mul(
                        obf[:, NQ * jc + c5: NQ * jc + c5 + n5],
                        oraw[:, NQ * jc + c5: NQ * jc + c5 + n5], drawt[:, c5:c5 + n5]),
                        reads=("draw2", "oraw", f"obf{jc}_{t5}"), writes=(f"obf{jc}_{t5}",))
                E("ge", lambda g_, jc=jc, Q=Q: g_.dma_start(
                    out=obf_h.ap()[:, NQ * (2 * Q + jc): NQ * (2 * Q + jc + 1)],
                    in_=obf[:, NQ * jc: NQ * (jc + 1)]),
                    reads=tuple(f"obf{jc}_{t5}" for t5 in range(5)), writes=(f"obfh{2 * Q + jc}",), dma=True)

    emit_wq(len(wq_pieces))
    sch.barrier()
    if _DBG_STOP:
        sch.ops["ge"].append(lambda h: h.wait_ge(SEMS["D:obfh1"], 16))
        return _finish(nc, sch, root)
    with contextlib.ExitStack() as p3:
        lt3 = make_ln(p3)
        wo_b = sb(p3, "wo_b", [128, 8 * D], BF16)
        wo_f = sb(p3, "wo_f", [128, 2048])
        obw = sb(p3, "obw", [128, 8 * 512], BF16)
        zbuf = sb(p3, "zbuf", [128, 8 * 512])
        xlw = sb(p3, "xlw", [128, 8 * 512])
        xaw = sb(p3, "xaw", [128, 8 * 512])
        for half in range(4):
            E("sy", lambda s, half=half: s.dma_start(out=wo_f[:, :],
              in_=wo_d.ap()[:, 2048 * half:2048 * (half + 1)]), reads=("wo_f",), writes=("wo_f",), dma=True)
            E("ve", lambda v, half=half: v.tensor_copy(wo_b[:, 2048 * half:2048 * (half + 1)], wo_f[:, :]),
              reads=("wo_f",), writes=(f"wo_b{half}",))
        WOrd = tuple(f"wo_b{half}" for half in range(4))
        for g in range(5):
            n = 512
            c0 = min(512 * g, NQ - 512)
            for ic in range(8):
                E("sy", lambda s, ic=ic, c0=c0, n=n: s.dma_start(out=obw[:, 512 * ic:512 * ic + n],
                  in_=obf_h.ap()[:, NQ * ic + c0: NQ * ic + c0 + n]),
                  reads=(f"obfh{ic}", "obw"), writes=("obw",), dma=True)
                E("sy", lambda s, ic=ic, c0=c0, n=n: s.dma_start(out=xlw[:, 512 * ic:512 * ic + n],
                  in_=xloc.ap()[:, NQ * ic + c0: NQ * ic + c0 + n]),
                  reads=("xlw",), writes=("xlw",), dma=True)
            for oc in range(8):
                pj = PS[4 + oc % 2]; pb = PSN[4 + oc % 2]
                for ic in range(8):
                    E("te", lambda t, oc=oc, ic=ic, n=n, pj=pj: t.matmul(
                        pj[:, 0:n], wo_b[:, D * ic + 128 * oc: D * ic + 128 * oc + 128],
                        obw[:, 512 * ic:512 * ic + n], start=(ic == 0), stop=(ic == 7)),
                        reads=("obw",) + WOrd,
                        writes=(pb,) if ic in (0, 7) else (), inc=(ic == 7))
                E("se", lambda s, oc=oc, n=n, pj=pj: s.activation(zbuf[:, 512 * oc:512 * oc + n],
                  pj[:, 0:n], AF.Identity, bias=biases[:, 16 + oc:17 + oc]),
                  reads=(pb, "bias_o", "zbuf"), writes=("zbuf",))
            emit_ln(lt3, n,
                    src=lambda ic: (zbuf[:, 512 * ic:512 * ic + n], ("zbuf",)),
                    resid=lambda ic: (xlw[:, 512 * ic:512 * ic + n], ("xlw",)),
                    res_out=lambda ic: (xaw[:, 512 * ic:512 * ic + n], f"xaw{ic}"))
            for ic in range(8):
                E("ge", lambda g_, ic=ic, c0=c0, n=n: g_.dma_start(
                    out=xatt_h.ap()[:, NQ * ic + c0: NQ * ic + c0 + n],
                    in_=xaw[:, 512 * ic:512 * ic + n]),
                    reads=(f"xaw{ic}",), writes=("xattdone",), dma=True)

    sch.barrier()
    with contextlib.ExitStack() as p4s:
        lt4 = make_ln(p4s)
        hslab = sb(p4s, "hslab", [128, 32 * SW])
        hbank = sb(p4s, "hbank", [128, 32 * NBANK])
        xattw = sb(p4s, "xattw", [128, 8 * SLABW])
        ln2f = sb(p4s, "ln2f", [128, 8 * SLABW])
        xqb = sb(p4s, "xqb", [128, 8 * SLABW], BF16)
        w1w = [sb(p4s, f"w1w{i}", [128, 8 * 128], BF16) for i in range(2)]
        w2w = [sb(p4s, f"w2w{i}", [128, 1024], BF16) for i in range(2)]
        tmpc = sb(p4s, "tmpc", [128, FSLAB])
        amtr = sb(p4s, "amtr", [128, 544])
        amtc = sb(p4s, "amtc", [128, 544])
        selw = sb(p4s, "selw", [128, 512])
        st32 = sb(p4s, "st32", [32, 544])
        amT = sb(p4s, "amT", [32, 544])
        amC = sb(p4s, "amC", [32, 544])
        srow = sb(p4s, "srow", [1, 1100])
        sbc = sb(p4s, "sbc", [128, SLABW]); sxinv = sb(p4s, "sxinv", [128, SLABW])
        s2bc = sb(p4s, "s2bc", [128, FSLAB]); s2inv = sb(p4s, "s2inv", [128, FSLAB])
        h2qb = [sb(p4s, f"h2qb{i}", [128, 512], BF16) for i in range(2)]
        zb2 = sb(p4s, "zb2", [128, 8 * FSLAB])
        ybuf = sb(p4s, "ybuf", [128, 8 * FSLAB])
        ecol = sb(p4s, "ecol", [128, 2 * 32])
        E("ve", lambda v: v.memset(amC[:, :], 0.0), writes=("amC",))
        E("ve", lambda v: v.memset(amT[:, :], 0.0), writes=("amT",))
        E("ve", lambda v: v.memset(amtr[:, :], 0.0), writes=("amtr",))

        def absmax_rows(tag, src_fn, nsrc, n, dst_c0):
            for i in range(nsrc):
                api, rdi = src_fn(i)
                if i == 0:
                    E("se", lambda s, api=api: s.activation(amtr[:, 0:n], api, AF.Abs),
                      reads=rdi + ("amtr",), writes=("amtr",))
                else:
                    E("se", lambda s, api=api: s.activation(amtc[:, 0:n], api, AF.Abs),
                      reads=rdi + ("amtc",), writes=("amtc",))
                    E("ve", lambda v: v.tensor_max(amtr[:, 0:n], amtr[:, 0:n], amtc[:, 0:n]),
                      reads=("amtc", "amtr"), writes=("amtr",))
            E("ve", lambda v: v.tensor_copy(amtc[0:64, 0:n], amtr[64:128, 0:n]),
              reads=("amtr", "amtc"), writes=("amtc",))
            E("ve", lambda v: v.tensor_max(amtr[0:64, 0:n], amtr[0:64, 0:n], amtc[0:64, 0:n]),
              reads=("amtc", "amtr"), writes=("amtr",))
            E("ve", lambda v: v.tensor_copy(amtc[0:32, 0:n], amtr[32:64, 0:n]),
              reads=("amtr", "amtc"), writes=("amtc",))
            E("ve", lambda v: v.tensor_max(amtr[0:32, 0:n], amtr[0:32, 0:n], amtc[0:32, 0:n]),
              reads=("amtc", "amtr"), writes=("amtr",))
            nb = (n + 31) // 32
            vguard()
            for j in range(nb):
                E("ve", lambda v, j=j: v.transpose(amT[0:32, 32 * j:32 * j + 32],
                  amtr[0:32, 32 * j:32 * j + 32]), reads=("amtr", "amT"), writes=("amT",), lazy=True)
            vguard()
            for j in range(nb):
                E("ve", lambda v, j=j: v.tensor_reduce(amC[0:32, 32 * j:32 * j + 1],
                  amT[0:32, 32 * j:32 * j + 32], AX.X, AO.max), reads=("amT", "amC"), writes=("amC",), lazy=True)
            vguard()
            for j in range(nb):
                E("ve", lambda v, j=j: v.transpose(st32[0:32, 32 * j:32 * j + 32],
                  amC[0:32, 32 * j:32 * j + 32]), reads=("amC", "st32"), writes=("st32",), lazy=True)
            vguard()

        def bcast_row(src_c0, n, dst, dstbuf):
            for it in range((n + 511) // 512):
                c0 = min(512 * it, max(0, n - 512))
                w = min(512, n)
                E("te", lambda t, c0=c0, w=w: t.matmul(PS[5][:, 0:w], onesf[0:1, :],
                  srow[0:1, src_c0 + c0: src_c0 + c0 + w], start=True, stop=True),
                  reads=("srow", "onesf", "PS5"), writes=("PS5",))
                E("ve", lambda v, c0=c0, w=w: v.tensor_copy(dst[:, c0:c0 + w], PS[5][:, 0:w]),
                  reads=("PS5", dstbuf), writes=(dstbuf,))

        def quant_cols(srcT, src_rd_w, n, sb_, sbuf_n, dstT, dst_w, stride):
            for ic in range(8):
                E("ve", lambda v, ic=ic: v.tensor_tensor(srcT[:, stride * ic:stride * ic + n],
                  srcT[:, stride * ic:stride * ic + n], sb_[:, 0:n], AO.mult),
                  reads=(src_rd_w(ic), sbuf_n), writes=(src_rd_w(ic),))
                E("ve", lambda v, ic=ic: v.tensor_scalar(dstT[:, stride * ic:stride * ic + n],
                  srcT[:, stride * ic:stride * ic + n], MAGIC, MAGIC, AO.add, AO.subtract),
                  reads=(src_rd_w(ic), dst_w(ic)), writes=(dst_w(ic),))

        BW = 512
        bc0 = NQ - BW
        for ic in range(8):
            E("sy", lambda s, ic=ic: s.dma_start(out=xattw[:, SLABW * ic:SLABW * ic + BW],
              in_=xatt_h.ap()[:, NQ * ic + bc0: NQ * ic + bc0 + BW]),
              reads=("xattdone", "xattw"), writes=("xattw",), dma=True)
        emit_ln(lt4, BW,
                src=lambda ic: (xattw[:, SLABW * ic:SLABW * ic + BW], ("xattw",)),
                dst_f=lambda ic: (ln2f[:, SLABW * ic:SLABW * ic + BW], f"ln2f{ic}"))
        absmax_rows("bk", lambda ic: (ln2f[:, SLABW * ic:SLABW * ic + BW],
                                      tuple(f"ln2f{i}" for i in range(8))), 8, BW, 0)
        E("ve", lambda v: v.tensor_scalar(srow[0:1, 550:550 + BW], st32[0:1, 0:BW], EPS, None, AO.max),
          reads=("srow", "st32"), writes=("srow",))
        E("ve", lambda v: v.reciprocal(srow[0:1, 0:BW], srow[0:1, 550:550 + BW]),
          reads=("srow",), writes=("srow",), lazy=True)
        E("ve", lambda v: v.tensor_scalar(srow[0:1, 0:BW], srow[0:1, 0:BW], 127.0, None, AO.mult),
          reads=("srow",), writes=("srow",))
        E("ve", lambda v: v.tensor_scalar(srow[0:1, 550:550 + BW], srow[0:1, 550:550 + BW],
          1.0 / 127.0, None, AO.mult), reads=("srow",), writes=("srow",))
        bcast_row(0, BW, sbc[:, 0:BW], "sbc")
        bcast_row(550, BW, sxinv[:, 0:BW], "sxinv")
        quant_cols(ln2f, lambda ic: f"ln2f{ic}", BW, sbc[:, 0:BW], "sbc",
                   xqb, lambda ic: f"xqb{ic}", SLABW)
        XQKRD = tuple(f"xqb{ic}" for ic in range(8))
        for oc in range(32):
            wsl = oc % 2
            E("sy", lambda s, oc=oc, wsl=wsl: s.dma_start(out=w1w[wsl][:, :],
              in_=bass.AP(w1q_h, 128 * oc, [[8 * DH, 128], [DH, 8], [1, 128]])),
              reads=("w1done", f"w1w{wsl}"), writes=(f"w1w{wsl}",), dma=True)
            pj = PS[4 + oc % 2]; pb = PSN[4 + oc % 2]
            for ic in range(8):
                E("te", lambda t, ic=ic, wsl=wsl, pj=pj: t.matmul(
                    pj[:, 0:BW], w1w[wsl][:, 128 * ic:128 * (ic + 1)],
                    xqb[:, SLABW * ic: SLABW * ic + BW], start=(ic == 0), stop=(ic == 7)),
                    reads=XQKRD + (f"w1w{wsl}",),
                    writes=(pb,) if ic in (0, 7) else (), inc=(ic == 7))
            E("ve", lambda v, oc=oc, pj=pj: v.scalar_tensor_tensor(hbank[:, NBANK * oc:NBANK * (oc + 1)],
              pj[:, BW - NBANK:BW], swinv[:, 0:1], sxinv[:, BW - NBANK:BW], AO.mult, AO.mult),
              reads=(pb, "swinv0", "sxinv", "hbank"), writes=("hbank",))

        for sl_i in range(NSLAB):
            t0 = FSLAB * sl_i
            lo = max(0, t0 - 1); hi = min(NQ, t0 + FSLAB + 1)
            hi = min(hi, NOWN + 1) if sl_i == NSLAB - 1 else hi
            off = lo - (t0 - 1)
            nw = hi - lo
            for ic in range(8):
                E("sy", lambda s, ic=ic, lo=lo, nw=nw, off=off: s.dma_start(
                    out=xattw[:, SLABW * ic + off: SLABW * ic + off + nw],
                    in_=xatt_h.ap()[:, NQ * ic + lo: NQ * ic + lo + nw]),
                    reads=("xattdone", "xattw"), writes=("xattw",), dma=True)
            if off:
                E("ve", lambda v: v.memset(bass.AP(xattw, 0, [[8 * SLABW, 128], [SLABW, 8], [1, 1]]), 0.0),
                  reads=("xattw",), writes=("xattw",))
            for (lc0, lw) in ((0, 512), (SLABW - 512, 512)):
                emit_ln(lt4, lw,
                        src=lambda ic, lc0=lc0, lw=lw: (
                            xattw[:, SLABW * ic + lc0:SLABW * ic + lc0 + lw], ("xattw",)),
                        dst_f=lambda ic, lc0=lc0, lw=lw: (
                            ln2f[:, SLABW * ic + lc0:SLABW * ic + lc0 + lw], f"ln2f{ic}"))
            LN2RD = tuple(f"ln2f{ic}" for ic in range(8))
            absmax_rows("xq", lambda ic: (ln2f[:, SLABW * ic:SLABW * (ic + 1)], LN2RD), 8, SLABW, 0)
            E("ve", lambda v: v.tensor_scalar(srow[0:1, 550:550 + SLABW], st32[0:1, 0:SLABW], EPS, None, AO.max),
              reads=("srow", "st32"), writes=("srow",))
            E("ve", lambda v: v.reciprocal(srow[0:1, 0:SLABW], srow[0:1, 550:550 + SLABW]),
              reads=("srow",), writes=("srow",), lazy=True)
            E("ve", lambda v: v.tensor_scalar(srow[0:1, 0:SLABW], srow[0:1, 0:SLABW], 127.0, None, AO.mult),
              reads=("srow",), writes=("srow",))
            E("ve", lambda v: v.tensor_scalar(srow[0:1, 550:550 + SLABW], srow[0:1, 550:550 + SLABW],
              1.0 / 127.0, None, AO.mult), reads=("srow",), writes=("srow",))
            bcast_row(0, SLABW, sbc, "sbc")
            bcast_row(550, SLABW, sxinv, "sxinv")
            quant_cols(ln2f, lambda ic: f"ln2f{ic}", SLABW, sbc[:, 0:SLABW], "sbc",
                       xqb, lambda ic: f"xqb{ic}", SLABW)
            XQRD = tuple(f"xqb{ic}" for ic in range(8))
            for oc in range(32):
                wsl = oc % 2
                E("sy", lambda s, oc=oc, wsl=wsl: s.dma_start(out=w1w[wsl][:, :],
                  in_=bass.AP(w1q_h, 128 * oc, [[8 * DH, 128], [DH, 8], [1, 128]])),
                  reads=("w1done", f"w1w{wsl}"), writes=(f"w1w{wsl}",), dma=True)
                for (c0, nn) in ((0, 512), (SLABW - 128, 128)):
                    pj = PS[4 + oc % 2]; pb = PSN[4 + oc % 2]
                    for ic in range(8):
                        E("te", lambda t, ic=ic, c0=c0, nn=nn, wsl=wsl, pj=pj: t.matmul(
                            pj[:, 0:nn], w1w[wsl][:, 128 * ic:128 * (ic + 1)],
                            xqb[:, SLABW * ic + c0: SLABW * ic + c0 + nn],
                            start=(ic == 0), stop=(ic == 7)),
                            reads=XQRD + (f"w1w{wsl}",),
                            writes=(pb,) if ic in (0, 7) else (), inc=(ic == 7))
                    E("ve", lambda v, oc=oc, c0=c0, nn=nn, pj=pj: v.scalar_tensor_tensor(
                        hslab[:, SW * oc + c0: SW * oc + c0 + nn], pj[:, 0:nn], swinv[:, 0:1],
                        sxinv[:, c0:c0 + nn], AO.mult, AO.mult),
                        reads=(pb, "swinv0", "sxinv", f"hs{oc}", f"h2_{oc}"), writes=(f"hs{oc}", f"h2_{oc}"))
            HSRD = tuple(f"hs{oc}" for oc in range(32))
            if dbg_h is not None and sl_i == 0:
                E("ge", lambda g_: g_.dma_start(out=dbg_h.ap()[:, 1548:1558], in_=hslab[:, 504:514]),
                  reads=HSRD, writes=("dbg3",), dma=True)
            DBGPRE = ("dbg3",) if dbg_h is not None and sl_i == 0 else ()
            for side, extc, selc in ((0, 0, 0), (1, SLABW - 1, 1)):
                E("sy", lambda s, sl_i=sl_i, selc=selc: s.dma_start(
                    out=selw[:, 0:512],
                    in_=(selL_d if selc == 0 else selR_d).ap()[:, 512 * sl_i:512 * (sl_i + 1)]),
                    reads=("selw",), writes=("selw",), dma=True)
                E("ve", lambda v, selc=selc: v.tensor_tensor(
                    amtc[:, 0:512],
                    hbank[:, 0:32 * NBANK], selw[:, 0:512], AO.mult),
                    reads=("hbank", "selw", "amtc"), writes=("amtc",))
                E("ve", lambda v, selc=selc: v.tensor_reduce(
                    ecol[:, 32 * selc:32 * (selc + 1)],
                    bass.AP(amtc, 0, [[544, 128], [NBANK, 32], [1, NBANK]]),
                    AX.X, AO.add), reads=("amtc", "ecol"), writes=("ecol",), lazy=True)
                hcols = bass.AP(hslab, extc, [[32 * SW, 128], [SW, 32], [1, 1]])
                E("ve", lambda v, selc=selc, hcols=hcols: v.tensor_sub(
                    amtc[:, 0:32], ecol[:, 32 * selc:32 * (selc + 1)], hcols),
                    reads=("ecol", "amtc") + HSRD + DBGPRE, writes=("amtc",), lazy=True)
                E("ve", lambda v, sl_i=sl_i, selc=selc: v.tensor_scalar(
                    amtc[:, 0:32], amtc[:, 0:32], cact_s[:, 2 * sl_i + selc:2 * sl_i + selc + 1],
                    None, AO.mult), reads=("amtc", "cact_s"), writes=("amtc",), lazy=True)
                E("ve", lambda v, hcols=hcols: v.tensor_add(hcols, hcols, amtc[:, 0:32]),
                  reads=("amtc",) + HSRD, writes=HSRD, lazy=True)
            for oc in range(32):
                E("ve", lambda v, oc=oc: v.tensor_scalar(tmpc[:, :],
                  hslab[:, SW * oc: SW * oc + FSLAB], cw_s[:, oc:oc + 1], None, AO.mult),
                  reads=(f"hs{oc}", "cw_s", "tmpc"), writes=("tmpc",))
                E("ve", lambda v, oc=oc: v.scalar_tensor_tensor(tmpc[:, :],
                  hslab[:, SW * oc + 1: SW * oc + 1 + FSLAB], cw_s[:, 32 + oc:33 + oc], tmpc[:, :],
                  AO.mult, AO.add), reads=(f"hs{oc}", "cw_s", "tmpc"), writes=("tmpc",))
                E("ve", lambda v, oc=oc: v.scalar_tensor_tensor(tmpc[:, :],
                  hslab[:, SW * oc + 2: SW * oc + 2 + FSLAB], cw_s[:, 64 + oc:65 + oc], tmpc[:, :],
                  AO.mult, AO.add), reads=(f"hs{oc}", "cw_s", "tmpc"), writes=("tmpc",))
                E("se", lambda s, oc=oc: s.activation(hslab[:, SW * oc + 1: SW * oc + 1 + FSLAB],
                  tmpc[:, :], AF.Silu, bias=cb_s[:, oc:oc + 1]),
                  reads=("tmpc", "cb_s", f"hs{oc}"), writes=(f"h2_{oc}",))
            H2RD = tuple(f"h2_{oc}" for oc in range(32))
            absmax_rows("h2", lambda oc: (hslab[:, SW * oc + 1: SW * oc + 1 + FSLAB], H2RD), 32, FSLAB, 0)
            E("ve", lambda v: v.tensor_scalar(srow[0:1, 550:550 + FSLAB], st32[0:1, 0:FSLAB], EPS, None, AO.max),
              reads=("srow", "st32"), writes=("srow",))
            E("ve", lambda v: v.reciprocal(srow[0:1, 0:FSLAB], srow[0:1, 550:550 + FSLAB]),
              reads=("srow",), writes=("srow",), lazy=True)
            E("ve", lambda v: v.tensor_scalar(srow[0:1, 0:FSLAB], srow[0:1, 0:FSLAB], 127.0, None, AO.mult),
              reads=("srow",), writes=("srow",))
            E("ve", lambda v: v.tensor_scalar(srow[0:1, 550:550 + FSLAB], srow[0:1, 550:550 + FSLAB],
              1.0 / 127.0, None, AO.mult), reads=("srow",), writes=("srow",))
            bcast_row(0, FSLAB, s2bc, "s2bc")
            bcast_row(550, FSLAB, s2inv, "s2inv")
            for ch in range(32):
                wsl = ch % 2
                E("sy", lambda s, ch=ch, wsl=wsl: s.dma_start(out=w2w[wsl][:, :],
                  in_=w2q_h.ap()[:, 1024 * ch:1024 * (ch + 1)]),
                  reads=("w2done", f"w2w{wsl}"), writes=(f"w2w{wsl}",), dma=True)
                hq = h2qb[ch % 2]
                E("ve", lambda v, ch=ch: v.tensor_tensor(tmpc[:, :],
                  hslab[:, SW * ch + 1: SW * ch + 1 + FSLAB], s2bc[:, 0:FSLAB], AO.mult),
                  reads=(f"h2_{ch}", "s2bc", "tmpc"), writes=("tmpc",))
                E("ve", lambda v, hq=hq, ch=ch: v.tensor_scalar(hq[:, :], tmpc[:, :], MAGIC, MAGIC,
                  AO.add, AO.subtract), reads=("tmpc", f"hq{ch % 2}"), writes=(f"hq{ch % 2}",))
                for oc in range(8):
                    pj = PS[oc]; pb = PSN[oc]
                    E("te", lambda t, ch=ch, oc=oc, wsl=wsl, hq=hq, pj=pj: t.matmul(
                        pj[:, 0:FSLAB], w2w[wsl][:, 128 * oc:128 * (oc + 1)], hq[:, :],
                        start=(ch == 0), stop=(ch == 31)),
                        reads=(f"hq{ch % 2}", f"w2w{wsl}"),
                        writes=(pb,) if ch in (0, 31) else (), inc=(ch == 31 or oc == 7))
            for oc in range(8):
                pj = PS[oc]; pb = PSN[oc]
                E("ve", lambda v, oc=oc, pj=pj: v.scalar_tensor_tensor(zb2[:, FSLAB * oc: FSLAB * (oc + 1)],
                  pj[:, 0:FSLAB], swinv[:, 1:2], s2inv[:, 0:FSLAB], AO.mult, AO.mult),
                  reads=(pb, "swinv1", "s2inv", "zb2"), writes=("zb2",))
            emit_ln(lt4, FSLAB,
                    src=lambda ic: (zb2[:, FSLAB * ic:FSLAB * (ic + 1)], ("zb2",)),
                    resid=lambda ic: (xattw[:, SLABW * ic + 1: SLABW * ic + 1 + FSLAB], ("xattw",)),
                    res_out=lambda ic: (ybuf[:, FSLAB * ic:FSLAB * (ic + 1)], f"yb{ic}"))
            for ic in range(8):
                E("ge", lambda g_, ic=ic, sl_i=sl_i: g_.dma_start(
                    out=yout.ap()[:, NOWN * ic + FSLAB * sl_i: NOWN * ic + FSLAB * (sl_i + 1)],
                    in_=ybuf[:, FSLAB * ic:FSLAB * (ic + 1)]),
                    reads=(f"yb{ic}",), writes=("ydone",), dma=True)
            if dbg_h is not None and sl_i == 0:
                E("ge", lambda g_: g_.dma_start(out=dbg_h.ap()[:, 0:514], in_=xqb[:, 0:SLABW]),
                  reads=tuple(f"xqb{i}" for i in range(8)), writes=("dbg3",), dma=True)
                E("ge", lambda g_: g_.dma_start(out=dbg_h.ap()[:, 514:1028], in_=hslab[:, 0:SLABW]),
                  reads=tuple(f"h2_{o}" for o in range(32)), writes=("dbg3",), dma=True)
                E("ge", lambda g_: g_.dma_start(out=dbg_h.ap()[:, 1028:1540], in_=zb2[:, 0:512]),
                  reads=("zb2",), writes=("dbg3",), dma=True)
                E("ge", lambda g_: g_.dma_start(out=dbg_h.ap()[:, 1540:1544], in_=s2bc[:, 4:8]),
                  reads=("s2bc",), writes=("dbg3",), dma=True)
                E("ge", lambda g_: g_.dma_start(out=dbg_h.ap()[:, 1544:1548], in_=sbc[:, 4:8]),
                  reads=("sbc",), writes=("dbg3",), dma=True)
                E("ge", lambda g_: g_.dma_start(out=dbg_h.ap()[:, 1558:1622], in_=ecol[:, 0:64]),
                  reads=("ecol",), writes=("dbg3",), dma=True)
                E("ge", lambda g_: g_.dma_start(out=dbg_h.ap()[:, 1622:1630], in_=cact_s[:, 0:8]),
                  reads=("cact_s",), writes=("dbg3",), dma=True)
                E("ge", lambda g_: g_.dma_start(out=dbg_h.ap()[:, 1630:1640], in_=sxinv[:, 504:514]),
                  reads=("sxinv",), writes=("dbg3",), dma=True)
                E("ge", lambda g_: g_.dma_start(out=dbg_h.ap()[:, 1640:1672], in_=selw[:, 0:32]),
                  reads=("selw",), writes=("dbg3",), dma=True)
                E("ge", lambda g_: g_.dma_start(out=dbg_h.ap()[:, 1672:1704], in_=hbank[:, 0:32]),
                  reads=("hbank",), writes=("dbg3",), dma=True)
                E("ge", lambda g_: g_.dma_start(out=dbg_h.ap()[:, 1704:1714], in_=sbc[:, 504:514]),
                  reads=("sbc",), writes=("dbg3",), dma=True)

    ytot = 16 * sch.dfam["ydone"]
    sch.ops["ge"].append(lambda h: h.wait_ge(SEMS["D:ydone"], ytot))
    return _finish(nc, sch, root)


def _finish(nc, sch, root):
    with nc.Block() as block:
        @block.sync
        def _(eng):
            for f in sch.ops["sy"]:
                f(eng)

        @block.tensor
        def _(eng):
            for f in sch.ops["te"]:
                f(eng)

        @block.vector
        def _(eng):
            for f in sch.ops["ve"]:
                f(eng)

        @block.scalar
        def _(eng):
            for f in sch.ops["se"]:
                f(eng)

        @block.gpsimd
        def _(eng):
            for f in sch.ops["ge"]:
                f(eng)

    root.close()
    return nc


SPLIT = np.concatenate([np.arange(0, 64, 2), np.arange(1, 64, 2)])


def _chunkmaj(a):
    rows, cols = a.shape
    nch = rows // 128
    return np.ascontiguousarray(a.reshape(nch, 128, cols).transpose(1, 0, 2).reshape(128, nch * cols))


def _feat(a):
    return _chunkmaj(np.ascontiguousarray(a.T))


_CACHE = {}


def kernel(L0_attn_x, L0_attn_wq, L0_attn_bq, L0_attn_wk, L0_attn_bk,
           L0_attn_wv, L0_attn_bv, L0_attn_wo, L0_attn_bo,
           L0_ffn_w1, L0_ffn_conv_w, L0_ffn_conv_b, L0_ffn_w2):
    x = np.asarray(L0_attn_x, np.float32)
    wq = np.asarray(L0_attn_wq, np.float32); wk = np.asarray(L0_attn_wk, np.float32)
    wv = np.asarray(L0_attn_wv, np.float32); wo = np.asarray(L0_attn_wo, np.float32)
    bq = np.asarray(L0_attn_bq, np.float32); bk = np.asarray(L0_attn_bk, np.float32)
    bv = np.asarray(L0_attn_bv, np.float32); bo = np.asarray(L0_attn_bo, np.float32)
    w1 = np.asarray(L0_ffn_w1, np.float32); w2 = np.asarray(L0_ffn_w2, np.float32)
    cwt = np.asarray(L0_ffn_conv_w, np.float32); cbt = np.asarray(L0_ffn_conv_b, np.float32)

    if "nc" not in _CACHE:
        _CACHE["nc"] = build_program()
    nc = _CACHE["nc"]

    perm = np.concatenate([64 * h + SPLIT for h in range(H)])
    wq_p = wq[:, perm]; wk_p = wk[:, perm]
    bq_p = bq[perm]; bk_p = bk[perm]
    d_wq = _chunkmaj(wq_p); d_wk = _chunkmaj(wk_p)
    d_wv = _chunkmaj(wv); d_wo = _chunkmaj(wo)
    d_bq = np.ascontiguousarray(bq_p.reshape(8, 128).T)
    d_bk = np.ascontiguousarray(bk_p.reshape(8, 128).T)
    d_bo = np.ascontiguousarray(bo.reshape(8, 128).T)
    d_bvr = bv.reshape(1, D)
    d_w1t = _chunkmaj(np.ascontiguousarray(w1.T))
    d_w2t = _chunkmaj(np.ascontiguousarray(w2.T))
    d_cw = np.concatenate([cwt[:, 0, t].reshape(32, 128).T for t in range(3)], axis=1)
    d_cw = np.ascontiguousarray(d_cw)
    d_cb = np.ascontiguousarray(cbt.reshape(32, 128).T)
    d_idf = np.eye(128, dtype=np.float32)

    s8 = 1.0 / np.sqrt(np.sqrt(64.0))
    s8 = 8.0 ** -0.5
    invf = (10000.0 ** (-np.arange(0, 64, 2, dtype=np.float64) / 64.0))

    def rope_tables(pos):
        ang = pos[None, :].astype(np.float64) * invf[:, None]
        c = np.cos(ang) * s8; s = np.sin(ang) * s8
        slot_c = np.concatenate([c, c], axis=0)
        slot_s = np.concatenate([s, -s], axis=0)
        C2 = np.concatenate([slot_c, slot_c], axis=0).astype(np.float32)
        NS = np.concatenate([slot_s, slot_s], axis=0).astype(np.float32)
        return np.ascontiguousarray(C2), np.ascontiguousarray(NS)

    kpos = np.arange(S)
    d_c2k, d_nsk = rope_tables(kpos)

    tri = (np.arange(128)[:, None] <= np.arange(128)[None, :]).astype(ml_dtypes.bfloat16)
    ones128 = np.ones((128, 128), ml_dtypes.bfloat16)
    tri4 = np.ascontiguousarray(np.tile(tri, (1, 4)))
    ones4 = np.ascontiguousarray(np.tile(ones128, (1, 4)))

    bm = np.zeros((128, KBANK * NBANK), ml_dtypes.bfloat16)
    for i in range(KBANK):
        for c in range(NBANK):
            p = BPOS[c]
            valid = (128 * i + np.arange(128)) <= p
            bm[:, NBANK * i + c] = valid.astype(ml_dtypes.bfloat16)
    bm = np.ascontiguousarray(bm)

    per_side = {}
    for side in range(2):
        qpos = np.concatenate([np.arange(side * 2048, side * 2048 + 2048), np.array(BPOS)])
        c2q, nsq = rope_tables(qpos)
        wkill = np.zeros((NKILL, S), ml_dtypes.bfloat16)
        off = 16 * side
        for g in range(G8):
            for l in range(2):
                thr = 128 * (off + 2 * g + l + 1)
                wkill[2 * g + l, thr:] = ml_dtypes.bfloat16(1.0)
        qkill = np.zeros((NKILL, (G8 + 1) * GN), ml_dtypes.bfloat16)
        for g in range(G8):
            for l in range(2):
                qkill[2 * g + l, GN * g + 128 * l: GN * g + 128 * (l + 1)] = ml_dtypes.bfloat16(KILL)
        m1 = tri4 if side == 0 else ones4
        m2 = ones4 if side == 0 else tri4
        selL = np.zeros((128, NSLAB * 512), np.float32)
        selR = np.zeros((128, NSLAB * 512), np.float32)
        cact = np.zeros((128, 2 * NSLAB), np.float32)
        for sl_i in range(NSLAB):
            tL = side * 2048 + FSLAB * sl_i
            tR = tL + FSLAB - 1
            if sl_i == 0:
                if side == 1:
                    pat = np.zeros(NBANK, np.float32); pat[6] = 1.0
                    selL[:, 0:512] = np.tile(pat, 32)[None, :].repeat(128, 0)
                    cact[:, 0] = 1.0
            if sl_i == NSLAB - 1:
                if side == 0:
                    pat = np.zeros(NBANK, np.float32); pat[7] = 1.0
                    selR[:, 512 * sl_i:512 * (sl_i + 1)] = np.tile(pat, 32)[None, :].repeat(128, 0)
                    cact[:, 2 * sl_i + 1] = 1.0
                else:
                    cact[:, 2 * sl_i + 1] = 1.0
        per_side[side] = dict(c2q=np.ascontiguousarray(c2q), nsq=np.ascontiguousarray(nsq),
                              wkill=np.ascontiguousarray(wkill), qkill=np.ascontiguousarray(qkill),
                              m1=np.ascontiguousarray(m1),
                              m2=np.ascontiguousarray(m2), selL=selL, selR=selR, cact=cact)

    in_maps = []
    for core in range(8):
        b = core // 2; side = core % 2
        ps = per_side[side]
        xb = x[b]
        own = xb[side * 2048: side * 2048 + 2048]
        bank = xb[BPOS]
        xloc = np.concatenate([own, bank], axis=0)
        in_maps.append({
            "xful": _feat(xb), "xloc": _feat(xloc),
            "wq": d_wq, "wk": d_wk, "wv": d_wv, "wo": d_wo,
            "bq": d_bq, "bk": d_bk, "bvr": d_bvr, "bo": d_bo,
            "w1t": d_w1t, "w2t": d_w2t, "cw": d_cw, "cb": d_cb,
            "c2q": ps["c2q"], "nsq": ps["nsq"], "c2k": d_c2k, "nsk": d_nsk,
            "wkill": ps["wkill"], "qkill": ps["qkill"], "m1": ps["m1"], "m2": ps["m2"], "bmask": bm,
            "selL": ps["selL"], "selR": ps["selR"], "cact": ps["cact"], "idf": d_idf,
        })

    res = run_bass_kernel_spmd(nc, in_maps, core_ids=list(range(8)))

    out = np.empty((B, S, D), np.float32)
    for core in range(8):
        b = core // 2; side = core % 2
        y = res.results[core]["y"]
        yt = y.reshape(128, 8, NOWN).transpose(1, 0, 2).reshape(D, NOWN)
        out[b, side * 2048: side * 2048 + 2048] = yt.T
    return out


# version 13
# speedup vs baseline: 1.0625x; 1.0230x over previous
import numpy as np
import ml_dtypes
import concourse.bass as bass
import concourse.mybir as mybir
from concourse.bass_utils import run_bass_kernel_spmd

F32 = mybir.dt.float32
BF16 = mybir.dt.bfloat16
AO = mybir.AluOpType
AF = mybir.ActivationFunctionType
AX = mybir.AxisListType

MAGIC = 12582912.0
KILL = -100000.0
EPS = 1e-5

B, S, D, H, DH = 4, 4096, 1024, 16, 4096
NOWN, NBANK = 2048, 16
NQ = NOWN + NBANK
G8, GN = 8, 256
KBANK = 29
NKILL = 2 * G8
KB0 = 96
KC = KB0 + NKILL

BPOS = [511, 512, 1023, 1024, 1535, 1536, 2047, 2048,
        2559, 2560, 3071, 3072, 3583, 3584, 3584, 3584]

NEED_A = [(256 * (g + 1) + 127) // 128 for g in range(G8)]
NEED_B = [(2048 + 256 * (g + 1) + 127) // 128 for g in range(G8)]
KSLOT = [max(a, b) for a, b in zip(NEED_A, NEED_B)]

FSLAB = 512
NSLAB = NOWN // FSLAB
SLABW = FSLAB + 2
SW = SLABW

ENGS = ("sy", "te", "ve", "se", "ge")
SEMS = {}
VJUNK = None


class Sch:

    def __init__(self):
        self.ops = {e: [] for e in ENGS}
        self.cnt = {e: 0 for e in ENGS}
        self.dfam = {}
        self.mkdsem = None
        self.bufs = {}
        self.lastw = {e: {} for e in ENGS}
        self.bar = None

    def barrier(self):
        self.bar = (dict(self.cnt), dict(self.dfam), set())

    def _wait(self, eng, semkey, val):
        if val <= self.lastw[eng].get(semkey, 0):
            return
        self.lastw[eng][semkey] = val
        self.ops[eng].append(lambda h, semkey=semkey, val=val: h.wait_ge(SEMS[semkey], val))

    def _resolve(self, sk, v):
        if sk.startswith("D:") and v < 0:
            return 16 * self.dfam[sk[2:]]
        return v

    def emit(self, eng, fn, reads=(), writes=(), dma=False, inc=True, lazy=False):
        if self.bar is not None and eng not in self.bar[2]:
            sc, sd, applied = self.bar
            for e2, v in sc.items():
                if e2 != eng and v > 0:
                    self._wait(eng, e2, v)
            for f, c in sd.items():
                if c > 0:
                    self._wait(eng, "D:" + f, 16 * c)
            applied.add(eng)
            if len(applied) == len(ENGS):
                self.bar = None
        need = {}
        for b in reads:
            rec = self.bufs.get(b)
            if rec and rec["w"]:
                sk, v = rec["w"]
                if sk != eng:
                    need[sk] = max(need.get(sk, 0), self._resolve(sk, v))
        for b in writes:
            rec = self.bufs.get(b)
            if rec:
                if rec["w"]:
                    sk, v = rec["w"]
                    if sk != eng:
                        need[sk] = max(need.get(sk, 0), self._resolve(sk, v))
                for sk, v in rec["r"].items():
                    if sk != eng:
                        need[sk] = max(need.get(sk, 0), self._resolve(sk, v))
        for sk, v in sorted(need.items()):
            self._wait(eng, sk, v)
        if dma:
            fam = writes[0]
            if fam not in self.dfam:
                self.dfam[fam] = 0
                self.mkdsem(fam)
            self.dfam[fam] += 1
            semkey, val = "D:" + fam, -1
            self.ops[eng].append(lambda h, fn=fn, semkey=semkey: fn(h).then_inc(SEMS[semkey], 16))
        elif inc:
            self.cnt[eng] += 1
            semkey, val = eng, self.cnt[eng]
            self.ops[eng].append(lambda h, fn=fn, semkey=semkey: fn(h).then_inc(SEMS[semkey], 1))
            if lazy:
                for _ in range(4):
                    self.cnt[eng] += 1
                    self.ops[eng].append(lambda h, semkey=semkey: h.memset(
                        VJUNK[:, :], 0.25).then_inc(SEMS[semkey], 1))
                val = self.cnt[eng]
        else:
            semkey, val = eng, self.cnt[eng] + 1
            self.ops[eng].append(fn)
        for b in writes:
            self.bufs[b] = {"w": (semkey, val), "r": {}}
        for b in reads:
            rec = self.bufs.setdefault(b, {"w": None, "r": {}})
            if semkey.startswith("D:"):
                rec["r"][semkey] = -1
            else:
                rec["r"][semkey] = max(rec["r"].get(semkey, 0), val)


def build_program():
    import contextlib
    nc = bass.Bass("TRN2", target_bir_lowering=False, detect_race_conditions=False)
    sch = Sch()
    E = sch.emit
    root = contextlib.ExitStack()

    din = lambda name, shape, dt=F32: nc.dram_tensor(name, shape, dt, kind="ExternalInput")
    xful = din("xful", [128, 8 * S])
    xloc = din("xloc", [128, 8 * NQ])
    wq_d = din("wq", [128, 8 * D]); wk_d = din("wk", [128, 8 * D])
    wv_d = din("wv", [128, 8 * D]); wo_d = din("wo", [128, 8 * D])
    bq_d = din("bq", [128, 8]); bk_d = din("bk", [128, 8])
    bvr_d = din("bvr", [1, D]); bo_d = din("bo", [128, 8])
    w1t_d = din("w1t", [128, 8 * DH])
    w2t_d = din("w2t", [128, 32 * D])
    cw_d = din("cw", [128, 3 * 32]); cb_d = din("cb", [128, 32])
    c2q_d = din("c2q", [128, NQ]); nsq_d = din("nsq", [128, NQ])
    c2k_d = din("c2k", [128, S]); nsk_d = din("nsk", [128, S])
    wkill_d = din("wkill", [NKILL, S], BF16)
    qkill_d = din("qkill", [NKILL, (G8 + 1) * GN], BF16)
    m1_d = din("m1", [128, 512], BF16); m2_d = din("m2", [128, 512], BF16)
    bmask_d = din("bmask", [128, KBANK * NBANK], BF16)
    selL_d = din("selL", [128, NSLAB * 512])
    selR_d = din("selR", [128, NSLAB * 512])
    cact_d = din("cact", [128, 2 * NSLAB])
    idf_d = din("idf", [128, 128])
    yout = nc.dram_tensor("y", [128, 8 * NOWN], F32, kind="ExternalOutput")

    import os as _osd
    _k = dict(kind="ExternalOutput") if _osd.environ.get("K_DBG_OUT") == "1" else {}
    ln1f_h = nc.dram_tensor("ln1f_h", [128, 8 * S], BF16, **_k)
    ln1o_h = nc.dram_tensor("ln1o_h", [128, 8 * NQ], BF16, **_k)
    xatt_h = nc.dram_tensor("xatt_h", [128, 8 * NQ], F32, **_k)
    obf_h = nc.dram_tensor("obf_h", [128, 8 * NQ], BF16, **_k)
    w1q_h = nc.dram_tensor("w1q_h", [128, 8 * DH], BF16, **_k)
    w2q_h = nc.dram_tensor("w2q_h", [128, 32 * D], BF16, **_k)
    dbg_h = nc.dram_tensor("dbg_h", [128, 2304], F32, **_k) if _k else None
    dbg2_h = nc.dram_tensor("dbg2_h", [128, 4 * KBANK * NBANK + 64], BF16, **_k) if _k else None

    sb = lambda st, name, shape, dt=F32: st.enter_context(nc.sbuf_tensor(name, shape, dt))
    SEMS.clear()
    for e in ENGS:
        SEMS[e] = root.enter_context(nc.semaphore(f"sem_{e}"))

    def _mkdsem(fam):
        SEMS["D:" + fam] = root.enter_context(nc.semaphore(f"d_{len(SEMS)}"))
    sch.mkdsem = _mkdsem

    onesbf = sb(root, "onesbf", [128, 128], BF16)
    onesf = sb(root, "onesf", [1, 128])
    onesw = sb(root, "onesw", [1, 512])
    idf_s = sb(root, "idf_s", [128, 128])
    selq = [sb(root, f"selq{j}", [128, 128]) for j in range(2)]
    vjunk = sb(root, "vjunk", [32, 32])
    global VJUNK
    VJUNK = vjunk
    hs4 = [sb(root, f"hs4{j}", [128, 4], BF16) for j in range(2)]
    biases = sb(root, "biases", [128, 24])
    cw_s = sb(root, "cw_s", [128, 3 * 32]); cb_s = sb(root, "cb_s", [128, 32])
    cact_s = sb(root, "cact_s", [128, 2 * NSLAB])
    swinv = sb(root, "swinv", [128, 2])
    m1_s = sb(root, "m1_s", [128, 512], BF16); m2_s = sb(root, "m2_s", [128, 512], BF16)
    bmask_s = sb(root, "bmask_s", [128, KBANK * NBANK], BF16)
    PS = [root.enter_context(nc.psum_tensor(f"ps{i}", [128, 512], F32)) for i in range(8)]
    PSN = [f"PS{i}" for i in range(8)]

    E("ve", lambda v: v.memset(onesbf[:, :], 1.0), writes=("onesbf",))

    def vguard():
        for _ in range(4):
            E("ve", lambda v: v.memset(vjunk[:, :], 0.25), reads=("vjunk",), writes=("vjunk",))
    E("ve", lambda v: v.memset(onesf[:, :], 1.0), writes=("onesf",))
    E("ve", lambda v: v.memset(onesw[:, :], 1.0), writes=("onesw",))
    E("sy", lambda s: s.dma_start(out=idf_s[:, :], in_=idf_d.ap()), writes=("idf_s",), dma=True)
    E("sy", lambda s: s.dma_start(out=m1_s[:, :], in_=m1_d.ap()), writes=("m1_s",), dma=True)
    E("sy", lambda s: s.dma_start(out=m2_s[:, :], in_=m2_d.ap()), writes=("m2_s",), dma=True)
    E("sy", lambda s: s.dma_start(out=bmask_s[:, :], in_=bmask_d.ap()), writes=("bmask_s",), dma=True)
    E("sy", lambda s: s.dma_start(out=biases[:, 0:8], in_=bq_d.ap()), writes=("bias_q",), dma=True)
    E("sy", lambda s: s.dma_start(out=biases[:, 8:16], in_=bk_d.ap()), writes=("bias_k",), dma=True)
    E("sy", lambda s: s.dma_start(out=biases[:, 16:24], in_=bo_d.ap()), writes=("bias_o",), dma=True)
    E("sy", lambda s: s.dma_start(out=cw_s[:, :], in_=cw_d.ap()), writes=("cw_s",), dma=True)
    E("sy", lambda s: s.dma_start(out=cb_s[:, :], in_=cb_d.ap()), writes=("cb_s",), dma=True)
    E("sy", lambda s: s.dma_start(out=cact_s[:, :], in_=cact_d.ap()), writes=("cact_s",), dma=True)
    for j in range(2):
        h_ = hs4[j]
        E("ve", lambda v, h_=h_: v.memset(h_[:, :], 0.0), writes=(f"hs4{j}",))
        E("ve", lambda v, h_=h_, j=j: v.memset(h_[0:64, 2 * j:2 * j + 1], 1.0),
          reads=(f"hs4{j}",), writes=(f"hs4{j}",))
        E("ve", lambda v, h_=h_, j=j: v.memset(h_[64:128, 2 * j + 1:2 * j + 2], 1.0),
          reads=(f"hs4{j}",), writes=(f"hs4{j}",))
        q_ = selq[j]
        E("ve", lambda v, q_=q_: v.memset(q_[:, :], 0.0), writes=(f"selq{j}",))
        E("ve", lambda v, q_=q_, j=j: v.memset(q_[64 * j:64 * j + 1, 0:64], 1.0),
          reads=(f"selq{j}",), writes=(f"selq{j}",))
        E("ve", lambda v, q_=q_, j=j: v.memset(q_[64 * j + 32:64 * j + 33, 64:128], 1.0),
          reads=(f"selq{j}",), writes=(f"selq{j}",))

    _ln_n = [0]

    LNW = 544

    def make_ln(st):
        i = _ln_n[0]; _ln_n[0] += 1
        lt = {"id": str(i),
              "xbf": sb(st, f"lnxbf{i}", [128, 2 * LNW], BF16),
              "sq": sb(st, f"lnsq{i}", [128, 2 * LNW], BF16),
              "tmp": sb(st, f"lntmp{i}", [128, 4 * LNW])}
        if dbg_h is not None and i == 0:
            for gap0 in (512, LNW + 512, 2 * LNW + 512):
                E("ve", lambda v, lt=lt, gap0=gap0: v.memset(lt["tmp"][:, gap0:gap0 + LNW - 512], 0.0),
                  writes=(f"lnm{i}",))
        return lt

    def emit_ln(lt, n, src, dst_bf=None, dst_f=None, resid=None, res_out=None, banks=(6, 7)):
        li = lt["id"]
        psA, psB = PS[banks[0]], PS[banks[1]]
        fA, fB = f"PS{banks[0]}", f"PS{banks[1]}"
        for ic in range(8):
            ap, rd = src(ic)
            sl = ic % 2
            xb = lt["xbf"][:, LNW * sl:LNW * sl + n]
            sq = lt["sq"][:, LNW * sl:LNW * sl + n]
            bxb, bsq2 = f"lnxb{sl}_{li}", f"lnsq{sl}_{li}"
            E("se", lambda s, ap=ap, xb=xb: s.copy(xb, ap), reads=rd + (bxb,), writes=(bxb,))
            E("se", lambda s, ap=ap, sq=sq: s.activation(sq, ap, AF.Square), reads=rd + (bsq2,), writes=(bsq2,))
            E("te", lambda t_, xb=xb, ic=ic, n=n: t_.matmul(psA[:, 0:n], onesbf[:, :], xb,
              start=(ic == 0), stop=(ic == 7)),
              reads=(bxb, "onesbf"),
              writes=(fA,) if ic in (0, 7) else (), inc=True)
            E("te", lambda t_, sq=sq, ic=ic, n=n: t_.matmul(psB[:, 0:n], onesbf[:, :], sq,
              start=(ic == 0), stop=(ic == 7)),
              reads=(bsq2, "onesbf"),
              writes=(fB,) if ic in (0, 7) else (), inc=True)
        tmp = lt["tmp"]
        mb = tmp[:, 0:n]; wv_ = tmp[:, LNW:LNW + n]; rb = tmp[:, 2 * LNW:2 * LNW + n]; dd = tmp[:, 3 * LNW:3 * LNW + n]
        E("ve", lambda v: v.tensor_scalar(mb, psA[:, 0:n], 1.0 / D, None, AO.mult),
          reads=(fA, f"lnm{li}"), writes=(f"lnm{li}",))
        E("ve", lambda v: v.tensor_mul(wv_, mb, mb), reads=(f"lnm{li}", f"lnw{li}"), writes=(f"lnw{li}",))
        E("ve", lambda v: v.scalar_tensor_tensor(wv_, psB[:, 0:n], 1.0 / D, wv_, AO.mult, AO.subtract),
          reads=(fB, f"lnw{li}"), writes=(f"lnw{li}",))
        E("ve", lambda v: v.tensor_scalar_add(wv_, wv_, EPS), reads=(f"lnw{li}",), writes=(f"lnw{li}",))
        E("se", lambda s: s.activation(rb, wv_, AF.Ln), reads=(f"lnw{li}", f"lnr{li}"), writes=(f"lnr{li}",))
        E("se", lambda s: s.activation(rb, rb, AF.Exp, scale=-0.5), reads=(f"lnr{li}",), writes=(f"lnr{li}",))
        for ic in range(8):
            ap, rd = src(ic)
            dn = f"lnd{li}"
            E("ve", lambda v, ap=ap: v.tensor_sub(dd, ap, mb), reads=rd + (f"lnm{li}", dn), writes=(dn,))
            if dst_bf is not None:
                oap, ob = dst_bf(ic)
                E("ve", lambda v, oap=oap: v.tensor_mul(oap, dd, rb), reads=(dn, f"lnr{li}"), writes=(ob,))
            if dst_f is not None:
                oap, ob = dst_f(ic)
                E("ve", lambda v, oap=oap: v.tensor_mul(oap, dd, rb), reads=(dn, f"lnr{li}"), writes=(ob,))
            if resid is not None:
                rap, rrd = resid(ic)
                oap, ob = res_out(ic)
                E("ve", lambda v: v.tensor_mul(dd, dd, rb), reads=(dn, f"lnr{li}"), writes=(dn,))
                E("ve", lambda v, oap=oap, rap=rap: v.tensor_add(oap, dd, rap),
                  reads=(dn,) + rrd, writes=(ob,))

    with contextlib.ExitStack() as p1:
        lt1 = [make_ln(p1), make_ln(p1)]
        xin = [sb(p1, f"xinb{i}", [128, 8 * 512]) for i in range(2)]
        lnout = [sb(p1, f"lnoutb{i}", [128, 8 * 512], BF16) for i in range(2)]
        _p1w = [0]

        def p1_pass(src_d, dst_d, total, tag0):
            for g in range((total + 511) // 512):
                c0 = min(512 * g, total - 512)
                n = 512
                sl = _p1w[0] % 2; _p1w[0] += 1
                xi = xin[sl]; lo = lnout[sl]
                for ic in range(8):
                    E("sy", lambda s, ic=ic, c0=c0, n=n, xi=xi, sl=sl: s.dma_start(
                        out=xi[:, 512 * ic:512 * ic + n],
                        in_=src_d.ap()[:, total * ic + c0: total * ic + c0 + n]),
                        reads=(f"xinw{sl}",), writes=(f"xinw{sl}",), dma=True)
                emit_ln(lt1[sl], n,
                        src=lambda ic, xi=xi, sl=sl: (xi[:, 512 * ic:512 * ic + n], (f"xinw{sl}",)),
                        dst_bf=lambda ic, lo=lo, sl=sl: (lo[:, 512 * ic:512 * ic + n], f"lno{ic}_{sl}"),
                        banks=(6, 7) if sl == 0 else (4, 5))
                for ic in range(8):
                    E("ge", lambda gg, ic=ic, c0=c0, n=n, lo=lo, sl=sl: gg.dma_start(
                        out=dst_d.ap()[:, total * ic + c0: total * ic + c0 + n],
                        in_=lo[:, 512 * ic:512 * ic + n]),
                        reads=(f"lno{ic}_{sl}",), writes=(f"{tag0}done",), dma=True)

        p1_pass(xful, ln1f_h, S, "lf")
        p1_pass(xloc, ln1o_h, NQ, "lo")
        if dbg_h is not None:
            for _s in range(2):
                E("ve", lambda v, _s=_s: v.tensor_copy(lt1[0]["tmp"][:, 1900 + 16 * _s:1916 + 16 * _s],
                  lt1[0]["sq"][:, 544 * _s:544 * _s + 16]),
                  reads=(f"lnsq{_s}_0", "lnm0"), writes=("lnm0",), inc=True)
                E("ve", lambda v, _s=_s: v.tensor_copy(lt1[0]["tmp"][:, 1932 + 16 * _s:1948 + 16 * _s],
                  lt1[0]["xbf"][:, 544 * _s:544 * _s + 16]),
                  reads=(f"lnxb{_s}_0", "lnm0"), writes=("lnm0",), inc=True)
            E("ge", lambda g: g.dma_start(out=dbg_h.ap()[:, 0:2048], in_=lt1[0]["tmp"][:, 0:2048]),
              reads=("lnm0", "lnw0", "lnr0", "lnd0"), writes=("dbgdone",), dma=True)
            E("ge", lambda g: g.dma_start(out=dbg_h.ap()[:, 2048:2064], in_=xin[0][:, 0:16]),
              reads=("xinw0",), writes=("dbgdone",), dma=True)
    sch.barrier()

    import os as _os0
    _DBG_STOP = _os0.environ.get("K_DBG_STOP") == "1"
    with contextlib.ExitStack() as pw_:
        pw = root if _DBG_STOP else pw_
        WTW = 1024
        wbuf = [sb(pw, f"wbufs{i}", [128, WTW]) for i in range(2)]
        wqb = [sb(pw, f"wqbs{i}", [128, WTW], BF16) for i in range(2)]
        wstat = sb(pw, "wstat", [128, 56])
        wtr = sb(pw, "wtr", [128, 32])
        wtrT = sb(pw, "wtrT", [32, 128])

        def quant_weights(src_d, dst_d, ncols, tag, sw_col):
            nt = ncols // WTW
            pieces = []

            def init_p():
                E("ve", lambda v: v.memset(wstat[:, 0:56], 0.0), reads=("wstat",), writes=("wstat",))
                E("ve", lambda v: v.memset(wtr[:, 0:32], 0.0), reads=("wtr",), writes=("wtr",))
            pieces.append(init_p)

            def mean_tile(it):
                sl = it % 2
                E("sy", lambda s, it=it, sl=sl: s.dma_start(out=wbuf[sl][:, :],
                  in_=src_d.ap()[:, WTW * it:WTW * (it + 1)]),
                  reads=(f"wbufs{sl}",), writes=(f"wbufs{sl}",), dma=True)
                E("ve", lambda v, it=it, sl=sl: v.tensor_reduce(wstat[:, 20 + it:21 + it], wbuf[sl][:, :],
                  AX.X, AO.add, apply_absolute_value=True),
                  reads=(f"wbufs{sl}", "wstat"), writes=("wstat",), lazy=True)
            for it in range(nt):
                pieces.append(lambda it=it: mean_tile(it))

            def finisher():
                E("ve", lambda v: v.tensor_reduce(wstat[:, 16:17], wstat[:, 20:20 + nt], AX.X, AO.add),
                  reads=("wstat",), writes=("wstat",), lazy=True)
                E("ve", lambda v: v.tensor_copy(wtr[:, 0:1], wstat[:, 16:17]),
                  reads=("wstat", "wtr"), writes=("wtr",), lazy=True)
                vguard()
                for kb_ in range(4):
                    E("ve", lambda v, kb_=kb_: v.transpose(wtrT[0:32, 32 * kb_:32 * kb_ + 32],
                      wtr[32 * kb_:32 * kb_ + 32, 0:32]),
                      reads=("wtr", "wtrT"), writes=("wtrT",), lazy=True)
                vguard()
                E("ve", lambda v: v.tensor_reduce(wstat[0:1, 17:18], wtrT[0:1, 0:128], AX.X, AO.add),
                  reads=("wtrT", "wstat"), writes=("wstat",), lazy=True)
                E("ve", lambda v: v.tensor_scalar(wstat[0:1, 17:18], wstat[0:1, 17:18], 1.0 / (ncols * 128.0), EPS,
                  AO.mult, AO.max), reads=("wstat",), writes=("wstat",), lazy=True)
                E("te", lambda t: t.matmul(PS[6][:, 0:1], onesf[0:1, :], wstat[0:1, 17:18], start=True, stop=True),
                  reads=("wstat", "onesf", "PS6"), writes=("PS6",))
                E("ve", lambda v: v.tensor_copy(swinv[:, sw_col:sw_col + 1], PS[6][:, 0:1]),
                  reads=("PS6",), writes=(f"swinv{sw_col}",), lazy=True)
                E("ve", lambda v: v.reciprocal(wstat[0:1, 18:19], wstat[0:1, 17:18]), reads=("wstat",), writes=("wstat",), lazy=True)
                E("te", lambda t: t.matmul(PS[6][:, 0:1], onesf[0:1, :], wstat[0:1, 18:19], start=True, stop=True),
                  reads=("wstat", "onesf", "PS6"), writes=("PS6",))
                E("ve", lambda v: v.tensor_copy(wstat[:, 19:20], PS[6][:, 0:1]), reads=("PS6", "wstat"), writes=("wsbc" + tag,), lazy=True)
            pieces.append(finisher)

            def quant_tile(it):
                sl = it % 2
                E("sy", lambda s, it=it, sl=sl: s.dma_start(out=wbuf[sl][:, :],
                  in_=src_d.ap()[:, WTW * it:WTW * (it + 1)]),
                  reads=(f"wbufs{sl}",), writes=(f"wbufs{sl}",), dma=True)
                E("ve", lambda v, sl=sl: v.tensor_scalar(wbuf[sl][:, :], wbuf[sl][:, :], wstat[:, 19:20], MAGIC, AO.mult, AO.add),
                  reads=(f"wbufs{sl}", "wsbc" + tag), writes=(f"wbufs{sl}",))
                E("ve", lambda v, sl=sl: v.tensor_scalar(wbuf[sl][:, :], wbuf[sl][:, :], MAGIC, 1.0, AO.subtract, AO.min),
                  reads=(f"wbufs{sl}",), writes=(f"wbufs{sl}",))
                E("ve", lambda v, sl=sl: v.tensor_scalar(wqb[sl][:, :], wbuf[sl][:, :], -1.0, None, AO.max),
                  reads=(f"wbufs{sl}", f"wqbs{sl}"), writes=(f"wqbs{sl}",))
                E("ge", lambda g, it=it, sl=sl: g.dma_start(out=dst_d.ap()[:, WTW * it:WTW * (it + 1)], in_=wqb[sl][:, :]),
                  reads=(f"wqbs{sl}",), writes=(f"{tag}done",), dma=True)
            for it in range(nt):
                pieces.append(lambda it=it: quant_tile(it))
            return pieces

        wq_pieces = quant_weights(w1t_d, w1q_h, 8 * DH, "w1", 0)
        if dbg_h is not None:
            def dbg_wq():
                E("ge", lambda g: g.dma_start(out=dbg_h.ap()[:, 2064:2112], in_=wstat[:, 0:48]),
                  reads=("wstat", "wsbcw1"), writes=("dbg2",), dma=True)
                E("ge", lambda g: g.dma_start(out=dbg_h.ap()[0:32, 2112:2240], in_=wtrT[:, :]),
                  reads=("wtrT",), writes=("dbg2",), dma=True)
            wq_pieces.append(dbg_wq)
        wq_pieces += quant_weights(w2t_d, w2q_h, 32 * D, "w2", 1)
        _wq_cursor = [0]

        def emit_wq(npiece):
            hi = min(_wq_cursor[0] + npiece, len(wq_pieces))
            for pi in range(_wq_cursor[0], hi):
                wq_pieces[pi]()
            _wq_cursor[0] = hi

        bvhl = sb(pw, "bvhl", [1, 2 * D], BF16)
        bvf = sb(pw, "bvf", [1, D])
        kaug = [sb(pw, f"kaug{h}", [KC, S], BF16) for h in range(4)]
        vqt = sb(pw, "vqt", [128, 32 * 260], BF16)
        qaug = [[sb(pw, f"qa{h}_{g}", [KC, GN if g < G8 else NBANK], BF16)
                 for g in range(G8 + 1)] for h in range(4)]
        lnfw = [sb(pw, f"lnfw{s_}", [128, 8 * 512], BF16) for s_ in range(2)]
        lnow = [sb(pw, f"lnow{s_}", [128, 8 * GN], BF16) for s_ in range(2)]
        rkw = [sb(pw, f"rkw{s_}", [128, 2 * 512]) for s_ in range(2)]
        rqw = [sb(pw, f"rqw{s_}", [128, 2 * GN]) for s_ in range(2)]
        wqkv = sb(pw, "wqkv", [128, 256])
        wqkv_b = sb(pw, "wqkv_b", [128, 8 * 256 * 3], BF16)
        xsb = sb(pw, "xsb", [128, 3 * 512])
        sqb = sb(pw, "sqb", [128, 512], BF16)
        p4t = sb(pw, "p4t", [128, 3 * 4 * GN], BF16)
        pbank = sb(pw, "pbank", [128, 4 * KBANK * NBANK], BF16)
        oraw = sb(pw, "oraw", [128, 2 * NQ])
        drawt = sb(pw, "drawt", [128, NQ])
        dinvb = sb(pw, "dinvb", [128, NQ])
        obf = sb(pw, "obf", [128, 2 * NQ], BF16)
        nstat = sb(pw, "nstat", [128, 40])
        ntr = sb(pw, "ntr", [32, 32])
        ntrT = sb(pw, "ntrT", [32, 32])
        ropeT = sb(pw, "ropeT", [128, 512]); ropeT2 = sb(pw, "ropeT2", [128, 512])
        valr = sb(pw, "valr", [1, 8])

        E("ve", lambda v: v.memset(bass.AP(vqt, 64, [[260 * 32, 128], [65, 128], [1, 1]]), 1.0),
          writes=("vq_ones",))
        E("sy", lambda s: s.dma_start(out=bvf[:, :], in_=bvr_d.ap()), writes=("bvf",), dma=True)
        E("ve", lambda v: v.tensor_copy(bvhl[0:1, 0:D], bvf[0:1, :]), reads=("bvf",), writes=("bvhl_h",))
        E("ve", lambda v: v.tensor_sub(bvhl[0:1, D:2 * D], bvf[0:1, :], bvhl[0:1, 0:D]),
          reads=("bvf", "bvhl_h"), writes=("bvhl_l",))

        def emit_rope(xs, xs_c0, ctab, cc0, stab, sc0, n, dsts, rd):
            E("ve", lambda v: v.tensor_mul(ropeT[:, 0:n], xs[:, xs_c0:xs_c0 + n], ctab[:, cc0:cc0 + n]),
              reads=rd + ("ropeT",), writes=("ropeT",))
            for b_ in (0, 64):
                E("ve", lambda v, b_=b_: v.tensor_mul(ropeT2[b_:b_ + 32, 0:n],
                  xs[b_ + 32:b_ + 64, xs_c0:xs_c0 + n], stab[b_ + 32:b_ + 64, sc0:sc0 + n]),
                  reads=rd + ("ropeT2",), writes=("ropeT2",))
                E("ve", lambda v, b_=b_: v.tensor_mul(ropeT2[b_ + 32:b_ + 64, 0:n],
                  xs[b_:b_ + 32, xs_c0:xs_c0 + n], stab[b_:b_ + 32, sc0:sc0 + n]),
                  reads=rd + ("ropeT2",), writes=("ropeT2",))
            for i, (dt_, p0, c0, dbuf) in enumerate(dsts):
                b_ = 64 * i
                E("ve", lambda v, dt_=dt_, p0=p0, c0=c0, b_=b_: v.tensor_add(
                    dt_[p0:p0 + 64, c0:c0 + n], ropeT[b_:b_ + 64, 0:n], ropeT2[b_:b_ + 64, 0:n]),
                    reads=("ropeT", "ropeT2", dbuf), writes=(dbuf,))

        import os as _os
        _QORD = [int(c) for c in _os.environ.get("K_QORDER", "0123")]
        _QPH = int(_os.environ.get("K_QPHASES", "9"))
        _QW = _os.environ.get("K_QW", "1") == "1"
        _Q64 = _os.environ.get("K_Q64", "1") == "1"
        _QMS = _os.environ.get("K_QMS", "1") == "1"
        for _Qi, Q in enumerate(_QORD):
            emit_wq(34 if _Qi < 3 else len(wq_pieces))
            for ic in (range(8) if (_Qi == 0 or _QW) else ()):
                for wi, wd in enumerate((wq_d, wk_d, wv_d)):
                    E("sy", lambda s, ic=ic, wd=wd, Q=Q: s.dma_start(
                        out=wqkv[:, 0:256],
                        in_=wd.ap()[:, D * ic + 256 * Q: D * ic + 256 * Q + 256]),
                        reads=("wqkv",), writes=("wqkv",), dma=True)
                    E("ve", lambda v, ic=ic, wi=wi: v.tensor_copy(
                        wqkv_b[:, 768 * ic + 256 * wi: 768 * ic + 256 * wi + 256], wqkv[:, 0:256]),
                        reads=("wqkv", f"wb{wi}_{ic}"), writes=(f"wb{wi}_{ic}",))
            WBap = lambda wi, ic, j, w=128: wqkv_b[:, 768 * ic + 256 * wi + 128 * j: 768 * ic + 256 * wi + 128 * j + w]
            WBrd = lambda wi, ic: (f"wb{wi}_{ic}",)

            if _Qi == 0:
                for h in range(4):
                    E("ve", lambda v, h=h: v.memset(kaug[h][64:96, :], 0.0),
                      reads=(f"ka{h}", f"kc{h}"), writes=(f"kc{h}",))
                    E("sy", lambda s, h=h: s.dma_start(out=kaug[h][KB0:KB0 + NKILL, :], in_=wkill_d.ap()),
                      reads=(f"kkill{h}",), writes=(f"kkill{h}",), dma=True)
            if _Qi == 0 or _QMS:
                E("ve", lambda v: v.memset(nstat[:, 0:40], 0.0), reads=("nstat",), writes=("nstat",))
                E("ve", lambda v: v.memset(ntr[:, :], 0.0), reads=("ntr",), writes=("ntr",))
                E("ve", lambda v: v.memset(drawt[:, :], 1.0), reads=("draw",), writes=("draw",))

            for tg in (range(8) if (_Qi == 0 or _QPH >= 1) else ()):
                sl = tg % 2
                lf = lnfw[sl]
                for ic in range(8):
                    E("sy", lambda s, ic=ic, tg=tg, lf=lf: s.dma_start(
                        out=lf[:, 512 * ic:512 * (ic + 1)],
                        in_=ln1f_h.ap()[:, S * ic + 512 * tg: S * ic + 512 * (tg + 1)]),
                        reads=("lfdone", f"lnfw{sl}"), writes=(f"lnfw{sl}",), dma=True)
                rk = rkw[sl]
                E("sy", lambda s, tg=tg, rk=rk: s.dma_start(out=rk[:, 0:512],
                  in_=c2k_d.ap()[:, 512 * tg:512 * (tg + 1)]), reads=(f"rkw{sl}",), writes=(f"rkw{sl}",), dma=True)
                E("sy", lambda s, tg=tg, rk=rk: s.dma_start(out=rk[:, 512:1024],
                  in_=nsk_d.ap()[:, 512 * tg:512 * (tg + 1)]), reads=(f"rkw{sl}",), writes=(f"rkw{sl}",), dma=True)
                for j in range(2):
                    pj = PS[4 + sl]; pb = PSN[4 + sl]
                    for ic in range(8):
                        E("te", lambda t, ic=ic, j=j, lf=lf, pj=pj: t.matmul(
                            pj[:, 0:512], WBap(1, ic, j), lf[:, 512 * ic:512 * (ic + 1)],
                            start=(ic == 0), stop=(ic == 7)),
                            reads=(f"lnfw{sl}",) + WBrd(1, ic),
                            writes=(pb,) if ic in (0, 7) else (), inc=(ic == 7))
                    xcol = 512 * sl
                    E("se", lambda s, pj=pj, j=j, xcol=xcol, Q=Q: s.activation(
                        xsb[:, xcol:xcol + 512], pj[:, 0:512], AF.Identity,
                        bias=biases[:, 8 + 2 * Q + j:9 + 2 * Q + j]),
                        reads=(pb, "bias_k", f"xsb{sl}"), writes=(f"xsb{sl}",))
                    E("ve", lambda v, xcol=xcol: v.tensor_mul(sqb[:, :], xsb[:, xcol:xcol + 512],
                      xsb[:, xcol:xcol + 512]), reads=(f"xsb{sl}", "sqb"), writes=("sqb",))
                    E("te", lambda t, j=j: t.matmul(PS[6][0:4, 0:512], hs4[j][:, :], sqb[:, :],
                      start=True, stop=True), reads=("sqb", f"hs4{j}", "PS6"), writes=("PS6",))
                    E("ve", lambda v: v.tensor_reduce(nstat[0:4, 4:5], PS[6][0:4, 0:512], AX.X, AO.max),
                      reads=("PS6", "nstat"), writes=("nstat",), lazy=True)
                    E("ve", lambda v: v.tensor_max(nstat[0:4, 0:1], nstat[0:4, 0:1], nstat[0:4, 4:5]),
                      reads=("nstat",), writes=("nstat",), lazy=True)
                    emit_rope(xsb, xcol, rk, 0, rk, 512, 512,
                              [(kaug[2 * j], 0, 512 * tg, f"ka{2 * j}"),
                               (kaug[2 * j + 1], 0, 512 * tg, f"ka{2 * j + 1}")],
                              (f"xsb{sl}", f"rkw{sl}"))
                for q4 in range(4):
                    tt = 4 * tg + q4
                    pv_ = PS[4 + (1 - sl)]; pvb = PSN[4 + (1 - sl)]
                    for ic in range(8):
                        E("te", lambda t, ic=ic, q4=q4, lf=lf, pv_=pv_: t.matmul(
                            pv_[:, 0:256], lf[:, 512 * ic + 128 * q4:512 * ic + 128 * q4 + 128],
                            WBap(2, ic, 0, 256), start=(ic == 0), stop=False),
                            reads=(f"lnfw{sl}",) + WBrd(2, ic),
                            writes=(pvb,) if ic == 0 else (), inc=False)
                    E("te", lambda t, pv_=pv_, Q=Q: t.matmul(pv_[:, 0:256], onesbf[0:1, :],
                      bvhl[0:1, 256 * Q:256 * Q + 256], start=False, stop=False),
                      reads=("bvhl_h", "onesbf"), inc=False)
                    E("te", lambda t, pv_=pv_, Q=Q: t.matmul(pv_[:, 0:256], onesbf[0:1, :],
                      bvhl[0:1, D + 256 * Q:D + 256 * Q + 256], start=False, stop=True),
                      reads=("bvhl_l", "onesbf"), writes=(pvb,))
                    E("se", lambda s, tt=tt, pv_=pv_: s.copy(
                        bass.AP(vqt, 260 * tt, [[260 * 32, 128], [65, 4], [1, 64]]), pv_[:, 0:256]),
                        reads=(pvb, "vq_ones", f"vq{tt}"), writes=(f"vq{tt}",))

            for g in (range(G8 + 1) if (_Qi == 0 or _QPH >= 2) else ()):
                n = GN if g < G8 else NBANK
                sl = g % 2
                lo = lnow[sl]
                for ic in range(8):
                    E("sy", lambda s, ic=ic, g=g, n=n, lo=lo: s.dma_start(
                        out=lo[:, GN * ic:GN * ic + n],
                        in_=ln1o_h.ap()[:, NQ * ic + GN * g: NQ * ic + GN * g + n]),
                        reads=("lodone", f"lnow{sl}"), writes=(f"lnow{sl}",), dma=True)
                rq = rqw[sl]
                E("sy", lambda s, g=g, n=n, rq=rq: s.dma_start(out=rq[:, 0:n],
                  in_=c2q_d.ap()[:, GN * g:GN * g + n]), reads=(f"rqw{sl}",), writes=(f"rqw{sl}",), dma=True)
                E("sy", lambda s, g=g, n=n, rq=rq: s.dma_start(out=rq[:, GN:GN + n],
                  in_=nsq_d.ap()[:, GN * g:GN * g + n]), reads=(f"rqw{sl}",), writes=(f"rqw{sl}",), dma=True)
                for j in range(2):
                    pj = PS[4 + sl]; pb = PSN[4 + sl]
                    for ic in range(8):
                        E("te", lambda t, ic=ic, j=j, n=n, lo=lo, pj=pj: t.matmul(
                            pj[:, 0:n], WBap(0, ic, j), lo[:, GN * ic:GN * ic + n],
                            start=(ic == 0), stop=(ic == 7)),
                            reads=(f"lnow{sl}",) + WBrd(0, ic),
                            writes=(pb,) if ic in (0, 7) else (), inc=(ic == 7))
                    E("se", lambda s, pj=pj, j=j, n=n, Q=Q: s.activation(xsb[:, 1024:1024 + n], pj[:, 0:n], AF.Identity,
                      bias=biases[:, 2 * Q + j:2 * Q + j + 1]),
                      reads=(pb, "bias_q", "xsbq"), writes=("xsbq",))
                    E("ve", lambda v, n=n: v.tensor_mul(sqb[:, 0:n], xsb[:, 1024:1024 + n], xsb[:, 1024:1024 + n]),
                      reads=("xsbq", "sqb"), writes=("sqb",))
                    E("te", lambda t, j=j, n=n: t.matmul(PS[6][0:4, 0:n], hs4[j][:, :], sqb[:, 0:n],
                      start=True, stop=True), reads=("sqb", f"hs4{j}", "PS6"), writes=("PS6",))
                    E("ve", lambda v, n=n: v.tensor_reduce(nstat[0:4, 5:6], PS[6][0:4, 0:n], AX.X, AO.max),
                      reads=("PS6", "nstat"), writes=("nstat",), lazy=True)
                    E("ve", lambda v: v.tensor_max(nstat[0:4, 1:2], nstat[0:4, 1:2], nstat[0:4, 5:6]),
                      reads=("nstat",), writes=("nstat",), lazy=True)
                    emit_rope(xsb, 1024, rq, 0, rq, GN, n,
                              [(qaug[2 * j][g], 0, 0, f"qa{2 * j}_{g}"),
                               (qaug[2 * j + 1][g], 0, 0, f"qa{2 * j + 1}_{g}")],
                              ("xsbq", f"rqw{sl}"))
            if _Qi == 0:
                for h in range(4):
                    for g in range(G8 + 1):
                        n = GN if g < G8 else NBANK
                        qa = qaug[h][g]
                        E("ve", lambda v, qa=qa, n=n: v.memset(qa[64:96, 0:n], 0.0), writes=(f"qc1_{h}_{g}",))
                        E("ve", lambda v, qa=qa, n=n: v.memset(qa[64:65, 0:n], 1.0),
                          reads=(f"qc1_{h}_{g}",), writes=(f"qc1_{h}_{g}",))
                        E("sy", lambda s, qa=qa, g=g, n=n: s.dma_start(
                            out=qa[KB0:KB0 + NKILL, 0:n],
                            in_=qkill_d.ap()[:, GN * g:GN * g + n]),
                            writes=(f"qc2_{h}_{g}",), dma=True)

            if _Qi > 0 and not _Q64:
                continue
            E("ve", lambda v: v.tensor_add(nstat[0:4, 6:7], nstat[0:4, 0:1], nstat[0:4, 1:2]),
              reads=("nstat",), writes=("nstat",), lazy=True)
            E("ve", lambda v: v.tensor_scalar(nstat[0:4, 6:7], nstat[0:4, 6:7], -1.0 / 16.0, None, AO.mult),
              reads=("nstat",), writes=("nstat",), lazy=True)
            E("ve", lambda v: v.tensor_copy(ntr[0:32, 0:1], nstat[0:32, 6:7]),
              reads=("nstat", "ntr"), writes=("ntr",), lazy=True)
            vguard()
            E("ve", lambda v: v.transpose(ntrT[0:32, 0:32], ntr[0:32, 0:32]),
              reads=("ntr", "ntrT"), writes=("ntrT",), lazy=True)
            vguard()
            E("ve", lambda v: v.tensor_copy(valr[0:1, 0:4], ntrT[0:1, 0:4]), reads=("ntrT", "valr"), writes=("valr",), lazy=True)
            for h in range(4):
                for kc8 in range(8):
                    E("ve", lambda v, h=h, kc8=kc8: v.tensor_scalar(
                        kaug[h][64:65, 512 * kc8:512 * (kc8 + 1)], onesw[0:1, :],
                        valr[0:1, h:h + 1], None, AO.mult),
                        reads=("valr", "onesw", f"kc{h}"), writes=(f"kc{h}",))

            for g in (range(G8 + 1) if (_Qi == 0 or _QPH >= 3) else ()):
                n = GN if g < G8 else NBANK
                K = KSLOT[g] if g < G8 else KBANK
                is_bank = (g == G8)

                def emit_score(i, g=g, n=n, is_bank=is_bank):
                    rot = i % 3
                    bs = 2 * (i % 2)
                    for h in range(4):
                        psc = PS[(0 if bs == 0 else 4) + h // 2]
                        pscb = PSN[(0 if bs == 0 else 4) + h // 2]
                        coff = 256 * (h % 2)
                        oap = psc[:, coff:coff + n]
                        kap = kaug[h][0:KC, 128 * i:128 * (i + 1)]
                        qap = qaug[h][g][0:KC, 0:n]
                        E("te", lambda t, oap=oap, kap=kap, qap=qap: t.matmul(
                            oap, kap, qap, start=True, stop=True),
                            reads=(f"ka{h}", f"kc{h}", f"kkill{h}", f"qa{h}_{g}",
                                   f"qc1_{h}_{g}", f"qc2_{h}_{g}", pscb),
                            writes=(pscb,))
                    if not is_bank:
                        for hp in (0, 2):
                            psc = PS[(0 if bs == 0 else 4) + hp // 2]
                            pscb = PSN[(0 if bs == 0 else 4) + hp // 2]
                            iap = psc[:, 0:2 * n]
                            pcol = (4 * rot + hp) * GN
                            oap = p4t[:, pcol:pcol + 2 * n]
                            E("se", lambda s, oap=oap, iap=iap: s.activation(oap, iap, AF.Exp),
                              reads=(pscb, f"p4_{rot}_{hp}", f"p4_{rot}_{hp + 1}"),
                              writes=(f"p4_{rot}_{hp}", f"p4_{rot}_{hp + 1}"))
                    else:
                        for h in range(4):
                            psc = PS[(0 if bs == 0 else 4) + h // 2]
                            pscb = PSN[(0 if bs == 0 else 4) + h // 2]
                            coff = 256 * (h % 2)
                            iap = psc[:, coff:coff + n]
                            pcol = h * KBANK * NBANK + i * NBANK
                            oap = pbank[:, pcol:pcol + n]
                            E("se", lambda s, oap=oap, iap=iap: s.activation(oap, iap, AF.Exp),
                              reads=(pscb, f"pbk{h}"), writes=(f"pbk{h}",))
                    if not is_bank:
                        for needs, mask, mbuf in ((NEED_A[g], m1_s, "m1_s"), (NEED_B[g], m2_s, "m2_s")):
                            for l in range(2):
                                if i == needs - 2 + l:
                                    pap = bass.AP(p4t, (4 * rot) * GN + 128 * l,
                                                  [[3 * 4 * GN, 128], [GN, 4], [1, 128]])
                                    map_ = bass.AP(mask, 0, [[512, 128], [128, 4], [1, 128]])
                                    E("ve", lambda v, pap=pap, map_=map_: v.tensor_tensor(
                                        pap, pap, map_, AO.mult),
                                        reads=(f"p4_{rot}_0", f"p4_{rot}_1", f"p4_{rot}_2", f"p4_{rot}_3", mbuf),
                                        writes=(f"p4_{rot}_0", f"p4_{rot}_1", f"p4_{rot}_2", f"p4_{rot}_3"))

                def emit_pv(i, g=g, n=n, K=K, is_bank=is_bank):
                    rot = i % 3
                    for h in range(4):
                        obank = (2, 3, 6, 7)[h]
                        po = PS[obank]; pob = PSN[obank]
                        coff = 0
                        if not is_bank:
                            pcol = (4 * rot + h) * GN
                            prd = f"p4_{rot}_{h}"
                            src = p4t
                        else:
                            pcol = h * KBANK * NBANK + i * NBANK
                            prd = f"pbm{h}"
                            src = pbank
                        oap = po[0:65, coff:coff + n]
                        vap = bass.AP(vqt, 260 * i + 65 * h, [[260 * 32, 128], [1, 65]])
                        pap = src[:, pcol:pcol + n]
                        E("te", lambda t, oap=oap, vap=vap, pap=pap, i=i, K=K: t.matmul(
                            oap, vap, pap, start=(i == 0), stop=(i == K - 1)),
                            reads=(f"vq{i}", prd, "vq_ones"),
                            writes=(pob,) if i in (0, K - 1) else (), inc=(i == K - 1))

                if not is_bank:
                    for i in range(K):
                        emit_score(i)
                        if i >= 1:
                            emit_pv(i - 1)
                    emit_pv(K - 1)
                else:
                    for i in range(K):
                        emit_score(i)
                    for h in range(4):
                        E("ve", lambda v, h=h: v.tensor_tensor(
                            pbank[:, h * KBANK * NBANK:(h + 1) * KBANK * NBANK],
                            pbank[:, h * KBANK * NBANK:(h + 1) * KBANK * NBANK],
                            bmask_s[:, :], AO.mult),
                            reads=(f"pbk{h}", "bmask_s"), writes=(f"pbm{h}", f"pbk{h}"))
                    for i in range(K):
                        emit_pv(i)

                gc0 = GN * g
                for h in range(4):
                    obank = (2, 3, 6, 7)[h]
                    po = PS[obank]; pob = PSN[obank]
                    coff = 0
                    E("se", lambda s, h=h, po=po, coff=coff, gc0=gc0, n=n: s.copy(
                        oraw[64 * (h % 2):64 * (h % 2) + 64, NQ * (h // 2) + gc0: NQ * (h // 2) + gc0 + n],
                        po[0:64, coff:coff + n]), reads=(pob, "oraw"), writes=("oraw",))
                    E("se", lambda s, h=h, po=po, coff=coff, gc0=gc0, n=n: s.copy(
                        drawt[32 * h:32 * h + 1, gc0:gc0 + n], po[64:65, coff:coff + n]),
                        reads=(pob, "draw"), writes=("draw",))
                if dbg2_h is not None and is_bank:
                    E("ge", lambda g_: g_.dma_start(out=dbg2_h.ap()[:, 0:4 * KBANK * NBANK],
                      in_=pbank[:, :]), reads=tuple(f"pbm{h}" for h in range(4)),
                      writes=("dbg4",), dma=True)
                    E("ge", lambda g_: g_.dma_start(
                      out=dbg2_h.ap()[:, 4 * KBANK * NBANK:4 * KBANK * NBANK + 16],
                      in_=drawt[:, 2048:2064]), reads=("draw",), writes=("dbg4",), dma=True)
                    E("ge", lambda g_: g_.dma_start(
                      out=dbg2_h.ap()[:, 4 * KBANK * NBANK + 16:4 * KBANK * NBANK + 32],
                      in_=oraw[:, 2048:2064]), reads=("oraw",), writes=("dbg4",), dma=True)
                    E("ge", lambda g_: g_.dma_start(
                      out=dbg2_h.ap()[:, 4 * KBANK * NBANK + 32:4 * KBANK * NBANK + 48],
                      in_=oraw[:, NQ + 2048:NQ + 2064]), reads=("oraw",), writes=("dbg4",), dma=True)

            if _Qi > 0 and _QPH < 4:
                continue
            E("ve", lambda v: v.reciprocal(dinvb[:, :], drawt[:, :]), reads=("draw", "dinv"), writes=("dinv",), lazy=True)
            for jc in range(2):
                for t5 in range(5):
                    c5 = min(512 * t5, NQ - 512)
                    n5 = 512
                    E("te", lambda t, jc=jc, c5=c5, n5=n5: t.matmul(PS[4][:, 0:n5], selq[jc][:, :],
                      dinvb[:, c5:c5 + n5], start=True, stop=True),
                      reads=("dinv", f"selq{jc}", "PS4"), writes=("PS4",))
                    E("ve", lambda v, c5=c5, n5=n5: v.tensor_copy(drawt[:, c5:c5 + n5], PS[4][:, 0:n5]),
                      reads=("PS4", "draw"), writes=("draw2",))
                    E("ve", lambda v, jc=jc, c5=c5, n5=n5: v.tensor_mul(
                        obf[:, NQ * jc + c5: NQ * jc + c5 + n5],
                        oraw[:, NQ * jc + c5: NQ * jc + c5 + n5], drawt[:, c5:c5 + n5]),
                        reads=("draw2", "oraw", f"obf{jc}_{t5}"), writes=(f"obf{jc}_{t5}",))
                E("ge", lambda g_, jc=jc, Q=Q: g_.dma_start(
                    out=obf_h.ap()[:, NQ * (2 * Q + jc): NQ * (2 * Q + jc + 1)],
                    in_=obf[:, NQ * jc: NQ * (jc + 1)]),
                    reads=tuple(f"obf{jc}_{t5}" for t5 in range(5)), writes=(f"obfh{2 * Q + jc}",), dma=True)

    emit_wq(len(wq_pieces))
    sch.barrier()
    if _DBG_STOP:
        sch.ops["ge"].append(lambda h: h.wait_ge(SEMS["D:obfh1"], 16))
        return _finish(nc, sch, root)
    with contextlib.ExitStack() as p3:
        lt3 = make_ln(p3)
        wo_b = sb(p3, "wo_b", [128, 8 * D], BF16)
        wo_f = sb(p3, "wo_f", [128, 2048])
        obw = sb(p3, "obw", [128, 8 * 512], BF16)
        zbuf = sb(p3, "zbuf", [128, 8 * 512])
        xlw = sb(p3, "xlw", [128, 8 * 512])
        xaw = sb(p3, "xaw", [128, 8 * 512])
        for half in range(4):
            E("sy", lambda s, half=half: s.dma_start(out=wo_f[:, :],
              in_=wo_d.ap()[:, 2048 * half:2048 * (half + 1)]), reads=("wo_f",), writes=("wo_f",), dma=True)
            E("ve", lambda v, half=half: v.tensor_copy(wo_b[:, 2048 * half:2048 * (half + 1)], wo_f[:, :]),
              reads=("wo_f",), writes=(f"wo_b{half}",))
        WOrd = tuple(f"wo_b{half}" for half in range(4))
        for g in range(5):
            n = 512
            c0 = min(512 * g, NQ - 512)
            for ic in range(8):
                E("sy", lambda s, ic=ic, c0=c0, n=n: s.dma_start(out=obw[:, 512 * ic:512 * ic + n],
                  in_=obf_h.ap()[:, NQ * ic + c0: NQ * ic + c0 + n]),
                  reads=(f"obfh{ic}", "obw"), writes=("obw",), dma=True)
                E("sy", lambda s, ic=ic, c0=c0, n=n: s.dma_start(out=xlw[:, 512 * ic:512 * ic + n],
                  in_=xloc.ap()[:, NQ * ic + c0: NQ * ic + c0 + n]),
                  reads=("xlw",), writes=("xlw",), dma=True)
            for oc in range(8):
                pj = PS[4 + oc % 2]; pb = PSN[4 + oc % 2]
                for ic in range(8):
                    E("te", lambda t, oc=oc, ic=ic, n=n, pj=pj: t.matmul(
                        pj[:, 0:n], wo_b[:, D * ic + 128 * oc: D * ic + 128 * oc + 128],
                        obw[:, 512 * ic:512 * ic + n], start=(ic == 0), stop=(ic == 7)),
                        reads=("obw",) + WOrd,
                        writes=(pb,) if ic in (0, 7) else (), inc=(ic == 7))
                E("se", lambda s, oc=oc, n=n, pj=pj: s.activation(zbuf[:, 512 * oc:512 * oc + n],
                  pj[:, 0:n], AF.Identity, bias=biases[:, 16 + oc:17 + oc]),
                  reads=(pb, "bias_o", "zbuf"), writes=("zbuf",))
            emit_ln(lt3, n,
                    src=lambda ic: (zbuf[:, 512 * ic:512 * ic + n], ("zbuf",)),
                    resid=lambda ic: (xlw[:, 512 * ic:512 * ic + n], ("xlw",)),
                    res_out=lambda ic: (xaw[:, 512 * ic:512 * ic + n], f"xaw{ic}"))
            for ic in range(8):
                E("ge", lambda g_, ic=ic, c0=c0, n=n: g_.dma_start(
                    out=xatt_h.ap()[:, NQ * ic + c0: NQ * ic + c0 + n],
                    in_=xaw[:, 512 * ic:512 * ic + n]),
                    reads=(f"xaw{ic}",), writes=("xattdone",), dma=True)

    sch.barrier()
    with contextlib.ExitStack() as p4s:
        lt4 = make_ln(p4s)
        hslab = sb(p4s, "hslab", [128, 32 * SW])
        hbank = sb(p4s, "hbank", [128, 32 * NBANK])
        xattw = sb(p4s, "xattw", [128, 8 * SLABW])
        ln2f = sb(p4s, "ln2f", [128, 8 * SLABW])
        xqb = sb(p4s, "xqb", [128, 8 * SLABW], BF16)
        w1w = [sb(p4s, f"w1w{i}", [128, 8 * 128], BF16) for i in range(2)]
        w2w = [sb(p4s, f"w2w{i}", [128, 1024], BF16) for i in range(2)]
        tmpc = sb(p4s, "tmpc", [128, FSLAB])
        amtr = sb(p4s, "amtr", [128, 544])
        amtc = sb(p4s, "amtc", [128, 544])
        selw = sb(p4s, "selw", [128, 512])
        st32 = sb(p4s, "st32", [32, 544])
        amT = sb(p4s, "amT", [32, 544])
        amC = sb(p4s, "amC", [32, 544])
        srow = sb(p4s, "srow", [1, 1100])
        sbc = sb(p4s, "sbc", [128, SLABW]); sxinv = sb(p4s, "sxinv", [128, SLABW])
        s2bc = sb(p4s, "s2bc", [128, FSLAB]); s2inv = sb(p4s, "s2inv", [128, FSLAB])
        h2qb = [sb(p4s, f"h2qb{i}", [128, 512], BF16) for i in range(2)]
        zb2 = sb(p4s, "zb2", [128, 8 * FSLAB])
        ybuf = sb(p4s, "ybuf", [128, 8 * FSLAB])
        ecol = sb(p4s, "ecol", [128, 2 * 32])
        E("ve", lambda v: v.memset(amC[:, :], 0.0), writes=("amC",))
        E("ve", lambda v: v.memset(amT[:, :], 0.0), writes=("amT",))
        E("ve", lambda v: v.memset(amtr[:, :], 0.0), writes=("amtr",))

        def absmax_rows(tag, src_fn, nsrc, n, dst_c0):
            for i in range(nsrc):
                api, rdi = src_fn(i)
                if i == 0:
                    E("se", lambda s, api=api: s.activation(amtr[:, 0:n], api, AF.Abs),
                      reads=rdi + ("amtr",), writes=("amtr",))
                else:
                    E("se", lambda s, api=api: s.activation(amtc[:, 0:n], api, AF.Abs),
                      reads=rdi + ("amtc",), writes=("amtc",))
                    E("ve", lambda v: v.tensor_max(amtr[:, 0:n], amtr[:, 0:n], amtc[:, 0:n]),
                      reads=("amtc", "amtr"), writes=("amtr",))
            E("ve", lambda v: v.tensor_copy(amtc[0:64, 0:n], amtr[64:128, 0:n]),
              reads=("amtr", "amtc"), writes=("amtc",))
            E("ve", lambda v: v.tensor_max(amtr[0:64, 0:n], amtr[0:64, 0:n], amtc[0:64, 0:n]),
              reads=("amtc", "amtr"), writes=("amtr",))
            E("ve", lambda v: v.tensor_copy(amtc[0:32, 0:n], amtr[32:64, 0:n]),
              reads=("amtr", "amtc"), writes=("amtc",))
            E("ve", lambda v: v.tensor_max(amtr[0:32, 0:n], amtr[0:32, 0:n], amtc[0:32, 0:n]),
              reads=("amtc", "amtr"), writes=("amtr",))
            nb = (n + 31) // 32
            vguard()
            for j in range(nb):
                E("ve", lambda v, j=j: v.transpose(amT[0:32, 32 * j:32 * j + 32],
                  amtr[0:32, 32 * j:32 * j + 32]), reads=("amtr", "amT"), writes=("amT",), lazy=True)
            vguard()
            for j in range(nb):
                E("ve", lambda v, j=j: v.tensor_reduce(amC[0:32, 32 * j:32 * j + 1],
                  amT[0:32, 32 * j:32 * j + 32], AX.X, AO.max), reads=("amT", "amC"), writes=("amC",), lazy=True)
            vguard()
            for j in range(nb):
                E("ve", lambda v, j=j: v.transpose(st32[0:32, 32 * j:32 * j + 32],
                  amC[0:32, 32 * j:32 * j + 32]), reads=("amC", "st32"), writes=("st32",), lazy=True)
            vguard()

        def bcast_row(src_c0, n, dst, dstbuf):
            for it in range((n + 511) // 512):
                c0 = min(512 * it, max(0, n - 512))
                w = min(512, n)
                E("te", lambda t, c0=c0, w=w: t.matmul(PS[5][:, 0:w], onesf[0:1, :],
                  srow[0:1, src_c0 + c0: src_c0 + c0 + w], start=True, stop=True),
                  reads=("srow", "onesf", "PS5"), writes=("PS5",))
                E("ve", lambda v, c0=c0, w=w: v.tensor_copy(dst[:, c0:c0 + w], PS[5][:, 0:w]),
                  reads=("PS5", dstbuf), writes=(dstbuf,))

        def quant_cols(srcT, src_rd_w, n, sb_, sbuf_n, dstT, dst_w, stride):
            for ic in range(8):
                E("ve", lambda v, ic=ic: v.tensor_tensor(srcT[:, stride * ic:stride * ic + n],
                  srcT[:, stride * ic:stride * ic + n], sb_[:, 0:n], AO.mult),
                  reads=(src_rd_w(ic), sbuf_n), writes=(src_rd_w(ic),))
                E("ve", lambda v, ic=ic: v.tensor_scalar(dstT[:, stride * ic:stride * ic + n],
                  srcT[:, stride * ic:stride * ic + n], MAGIC, MAGIC, AO.add, AO.subtract),
                  reads=(src_rd_w(ic), dst_w(ic)), writes=(dst_w(ic),))

        BW = 512
        bc0 = NQ - BW
        for ic in range(8):
            E("sy", lambda s, ic=ic: s.dma_start(out=xattw[:, SLABW * ic:SLABW * ic + BW],
              in_=xatt_h.ap()[:, NQ * ic + bc0: NQ * ic + bc0 + BW]),
              reads=("xattdone", "xattw"), writes=("xattw",), dma=True)
        emit_ln(lt4, BW,
                src=lambda ic: (xattw[:, SLABW * ic:SLABW * ic + BW], ("xattw",)),
                dst_f=lambda ic: (ln2f[:, SLABW * ic:SLABW * ic + BW], f"ln2f{ic}"))
        absmax_rows("bk", lambda ic: (ln2f[:, SLABW * ic:SLABW * ic + BW],
                                      tuple(f"ln2f{i}" for i in range(8))), 8, BW, 0)
        E("ve", lambda v: v.tensor_scalar(srow[0:1, 550:550 + BW], st32[0:1, 0:BW], EPS, None, AO.max),
          reads=("srow", "st32"), writes=("srow",))
        E("ve", lambda v: v.reciprocal(srow[0:1, 0:BW], srow[0:1, 550:550 + BW]),
          reads=("srow",), writes=("srow",), lazy=True)
        E("ve", lambda v: v.tensor_scalar(srow[0:1, 0:BW], srow[0:1, 0:BW], 127.0, None, AO.mult),
          reads=("srow",), writes=("srow",))
        E("ve", lambda v: v.tensor_scalar(srow[0:1, 550:550 + BW], srow[0:1, 550:550 + BW],
          1.0 / 127.0, None, AO.mult), reads=("srow",), writes=("srow",))
        bcast_row(0, BW, sbc[:, 0:BW], "sbc")
        bcast_row(550, BW, sxinv[:, 0:BW], "sxinv")
        quant_cols(ln2f, lambda ic: f"ln2f{ic}", BW, sbc[:, 0:BW], "sbc",
                   xqb, lambda ic: f"xqb{ic}", SLABW)
        XQKRD = tuple(f"xqb{ic}" for ic in range(8))
        for oc in range(32):
            wsl = oc % 2
            E("sy", lambda s, oc=oc, wsl=wsl: s.dma_start(out=w1w[wsl][:, :],
              in_=bass.AP(w1q_h, 128 * oc, [[8 * DH, 128], [DH, 8], [1, 128]])),
              reads=("w1done", f"w1w{wsl}"), writes=(f"w1w{wsl}",), dma=True)
            pj = PS[4 + oc % 2]; pb = PSN[4 + oc % 2]
            for ic in range(8):
                E("te", lambda t, ic=ic, wsl=wsl, pj=pj: t.matmul(
                    pj[:, 0:BW], w1w[wsl][:, 128 * ic:128 * (ic + 1)],
                    xqb[:, SLABW * ic: SLABW * ic + BW], start=(ic == 0), stop=(ic == 7)),
                    reads=XQKRD + (f"w1w{wsl}",),
                    writes=(pb,) if ic in (0, 7) else (), inc=(ic == 7))
            E("ve", lambda v, oc=oc, pj=pj: v.scalar_tensor_tensor(hbank[:, NBANK * oc:NBANK * (oc + 1)],
              pj[:, BW - NBANK:BW], swinv[:, 0:1], sxinv[:, BW - NBANK:BW], AO.mult, AO.mult),
              reads=(pb, "swinv0", "sxinv", "hbank"), writes=("hbank",))

        for sl_i in range(NSLAB):
            t0 = FSLAB * sl_i
            lo = max(0, t0 - 1); hi = min(NQ, t0 + FSLAB + 1)
            hi = min(hi, NOWN + 1) if sl_i == NSLAB - 1 else hi
            off = lo - (t0 - 1)
            nw = hi - lo
            for ic in range(8):
                E("sy", lambda s, ic=ic, lo=lo, nw=nw, off=off: s.dma_start(
                    out=xattw[:, SLABW * ic + off: SLABW * ic + off + nw],
                    in_=xatt_h.ap()[:, NQ * ic + lo: NQ * ic + lo + nw]),
                    reads=("xattdone", "xattw"), writes=("xattw",), dma=True)
            if off:
                E("ve", lambda v: v.memset(bass.AP(xattw, 0, [[8 * SLABW, 128], [SLABW, 8], [1, 1]]), 0.0),
                  reads=("xattw",), writes=("xattw",))
            for (lc0, lw) in ((0, 512), (SLABW - 512, 512)):
                emit_ln(lt4, lw,
                        src=lambda ic, lc0=lc0, lw=lw: (
                            xattw[:, SLABW * ic + lc0:SLABW * ic + lc0 + lw], ("xattw",)),
                        dst_f=lambda ic, lc0=lc0, lw=lw: (
                            ln2f[:, SLABW * ic + lc0:SLABW * ic + lc0 + lw], f"ln2f{ic}"))
            LN2RD = tuple(f"ln2f{ic}" for ic in range(8))
            absmax_rows("xq", lambda ic: (ln2f[:, SLABW * ic:SLABW * (ic + 1)], LN2RD), 8, SLABW, 0)
            E("ve", lambda v: v.tensor_scalar(srow[0:1, 550:550 + SLABW], st32[0:1, 0:SLABW], EPS, None, AO.max),
              reads=("srow", "st32"), writes=("srow",))
            E("ve", lambda v: v.reciprocal(srow[0:1, 0:SLABW], srow[0:1, 550:550 + SLABW]),
              reads=("srow",), writes=("srow",), lazy=True)
            E("ve", lambda v: v.tensor_scalar(srow[0:1, 0:SLABW], srow[0:1, 0:SLABW], 127.0, None, AO.mult),
              reads=("srow",), writes=("srow",))
            E("ve", lambda v: v.tensor_scalar(srow[0:1, 550:550 + SLABW], srow[0:1, 550:550 + SLABW],
              1.0 / 127.0, None, AO.mult), reads=("srow",), writes=("srow",))
            bcast_row(0, SLABW, sbc, "sbc")
            bcast_row(550, SLABW, sxinv, "sxinv")
            quant_cols(ln2f, lambda ic: f"ln2f{ic}", SLABW, sbc[:, 0:SLABW], "sbc",
                       xqb, lambda ic: f"xqb{ic}", SLABW)
            XQRD = tuple(f"xqb{ic}" for ic in range(8))
            for oc in range(32):
                wsl = oc % 2
                E("sy", lambda s, oc=oc, wsl=wsl: s.dma_start(out=w1w[wsl][:, :],
                  in_=bass.AP(w1q_h, 128 * oc, [[8 * DH, 128], [DH, 8], [1, 128]])),
                  reads=("w1done", f"w1w{wsl}"), writes=(f"w1w{wsl}",), dma=True)
                for (c0, nn) in ((0, 512), (SLABW - 128, 128)):
                    pj = PS[4 + oc % 2]; pb = PSN[4 + oc % 2]
                    for ic in range(8):
                        E("te", lambda t, ic=ic, c0=c0, nn=nn, wsl=wsl, pj=pj: t.matmul(
                            pj[:, 0:nn], w1w[wsl][:, 128 * ic:128 * (ic + 1)],
                            xqb[:, SLABW * ic + c0: SLABW * ic + c0 + nn],
                            start=(ic == 0), stop=(ic == 7)),
                            reads=XQRD + (f"w1w{wsl}",),
                            writes=(pb,) if ic in (0, 7) else (), inc=(ic == 7))
                    E("ve", lambda v, oc=oc, c0=c0, nn=nn, pj=pj: v.scalar_tensor_tensor(
                        hslab[:, SW * oc + c0: SW * oc + c0 + nn], pj[:, 0:nn], swinv[:, 0:1],
                        sxinv[:, c0:c0 + nn], AO.mult, AO.mult),
                        reads=(pb, "swinv0", "sxinv", f"hs{oc}", f"h2_{oc}"), writes=(f"hs{oc}", f"h2_{oc}"))
            HSRD = tuple(f"hs{oc}" for oc in range(32))
            if dbg_h is not None and sl_i == 0:
                E("ge", lambda g_: g_.dma_start(out=dbg_h.ap()[:, 1548:1558], in_=hslab[:, 504:514]),
                  reads=HSRD, writes=("dbg3",), dma=True)
            DBGPRE = ("dbg3",) if dbg_h is not None and sl_i == 0 else ()
            for side, extc, selc in ((0, 0, 0), (1, SLABW - 1, 1)):
                E("sy", lambda s, sl_i=sl_i, selc=selc: s.dma_start(
                    out=selw[:, 0:512],
                    in_=(selL_d if selc == 0 else selR_d).ap()[:, 512 * sl_i:512 * (sl_i + 1)]),
                    reads=("selw",), writes=("selw",), dma=True)
                E("ve", lambda v, selc=selc: v.tensor_tensor(
                    amtc[:, 0:512],
                    hbank[:, 0:32 * NBANK], selw[:, 0:512], AO.mult),
                    reads=("hbank", "selw", "amtc"), writes=("amtc",))
                E("ve", lambda v, selc=selc: v.tensor_reduce(
                    ecol[:, 32 * selc:32 * (selc + 1)],
                    bass.AP(amtc, 0, [[544, 128], [NBANK, 32], [1, NBANK]]),
                    AX.X, AO.add), reads=("amtc", "ecol"), writes=("ecol",), lazy=True)
                hcols = bass.AP(hslab, extc, [[32 * SW, 128], [SW, 32], [1, 1]])
                E("ve", lambda v, selc=selc, hcols=hcols: v.tensor_sub(
                    amtc[:, 0:32], ecol[:, 32 * selc:32 * (selc + 1)], hcols),
                    reads=("ecol", "amtc") + HSRD + DBGPRE, writes=("amtc",), lazy=True)
                E("ve", lambda v, sl_i=sl_i, selc=selc: v.tensor_scalar(
                    amtc[:, 0:32], amtc[:, 0:32], cact_s[:, 2 * sl_i + selc:2 * sl_i + selc + 1],
                    None, AO.mult), reads=("amtc", "cact_s"), writes=("amtc",), lazy=True)
                E("ve", lambda v, hcols=hcols: v.tensor_add(hcols, hcols, amtc[:, 0:32]),
                  reads=("amtc",) + HSRD, writes=HSRD, lazy=True)
            for oc in range(32):
                E("ve", lambda v, oc=oc: v.tensor_scalar(tmpc[:, :],
                  hslab[:, SW * oc: SW * oc + FSLAB], cw_s[:, oc:oc + 1], None, AO.mult),
                  reads=(f"hs{oc}", "cw_s", "tmpc"), writes=("tmpc",))
                E("ve", lambda v, oc=oc: v.scalar_tensor_tensor(tmpc[:, :],
                  hslab[:, SW * oc + 1: SW * oc + 1 + FSLAB], cw_s[:, 32 + oc:33 + oc], tmpc[:, :],
                  AO.mult, AO.add), reads=(f"hs{oc}", "cw_s", "tmpc"), writes=("tmpc",))
                E("ve", lambda v, oc=oc: v.scalar_tensor_tensor(tmpc[:, :],
                  hslab[:, SW * oc + 2: SW * oc + 2 + FSLAB], cw_s[:, 64 + oc:65 + oc], tmpc[:, :],
                  AO.mult, AO.add), reads=(f"hs{oc}", "cw_s", "tmpc"), writes=("tmpc",))
                E("se", lambda s, oc=oc: s.activation(hslab[:, SW * oc + 1: SW * oc + 1 + FSLAB],
                  tmpc[:, :], AF.Silu, bias=cb_s[:, oc:oc + 1]),
                  reads=("tmpc", "cb_s", f"hs{oc}"), writes=(f"h2_{oc}",))
            H2RD = tuple(f"h2_{oc}" for oc in range(32))
            absmax_rows("h2", lambda oc: (hslab[:, SW * oc + 1: SW * oc + 1 + FSLAB], H2RD), 32, FSLAB, 0)
            E("ve", lambda v: v.tensor_scalar(srow[0:1, 550:550 + FSLAB], st32[0:1, 0:FSLAB], EPS, None, AO.max),
              reads=("srow", "st32"), writes=("srow",))
            E("ve", lambda v: v.reciprocal(srow[0:1, 0:FSLAB], srow[0:1, 550:550 + FSLAB]),
              reads=("srow",), writes=("srow",), lazy=True)
            E("ve", lambda v: v.tensor_scalar(srow[0:1, 0:FSLAB], srow[0:1, 0:FSLAB], 127.0, None, AO.mult),
              reads=("srow",), writes=("srow",))
            E("ve", lambda v: v.tensor_scalar(srow[0:1, 550:550 + FSLAB], srow[0:1, 550:550 + FSLAB],
              1.0 / 127.0, None, AO.mult), reads=("srow",), writes=("srow",))
            bcast_row(0, FSLAB, s2bc, "s2bc")
            bcast_row(550, FSLAB, s2inv, "s2inv")
            for ch in range(32):
                wsl = ch % 2
                E("sy", lambda s, ch=ch, wsl=wsl: s.dma_start(out=w2w[wsl][:, :],
                  in_=w2q_h.ap()[:, 1024 * ch:1024 * (ch + 1)]),
                  reads=("w2done", f"w2w{wsl}"), writes=(f"w2w{wsl}",), dma=True)
                hq = h2qb[ch % 2]
                E("ve", lambda v, ch=ch: v.tensor_tensor(tmpc[:, :],
                  hslab[:, SW * ch + 1: SW * ch + 1 + FSLAB], s2bc[:, 0:FSLAB], AO.mult),
                  reads=(f"h2_{ch}", "s2bc", "tmpc"), writes=("tmpc",))
                E("ve", lambda v, hq=hq, ch=ch: v.tensor_scalar(hq[:, :], tmpc[:, :], MAGIC, MAGIC,
                  AO.add, AO.subtract), reads=("tmpc", f"hq{ch % 2}"), writes=(f"hq{ch % 2}",))
                for oc in range(8):
                    pj = PS[oc]; pb = PSN[oc]
                    E("te", lambda t, ch=ch, oc=oc, wsl=wsl, hq=hq, pj=pj: t.matmul(
                        pj[:, 0:FSLAB], w2w[wsl][:, 128 * oc:128 * (oc + 1)], hq[:, :],
                        start=(ch == 0), stop=(ch == 31)),
                        reads=(f"hq{ch % 2}", f"w2w{wsl}"),
                        writes=(pb,) if ch in (0, 31) else (), inc=(ch == 31 or oc == 7))
            for oc in range(8):
                pj = PS[oc]; pb = PSN[oc]
                E("ve", lambda v, oc=oc, pj=pj: v.scalar_tensor_tensor(zb2[:, FSLAB * oc: FSLAB * (oc + 1)],
                  pj[:, 0:FSLAB], swinv[:, 1:2], s2inv[:, 0:FSLAB], AO.mult, AO.mult),
                  reads=(pb, "swinv1", "s2inv", "zb2"), writes=("zb2",))
            emit_ln(lt4, FSLAB,
                    src=lambda ic: (zb2[:, FSLAB * ic:FSLAB * (ic + 1)], ("zb2",)),
                    resid=lambda ic: (xattw[:, SLABW * ic + 1: SLABW * ic + 1 + FSLAB], ("xattw",)),
                    res_out=lambda ic: (ybuf[:, FSLAB * ic:FSLAB * (ic + 1)], f"yb{ic}"))
            for ic in range(8):
                E("ge", lambda g_, ic=ic, sl_i=sl_i: g_.dma_start(
                    out=yout.ap()[:, NOWN * ic + FSLAB * sl_i: NOWN * ic + FSLAB * (sl_i + 1)],
                    in_=ybuf[:, FSLAB * ic:FSLAB * (ic + 1)]),
                    reads=(f"yb{ic}",), writes=("ydone",), dma=True)
            if dbg_h is not None and sl_i == 0:
                E("ge", lambda g_: g_.dma_start(out=dbg_h.ap()[:, 0:514], in_=xqb[:, 0:SLABW]),
                  reads=tuple(f"xqb{i}" for i in range(8)), writes=("dbg3",), dma=True)
                E("ge", lambda g_: g_.dma_start(out=dbg_h.ap()[:, 514:1028], in_=hslab[:, 0:SLABW]),
                  reads=tuple(f"h2_{o}" for o in range(32)), writes=("dbg3",), dma=True)
                E("ge", lambda g_: g_.dma_start(out=dbg_h.ap()[:, 1028:1540], in_=zb2[:, 0:512]),
                  reads=("zb2",), writes=("dbg3",), dma=True)
                E("ge", lambda g_: g_.dma_start(out=dbg_h.ap()[:, 1540:1544], in_=s2bc[:, 4:8]),
                  reads=("s2bc",), writes=("dbg3",), dma=True)
                E("ge", lambda g_: g_.dma_start(out=dbg_h.ap()[:, 1544:1548], in_=sbc[:, 4:8]),
                  reads=("sbc",), writes=("dbg3",), dma=True)
                E("ge", lambda g_: g_.dma_start(out=dbg_h.ap()[:, 1558:1622], in_=ecol[:, 0:64]),
                  reads=("ecol",), writes=("dbg3",), dma=True)
                E("ge", lambda g_: g_.dma_start(out=dbg_h.ap()[:, 1622:1630], in_=cact_s[:, 0:8]),
                  reads=("cact_s",), writes=("dbg3",), dma=True)
                E("ge", lambda g_: g_.dma_start(out=dbg_h.ap()[:, 1630:1640], in_=sxinv[:, 504:514]),
                  reads=("sxinv",), writes=("dbg3",), dma=True)
                E("ge", lambda g_: g_.dma_start(out=dbg_h.ap()[:, 1640:1672], in_=selw[:, 0:32]),
                  reads=("selw",), writes=("dbg3",), dma=True)
                E("ge", lambda g_: g_.dma_start(out=dbg_h.ap()[:, 1672:1704], in_=hbank[:, 0:32]),
                  reads=("hbank",), writes=("dbg3",), dma=True)
                E("ge", lambda g_: g_.dma_start(out=dbg_h.ap()[:, 1704:1714], in_=sbc[:, 504:514]),
                  reads=("sbc",), writes=("dbg3",), dma=True)

    ytot = 16 * sch.dfam["ydone"]
    sch.ops["ge"].append(lambda h: h.wait_ge(SEMS["D:ydone"], ytot))
    return _finish(nc, sch, root)


def _finish(nc, sch, root):
    with nc.Block() as block:
        @block.sync
        def _(eng):
            for f in sch.ops["sy"]:
                f(eng)

        @block.tensor
        def _(eng):
            for f in sch.ops["te"]:
                f(eng)

        @block.vector
        def _(eng):
            for f in sch.ops["ve"]:
                f(eng)

        @block.scalar
        def _(eng):
            for f in sch.ops["se"]:
                f(eng)

        @block.gpsimd
        def _(eng):
            for f in sch.ops["ge"]:
                f(eng)

    root.close()
    return nc


SPLIT = np.concatenate([np.arange(0, 64, 2), np.arange(1, 64, 2)])


def _chunkmaj(a):
    rows, cols = a.shape
    nch = rows // 128
    return np.ascontiguousarray(a.reshape(nch, 128, cols).transpose(1, 0, 2).reshape(128, nch * cols))


def _feat(a):
    return _chunkmaj(np.ascontiguousarray(a.T))


_CACHE = {}


def kernel(L0_attn_x, L0_attn_wq, L0_attn_bq, L0_attn_wk, L0_attn_bk,
           L0_attn_wv, L0_attn_bv, L0_attn_wo, L0_attn_bo,
           L0_ffn_w1, L0_ffn_conv_w, L0_ffn_conv_b, L0_ffn_w2):
    x = np.asarray(L0_attn_x, np.float32)
    wq = np.asarray(L0_attn_wq, np.float32); wk = np.asarray(L0_attn_wk, np.float32)
    wv = np.asarray(L0_attn_wv, np.float32); wo = np.asarray(L0_attn_wo, np.float32)
    bq = np.asarray(L0_attn_bq, np.float32); bk = np.asarray(L0_attn_bk, np.float32)
    bv = np.asarray(L0_attn_bv, np.float32); bo = np.asarray(L0_attn_bo, np.float32)
    w1 = np.asarray(L0_ffn_w1, np.float32); w2 = np.asarray(L0_ffn_w2, np.float32)
    cwt = np.asarray(L0_ffn_conv_w, np.float32); cbt = np.asarray(L0_ffn_conv_b, np.float32)

    if "nc" not in _CACHE:
        _CACHE["nc"] = build_program()
    nc = _CACHE["nc"]

    perm = np.concatenate([64 * h + SPLIT for h in range(H)])
    wq_p = wq[:, perm]; wk_p = wk[:, perm]
    bq_p = bq[perm]; bk_p = bk[perm]
    d_wq = _chunkmaj(wq_p); d_wk = _chunkmaj(wk_p)
    d_wv = _chunkmaj(wv); d_wo = _chunkmaj(wo)
    d_bq = np.ascontiguousarray(bq_p.reshape(8, 128).T)
    d_bk = np.ascontiguousarray(bk_p.reshape(8, 128).T)
    d_bo = np.ascontiguousarray(bo.reshape(8, 128).T)
    d_bvr = bv.reshape(1, D)
    d_w1t = _chunkmaj(np.ascontiguousarray(w1.T))
    d_w2t = _chunkmaj(np.ascontiguousarray(w2.T))
    d_cw = np.concatenate([cwt[:, 0, t].reshape(32, 128).T for t in range(3)], axis=1)
    d_cw = np.ascontiguousarray(d_cw)
    d_cb = np.ascontiguousarray(cbt.reshape(32, 128).T)
    d_idf = np.eye(128, dtype=np.float32)

    s8 = 1.0 / np.sqrt(np.sqrt(64.0))
    s8 = 8.0 ** -0.5
    invf = (10000.0 ** (-np.arange(0, 64, 2, dtype=np.float64) / 64.0))

    def rope_tables(pos):
        ang = pos[None, :].astype(np.float64) * invf[:, None]
        c = np.cos(ang) * s8; s = np.sin(ang) * s8
        slot_c = np.concatenate([c, c], axis=0)
        slot_s = np.concatenate([s, -s], axis=0)
        C2 = np.concatenate([slot_c, slot_c], axis=0).astype(np.float32)
        NS = np.concatenate([slot_s, slot_s], axis=0).astype(np.float32)
        return np.ascontiguousarray(C2), np.ascontiguousarray(NS)

    kpos = np.arange(S)
    d_c2k, d_nsk = rope_tables(kpos)

    tri = (np.arange(128)[:, None] <= np.arange(128)[None, :]).astype(ml_dtypes.bfloat16)
    ones128 = np.ones((128, 128), ml_dtypes.bfloat16)
    tri4 = np.ascontiguousarray(np.tile(tri, (1, 4)))
    ones4 = np.ascontiguousarray(np.tile(ones128, (1, 4)))

    bm = np.zeros((128, KBANK * NBANK), ml_dtypes.bfloat16)
    for i in range(KBANK):
        for c in range(NBANK):
            p = BPOS[c]
            valid = (128 * i + np.arange(128)) <= p
            bm[:, NBANK * i + c] = valid.astype(ml_dtypes.bfloat16)
    bm = np.ascontiguousarray(bm)

    per_side = {}
    for side in range(2):
        qpos = np.concatenate([np.arange(side * 2048, side * 2048 + 2048), np.array(BPOS)])
        c2q, nsq = rope_tables(qpos)
        wkill = np.zeros((NKILL, S), ml_dtypes.bfloat16)
        off = 16 * side
        for g in range(G8):
            for l in range(2):
                thr = 128 * (off + 2 * g + l + 1)
                wkill[2 * g + l, thr:] = ml_dtypes.bfloat16(1.0)
        qkill = np.zeros((NKILL, (G8 + 1) * GN), ml_dtypes.bfloat16)
        for g in range(G8):
            for l in range(2):
                qkill[2 * g + l, GN * g + 128 * l: GN * g + 128 * (l + 1)] = ml_dtypes.bfloat16(KILL)
        m1 = tri4 if side == 0 else ones4
        m2 = ones4 if side == 0 else tri4
        selL = np.zeros((128, NSLAB * 512), np.float32)
        selR = np.zeros((128, NSLAB * 512), np.float32)
        cact = np.zeros((128, 2 * NSLAB), np.float32)
        for sl_i in range(NSLAB):
            tL = side * 2048 + FSLAB * sl_i
            tR = tL + FSLAB - 1
            if sl_i == 0:
                if side == 1:
                    pat = np.zeros(NBANK, np.float32); pat[6] = 1.0
                    selL[:, 0:512] = np.tile(pat, 32)[None, :].repeat(128, 0)
                    cact[:, 0] = 1.0
            if sl_i == NSLAB - 1:
                if side == 0:
                    pat = np.zeros(NBANK, np.float32); pat[7] = 1.0
                    selR[:, 512 * sl_i:512 * (sl_i + 1)] = np.tile(pat, 32)[None, :].repeat(128, 0)
                    cact[:, 2 * sl_i + 1] = 1.0
                else:
                    cact[:, 2 * sl_i + 1] = 1.0
        per_side[side] = dict(c2q=np.ascontiguousarray(c2q), nsq=np.ascontiguousarray(nsq),
                              wkill=np.ascontiguousarray(wkill), qkill=np.ascontiguousarray(qkill),
                              m1=np.ascontiguousarray(m1),
                              m2=np.ascontiguousarray(m2), selL=selL, selR=selR, cact=cact)

    in_maps = []
    for core in range(8):
        b = core // 2; side = core % 2
        ps = per_side[side]
        xb = x[b]
        own = xb[side * 2048: side * 2048 + 2048]
        bank = xb[BPOS]
        xloc = np.concatenate([own, bank], axis=0)
        in_maps.append({
            "xful": _feat(xb), "xloc": _feat(xloc),
            "wq": d_wq, "wk": d_wk, "wv": d_wv, "wo": d_wo,
            "bq": d_bq, "bk": d_bk, "bvr": d_bvr, "bo": d_bo,
            "w1t": d_w1t, "w2t": d_w2t, "cw": d_cw, "cb": d_cb,
            "c2q": ps["c2q"], "nsq": ps["nsq"], "c2k": d_c2k, "nsk": d_nsk,
            "wkill": ps["wkill"], "qkill": ps["qkill"], "m1": ps["m1"], "m2": ps["m2"], "bmask": bm,
            "selL": ps["selL"], "selR": ps["selR"], "cact": ps["cact"], "idf": d_idf,
        })

    res = run_bass_kernel_spmd(nc, in_maps, core_ids=list(range(8)))

    out = np.empty((B, S, D), np.float32)
    for core in range(8):
        b = core // 2; side = core % 2
        y = res.results[core]["y"]
        yt = y.reshape(128, 8, NOWN).transpose(1, 0, 2).reshape(D, NOWN)
        out[b, side * 2048: side * 2048 + 2048] = yt.T
    return out


# version 14
# speedup vs baseline: 1.1122x; 1.0468x over previous
import numpy as np
import ml_dtypes
import concourse.bass as bass
import concourse.mybir as mybir
from concourse.bass_utils import run_bass_kernel_spmd

F32 = mybir.dt.float32
BF16 = mybir.dt.bfloat16
AO = mybir.AluOpType
AF = mybir.ActivationFunctionType
AX = mybir.AxisListType

MAGIC = 12582912.0
KILL = -100000.0
EPS = 1e-5

B, S, D, H, DH = 4, 4096, 1024, 16, 4096
NOWN, NBANK = 2048, 16
NQ = NOWN + NBANK
G8, GN = 8, 256
KBANK = 30
NKILL = 2 * G8
KB0 = 96
KC = KB0 + NKILL

def own_idx_side(side):
    import numpy as _np
    return _np.concatenate([_np.arange(512 * u + 256 * side, 512 * u + 256 * side + 256)
                            for u in range(8)])

def bank_pos_side(side):
    def tok_of_own(c):
        return 512 * (c // 256) + 256 * side + (c % 256)
    slots = []
    slots.append(tok_of_own(0) - 1 if side == 1 else 0)
    for u in range(1, 8):
        slots.append(tok_of_own(256 * u - 1) + 1)
        slots.append(tok_of_own(256 * u) - 1)
    slots.append(tok_of_own(2047) + 1 if side == 0 else 0)
    return slots

KSLOT = [4 * g + 4 for g in range(G8)]

FSLAB = 512
NSLAB = NOWN // FSLAB
SLABW = FSLAB + 4
SW = SLABW

ENGS = ("sy", "te", "ve", "se", "ge")
SEMS = {}
VJUNK = None


class Sch:

    def __init__(self):
        self.ops = {e: [] for e in ENGS}
        self.cnt = {e: 0 for e in ENGS}
        self.dfam = {}
        self.mkdsem = None
        self.bufs = {}
        self.lastw = {e: {} for e in ENGS}
        self.bar = None

    def barrier(self):
        self.bar = (dict(self.cnt), dict(self.dfam), set())

    def _wait(self, eng, semkey, val):
        if val <= self.lastw[eng].get(semkey, 0):
            return
        self.lastw[eng][semkey] = val
        self.ops[eng].append(lambda h, semkey=semkey, val=val: h.wait_ge(SEMS[semkey], val))

    def _resolve(self, sk, v):
        if sk.startswith("D:") and v < 0:
            return 16 * self.dfam[sk[2:]]
        return v

    def emit(self, eng, fn, reads=(), writes=(), dma=False, inc=True, lazy=False):
        if self.bar is not None and eng not in self.bar[2]:
            sc, sd, applied = self.bar
            for e2, v in sc.items():
                if e2 != eng and v > 0:
                    self._wait(eng, e2, v)
            for f, c in sd.items():
                if c > 0:
                    self._wait(eng, "D:" + f, 16 * c)
            applied.add(eng)
            if len(applied) == len(ENGS):
                self.bar = None
        need = {}
        for b in reads:
            rec = self.bufs.get(b)
            if rec and rec["w"]:
                sk, v = rec["w"]
                if sk != eng:
                    need[sk] = max(need.get(sk, 0), self._resolve(sk, v))
        for b in writes:
            rec = self.bufs.get(b)
            if rec:
                if rec["w"]:
                    sk, v = rec["w"]
                    if sk != eng:
                        need[sk] = max(need.get(sk, 0), self._resolve(sk, v))
                for sk, v in rec["r"].items():
                    if sk != eng:
                        need[sk] = max(need.get(sk, 0), self._resolve(sk, v))
        for sk, v in sorted(need.items()):
            self._wait(eng, sk, v)
        if dma:
            fam = writes[0]
            if fam not in self.dfam:
                self.dfam[fam] = 0
                self.mkdsem(fam)
            self.dfam[fam] += 1
            semkey, val = "D:" + fam, -1
            self.ops[eng].append(lambda h, fn=fn, semkey=semkey: fn(h).then_inc(SEMS[semkey], 16))
        elif inc:
            self.cnt[eng] += 1
            semkey, val = eng, self.cnt[eng]
            self.ops[eng].append(lambda h, fn=fn, semkey=semkey: fn(h).then_inc(SEMS[semkey], 1))
            if lazy:
                for _ in range(4):
                    self.cnt[eng] += 1
                    self.ops[eng].append(lambda h, semkey=semkey: h.memset(
                        VJUNK[:, :], 0.25).then_inc(SEMS[semkey], 1))
                val = self.cnt[eng]
        else:
            semkey, val = eng, self.cnt[eng] + 1
            self.ops[eng].append(fn)
        for b in writes:
            self.bufs[b] = {"w": (semkey, val), "r": {}}
        for b in reads:
            rec = self.bufs.setdefault(b, {"w": None, "r": {}})
            if semkey.startswith("D:"):
                rec["r"][semkey] = -1
            else:
                rec["r"][semkey] = max(rec["r"].get(semkey, 0), val)


def build_program():
    import contextlib
    nc = bass.Bass("TRN2", target_bir_lowering=False, detect_race_conditions=False)
    sch = Sch()
    E = sch.emit
    root = contextlib.ExitStack()

    din = lambda name, shape, dt=F32: nc.dram_tensor(name, shape, dt, kind="ExternalInput")
    xful = din("xful", [128, 8 * S])
    xloc = din("xloc", [128, 8 * NQ])
    wq_d = din("wq", [128, 8 * D]); wk_d = din("wk", [128, 8 * D])
    wv_d = din("wv", [128, 8 * D]); wo_d = din("wo", [128, 8 * D])
    bq_d = din("bq", [128, 8]); bk_d = din("bk", [128, 8])
    bvr_d = din("bvr", [1, D]); bo_d = din("bo", [128, 8])
    w1t_d = din("w1t", [128, 8 * DH])
    w2t_d = din("w2t", [128, 32 * D])
    cw_d = din("cw", [128, 3 * 32]); cb_d = din("cb", [128, 32])
    c2q_d = din("c2q", [128, NQ]); nsq_d = din("nsq", [128, NQ])
    c2k_d = din("c2k", [128, S]); nsk_d = din("nsk", [128, S])
    wkill_d = din("wkill", [NKILL, S], BF16)
    qkill_d = din("qkill", [NKILL, (G8 + 1) * GN], BF16)
    m1_d = din("m1", [128, 512], BF16); m2_d = din("m2", [128, 512], BF16)
    bmask_d = din("bmask", [128, KBANK * NBANK], BF16)
    cfil_d = din("cfil", [128, NBANK])
    idf_d = din("idf", [128, 128])
    yout = nc.dram_tensor("y", [128, 8 * NOWN], F32, kind="ExternalOutput")

    import os as _osd
    _k = dict(kind="ExternalOutput") if _osd.environ.get("K_DBG_OUT") == "1" else {}
    ln1f_h = nc.dram_tensor("ln1f_h", [128, 8 * S], BF16, **_k)
    ln1o_h = nc.dram_tensor("ln1o_h", [128, 8 * NQ], BF16, **_k)
    xatt_h = nc.dram_tensor("xatt_h", [128, 8 * NQ], F32, **_k)
    obf_h = nc.dram_tensor("obf_h", [128, 8 * NQ], BF16, **_k)
    w1q_h = nc.dram_tensor("w1q_h", [128, 8 * DH], BF16, **_k)
    w2q_h = nc.dram_tensor("w2q_h", [128, 32 * D], BF16, **_k)
    dbg_h = nc.dram_tensor("dbg_h", [128, 2304], F32, **_k) if _k else None
    dbg2_h = nc.dram_tensor("dbg2_h", [128, 4 * KBANK * NBANK + 64], BF16, **_k) if _k else None

    sb = lambda st, name, shape, dt=F32: st.enter_context(nc.sbuf_tensor(name, shape, dt))
    SEMS.clear()
    for e in ENGS:
        SEMS[e] = root.enter_context(nc.semaphore(f"sem_{e}"))

    def _mkdsem(fam):
        SEMS["D:" + fam] = root.enter_context(nc.semaphore(f"d_{len(SEMS)}"))
    sch.mkdsem = _mkdsem

    onesbf = sb(root, "onesbf", [128, 128], BF16)
    onesf = sb(root, "onesf", [1, 128])
    onesw = sb(root, "onesw", [1, 512])
    idf_s = sb(root, "idf_s", [128, 128])
    selq = [sb(root, f"selq{j}", [128, 128]) for j in range(2)]
    vjunk = sb(root, "vjunk", [32, 32])
    global VJUNK
    VJUNK = vjunk
    hs4 = [sb(root, f"hs4{j}", [128, 4], BF16) for j in range(2)]
    biases = sb(root, "biases", [128, 24])
    cw_s = sb(root, "cw_s", [128, 3 * 32]); cb_s = sb(root, "cb_s", [128, 32])
    cfil_s = sb(root, "cfil_s", [128, NBANK])
    swinv = sb(root, "swinv", [128, 2])
    m1_s = sb(root, "m1_s", [128, 512], BF16); m2_s = sb(root, "m2_s", [128, 512], BF16)
    bmask_s = sb(root, "bmask_s", [128, KBANK * NBANK], BF16)
    PS = [root.enter_context(nc.psum_tensor(f"ps{i}", [128, 512], F32)) for i in range(8)]
    PSN = [f"PS{i}" for i in range(8)]

    E("ve", lambda v: v.memset(onesbf[:, :], 1.0), writes=("onesbf",))

    def vguard():
        for _ in range(4):
            E("ve", lambda v: v.memset(vjunk[:, :], 0.25), reads=("vjunk",), writes=("vjunk",))
    E("ve", lambda v: v.memset(onesf[:, :], 1.0), writes=("onesf",))
    E("ve", lambda v: v.memset(onesw[:, :], 1.0), writes=("onesw",))
    E("sy", lambda s: s.dma_start(out=idf_s[:, :], in_=idf_d.ap()), writes=("idf_s",), dma=True)
    E("sy", lambda s: s.dma_start(out=m1_s[:, :], in_=m1_d.ap()), writes=("m1_s",), dma=True)
    E("sy", lambda s: s.dma_start(out=m2_s[:, :], in_=m2_d.ap()), writes=("m2_s",), dma=True)
    E("sy", lambda s: s.dma_start(out=bmask_s[:, :], in_=bmask_d.ap()), writes=("bmask_s",), dma=True)
    E("sy", lambda s: s.dma_start(out=biases[:, 0:8], in_=bq_d.ap()), writes=("bias_q",), dma=True)
    E("sy", lambda s: s.dma_start(out=biases[:, 8:16], in_=bk_d.ap()), writes=("bias_k",), dma=True)
    E("sy", lambda s: s.dma_start(out=biases[:, 16:24], in_=bo_d.ap()), writes=("bias_o",), dma=True)
    E("sy", lambda s: s.dma_start(out=cw_s[:, :], in_=cw_d.ap()), writes=("cw_s",), dma=True)
    E("sy", lambda s: s.dma_start(out=cb_s[:, :], in_=cb_d.ap()), writes=("cb_s",), dma=True)
    E("sy", lambda s: s.dma_start(out=cfil_s[:, :], in_=cfil_d.ap()), writes=("cfil_s",), dma=True)
    for j in range(2):
        h_ = hs4[j]
        E("ve", lambda v, h_=h_: v.memset(h_[:, :], 0.0), writes=(f"hs4{j}",))
        E("ve", lambda v, h_=h_, j=j: v.memset(h_[0:64, 2 * j:2 * j + 1], 1.0),
          reads=(f"hs4{j}",), writes=(f"hs4{j}",))
        E("ve", lambda v, h_=h_, j=j: v.memset(h_[64:128, 2 * j + 1:2 * j + 2], 1.0),
          reads=(f"hs4{j}",), writes=(f"hs4{j}",))
        q_ = selq[j]
        E("ve", lambda v, q_=q_: v.memset(q_[:, :], 0.0), writes=(f"selq{j}",))
        E("ve", lambda v, q_=q_, j=j: v.memset(q_[64 * j:64 * j + 1, 0:64], 1.0),
          reads=(f"selq{j}",), writes=(f"selq{j}",))
        E("ve", lambda v, q_=q_, j=j: v.memset(q_[64 * j + 32:64 * j + 33, 64:128], 1.0),
          reads=(f"selq{j}",), writes=(f"selq{j}",))

    _ln_n = [0]

    LNW = 544

    def make_ln(st):
        i = _ln_n[0]; _ln_n[0] += 1
        lt = {"id": str(i),
              "xbf": sb(st, f"lnxbf{i}", [128, 2 * LNW], BF16),
              "sq": sb(st, f"lnsq{i}", [128, 2 * LNW], BF16),
              "tmp": sb(st, f"lntmp{i}", [128, 4 * LNW])}
        if dbg_h is not None and i == 0:
            for gap0 in (512, LNW + 512, 2 * LNW + 512):
                E("ve", lambda v, lt=lt, gap0=gap0: v.memset(lt["tmp"][:, gap0:gap0 + LNW - 512], 0.0),
                  writes=(f"lnm{i}",))
        return lt

    def emit_ln(lt, n, src, dst_bf=None, dst_f=None, resid=None, res_out=None, banks=(6, 7)):
        li = lt["id"]
        psA, psB = PS[banks[0]], PS[banks[1]]
        fA, fB = f"PS{banks[0]}", f"PS{banks[1]}"
        for ic in range(8):
            ap, rd = src(ic)
            sl = ic % 2
            xb = lt["xbf"][:, LNW * sl:LNW * sl + n]
            sq = lt["sq"][:, LNW * sl:LNW * sl + n]
            bxb, bsq2 = f"lnxb{sl}_{li}", f"lnsq{sl}_{li}"
            E("se", lambda s, ap=ap, xb=xb: s.copy(xb, ap), reads=rd + (bxb,), writes=(bxb,))
            E("se", lambda s, ap=ap, sq=sq: s.activation(sq, ap, AF.Square), reads=rd + (bsq2,), writes=(bsq2,))
            E("te", lambda t_, xb=xb, ic=ic, n=n: t_.matmul(psA[:, 0:n], onesbf[:, :], xb,
              start=(ic == 0), stop=(ic == 7)),
              reads=(bxb, "onesbf"),
              writes=(fA,) if ic in (0, 7) else (), inc=True)
            E("te", lambda t_, sq=sq, ic=ic, n=n: t_.matmul(psB[:, 0:n], onesbf[:, :], sq,
              start=(ic == 0), stop=(ic == 7)),
              reads=(bsq2, "onesbf"),
              writes=(fB,) if ic in (0, 7) else (), inc=True)
        tmp = lt["tmp"]
        mb = tmp[:, 0:n]; wv_ = tmp[:, LNW:LNW + n]; rb = tmp[:, 2 * LNW:2 * LNW + n]; dd = tmp[:, 3 * LNW:3 * LNW + n]
        E("ve", lambda v: v.tensor_scalar(mb, psA[:, 0:n], 1.0 / D, None, AO.mult),
          reads=(fA, f"lnm{li}"), writes=(f"lnm{li}",))
        E("ve", lambda v: v.tensor_mul(wv_, mb, mb), reads=(f"lnm{li}", f"lnw{li}"), writes=(f"lnw{li}",))
        E("ve", lambda v: v.scalar_tensor_tensor(wv_, psB[:, 0:n], 1.0 / D, wv_, AO.mult, AO.subtract),
          reads=(fB, f"lnw{li}"), writes=(f"lnw{li}",))
        E("ve", lambda v: v.tensor_scalar_add(wv_, wv_, EPS), reads=(f"lnw{li}",), writes=(f"lnw{li}",))
        E("se", lambda s: s.activation(rb, wv_, AF.Ln), reads=(f"lnw{li}", f"lnr{li}"), writes=(f"lnr{li}",))
        E("se", lambda s: s.activation(rb, rb, AF.Exp, scale=-0.5), reads=(f"lnr{li}",), writes=(f"lnr{li}",))
        for ic in range(8):
            ap, rd = src(ic)
            dn = f"lnd{li}"
            E("ve", lambda v, ap=ap: v.tensor_sub(dd, ap, mb), reads=rd + (f"lnm{li}", dn), writes=(dn,))
            if dst_bf is not None:
                oap, ob = dst_bf(ic)
                E("ve", lambda v, oap=oap: v.tensor_mul(oap, dd, rb), reads=(dn, f"lnr{li}"), writes=(ob,))
            if dst_f is not None:
                oap, ob = dst_f(ic)
                E("ve", lambda v, oap=oap: v.tensor_mul(oap, dd, rb), reads=(dn, f"lnr{li}"), writes=(ob,))
            if resid is not None:
                rap, rrd = resid(ic)
                oap, ob = res_out(ic)
                E("ve", lambda v: v.tensor_mul(dd, dd, rb), reads=(dn, f"lnr{li}"), writes=(dn,))
                E("ve", lambda v, oap=oap, rap=rap: v.tensor_add(oap, dd, rap),
                  reads=(dn,) + rrd, writes=(ob,))

    with contextlib.ExitStack() as p1:
        lt1 = [make_ln(p1), make_ln(p1)]
        xin = [sb(p1, f"xinb{i}", [128, 8 * 512]) for i in range(2)]
        lnout = [sb(p1, f"lnoutb{i}", [128, 8 * 512], BF16) for i in range(2)]
        _p1w = [0]

        def p1_pass(src_d, dst_d, total, tag0):
            for g in range((total + 511) // 512):
                c0 = min(512 * g, total - 512)
                n = 512
                sl = _p1w[0] % 2; _p1w[0] += 1
                xi = xin[sl]; lo = lnout[sl]
                for ic in range(8):
                    E("sy", lambda s, ic=ic, c0=c0, n=n, xi=xi, sl=sl: s.dma_start(
                        out=xi[:, 512 * ic:512 * ic + n],
                        in_=src_d.ap()[:, total * ic + c0: total * ic + c0 + n]),
                        reads=(f"xinw{sl}",), writes=(f"xinw{sl}",), dma=True)
                emit_ln(lt1[sl], n,
                        src=lambda ic, xi=xi, sl=sl: (xi[:, 512 * ic:512 * ic + n], (f"xinw{sl}",)),
                        dst_bf=lambda ic, lo=lo, sl=sl: (lo[:, 512 * ic:512 * ic + n], f"lno{ic}_{sl}"),
                        banks=(6, 7) if sl == 0 else (4, 5))
                for ic in range(8):
                    E("ge", lambda gg, ic=ic, c0=c0, n=n, lo=lo, sl=sl: gg.dma_start(
                        out=dst_d.ap()[:, total * ic + c0: total * ic + c0 + n],
                        in_=lo[:, 512 * ic:512 * ic + n]),
                        reads=(f"lno{ic}_{sl}",), writes=(f"{tag0}done",), dma=True)

        p1_pass(xful, ln1f_h, S, "lf")
        p1_pass(xloc, ln1o_h, NQ, "lo")
        if dbg_h is not None:
            for _s in range(2):
                E("ve", lambda v, _s=_s: v.tensor_copy(lt1[0]["tmp"][:, 1900 + 16 * _s:1916 + 16 * _s],
                  lt1[0]["sq"][:, 544 * _s:544 * _s + 16]),
                  reads=(f"lnsq{_s}_0", "lnm0"), writes=("lnm0",), inc=True)
                E("ve", lambda v, _s=_s: v.tensor_copy(lt1[0]["tmp"][:, 1932 + 16 * _s:1948 + 16 * _s],
                  lt1[0]["xbf"][:, 544 * _s:544 * _s + 16]),
                  reads=(f"lnxb{_s}_0", "lnm0"), writes=("lnm0",), inc=True)
            E("ge", lambda g: g.dma_start(out=dbg_h.ap()[:, 0:2048], in_=lt1[0]["tmp"][:, 0:2048]),
              reads=("lnm0", "lnw0", "lnr0", "lnd0"), writes=("dbgdone",), dma=True)
            E("ge", lambda g: g.dma_start(out=dbg_h.ap()[:, 2048:2064], in_=xin[0][:, 0:16]),
              reads=("xinw0",), writes=("dbgdone",), dma=True)
    sch.barrier()

    import os as _os0
    _DBG_STOP = _os0.environ.get("K_DBG_STOP") == "1"
    with contextlib.ExitStack() as pw_:
        pw = root if _DBG_STOP else pw_
        WTW = 1024
        wbuf = [sb(pw, f"wbufs{i}", [128, WTW]) for i in range(2)]
        wqb = [sb(pw, f"wqbs{i}", [128, WTW], BF16) for i in range(2)]
        wstat = sb(pw, "wstat", [128, 56])
        wtr = sb(pw, "wtr", [128, 32])
        wtrT = sb(pw, "wtrT", [32, 128])

        def quant_weights(src_d, dst_d, ncols, tag, sw_col):
            nt = ncols // WTW
            pieces = []

            def init_p():
                E("ve", lambda v: v.memset(wstat[:, 0:56], 0.0), reads=("wstat",), writes=("wstat",))
                E("ve", lambda v: v.memset(wtr[:, 0:32], 0.0), reads=("wtr",), writes=("wtr",))
            pieces.append(init_p)

            def mean_tile(it):
                sl = it % 2
                E("sy", lambda s, it=it, sl=sl: s.dma_start(out=wbuf[sl][:, :],
                  in_=src_d.ap()[:, WTW * it:WTW * (it + 1)]),
                  reads=(f"wbufs{sl}",), writes=(f"wbufs{sl}",), dma=True)
                E("ve", lambda v, it=it, sl=sl: v.tensor_reduce(wstat[:, 20 + it:21 + it], wbuf[sl][:, :],
                  AX.X, AO.add, apply_absolute_value=True),
                  reads=(f"wbufs{sl}", "wstat"), writes=("wstat",), lazy=True)
            for it in range(nt):
                pieces.append(lambda it=it: mean_tile(it))

            def finisher():
                E("ve", lambda v: v.tensor_reduce(wstat[:, 16:17], wstat[:, 20:20 + nt], AX.X, AO.add),
                  reads=("wstat",), writes=("wstat",), lazy=True)
                E("ve", lambda v: v.tensor_copy(wtr[:, 0:1], wstat[:, 16:17]),
                  reads=("wstat", "wtr"), writes=("wtr",), lazy=True)
                vguard()
                for kb_ in range(4):
                    E("ve", lambda v, kb_=kb_: v.transpose(wtrT[0:32, 32 * kb_:32 * kb_ + 32],
                      wtr[32 * kb_:32 * kb_ + 32, 0:32]),
                      reads=("wtr", "wtrT"), writes=("wtrT",), lazy=True)
                vguard()
                E("ve", lambda v: v.tensor_reduce(wstat[0:1, 17:18], wtrT[0:1, 0:128], AX.X, AO.add),
                  reads=("wtrT", "wstat"), writes=("wstat",), lazy=True)
                E("ve", lambda v: v.tensor_scalar(wstat[0:1, 17:18], wstat[0:1, 17:18], 1.0 / (ncols * 128.0), EPS,
                  AO.mult, AO.max), reads=("wstat",), writes=("wstat",), lazy=True)
                E("te", lambda t: t.matmul(PS[6][:, 0:1], onesf[0:1, :], wstat[0:1, 17:18], start=True, stop=True),
                  reads=("wstat", "onesf", "PS6"), writes=("PS6",))
                E("ve", lambda v: v.tensor_copy(swinv[:, sw_col:sw_col + 1], PS[6][:, 0:1]),
                  reads=("PS6",), writes=(f"swinv{sw_col}",), lazy=True)
                E("ve", lambda v: v.reciprocal(wstat[0:1, 18:19], wstat[0:1, 17:18]), reads=("wstat",), writes=("wstat",), lazy=True)
                E("te", lambda t: t.matmul(PS[6][:, 0:1], onesf[0:1, :], wstat[0:1, 18:19], start=True, stop=True),
                  reads=("wstat", "onesf", "PS6"), writes=("PS6",))
                E("ve", lambda v: v.tensor_copy(wstat[:, 19:20], PS[6][:, 0:1]), reads=("PS6", "wstat"), writes=("wsbc" + tag,), lazy=True)
            pieces.append(finisher)

            def quant_tile(it):
                sl = it % 2
                E("sy", lambda s, it=it, sl=sl: s.dma_start(out=wbuf[sl][:, :],
                  in_=src_d.ap()[:, WTW * it:WTW * (it + 1)]),
                  reads=(f"wbufs{sl}",), writes=(f"wbufs{sl}",), dma=True)
                E("ve", lambda v, sl=sl: v.tensor_scalar(wbuf[sl][:, :], wbuf[sl][:, :], wstat[:, 19:20], MAGIC, AO.mult, AO.add),
                  reads=(f"wbufs{sl}", "wsbc" + tag), writes=(f"wbufs{sl}",))
                E("ve", lambda v, sl=sl: v.tensor_scalar(wbuf[sl][:, :], wbuf[sl][:, :], MAGIC, 1.0, AO.subtract, AO.min),
                  reads=(f"wbufs{sl}",), writes=(f"wbufs{sl}",))
                E("ve", lambda v, sl=sl: v.tensor_scalar(wqb[sl][:, :], wbuf[sl][:, :], -1.0, None, AO.max),
                  reads=(f"wbufs{sl}", f"wqbs{sl}"), writes=(f"wqbs{sl}",))
                E("ge", lambda g, it=it, sl=sl: g.dma_start(out=dst_d.ap()[:, WTW * it:WTW * (it + 1)], in_=wqb[sl][:, :]),
                  reads=(f"wqbs{sl}",), writes=(f"{tag}done",), dma=True)
            for it in range(nt):
                pieces.append(lambda it=it: quant_tile(it))
            return pieces

        wq_pieces = quant_weights(w1t_d, w1q_h, 8 * DH, "w1", 0)
        if dbg_h is not None:
            def dbg_wq():
                E("ge", lambda g: g.dma_start(out=dbg_h.ap()[:, 2064:2112], in_=wstat[:, 0:48]),
                  reads=("wstat", "wsbcw1"), writes=("dbg2",), dma=True)
                E("ge", lambda g: g.dma_start(out=dbg_h.ap()[0:32, 2112:2240], in_=wtrT[:, :]),
                  reads=("wtrT",), writes=("dbg2",), dma=True)
            wq_pieces.append(dbg_wq)
        wq_pieces += quant_weights(w2t_d, w2q_h, 32 * D, "w2", 1)
        _wq_cursor = [0]

        def emit_wq(npiece):
            hi = min(_wq_cursor[0] + npiece, len(wq_pieces))
            for pi in range(_wq_cursor[0], hi):
                wq_pieces[pi]()
            _wq_cursor[0] = hi

        bvhl = sb(pw, "bvhl", [1, 2 * D], BF16)
        bvf = sb(pw, "bvf", [1, D])
        kaug = [sb(pw, f"kaug{h}", [KC, S], BF16) for h in range(4)]
        vqt = sb(pw, "vqt", [128, 32 * 260], BF16)
        qaug = [[sb(pw, f"qa{h}_{g}", [KC, GN if g < G8 else NBANK], BF16)
                 for g in range(G8 + 1)] for h in range(4)]
        lnfw = [sb(pw, f"lnfw{s_}", [128, 8 * 512], BF16) for s_ in range(2)]
        lnow = [sb(pw, f"lnow{s_}", [128, 8 * GN], BF16) for s_ in range(2)]
        rkw = [sb(pw, f"rkw{s_}", [128, 2 * 512]) for s_ in range(2)]
        rqw = [sb(pw, f"rqw{s_}", [128, 2 * GN]) for s_ in range(2)]
        wqkv = sb(pw, "wqkv", [128, 256])
        wqkv_b = sb(pw, "wqkv_b", [128, 8 * 256 * 3], BF16)
        xsb = sb(pw, "xsb", [128, 3 * 512])
        sqb = sb(pw, "sqb", [128, 512], BF16)
        p4t = sb(pw, "p4t", [128, 3 * 4 * GN], BF16)
        pbank = sb(pw, "pbank", [128, 4 * KBANK * NBANK], BF16)
        oraw = sb(pw, "oraw", [128, 2 * NQ])
        drawt = sb(pw, "drawt", [128, NQ])
        dinvb = sb(pw, "dinvb", [128, NQ])
        obf = sb(pw, "obf", [128, 2 * NQ], BF16)
        nstat = sb(pw, "nstat", [128, 40])
        ntr = sb(pw, "ntr", [32, 32])
        ntrT = sb(pw, "ntrT", [32, 32])
        ropeT = sb(pw, "ropeT", [128, 512]); ropeT2 = sb(pw, "ropeT2", [128, 512])
        valr = sb(pw, "valr", [1, 8])

        E("ve", lambda v: v.memset(bass.AP(vqt, 64, [[260 * 32, 128], [65, 128], [1, 1]]), 1.0),
          writes=("vq_ones",))
        E("sy", lambda s: s.dma_start(out=bvf[:, :], in_=bvr_d.ap()), writes=("bvf",), dma=True)
        E("ve", lambda v: v.tensor_copy(bvhl[0:1, 0:D], bvf[0:1, :]), reads=("bvf",), writes=("bvhl_h",))
        E("ve", lambda v: v.tensor_sub(bvhl[0:1, D:2 * D], bvf[0:1, :], bvhl[0:1, 0:D]),
          reads=("bvf", "bvhl_h"), writes=("bvhl_l",))

        def emit_rope(xs, xs_c0, ctab, cc0, stab, sc0, n, dsts, rd):
            E("ve", lambda v: v.tensor_mul(ropeT[:, 0:n], xs[:, xs_c0:xs_c0 + n], ctab[:, cc0:cc0 + n]),
              reads=rd + ("ropeT",), writes=("ropeT",))
            for b_ in (0, 64):
                E("ve", lambda v, b_=b_: v.tensor_mul(ropeT2[b_:b_ + 32, 0:n],
                  xs[b_ + 32:b_ + 64, xs_c0:xs_c0 + n], stab[b_ + 32:b_ + 64, sc0:sc0 + n]),
                  reads=rd + ("ropeT2",), writes=("ropeT2",))
                E("ve", lambda v, b_=b_: v.tensor_mul(ropeT2[b_ + 32:b_ + 64, 0:n],
                  xs[b_:b_ + 32, xs_c0:xs_c0 + n], stab[b_:b_ + 32, sc0:sc0 + n]),
                  reads=rd + ("ropeT2",), writes=("ropeT2",))
            for i, (dt_, p0, c0, dbuf) in enumerate(dsts):
                b_ = 64 * i
                E("ve", lambda v, dt_=dt_, p0=p0, c0=c0, b_=b_: v.tensor_add(
                    dt_[p0:p0 + 64, c0:c0 + n], ropeT[b_:b_ + 64, 0:n], ropeT2[b_:b_ + 64, 0:n]),
                    reads=("ropeT", "ropeT2", dbuf), writes=(dbuf,))

        import os as _os
        _QORD = [int(c) for c in _os.environ.get("K_QORDER", "0123")]
        _QPH = int(_os.environ.get("K_QPHASES", "9"))
        _QW = _os.environ.get("K_QW", "1") == "1"
        _Q64 = _os.environ.get("K_Q64", "1") == "1"
        _QMS = _os.environ.get("K_QMS", "1") == "1"
        for _Qi, Q in enumerate(_QORD):
            emit_wq(34 if _Qi < 3 else len(wq_pieces))
            for ic in (range(8) if (_Qi == 0 or _QW) else ()):
                for wi, wd in enumerate((wq_d, wk_d, wv_d)):
                    E("sy", lambda s, ic=ic, wd=wd, Q=Q: s.dma_start(
                        out=wqkv[:, 0:256],
                        in_=wd.ap()[:, D * ic + 256 * Q: D * ic + 256 * Q + 256]),
                        reads=("wqkv",), writes=("wqkv",), dma=True)
                    E("ve", lambda v, ic=ic, wi=wi: v.tensor_copy(
                        wqkv_b[:, 768 * ic + 256 * wi: 768 * ic + 256 * wi + 256], wqkv[:, 0:256]),
                        reads=("wqkv", f"wb{wi}_{ic}"), writes=(f"wb{wi}_{ic}",))
            WBap = lambda wi, ic, j, w=128: wqkv_b[:, 768 * ic + 256 * wi + 128 * j: 768 * ic + 256 * wi + 128 * j + w]
            WBrd = lambda wi, ic: (f"wb{wi}_{ic}",)

            if _Qi == 0:
                for h in range(4):
                    E("ve", lambda v, h=h: v.memset(kaug[h][64:96, :], 0.0),
                      reads=(f"ka{h}", f"kc{h}"), writes=(f"kc{h}",))
                    E("sy", lambda s, h=h: s.dma_start(out=kaug[h][KB0:KB0 + NKILL, :], in_=wkill_d.ap()),
                      reads=(f"kkill{h}",), writes=(f"kkill{h}",), dma=True)
            if _Qi == 0 or _QMS:
                E("ve", lambda v: v.memset(nstat[:, 0:40], 0.0), reads=("nstat",), writes=("nstat",))
                E("ve", lambda v: v.memset(ntr[:, :], 0.0), reads=("ntr",), writes=("ntr",))
                E("ve", lambda v: v.memset(drawt[:, :], 1.0), reads=("draw",), writes=("draw",))

            for tg in (range(8) if (_Qi == 0 or _QPH >= 1) else ()):
                sl = tg % 2
                lf = lnfw[sl]
                for ic in range(8):
                    E("sy", lambda s, ic=ic, tg=tg, lf=lf: s.dma_start(
                        out=lf[:, 512 * ic:512 * (ic + 1)],
                        in_=ln1f_h.ap()[:, S * ic + 512 * tg: S * ic + 512 * (tg + 1)]),
                        reads=("lfdone", f"lnfw{sl}"), writes=(f"lnfw{sl}",), dma=True)
                rk = rkw[sl]
                E("sy", lambda s, tg=tg, rk=rk: s.dma_start(out=rk[:, 0:512],
                  in_=c2k_d.ap()[:, 512 * tg:512 * (tg + 1)]), reads=(f"rkw{sl}",), writes=(f"rkw{sl}",), dma=True)
                E("sy", lambda s, tg=tg, rk=rk: s.dma_start(out=rk[:, 512:1024],
                  in_=nsk_d.ap()[:, 512 * tg:512 * (tg + 1)]), reads=(f"rkw{sl}",), writes=(f"rkw{sl}",), dma=True)
                for j in range(2):
                    pj = PS[4 + sl]; pb = PSN[4 + sl]
                    for ic in range(8):
                        E("te", lambda t, ic=ic, j=j, lf=lf, pj=pj: t.matmul(
                            pj[:, 0:512], WBap(1, ic, j), lf[:, 512 * ic:512 * (ic + 1)],
                            start=(ic == 0), stop=(ic == 7)),
                            reads=(f"lnfw{sl}",) + WBrd(1, ic),
                            writes=(pb,) if ic in (0, 7) else (), inc=(ic == 7))
                    xcol = 512 * sl
                    E("se", lambda s, pj=pj, j=j, xcol=xcol, Q=Q: s.activation(
                        xsb[:, xcol:xcol + 512], pj[:, 0:512], AF.Identity,
                        bias=biases[:, 8 + 2 * Q + j:9 + 2 * Q + j]),
                        reads=(pb, "bias_k", f"xsb{sl}"), writes=(f"xsb{sl}",))
                    E("ve", lambda v, xcol=xcol: v.tensor_mul(sqb[:, :], xsb[:, xcol:xcol + 512],
                      xsb[:, xcol:xcol + 512]), reads=(f"xsb{sl}", "sqb"), writes=("sqb",))
                    E("te", lambda t, j=j: t.matmul(PS[6][0:4, 0:512], hs4[j][:, :], sqb[:, :],
                      start=True, stop=True), reads=("sqb", f"hs4{j}", "PS6"), writes=("PS6",))
                    E("ve", lambda v: v.tensor_reduce(nstat[0:4, 4:5], PS[6][0:4, 0:512], AX.X, AO.max),
                      reads=("PS6", "nstat"), writes=("nstat",), lazy=True)
                    E("ve", lambda v: v.tensor_max(nstat[0:4, 0:1], nstat[0:4, 0:1], nstat[0:4, 4:5]),
                      reads=("nstat",), writes=("nstat",), lazy=True)
                    emit_rope(xsb, xcol, rk, 0, rk, 512, 512,
                              [(kaug[2 * j], 0, 512 * tg, f"ka{2 * j}"),
                               (kaug[2 * j + 1], 0, 512 * tg, f"ka{2 * j + 1}")],
                              (f"xsb{sl}", f"rkw{sl}"))
                for q4 in range(4):
                    tt = 4 * tg + q4
                    pv_ = PS[4 + (1 - sl)]; pvb = PSN[4 + (1 - sl)]
                    for ic in range(8):
                        E("te", lambda t, ic=ic, q4=q4, lf=lf, pv_=pv_: t.matmul(
                            pv_[:, 0:256], lf[:, 512 * ic + 128 * q4:512 * ic + 128 * q4 + 128],
                            WBap(2, ic, 0, 256), start=(ic == 0), stop=False),
                            reads=(f"lnfw{sl}",) + WBrd(2, ic),
                            writes=(pvb,) if ic == 0 else (), inc=False)
                    E("te", lambda t, pv_=pv_, Q=Q: t.matmul(pv_[:, 0:256], onesbf[0:1, :],
                      bvhl[0:1, 256 * Q:256 * Q + 256], start=False, stop=False),
                      reads=("bvhl_h", "onesbf"), inc=False)
                    E("te", lambda t, pv_=pv_, Q=Q: t.matmul(pv_[:, 0:256], onesbf[0:1, :],
                      bvhl[0:1, D + 256 * Q:D + 256 * Q + 256], start=False, stop=True),
                      reads=("bvhl_l", "onesbf"), writes=(pvb,))
                    E("se", lambda s, tt=tt, pv_=pv_: s.copy(
                        bass.AP(vqt, 260 * tt, [[260 * 32, 128], [65, 4], [1, 64]]), pv_[:, 0:256]),
                        reads=(pvb, "vq_ones", f"vq{tt}"), writes=(f"vq{tt}",))

            for g in (range(G8 + 1) if (_Qi == 0 or _QPH >= 2) else ()):
                n = GN if g < G8 else NBANK
                sl = g % 2
                lo = lnow[sl]
                for ic in range(8):
                    E("sy", lambda s, ic=ic, g=g, n=n, lo=lo: s.dma_start(
                        out=lo[:, GN * ic:GN * ic + n],
                        in_=ln1o_h.ap()[:, NQ * ic + GN * g: NQ * ic + GN * g + n]),
                        reads=("lodone", f"lnow{sl}"), writes=(f"lnow{sl}",), dma=True)
                rq = rqw[sl]
                E("sy", lambda s, g=g, n=n, rq=rq: s.dma_start(out=rq[:, 0:n],
                  in_=c2q_d.ap()[:, GN * g:GN * g + n]), reads=(f"rqw{sl}",), writes=(f"rqw{sl}",), dma=True)
                E("sy", lambda s, g=g, n=n, rq=rq: s.dma_start(out=rq[:, GN:GN + n],
                  in_=nsq_d.ap()[:, GN * g:GN * g + n]), reads=(f"rqw{sl}",), writes=(f"rqw{sl}",), dma=True)
                for j in range(2):
                    pj = PS[4 + sl]; pb = PSN[4 + sl]
                    for ic in range(8):
                        E("te", lambda t, ic=ic, j=j, n=n, lo=lo, pj=pj: t.matmul(
                            pj[:, 0:n], WBap(0, ic, j), lo[:, GN * ic:GN * ic + n],
                            start=(ic == 0), stop=(ic == 7)),
                            reads=(f"lnow{sl}",) + WBrd(0, ic),
                            writes=(pb,) if ic in (0, 7) else (), inc=(ic == 7))
                    E("se", lambda s, pj=pj, j=j, n=n, Q=Q: s.activation(xsb[:, 1024:1024 + n], pj[:, 0:n], AF.Identity,
                      bias=biases[:, 2 * Q + j:2 * Q + j + 1]),
                      reads=(pb, "bias_q", "xsbq"), writes=("xsbq",))
                    E("ve", lambda v, n=n: v.tensor_mul(sqb[:, 0:n], xsb[:, 1024:1024 + n], xsb[:, 1024:1024 + n]),
                      reads=("xsbq", "sqb"), writes=("sqb",))
                    E("te", lambda t, j=j, n=n: t.matmul(PS[6][0:4, 0:n], hs4[j][:, :], sqb[:, 0:n],
                      start=True, stop=True), reads=("sqb", f"hs4{j}", "PS6"), writes=("PS6",))
                    E("ve", lambda v, n=n: v.tensor_reduce(nstat[0:4, 5:6], PS[6][0:4, 0:n], AX.X, AO.max),
                      reads=("PS6", "nstat"), writes=("nstat",), lazy=True)
                    E("ve", lambda v: v.tensor_max(nstat[0:4, 1:2], nstat[0:4, 1:2], nstat[0:4, 5:6]),
                      reads=("nstat",), writes=("nstat",), lazy=True)
                    emit_rope(xsb, 1024, rq, 0, rq, GN, n,
                              [(qaug[2 * j][g], 0, 0, f"qa{2 * j}_{g}"),
                               (qaug[2 * j + 1][g], 0, 0, f"qa{2 * j + 1}_{g}")],
                              ("xsbq", f"rqw{sl}"))
            if _Qi == 0:
                for h in range(4):
                    for g in range(G8 + 1):
                        n = GN if g < G8 else NBANK
                        qa = qaug[h][g]
                        E("ve", lambda v, qa=qa, n=n: v.memset(qa[64:96, 0:n], 0.0), writes=(f"qc1_{h}_{g}",))
                        E("ve", lambda v, qa=qa, n=n: v.memset(qa[64:65, 0:n], 1.0),
                          reads=(f"qc1_{h}_{g}",), writes=(f"qc1_{h}_{g}",))
                        E("sy", lambda s, qa=qa, g=g, n=n: s.dma_start(
                            out=qa[KB0:KB0 + NKILL, 0:n],
                            in_=qkill_d.ap()[:, GN * g:GN * g + n]),
                            writes=(f"qc2_{h}_{g}",), dma=True)

            if _Qi > 0 and not _Q64:
                continue
            E("ve", lambda v: v.tensor_add(nstat[0:4, 6:7], nstat[0:4, 0:1], nstat[0:4, 1:2]),
              reads=("nstat",), writes=("nstat",), lazy=True)
            E("ve", lambda v: v.tensor_scalar(nstat[0:4, 6:7], nstat[0:4, 6:7], -1.0 / 16.0, None, AO.mult),
              reads=("nstat",), writes=("nstat",), lazy=True)
            E("ve", lambda v: v.tensor_copy(ntr[0:32, 0:1], nstat[0:32, 6:7]),
              reads=("nstat", "ntr"), writes=("ntr",), lazy=True)
            vguard()
            E("ve", lambda v: v.transpose(ntrT[0:32, 0:32], ntr[0:32, 0:32]),
              reads=("ntr", "ntrT"), writes=("ntrT",), lazy=True)
            vguard()
            E("ve", lambda v: v.tensor_copy(valr[0:1, 0:4], ntrT[0:1, 0:4]), reads=("ntrT", "valr"), writes=("valr",), lazy=True)
            for h in range(4):
                for kc8 in range(8):
                    E("ve", lambda v, h=h, kc8=kc8: v.tensor_scalar(
                        kaug[h][64:65, 512 * kc8:512 * (kc8 + 1)], onesw[0:1, :],
                        valr[0:1, h:h + 1], None, AO.mult),
                        reads=("valr", "onesw", f"kc{h}"), writes=(f"kc{h}",))

            for g in (range(G8 + 1) if (_Qi == 0 or _QPH >= 3) else ()):
                n = GN if g < G8 else NBANK
                K = KSLOT[g] if g < G8 else KBANK
                is_bank = (g == G8)

                def emit_score(i, g=g, n=n, is_bank=is_bank):
                    rot = i % 3
                    bs = 2 * (i % 2)
                    for h in range(4):
                        psc = PS[(0 if bs == 0 else 4) + h // 2]
                        pscb = PSN[(0 if bs == 0 else 4) + h // 2]
                        coff = 256 * (h % 2)
                        oap = psc[:, coff:coff + n]
                        kap = kaug[h][0:KC, 128 * i:128 * (i + 1)]
                        qap = qaug[h][g][0:KC, 0:n]
                        E("te", lambda t, oap=oap, kap=kap, qap=qap: t.matmul(
                            oap, kap, qap, start=True, stop=True),
                            reads=(f"ka{h}", f"kc{h}", f"kkill{h}", f"qa{h}_{g}",
                                   f"qc1_{h}_{g}", f"qc2_{h}_{g}", pscb),
                            writes=(pscb,))
                    if not is_bank:
                        for hp in (0, 2):
                            psc = PS[(0 if bs == 0 else 4) + hp // 2]
                            pscb = PSN[(0 if bs == 0 else 4) + hp // 2]
                            iap = psc[:, 0:2 * n]
                            pcol = (4 * rot + hp) * GN
                            oap = p4t[:, pcol:pcol + 2 * n]
                            E("se", lambda s, oap=oap, iap=iap: s.activation(oap, iap, AF.Exp),
                              reads=(pscb, f"p4_{rot}_{hp}", f"p4_{rot}_{hp + 1}"),
                              writes=(f"p4_{rot}_{hp}", f"p4_{rot}_{hp + 1}"))
                    else:
                        for h in range(4):
                            psc = PS[(0 if bs == 0 else 4) + h // 2]
                            pscb = PSN[(0 if bs == 0 else 4) + h // 2]
                            coff = 256 * (h % 2)
                            iap = psc[:, coff:coff + n]
                            pcol = h * KBANK * NBANK + i * NBANK
                            oap = pbank[:, pcol:pcol + n]
                            E("se", lambda s, oap=oap, iap=iap: s.activation(oap, iap, AF.Exp),
                              reads=(pscb, f"pbk{h}"), writes=(f"pbk{h}",))
                    if not is_bank:
                        for dpos, mask, mbuf in (((4 * g, 4 * g + 1), m1_s, "m1_s"),
                                                 ((4 * g + 2, 4 * g + 3), m2_s, "m2_s")):
                            for l in range(2):
                                if i == dpos[l]:
                                    pap = bass.AP(p4t, (4 * rot) * GN + 128 * l,
                                                  [[3 * 4 * GN, 128], [GN, 4], [1, 128]])
                                    map_ = bass.AP(mask, 0, [[512, 128], [128, 4], [1, 128]])
                                    E("ve", lambda v, pap=pap, map_=map_: v.tensor_tensor(
                                        pap, pap, map_, AO.mult),
                                        reads=(f"p4_{rot}_0", f"p4_{rot}_1", f"p4_{rot}_2", f"p4_{rot}_3", mbuf),
                                        writes=(f"p4_{rot}_0", f"p4_{rot}_1", f"p4_{rot}_2", f"p4_{rot}_3"))

                def emit_pv(i, g=g, n=n, K=K, is_bank=is_bank):
                    rot = i % 3
                    for h in range(4):
                        obank = (2, 3, 6, 7)[h]
                        po = PS[obank]; pob = PSN[obank]
                        coff = 0
                        if not is_bank:
                            pcol = (4 * rot + h) * GN
                            prd = f"p4_{rot}_{h}"
                            src = p4t
                        else:
                            pcol = h * KBANK * NBANK + i * NBANK
                            prd = f"pbm{h}"
                            src = pbank
                        oap = po[0:65, coff:coff + n]
                        vap = bass.AP(vqt, 260 * i + 65 * h, [[260 * 32, 128], [1, 65]])
                        pap = src[:, pcol:pcol + n]
                        E("te", lambda t, oap=oap, vap=vap, pap=pap, i=i, K=K: t.matmul(
                            oap, vap, pap, start=(i == 0), stop=(i == K - 1)),
                            reads=(f"vq{i}", prd, "vq_ones"),
                            writes=(pob,) if i in (0, K - 1) else (), inc=(i == K - 1))

                if not is_bank:
                    for i in range(K):
                        emit_score(i)
                        if i >= 1:
                            emit_pv(i - 1)
                    emit_pv(K - 1)
                else:
                    for i in range(K):
                        emit_score(i)
                    for h in range(4):
                        E("ve", lambda v, h=h: v.tensor_tensor(
                            pbank[:, h * KBANK * NBANK:(h + 1) * KBANK * NBANK],
                            pbank[:, h * KBANK * NBANK:(h + 1) * KBANK * NBANK],
                            bmask_s[:, :], AO.mult),
                            reads=(f"pbk{h}", "bmask_s"), writes=(f"pbm{h}", f"pbk{h}"))
                    for i in range(K):
                        emit_pv(i)

                gc0 = GN * g
                for h in range(4):
                    obank = (2, 3, 6, 7)[h]
                    po = PS[obank]; pob = PSN[obank]
                    coff = 0
                    E("se", lambda s, h=h, po=po, coff=coff, gc0=gc0, n=n: s.copy(
                        oraw[64 * (h % 2):64 * (h % 2) + 64, NQ * (h // 2) + gc0: NQ * (h // 2) + gc0 + n],
                        po[0:64, coff:coff + n]), reads=(pob, "oraw"), writes=("oraw",))
                    E("se", lambda s, h=h, po=po, coff=coff, gc0=gc0, n=n: s.copy(
                        drawt[32 * h:32 * h + 1, gc0:gc0 + n], po[64:65, coff:coff + n]),
                        reads=(pob, "draw"), writes=("draw",))
                if dbg2_h is not None and is_bank:
                    E("ge", lambda g_: g_.dma_start(out=dbg2_h.ap()[:, 0:4 * KBANK * NBANK],
                      in_=pbank[:, :]), reads=tuple(f"pbm{h}" for h in range(4)),
                      writes=("dbg4",), dma=True)
                    E("ge", lambda g_: g_.dma_start(
                      out=dbg2_h.ap()[:, 4 * KBANK * NBANK:4 * KBANK * NBANK + 16],
                      in_=drawt[:, 2048:2064]), reads=("draw",), writes=("dbg4",), dma=True)
                    E("ge", lambda g_: g_.dma_start(
                      out=dbg2_h.ap()[:, 4 * KBANK * NBANK + 16:4 * KBANK * NBANK + 32],
                      in_=oraw[:, 2048:2064]), reads=("oraw",), writes=("dbg4",), dma=True)
                    E("ge", lambda g_: g_.dma_start(
                      out=dbg2_h.ap()[:, 4 * KBANK * NBANK + 32:4 * KBANK * NBANK + 48],
                      in_=oraw[:, NQ + 2048:NQ + 2064]), reads=("oraw",), writes=("dbg4",), dma=True)

            if _Qi > 0 and _QPH < 4:
                continue
            E("ve", lambda v: v.reciprocal(dinvb[:, :], drawt[:, :]), reads=("draw", "dinv"), writes=("dinv",), lazy=True)
            for jc in range(2):
                for t5 in range(5):
                    c5 = min(512 * t5, NQ - 512)
                    n5 = 512
                    E("te", lambda t, jc=jc, c5=c5, n5=n5: t.matmul(PS[4][:, 0:n5], selq[jc][:, :],
                      dinvb[:, c5:c5 + n5], start=True, stop=True),
                      reads=("dinv", f"selq{jc}", "PS4"), writes=("PS4",))
                    E("ve", lambda v, c5=c5, n5=n5: v.tensor_copy(drawt[:, c5:c5 + n5], PS[4][:, 0:n5]),
                      reads=("PS4", "draw"), writes=("draw2",))
                    E("ve", lambda v, jc=jc, c5=c5, n5=n5: v.tensor_mul(
                        obf[:, NQ * jc + c5: NQ * jc + c5 + n5],
                        oraw[:, NQ * jc + c5: NQ * jc + c5 + n5], drawt[:, c5:c5 + n5]),
                        reads=("draw2", "oraw", f"obf{jc}_{t5}"), writes=(f"obf{jc}_{t5}",))
                E("ge", lambda g_, jc=jc, Q=Q: g_.dma_start(
                    out=obf_h.ap()[:, NQ * (2 * Q + jc): NQ * (2 * Q + jc + 1)],
                    in_=obf[:, NQ * jc: NQ * (jc + 1)]),
                    reads=tuple(f"obf{jc}_{t5}" for t5 in range(5)), writes=(f"obfh{2 * Q + jc}",), dma=True)

    emit_wq(len(wq_pieces))
    sch.barrier()
    if _DBG_STOP:
        sch.ops["ge"].append(lambda h: h.wait_ge(SEMS["D:obfh1"], 16))
        return _finish(nc, sch, root)
    with contextlib.ExitStack() as p3:
        lt3 = make_ln(p3)
        wo_b = sb(p3, "wo_b", [128, 8 * D], BF16)
        wo_f = sb(p3, "wo_f", [128, 2048])
        obw = sb(p3, "obw", [128, 8 * 512], BF16)
        zbuf = sb(p3, "zbuf", [128, 8 * 512])
        xlw = sb(p3, "xlw", [128, 8 * 512])
        xaw = sb(p3, "xaw", [128, 8 * 512])
        for half in range(4):
            E("sy", lambda s, half=half: s.dma_start(out=wo_f[:, :],
              in_=wo_d.ap()[:, 2048 * half:2048 * (half + 1)]), reads=("wo_f",), writes=("wo_f",), dma=True)
            E("ve", lambda v, half=half: v.tensor_copy(wo_b[:, 2048 * half:2048 * (half + 1)], wo_f[:, :]),
              reads=("wo_f",), writes=(f"wo_b{half}",))
        WOrd = tuple(f"wo_b{half}" for half in range(4))
        for g in range(5):
            n = 512
            c0 = min(512 * g, NQ - 512)
            for ic in range(8):
                E("sy", lambda s, ic=ic, c0=c0, n=n: s.dma_start(out=obw[:, 512 * ic:512 * ic + n],
                  in_=obf_h.ap()[:, NQ * ic + c0: NQ * ic + c0 + n]),
                  reads=(f"obfh{ic}", "obw"), writes=("obw",), dma=True)
                E("sy", lambda s, ic=ic, c0=c0, n=n: s.dma_start(out=xlw[:, 512 * ic:512 * ic + n],
                  in_=xloc.ap()[:, NQ * ic + c0: NQ * ic + c0 + n]),
                  reads=("xlw",), writes=("xlw",), dma=True)
            for oc in range(8):
                pj = PS[4 + oc % 2]; pb = PSN[4 + oc % 2]
                for ic in range(8):
                    E("te", lambda t, oc=oc, ic=ic, n=n, pj=pj: t.matmul(
                        pj[:, 0:n], wo_b[:, D * ic + 128 * oc: D * ic + 128 * oc + 128],
                        obw[:, 512 * ic:512 * ic + n], start=(ic == 0), stop=(ic == 7)),
                        reads=("obw",) + WOrd,
                        writes=(pb,) if ic in (0, 7) else (), inc=(ic == 7))
                E("se", lambda s, oc=oc, n=n, pj=pj: s.activation(zbuf[:, 512 * oc:512 * oc + n],
                  pj[:, 0:n], AF.Identity, bias=biases[:, 16 + oc:17 + oc]),
                  reads=(pb, "bias_o", "zbuf"), writes=("zbuf",))
            emit_ln(lt3, n,
                    src=lambda ic: (zbuf[:, 512 * ic:512 * ic + n], ("zbuf",)),
                    resid=lambda ic: (xlw[:, 512 * ic:512 * ic + n], ("xlw",)),
                    res_out=lambda ic: (xaw[:, 512 * ic:512 * ic + n], f"xaw{ic}"))
            for ic in range(8):
                E("ge", lambda g_, ic=ic, c0=c0, n=n: g_.dma_start(
                    out=xatt_h.ap()[:, NQ * ic + c0: NQ * ic + c0 + n],
                    in_=xaw[:, 512 * ic:512 * ic + n]),
                    reads=(f"xaw{ic}",), writes=("xattdone",), dma=True)

    sch.barrier()
    with contextlib.ExitStack() as p4s:
        lt4 = make_ln(p4s)
        hslab = sb(p4s, "hslab", [128, 32 * SW])
        xattw = sb(p4s, "xattw", [128, 8 * SLABW])
        ln2f = sb(p4s, "ln2f", [128, 8 * SLABW])
        xqb = sb(p4s, "xqb", [128, 8 * SLABW], BF16)
        w1w = [sb(p4s, f"w1w{i}", [128, 8 * 128], BF16) for i in range(2)]
        w2w = [sb(p4s, f"w2w{i}", [128, 1024], BF16) for i in range(2)]
        tmpc = sb(p4s, "tmpc", [128, FSLAB + 2])
        fcols = sb(p4s, "fcols", [128, 32])
        amtr = sb(p4s, "amtr", [128, 544])
        amtc = sb(p4s, "amtc", [128, 544])
        st32 = sb(p4s, "st32", [32, 544])
        amT = sb(p4s, "amT", [32, 544])
        amC = sb(p4s, "amC", [32, 544])
        srow = sb(p4s, "srow", [1, 1100])
        sbc = sb(p4s, "sbc", [128, SLABW]); sxinv = sb(p4s, "sxinv", [128, SLABW])
        s2bc = sb(p4s, "s2bc", [128, FSLAB + 2]); s2inv = sb(p4s, "s2inv", [128, FSLAB + 2])
        h2qb = [sb(p4s, f"h2qb{i}", [128, 512], BF16) for i in range(2)]
        zb2 = sb(p4s, "zb2", [128, 8 * FSLAB])
        ybuf = sb(p4s, "ybuf", [128, 8 * FSLAB])
        E("ve", lambda v: v.memset(amC[:, :], 0.0), writes=("amC",))
        E("ve", lambda v: v.memset(amT[:, :], 0.0), writes=("amT",))
        E("ve", lambda v: v.memset(amtr[:, :], 0.0), writes=("amtr",))

        def absmax_rows(tag, src_fn, nsrc, n, dst_c0):
            for i in range(nsrc):
                api, rdi = src_fn(i)
                if i == 0:
                    E("se", lambda s, api=api: s.activation(amtr[:, 0:n], api, AF.Abs),
                      reads=rdi + ("amtr",), writes=("amtr",))
                else:
                    E("se", lambda s, api=api: s.activation(amtc[:, 0:n], api, AF.Abs),
                      reads=rdi + ("amtc",), writes=("amtc",))
                    E("ve", lambda v: v.tensor_max(amtr[:, 0:n], amtr[:, 0:n], amtc[:, 0:n]),
                      reads=("amtc", "amtr"), writes=("amtr",))
            E("ve", lambda v: v.tensor_copy(amtc[0:64, 0:n], amtr[64:128, 0:n]),
              reads=("amtr", "amtc"), writes=("amtc",))
            E("ve", lambda v: v.tensor_max(amtr[0:64, 0:n], amtr[0:64, 0:n], amtc[0:64, 0:n]),
              reads=("amtc", "amtr"), writes=("amtr",))
            E("ve", lambda v: v.tensor_copy(amtc[0:32, 0:n], amtr[32:64, 0:n]),
              reads=("amtr", "amtc"), writes=("amtc",))
            E("ve", lambda v: v.tensor_max(amtr[0:32, 0:n], amtr[0:32, 0:n], amtc[0:32, 0:n]),
              reads=("amtc", "amtr"), writes=("amtr",))
            nb = (n + 31) // 32
            vguard()
            for j in range(nb):
                E("ve", lambda v, j=j: v.transpose(amT[0:32, 32 * j:32 * j + 32],
                  amtr[0:32, 32 * j:32 * j + 32]), reads=("amtr", "amT"), writes=("amT",), lazy=True)
            vguard()
            for j in range(nb):
                E("ve", lambda v, j=j: v.tensor_reduce(amC[0:32, 32 * j:32 * j + 1],
                  amT[0:32, 32 * j:32 * j + 32], AX.X, AO.max), reads=("amT", "amC"), writes=("amC",), lazy=True)
            vguard()
            for j in range(nb):
                E("ve", lambda v, j=j: v.transpose(st32[0:32, 32 * j:32 * j + 32],
                  amC[0:32, 32 * j:32 * j + 32]), reads=("amC", "st32"), writes=("st32",), lazy=True)
            vguard()

        def bcast_row(src_c0, n, dst, dstbuf):
            for it in range((n + 511) // 512):
                c0 = min(512 * it, max(0, n - 512))
                w = min(512, n)
                E("te", lambda t, c0=c0, w=w: t.matmul(PS[5][:, 0:w], onesf[0:1, :],
                  srow[0:1, src_c0 + c0: src_c0 + c0 + w], start=True, stop=True),
                  reads=("srow", "onesf", "PS5"), writes=("PS5",))
                E("ve", lambda v, c0=c0, w=w: v.tensor_copy(dst[:, c0:c0 + w], PS[5][:, 0:w]),
                  reads=("PS5", dstbuf), writes=(dstbuf,))

        def quant_cols(srcT, src_rd_w, n, sb_, sbuf_n, dstT, dst_w, stride):
            for ic in range(8):
                E("ve", lambda v, ic=ic: v.tensor_tensor(srcT[:, stride * ic:stride * ic + n],
                  srcT[:, stride * ic:stride * ic + n], sb_[:, 0:n], AO.mult),
                  reads=(src_rd_w(ic), sbuf_n), writes=(src_rd_w(ic),))
                E("ve", lambda v, ic=ic: v.tensor_scalar(dstT[:, stride * ic:stride * ic + n],
                  srcT[:, stride * ic:stride * ic + n], MAGIC, MAGIC, AO.add, AO.subtract),
                  reads=(src_rd_w(ic), dst_w(ic)), writes=(dst_w(ic),))

        for sl_i in range(NSLAB):
            t0 = FSLAB * sl_i
            for ic in range(8):
                for (wo, so, nn) in ((1, t0, 256), (259, t0 + 256, 256)):
                    E("sy", lambda s, ic=ic, wo=wo, so=so, nn=nn: s.dma_start(
                        out=xattw[:, SLABW * ic + wo: SLABW * ic + wo + nn],
                        in_=xatt_h.ap()[:, NQ * ic + so: NQ * ic + so + nn]),
                        reads=("xattdone", "xattw"), writes=("xattw",), dma=True)
                E("sy", lambda s, ic=ic, sl_i=sl_i: s.dma_start(
                    out=fcols[:, 4 * ic: 4 * ic + 4],
                    in_=xatt_h.ap()[:, NQ * ic + NOWN + 4 * sl_i: NQ * ic + NOWN + 4 * sl_i + 4]),
                    reads=("xattdone", "fcols"), writes=("fcols",), dma=True)
            for (wo, fo, nn) in ((0, 0, 1), (257, 1, 2), (515, 3, 1)):
                E("ve", lambda v, wo=wo, fo=fo, nn=nn: v.tensor_copy(
                    bass.AP(xattw, wo, [[8 * SLABW, 128], [SLABW, 8], [1, nn]]),
                    bass.AP(fcols, fo, [[32, 128], [4, 8], [1, nn]])),
                    reads=("fcols", "xattw"), writes=("xattw",), lazy=True)
            for (lc0, lw) in ((0, 512), (SLABW - 512, 512)):
                emit_ln(lt4, lw,
                        src=lambda ic, lc0=lc0, lw=lw: (
                            xattw[:, SLABW * ic + lc0:SLABW * ic + lc0 + lw], ("xattw",)),
                        dst_f=lambda ic, lc0=lc0, lw=lw: (
                            ln2f[:, SLABW * ic + lc0:SLABW * ic + lc0 + lw], f"ln2f{ic}"))
            LN2RD = tuple(f"ln2f{ic}" for ic in range(8))
            absmax_rows("xq", lambda ic: (ln2f[:, SLABW * ic:SLABW * (ic + 1)], LN2RD), 8, SLABW, 0)
            E("ve", lambda v: v.tensor_scalar(srow[0:1, 550:550 + SLABW], st32[0:1, 0:SLABW], EPS, None, AO.max),
              reads=("srow", "st32"), writes=("srow",))
            E("ve", lambda v: v.reciprocal(srow[0:1, 0:SLABW], srow[0:1, 550:550 + SLABW]),
              reads=("srow",), writes=("srow",), lazy=True)
            E("ve", lambda v: v.tensor_scalar(srow[0:1, 0:SLABW], srow[0:1, 0:SLABW], 127.0, None, AO.mult),
              reads=("srow",), writes=("srow",))
            E("ve", lambda v: v.tensor_scalar(srow[0:1, 550:550 + SLABW], srow[0:1, 550:550 + SLABW],
              1.0 / 127.0, None, AO.mult), reads=("srow",), writes=("srow",))
            bcast_row(0, SLABW, sbc, "sbc")
            bcast_row(550, SLABW, sxinv, "sxinv")
            quant_cols(ln2f, lambda ic: f"ln2f{ic}", SLABW, sbc[:, 0:SLABW], "sbc",
                       xqb, lambda ic: f"xqb{ic}", SLABW)
            XQRD = tuple(f"xqb{ic}" for ic in range(8))
            for ic in range(8):
                E("sy", lambda s, ic=ic, sl_i=sl_i: s.dma_start(
                    out=ln2f[:, SLABW * ic: SLABW * ic + FSLAB],
                    in_=xatt_h.ap()[:, NQ * ic + FSLAB * sl_i: NQ * ic + FSLAB * (sl_i + 1)]),
                    reads=("xattdone", f"ln2f{ic}"), writes=(f"ln2f{ic}",), dma=True)
            for oc in range(32):
                wsl = oc % 2
                E("sy", lambda s, oc=oc, wsl=wsl: s.dma_start(out=w1w[wsl][:, :],
                  in_=bass.AP(w1q_h, 128 * oc, [[8 * DH, 128], [DH, 8], [1, 128]])),
                  reads=("w1done", f"w1w{wsl}"), writes=(f"w1w{wsl}",), dma=True)
                for (c0, nn) in ((0, 512), (SLABW - 128, 128)):
                    pj = PS[4 + oc % 2]; pb = PSN[4 + oc % 2]
                    for ic in range(8):
                        E("te", lambda t, ic=ic, c0=c0, nn=nn, wsl=wsl, pj=pj: t.matmul(
                            pj[:, 0:nn], w1w[wsl][:, 128 * ic:128 * (ic + 1)],
                            xqb[:, SLABW * ic + c0: SLABW * ic + c0 + nn],
                            start=(ic == 0), stop=(ic == 7)),
                            reads=XQRD + (f"w1w{wsl}",),
                            writes=(pb,) if ic in (0, 7) else (), inc=(ic == 7))
                    E("ve", lambda v, oc=oc, c0=c0, nn=nn, pj=pj: v.scalar_tensor_tensor(
                        hslab[:, SW * oc + c0: SW * oc + c0 + nn], pj[:, 0:nn], swinv[:, 0:1],
                        sxinv[:, c0:c0 + nn], AO.mult, AO.mult),
                        reads=(pb, "swinv0", "sxinv", f"hs{oc}", f"h2_{oc}"), writes=(f"hs{oc}", f"h2_{oc}"))
            HSRD = tuple(f"hs{oc}" for oc in range(32))
            if dbg_h is not None and sl_i == 0:
                E("ge", lambda g_: g_.dma_start(out=dbg_h.ap()[:, 1548:1558], in_=hslab[:, 506:516]),
                  reads=HSRD, writes=("dbg3",), dma=True)
            for (pos, extc) in ((0, 0), (3, SLABW - 1)):
                slot = 4 * sl_i + pos
                if slot not in (0, 15):
                    continue
                hcols = bass.AP(hslab, extc, [[32 * SW, 128], [SW, 32], [1, 1]])
                E("ve", lambda v, hcols=hcols, slot=slot: v.tensor_scalar(
                    hcols, hcols, cfil_s[:, slot:slot + 1], None, AO.mult),
                    reads=("cfil_s",) + HSRD, writes=HSRD, lazy=True)
            CVW = FSLAB + 2
            for oc in range(32):
                E("ve", lambda v, oc=oc: v.tensor_scalar(tmpc[:, 0:CVW],
                  hslab[:, SW * oc: SW * oc + CVW], cw_s[:, oc:oc + 1], None, AO.mult),
                  reads=(f"hs{oc}", "cw_s", "tmpc"), writes=("tmpc",))
                E("ve", lambda v, oc=oc: v.scalar_tensor_tensor(tmpc[:, 0:CVW],
                  hslab[:, SW * oc + 1: SW * oc + 1 + CVW], cw_s[:, 32 + oc:33 + oc], tmpc[:, 0:CVW],
                  AO.mult, AO.add), reads=(f"hs{oc}", "cw_s", "tmpc"), writes=("tmpc",))
                E("ve", lambda v, oc=oc: v.scalar_tensor_tensor(tmpc[:, 0:CVW],
                  hslab[:, SW * oc + 2: SW * oc + 2 + CVW], cw_s[:, 64 + oc:65 + oc], tmpc[:, 0:CVW],
                  AO.mult, AO.add), reads=(f"hs{oc}", "cw_s", "tmpc"), writes=("tmpc",))
                E("se", lambda s, oc=oc: s.activation(hslab[:, SW * oc + 1: SW * oc + 1 + CVW],
                  tmpc[:, 0:CVW], AF.Silu, bias=cb_s[:, oc:oc + 1]),
                  reads=("tmpc", "cb_s", f"hs{oc}"), writes=(f"h2_{oc}",))
            H2RD = tuple(f"h2_{oc}" for oc in range(32))
            absmax_rows("h2", lambda oc: (hslab[:, SW * oc + 1: SW * oc + 1 + CVW], H2RD), 32, CVW, 0)
            E("ve", lambda v: v.tensor_scalar(srow[0:1, 550:550 + CVW], st32[0:1, 0:CVW], EPS, None, AO.max),
              reads=("srow", "st32"), writes=("srow",))
            E("ve", lambda v: v.reciprocal(srow[0:1, 0:CVW], srow[0:1, 550:550 + CVW]),
              reads=("srow",), writes=("srow",), lazy=True)
            E("ve", lambda v: v.tensor_scalar(srow[0:1, 0:CVW], srow[0:1, 0:CVW], 127.0, None, AO.mult),
              reads=("srow",), writes=("srow",))
            E("ve", lambda v: v.tensor_scalar(srow[0:1, 550:550 + CVW], srow[0:1, 550:550 + CVW],
              1.0 / 127.0, None, AO.mult), reads=("srow",), writes=("srow",))
            bcast_row(0, CVW, s2bc, "s2bc")
            bcast_row(550, CVW, s2inv, "s2inv")
            for ch in range(32):
                wsl = ch % 2
                E("sy", lambda s, ch=ch, wsl=wsl: s.dma_start(out=w2w[wsl][:, :],
                  in_=w2q_h.ap()[:, 1024 * ch:1024 * (ch + 1)]),
                  reads=("w2done", f"w2w{wsl}"), writes=(f"w2w{wsl}",), dma=True)
                hq = h2qb[ch % 2]
                E("ve", lambda v, ch=ch: v.tensor_tensor(tmpc[:, 0:CVW],
                  hslab[:, SW * ch + 1: SW * ch + 1 + CVW], s2bc[:, 0:CVW], AO.mult),
                  reads=(f"h2_{ch}", "s2bc", "tmpc"), writes=("tmpc",))
                E("ve", lambda v, hq=hq: v.tensor_scalar(hq[:, 0:256], tmpc[:, 0:256], MAGIC, MAGIC,
                  AO.add, AO.subtract), reads=("tmpc", f"hq{ch % 2}"), writes=(f"hq{ch % 2}",))
                E("ve", lambda v, hq=hq: v.tensor_scalar(hq[:, 256:512], tmpc[:, 258:514], MAGIC, MAGIC,
                  AO.add, AO.subtract), reads=("tmpc", f"hq{ch % 2}"), writes=(f"hq{ch % 2}",))
                for oc in range(8):
                    pj = PS[oc]; pb = PSN[oc]
                    E("te", lambda t, ch=ch, oc=oc, wsl=wsl, hq=hq, pj=pj: t.matmul(
                        pj[:, 0:FSLAB], w2w[wsl][:, 128 * oc:128 * (oc + 1)], hq[:, :],
                        start=(ch == 0), stop=(ch == 31)),
                        reads=(f"hq{ch % 2}", f"w2w{wsl}"),
                        writes=(pb,) if ch in (0, 31) else (), inc=(ch == 31 or oc == 7))
            for oc in range(8):
                pj = PS[oc]; pb = PSN[oc]
                E("ve", lambda v, oc=oc, pj=pj: v.scalar_tensor_tensor(zb2[:, FSLAB * oc: FSLAB * oc + 256],
                  pj[:, 0:256], swinv[:, 1:2], s2inv[:, 0:256], AO.mult, AO.mult),
                  reads=(pb, "swinv1", "s2inv", "zb2"), writes=("zb2",))
                E("ve", lambda v, oc=oc, pj=pj: v.scalar_tensor_tensor(zb2[:, FSLAB * oc + 256: FSLAB * oc + 512],
                  pj[:, 256:512], swinv[:, 1:2], s2inv[:, 258:514], AO.mult, AO.mult),
                  reads=(pb, "swinv1", "s2inv", "zb2"), writes=("zb2",))
            emit_ln(lt4, FSLAB,
                    src=lambda ic: (zb2[:, FSLAB * ic:FSLAB * (ic + 1)], ("zb2",)),
                    resid=lambda ic: (ln2f[:, SLABW * ic:SLABW * ic + FSLAB], (f"ln2f{ic}",)),
                    res_out=lambda ic: (ybuf[:, FSLAB * ic:FSLAB * (ic + 1)], f"yb{ic}"))
            for ic in range(8):
                E("ge", lambda g_, ic=ic, sl_i=sl_i: g_.dma_start(
                    out=yout.ap()[:, NOWN * ic + FSLAB * sl_i: NOWN * ic + FSLAB * (sl_i + 1)],
                    in_=ybuf[:, FSLAB * ic:FSLAB * (ic + 1)]),
                    reads=(f"yb{ic}",), writes=("ydone",), dma=True)
            if dbg_h is not None and sl_i == 0:
                E("ge", lambda g_: g_.dma_start(out=dbg_h.ap()[:, 0:514], in_=xqb[:, 0:514]),
                  reads=tuple(f"xqb{i}" for i in range(8)), writes=("dbg3",), dma=True)
                E("ge", lambda g_: g_.dma_start(out=dbg_h.ap()[:, 514:1028], in_=hslab[:, 0:514]),
                  reads=tuple(f"h2_{o}" for o in range(32)), writes=("dbg3",), dma=True)
                E("ge", lambda g_: g_.dma_start(out=dbg_h.ap()[:, 1028:1540], in_=zb2[:, 0:512]),
                  reads=("zb2",), writes=("dbg3",), dma=True)
                E("ge", lambda g_: g_.dma_start(out=dbg_h.ap()[:, 1540:1544], in_=s2bc[:, 4:8]),
                  reads=("s2bc",), writes=("dbg3",), dma=True)
                E("ge", lambda g_: g_.dma_start(out=dbg_h.ap()[:, 1544:1548], in_=sbc[:, 4:8]),
                  reads=("sbc",), writes=("dbg3",), dma=True)
                E("ge", lambda g_: g_.dma_start(out=dbg_h.ap()[:, 1630:1640], in_=sxinv[:, 506:516]),
                  reads=("sxinv",), writes=("dbg3",), dma=True)
                E("ge", lambda g_: g_.dma_start(out=dbg_h.ap()[:, 1704:1714], in_=sbc[:, 506:516]),
                  reads=("sbc",), writes=("dbg3",), dma=True)

    ytot = 16 * sch.dfam["ydone"]
    sch.ops["ge"].append(lambda h: h.wait_ge(SEMS["D:ydone"], ytot))
    return _finish(nc, sch, root)


def _finish(nc, sch, root):
    with nc.Block() as block:
        @block.sync
        def _(eng):
            for f in sch.ops["sy"]:
                f(eng)

        @block.tensor
        def _(eng):
            for f in sch.ops["te"]:
                f(eng)

        @block.vector
        def _(eng):
            for f in sch.ops["ve"]:
                f(eng)

        @block.scalar
        def _(eng):
            for f in sch.ops["se"]:
                f(eng)

        @block.gpsimd
        def _(eng):
            for f in sch.ops["ge"]:
                f(eng)

    root.close()
    return nc


SPLIT = np.concatenate([np.arange(0, 64, 2), np.arange(1, 64, 2)])


def _chunkmaj(a):
    rows, cols = a.shape
    nch = rows // 128
    return np.ascontiguousarray(a.reshape(nch, 128, cols).transpose(1, 0, 2).reshape(128, nch * cols))


def _feat(a):
    return _chunkmaj(np.ascontiguousarray(a.T))


_CACHE = {}


def kernel(L0_attn_x, L0_attn_wq, L0_attn_bq, L0_attn_wk, L0_attn_bk,
           L0_attn_wv, L0_attn_bv, L0_attn_wo, L0_attn_bo,
           L0_ffn_w1, L0_ffn_conv_w, L0_ffn_conv_b, L0_ffn_w2):
    x = np.asarray(L0_attn_x, np.float32)
    wq = np.asarray(L0_attn_wq, np.float32); wk = np.asarray(L0_attn_wk, np.float32)
    wv = np.asarray(L0_attn_wv, np.float32); wo = np.asarray(L0_attn_wo, np.float32)
    bq = np.asarray(L0_attn_bq, np.float32); bk = np.asarray(L0_attn_bk, np.float32)
    bv = np.asarray(L0_attn_bv, np.float32); bo = np.asarray(L0_attn_bo, np.float32)
    w1 = np.asarray(L0_ffn_w1, np.float32); w2 = np.asarray(L0_ffn_w2, np.float32)
    cwt = np.asarray(L0_ffn_conv_w, np.float32); cbt = np.asarray(L0_ffn_conv_b, np.float32)

    if "nc" not in _CACHE:
        _CACHE["nc"] = build_program()
    nc = _CACHE["nc"]

    perm = np.concatenate([64 * h + SPLIT for h in range(H)])
    wq_p = wq[:, perm]; wk_p = wk[:, perm]
    bq_p = bq[perm]; bk_p = bk[perm]
    d_wq = _chunkmaj(wq_p); d_wk = _chunkmaj(wk_p)
    d_wv = _chunkmaj(wv); d_wo = _chunkmaj(wo)
    d_bq = np.ascontiguousarray(bq_p.reshape(8, 128).T)
    d_bk = np.ascontiguousarray(bk_p.reshape(8, 128).T)
    d_bo = np.ascontiguousarray(bo.reshape(8, 128).T)
    d_bvr = bv.reshape(1, D)
    d_w1t = _chunkmaj(np.ascontiguousarray(w1.T))
    d_w2t = _chunkmaj(np.ascontiguousarray(w2.T))
    d_cw = np.concatenate([cwt[:, 0, t].reshape(32, 128).T for t in range(3)], axis=1)
    d_cw = np.ascontiguousarray(d_cw)
    d_cb = np.ascontiguousarray(cbt.reshape(32, 128).T)
    d_idf = np.eye(128, dtype=np.float32)

    s8 = 1.0 / np.sqrt(np.sqrt(64.0))
    s8 = 8.0 ** -0.5
    invf = (10000.0 ** (-np.arange(0, 64, 2, dtype=np.float64) / 64.0))

    def rope_tables(pos):
        ang = pos[None, :].astype(np.float64) * invf[:, None]
        c = np.cos(ang) * s8; s = np.sin(ang) * s8
        slot_c = np.concatenate([c, c], axis=0)
        slot_s = np.concatenate([s, -s], axis=0)
        C2 = np.concatenate([slot_c, slot_c], axis=0).astype(np.float32)
        NS = np.concatenate([slot_s, slot_s], axis=0).astype(np.float32)
        return np.ascontiguousarray(C2), np.ascontiguousarray(NS)

    kpos = np.arange(S)
    d_c2k, d_nsk = rope_tables(kpos)

    tri = (np.arange(128)[:, None] <= np.arange(128)[None, :]).astype(ml_dtypes.bfloat16)
    ones128 = np.ones((128, 128), ml_dtypes.bfloat16)
    tri4 = np.ascontiguousarray(np.tile(tri, (1, 4)))
    ones4 = np.ascontiguousarray(np.tile(ones128, (1, 4)))

    per_side = {}
    for side in range(2):
        own_idx = own_idx_side(side)
        bpos = bank_pos_side(side)
        qpos = np.concatenate([own_idx, np.array(bpos)])
        c2q, nsq = rope_tables(qpos)
        wkill = np.zeros((NKILL, S), ml_dtypes.bfloat16)
        for g in range(G8):
            for l in range(2):
                thr = 128 * (4 * g + 2 * side + l + 1)
                wkill[2 * g + l, thr:] = ml_dtypes.bfloat16(1.0)
        qkill = np.zeros((NKILL, (G8 + 1) * GN), ml_dtypes.bfloat16)
        for g in range(G8):
            for l in range(2):
                qkill[2 * g + l, GN * g + 128 * l: GN * g + 128 * (l + 1)] = ml_dtypes.bfloat16(KILL)
        m1 = tri4 if side == 0 else ones4
        m2 = ones4 if side == 0 else tri4
        bm = np.zeros((128, KBANK * NBANK), ml_dtypes.bfloat16)
        for i in range(KBANK):
            for c in range(NBANK):
                valid = (128 * i + np.arange(128)) <= bpos[c]
                bm[:, NBANK * i + c] = valid.astype(ml_dtypes.bfloat16)
        cfil = np.ones((128, NBANK), np.float32)
        if side == 0:
            cfil[:, 0] = 0.0
        else:
            cfil[:, 15] = 0.0
        per_side[side] = dict(c2q=np.ascontiguousarray(c2q), nsq=np.ascontiguousarray(nsq),
                              wkill=np.ascontiguousarray(wkill), qkill=np.ascontiguousarray(qkill),
                              m1=np.ascontiguousarray(m1), m2=np.ascontiguousarray(m2),
                              bmask=np.ascontiguousarray(bm), cfil=cfil,
                              own_idx=own_idx, bpos=bpos)

    in_maps = []
    for core in range(8):
        b = core // 2; side = core % 2
        ps = per_side[side]
        xb = x[b]
        own = xb[ps["own_idx"]]
        bank = xb[ps["bpos"]]
        xloc = np.concatenate([own, bank], axis=0)
        in_maps.append({
            "xful": _feat(xb), "xloc": _feat(xloc),
            "wq": d_wq, "wk": d_wk, "wv": d_wv, "wo": d_wo,
            "bq": d_bq, "bk": d_bk, "bvr": d_bvr, "bo": d_bo,
            "w1t": d_w1t, "w2t": d_w2t, "cw": d_cw, "cb": d_cb,
            "c2q": ps["c2q"], "nsq": ps["nsq"], "c2k": d_c2k, "nsk": d_nsk,
            "wkill": ps["wkill"], "qkill": ps["qkill"], "m1": ps["m1"], "m2": ps["m2"],
            "bmask": ps["bmask"], "cfil": ps["cfil"], "idf": d_idf,
        })

    res = run_bass_kernel_spmd(nc, in_maps, core_ids=list(range(8)))

    out = np.empty((B, S, D), np.float32)
    for core in range(8):
        b = core // 2; side = core % 2
        y = res.results[core]["y"]
        yt = y.reshape(128, 8, NOWN).transpose(1, 0, 2).reshape(D, NOWN)
        out[b, per_side[side]["own_idx"]] = yt.T
    return out


# version 15
# speedup vs baseline: 1.1576x; 1.0408x over previous
import numpy as np
import ml_dtypes
import concourse.bass as bass
import concourse.mybir as mybir
from concourse.bass_utils import run_bass_kernel_spmd

F32 = mybir.dt.float32
BF16 = mybir.dt.bfloat16
AO = mybir.AluOpType
AF = mybir.ActivationFunctionType
AX = mybir.AxisListType

MAGIC = 12582912.0
KILL = -100000.0
EPS = 1e-5

B, S, D, H, DH = 4, 4096, 1024, 16, 4096
NOWN, NBANK = 2048, 16
NQ = NOWN + NBANK
G8, GN = 8, 256
KBANK = 30
NKILL = 2 * G8
KB0 = 96
KC = KB0 + NKILL

def own_idx_side(side):
    import numpy as _np
    return _np.concatenate([_np.arange(512 * u + 256 * side, 512 * u + 256 * side + 256)
                            for u in range(8)])

def bank_pos_side(side):
    def tok_of_own(c):
        return 512 * (c // 256) + 256 * side + (c % 256)
    slots = []
    slots.append(tok_of_own(0) - 1 if side == 1 else 0)
    for u in range(1, 8):
        slots.append(tok_of_own(256 * u - 1) + 1)
        slots.append(tok_of_own(256 * u) - 1)
    slots.append(tok_of_own(2047) + 1 if side == 0 else 0)
    return slots

KSLOT = [4 * g + 4 for g in range(G8)]

FSLAB = 512
NSLAB = NOWN // FSLAB
SLABW = FSLAB + 4
SW = SLABW

ENGS = ("sy", "te", "ve", "se", "ge")
SEMS = {}
VJUNK = None


class Sch:

    def __init__(self):
        self.ops = {e: [] for e in ENGS}
        self.cnt = {e: 0 for e in ENGS}
        self.dfam = {}
        self.mkdsem = None
        self.bufs = {}
        self.lastw = {e: {} for e in ENGS}
        self.bar = None

    def barrier(self):
        self.bar = (dict(self.cnt), dict(self.dfam), set())

    def _wait(self, eng, semkey, val):
        if val <= self.lastw[eng].get(semkey, 0):
            return
        self.lastw[eng][semkey] = val
        self.ops[eng].append(lambda h, semkey=semkey, val=val: h.wait_ge(SEMS[semkey], val))

    def _resolve(self, sk, v):
        if sk.startswith("D:") and v < 0:
            return 16 * self.dfam[sk[2:]]
        return v

    def emit(self, eng, fn, reads=(), writes=(), dma=False, inc=True, lazy=False):
        if self.bar is not None and eng not in self.bar[2]:
            sc, sd, applied = self.bar
            for e2, v in sc.items():
                if e2 != eng and v > 0:
                    self._wait(eng, e2, v)
            for f, c in sd.items():
                if c > 0:
                    self._wait(eng, "D:" + f, 16 * c)
            applied.add(eng)
            if len(applied) == len(ENGS):
                self.bar = None
        need = {}
        for b in reads:
            rec = self.bufs.get(b)
            if rec and rec["w"]:
                sk, v = rec["w"]
                if sk != eng:
                    need[sk] = max(need.get(sk, 0), self._resolve(sk, v))
        for b in writes:
            rec = self.bufs.get(b)
            if rec:
                if rec["w"]:
                    sk, v = rec["w"]
                    if sk != eng:
                        need[sk] = max(need.get(sk, 0), self._resolve(sk, v))
                for sk, v in rec["r"].items():
                    if sk != eng:
                        need[sk] = max(need.get(sk, 0), self._resolve(sk, v))
        for sk, v in sorted(need.items()):
            self._wait(eng, sk, v)
        if dma:
            fam = writes[0]
            if fam not in self.dfam:
                self.dfam[fam] = 0
                self.mkdsem(fam)
            self.dfam[fam] += 1
            semkey, val = "D:" + fam, -1
            self.ops[eng].append(lambda h, fn=fn, semkey=semkey: fn(h).then_inc(SEMS[semkey], 16))
        elif inc:
            self.cnt[eng] += 1
            semkey, val = eng, self.cnt[eng]
            self.ops[eng].append(lambda h, fn=fn, semkey=semkey: fn(h).then_inc(SEMS[semkey], 1))
            if lazy:
                for _ in range(4):
                    self.cnt[eng] += 1
                    self.ops[eng].append(lambda h, semkey=semkey: h.memset(
                        VJUNK[:, :], 0.25).then_inc(SEMS[semkey], 1))
                val = self.cnt[eng]
        else:
            semkey, val = eng, self.cnt[eng] + 1
            self.ops[eng].append(fn)
        for b in writes:
            self.bufs[b] = {"w": (semkey, val), "r": {}}
        for b in reads:
            rec = self.bufs.setdefault(b, {"w": None, "r": {}})
            if semkey.startswith("D:"):
                rec["r"][semkey] = -1
            else:
                rec["r"][semkey] = max(rec["r"].get(semkey, 0), val)


def build_program():
    import contextlib
    nc = bass.Bass("TRN2", target_bir_lowering=False, detect_race_conditions=False)
    sch = Sch()
    E = sch.emit
    root = contextlib.ExitStack()

    din = lambda name, shape, dt=F32: nc.dram_tensor(name, shape, dt, kind="ExternalInput")
    xful = din("xful", [128, 8 * S])
    xloc = din("xloc", [128, 8 * NQ])
    wq_d = din("wq", [128, 8 * D]); wk_d = din("wk", [128, 8 * D])
    wv_d = din("wv", [128, 8 * D]); wo_d = din("wo", [128, 8 * D])
    bq_d = din("bq", [128, 8]); bk_d = din("bk", [128, 8])
    bvr_d = din("bvr", [1, D]); bo_d = din("bo", [128, 8])
    w1t_d = din("w1t", [128, 8 * DH])
    w2t_d = din("w2t", [128, 32 * D])
    cw_d = din("cw", [128, 3 * 32]); cb_d = din("cb", [128, 32])
    c2q_d = din("c2q", [128, NQ]); nsq_d = din("nsq", [128, NQ])
    c2k_d = din("c2k", [128, S]); nsk_d = din("nsk", [128, S])
    wkill_d = din("wkill", [NKILL, S], BF16)
    qkill_d = din("qkill", [NKILL, (G8 + 1) * GN], BF16)
    m1_d = din("m1", [128, 512], BF16); m2_d = din("m2", [128, 512], BF16)
    bmask_d = din("bmask", [128, KBANK * NBANK], BF16)
    cfil_d = din("cfil", [128, NBANK])
    idf_d = din("idf", [128, 128])
    yout = nc.dram_tensor("y", [128, 8 * NOWN], F32, kind="ExternalOutput")

    import os as _osd
    _k = dict(kind="ExternalOutput") if _osd.environ.get("K_DBG_OUT") == "1" else {}
    ln1f_h = nc.dram_tensor("ln1f_h", [128, 8 * S], BF16, **_k)
    ln1o_h = nc.dram_tensor("ln1o_h", [128, 8 * NQ], BF16, **_k)
    xatt_h = nc.dram_tensor("xatt_h", [128, 8 * NQ], F32, **_k)
    obf_h = nc.dram_tensor("obf_h", [128, 8 * NQ], BF16, **_k)
    w1q_h = nc.dram_tensor("w1q_h", [128, 8 * DH], BF16, **_k)
    w2q_h = nc.dram_tensor("w2q_h", [128, 32 * D], BF16, **_k)
    dbg_h = nc.dram_tensor("dbg_h", [128, 2304], F32, **_k) if _k else None
    dbg2_h = nc.dram_tensor("dbg2_h", [128, 4 * KBANK * NBANK + 64], BF16, **_k) if _k else None

    sb = lambda st, name, shape, dt=F32: st.enter_context(nc.sbuf_tensor(name, shape, dt))
    SEMS.clear()
    for e in ENGS:
        SEMS[e] = root.enter_context(nc.semaphore(f"sem_{e}"))

    def _mkdsem(fam):
        SEMS["D:" + fam] = root.enter_context(nc.semaphore(f"d_{len(SEMS)}"))
    sch.mkdsem = _mkdsem

    onesbf = sb(root, "onesbf", [128, 128], BF16)
    onesf = sb(root, "onesf", [1, 128])
    onesw = sb(root, "onesw", [1, 512])
    idf_s = sb(root, "idf_s", [128, 128])
    selq = [sb(root, f"selq{j}", [128, 128]) for j in range(2)]
    vjunk = sb(root, "vjunk", [32, 32])
    global VJUNK
    VJUNK = vjunk
    hs4 = [sb(root, f"hs4{j}", [128, 4], BF16) for j in range(2)]
    biases = sb(root, "biases", [128, 24])
    cw_s = sb(root, "cw_s", [128, 3 * 32]); cb_s = sb(root, "cb_s", [128, 32])
    cfil_s = sb(root, "cfil_s", [128, NBANK])
    swinv = sb(root, "swinv", [128, 2])
    m1_s = sb(root, "m1_s", [128, 512], BF16); m2_s = sb(root, "m2_s", [128, 512], BF16)
    bmask_s = sb(root, "bmask_s", [128, KBANK * NBANK], BF16)
    PS = [root.enter_context(nc.psum_tensor(f"ps{i}", [128, 512], F32)) for i in range(8)]
    PSN = [f"PS{i}" for i in range(8)]

    E("ve", lambda v: v.memset(onesbf[:, :], 1.0), writes=("onesbf",))

    def vguard():
        for _ in range(4):
            E("ve", lambda v: v.memset(vjunk[:, :], 0.25), reads=("vjunk",), writes=("vjunk",))
    E("ve", lambda v: v.memset(onesf[:, :], 1.0), writes=("onesf",))
    E("ve", lambda v: v.memset(onesw[:, :], 1.0), writes=("onesw",))
    E("sy", lambda s: s.dma_start(out=idf_s[:, :], in_=idf_d.ap()), writes=("idf_s",), dma=True)
    E("sy", lambda s: s.dma_start(out=m1_s[:, :], in_=m1_d.ap()), writes=("m1_s",), dma=True)
    E("sy", lambda s: s.dma_start(out=m2_s[:, :], in_=m2_d.ap()), writes=("m2_s",), dma=True)
    E("sy", lambda s: s.dma_start(out=bmask_s[:, :], in_=bmask_d.ap()), writes=("bmask_s",), dma=True)
    E("sy", lambda s: s.dma_start(out=biases[:, 0:8], in_=bq_d.ap()), writes=("bias_q",), dma=True)
    E("sy", lambda s: s.dma_start(out=biases[:, 8:16], in_=bk_d.ap()), writes=("bias_k",), dma=True)
    E("sy", lambda s: s.dma_start(out=biases[:, 16:24], in_=bo_d.ap()), writes=("bias_o",), dma=True)
    E("sy", lambda s: s.dma_start(out=cw_s[:, :], in_=cw_d.ap()), writes=("cw_s",), dma=True)
    E("sy", lambda s: s.dma_start(out=cb_s[:, :], in_=cb_d.ap()), writes=("cb_s",), dma=True)
    E("sy", lambda s: s.dma_start(out=cfil_s[:, :], in_=cfil_d.ap()), writes=("cfil_s",), dma=True)
    for j in range(2):
        h_ = hs4[j]
        E("ve", lambda v, h_=h_: v.memset(h_[:, :], 0.0), writes=(f"hs4{j}",))
        E("ve", lambda v, h_=h_, j=j: v.memset(h_[0:64, 2 * j:2 * j + 1], 1.0),
          reads=(f"hs4{j}",), writes=(f"hs4{j}",))
        E("ve", lambda v, h_=h_, j=j: v.memset(h_[64:128, 2 * j + 1:2 * j + 2], 1.0),
          reads=(f"hs4{j}",), writes=(f"hs4{j}",))
        q_ = selq[j]
        E("ve", lambda v, q_=q_: v.memset(q_[:, :], 0.0), writes=(f"selq{j}",))
        E("ve", lambda v, q_=q_, j=j: v.memset(q_[64 * j:64 * j + 1, 0:64], 1.0),
          reads=(f"selq{j}",), writes=(f"selq{j}",))
        E("ve", lambda v, q_=q_, j=j: v.memset(q_[64 * j + 32:64 * j + 33, 64:128], 1.0),
          reads=(f"selq{j}",), writes=(f"selq{j}",))

    _ln_n = [0]

    LNW = 544

    def make_ln(st):
        i = _ln_n[0]; _ln_n[0] += 1
        lt = {"id": str(i),
              "xbf": sb(st, f"lnxbf{i}", [128, 2 * LNW], BF16),
              "sq": sb(st, f"lnsq{i}", [128, 2 * LNW], BF16),
              "tmp": sb(st, f"lntmp{i}", [128, 4 * LNW])}
        if dbg_h is not None and i == 0:
            for gap0 in (512, LNW + 512, 2 * LNW + 512):
                E("ve", lambda v, lt=lt, gap0=gap0: v.memset(lt["tmp"][:, gap0:gap0 + LNW - 512], 0.0),
                  writes=(f"lnm{i}",))
        return lt

    def emit_ln(lt, n, src, dst_bf=None, dst_f=None, resid=None, res_out=None, banks=(6, 7)):
        li = lt["id"]
        psA, psB = PS[banks[0]], PS[banks[1]]
        fA, fB = f"PS{banks[0]}", f"PS{banks[1]}"
        for ic in range(8):
            ap, rd = src(ic)
            sl = ic % 2
            xb = lt["xbf"][:, LNW * sl:LNW * sl + n]
            sq = lt["sq"][:, LNW * sl:LNW * sl + n]
            bxb, bsq2 = f"lnxb{sl}_{li}", f"lnsq{sl}_{li}"
            E("se", lambda s, ap=ap, xb=xb: s.copy(xb, ap), reads=rd + (bxb,), writes=(bxb,))
            E("se", lambda s, ap=ap, sq=sq: s.activation(sq, ap, AF.Square), reads=rd + (bsq2,), writes=(bsq2,))
            E("te", lambda t_, xb=xb, ic=ic, n=n: t_.matmul(psA[:, 0:n], onesbf[:, :], xb,
              start=(ic == 0), stop=(ic == 7)),
              reads=(bxb, "onesbf"),
              writes=(fA,) if ic in (0, 7) else (), inc=True)
            E("te", lambda t_, sq=sq, ic=ic, n=n: t_.matmul(psB[:, 0:n], onesbf[:, :], sq,
              start=(ic == 0), stop=(ic == 7)),
              reads=(bsq2, "onesbf"),
              writes=(fB,) if ic in (0, 7) else (), inc=True)
        tmp = lt["tmp"]
        mb = tmp[:, 0:n]; wv_ = tmp[:, LNW:LNW + n]; rb = tmp[:, 2 * LNW:2 * LNW + n]; dd = tmp[:, 3 * LNW:3 * LNW + n]
        E("ve", lambda v: v.tensor_scalar(mb, psA[:, 0:n], 1.0 / D, None, AO.mult),
          reads=(fA, f"lnm{li}"), writes=(f"lnm{li}",))
        E("ve", lambda v: v.tensor_mul(wv_, mb, mb), reads=(f"lnm{li}", f"lnw{li}"), writes=(f"lnw{li}",))
        E("ve", lambda v: v.scalar_tensor_tensor(wv_, psB[:, 0:n], 1.0 / D, wv_, AO.mult, AO.subtract),
          reads=(fB, f"lnw{li}"), writes=(f"lnw{li}",))
        E("ve", lambda v: v.tensor_scalar_add(wv_, wv_, EPS), reads=(f"lnw{li}",), writes=(f"lnw{li}",))
        E("se", lambda s: s.activation(rb, wv_, AF.Ln), reads=(f"lnw{li}", f"lnr{li}"), writes=(f"lnr{li}",))
        E("se", lambda s: s.activation(rb, rb, AF.Exp, scale=-0.5), reads=(f"lnr{li}",), writes=(f"lnr{li}",))
        for ic in range(8):
            ap, rd = src(ic)
            dn = f"lnd{li}"
            E("ve", lambda v, ap=ap: v.tensor_sub(dd, ap, mb), reads=rd + (f"lnm{li}", dn), writes=(dn,))
            if dst_bf is not None:
                oap, ob = dst_bf(ic)
                E("ve", lambda v, oap=oap: v.tensor_mul(oap, dd, rb), reads=(dn, f"lnr{li}"), writes=(ob,))
            if dst_f is not None:
                oap, ob = dst_f(ic)
                E("ve", lambda v, oap=oap: v.tensor_mul(oap, dd, rb), reads=(dn, f"lnr{li}"), writes=(ob,))
            if resid is not None:
                rap, rrd = resid(ic)
                oap, ob = res_out(ic)
                E("ve", lambda v: v.tensor_mul(dd, dd, rb), reads=(dn, f"lnr{li}"), writes=(dn,))
                E("ve", lambda v, oap=oap, rap=rap: v.tensor_add(oap, dd, rap),
                  reads=(dn,) + rrd, writes=(ob,))

    with contextlib.ExitStack() as p1:
        lt1 = [make_ln(p1), make_ln(p1)]
        xin = [sb(p1, f"xinb{i}", [128, 8 * 512]) for i in range(2)]
        lnout = [sb(p1, f"lnoutb{i}", [128, 8 * 512], BF16) for i in range(2)]
        _p1w = [0]

        def p1_pass(src_d, dst_d, total, tag0):
            for g in range((total + 511) // 512):
                c0 = min(512 * g, total - 512)
                n = 512
                sl = _p1w[0] % 2; _p1w[0] += 1
                xi = xin[sl]; lo = lnout[sl]
                for ic in range(8):
                    E("sy", lambda s, ic=ic, c0=c0, n=n, xi=xi, sl=sl: s.dma_start(
                        out=xi[:, 512 * ic:512 * ic + n],
                        in_=src_d.ap()[:, total * ic + c0: total * ic + c0 + n]),
                        reads=(f"xinw{sl}",), writes=(f"xinw{sl}",), dma=True)
                emit_ln(lt1[sl], n,
                        src=lambda ic, xi=xi, sl=sl: (xi[:, 512 * ic:512 * ic + n], (f"xinw{sl}",)),
                        dst_bf=lambda ic, lo=lo, sl=sl: (lo[:, 512 * ic:512 * ic + n], f"lno{ic}_{sl}"),
                        banks=(6, 7) if sl == 0 else (4, 5))
                for ic in range(8):
                    E("ge", lambda gg, ic=ic, c0=c0, n=n, lo=lo, sl=sl: gg.dma_start(
                        out=dst_d.ap()[:, total * ic + c0: total * ic + c0 + n],
                        in_=lo[:, 512 * ic:512 * ic + n]),
                        reads=(f"lno{ic}_{sl}",), writes=(f"{tag0}done",), dma=True)

        p1_pass(xful, ln1f_h, S, "lf")
        p1_pass(xloc, ln1o_h, NQ, "lo")
        if dbg_h is not None:
            for _s in range(2):
                E("ve", lambda v, _s=_s: v.tensor_copy(lt1[0]["tmp"][:, 1900 + 16 * _s:1916 + 16 * _s],
                  lt1[0]["sq"][:, 544 * _s:544 * _s + 16]),
                  reads=(f"lnsq{_s}_0", "lnm0"), writes=("lnm0",), inc=True)
                E("ve", lambda v, _s=_s: v.tensor_copy(lt1[0]["tmp"][:, 1932 + 16 * _s:1948 + 16 * _s],
                  lt1[0]["xbf"][:, 544 * _s:544 * _s + 16]),
                  reads=(f"lnxb{_s}_0", "lnm0"), writes=("lnm0",), inc=True)
            E("ge", lambda g: g.dma_start(out=dbg_h.ap()[:, 0:2048], in_=lt1[0]["tmp"][:, 0:2048]),
              reads=("lnm0", "lnw0", "lnr0", "lnd0"), writes=("dbgdone",), dma=True)
            E("ge", lambda g: g.dma_start(out=dbg_h.ap()[:, 2048:2064], in_=xin[0][:, 0:16]),
              reads=("xinw0",), writes=("dbgdone",), dma=True)
    sch.barrier()

    import os as _os0
    _DBG_STOP = _os0.environ.get("K_DBG_STOP") == "1"
    with contextlib.ExitStack() as pw_:
        pw = root if _DBG_STOP else pw_
        WTW = 1024
        wbuf = [sb(pw, f"wbufs{i}", [128, WTW]) for i in range(2)]
        wqb = [sb(pw, f"wqbs{i}", [128, WTW], BF16) for i in range(2)]
        wstat = sb(pw, "wstat", [128, 56])
        wtr = sb(pw, "wtr", [128, 32])
        wtrT = sb(pw, "wtrT", [32, 128])

        def quant_weights(src_d, dst_d, ncols, tag, sw_col):
            nt = ncols // WTW
            pieces = []

            def init_p():
                E("ve", lambda v: v.memset(wstat[:, 0:56], 0.0), reads=("wstat",), writes=("wstat",))
                E("ve", lambda v: v.memset(wtr[:, 0:32], 0.0), reads=("wtr",), writes=("wtr",))
            pieces.append(init_p)

            def mean_tile(it):
                sl = it % 2
                E("sy", lambda s, it=it, sl=sl: s.dma_start(out=wbuf[sl][:, :],
                  in_=src_d.ap()[:, WTW * it:WTW * (it + 1)]),
                  reads=(f"wbufs{sl}",), writes=(f"wbufs{sl}",), dma=True)
                E("ve", lambda v, it=it, sl=sl: v.tensor_reduce(wstat[:, 20 + it:21 + it], wbuf[sl][:, :],
                  AX.X, AO.add, apply_absolute_value=True),
                  reads=(f"wbufs{sl}", "wstat"), writes=("wstat",), lazy=True)
            for it in range(nt):
                pieces.append(lambda it=it: mean_tile(it))

            def finisher():
                E("ve", lambda v: v.tensor_reduce(wstat[:, 16:17], wstat[:, 20:20 + nt], AX.X, AO.add),
                  reads=("wstat",), writes=("wstat",), lazy=True)
                E("ve", lambda v: v.tensor_copy(wtr[:, 0:1], wstat[:, 16:17]),
                  reads=("wstat", "wtr"), writes=("wtr",), lazy=True)
                vguard()
                for kb_ in range(4):
                    E("ve", lambda v, kb_=kb_: v.transpose(wtrT[0:32, 32 * kb_:32 * kb_ + 32],
                      wtr[32 * kb_:32 * kb_ + 32, 0:32]),
                      reads=("wtr", "wtrT"), writes=("wtrT",), lazy=True)
                vguard()
                E("ve", lambda v: v.tensor_reduce(wstat[0:1, 17:18], wtrT[0:1, 0:128], AX.X, AO.add),
                  reads=("wtrT", "wstat"), writes=("wstat",), lazy=True)
                E("ve", lambda v: v.tensor_scalar(wstat[0:1, 17:18], wstat[0:1, 17:18], 1.0 / (ncols * 128.0), EPS,
                  AO.mult, AO.max), reads=("wstat",), writes=("wstat",), lazy=True)
                E("te", lambda t: t.matmul(PS[6][:, 0:1], onesf[0:1, :], wstat[0:1, 17:18], start=True, stop=True),
                  reads=("wstat", "onesf", "PS6"), writes=("PS6",))
                E("ve", lambda v: v.tensor_copy(swinv[:, sw_col:sw_col + 1], PS[6][:, 0:1]),
                  reads=("PS6",), writes=(f"swinv{sw_col}",), lazy=True)
                E("ve", lambda v: v.reciprocal(wstat[0:1, 18:19], wstat[0:1, 17:18]), reads=("wstat",), writes=("wstat",), lazy=True)
                E("te", lambda t: t.matmul(PS[6][:, 0:1], onesf[0:1, :], wstat[0:1, 18:19], start=True, stop=True),
                  reads=("wstat", "onesf", "PS6"), writes=("PS6",))
                E("ve", lambda v: v.tensor_copy(wstat[:, 19:20], PS[6][:, 0:1]), reads=("PS6", "wstat"), writes=("wsbc" + tag,), lazy=True)
            pieces.append(finisher)

            def quant_tile(it):
                sl = it % 2
                E("sy", lambda s, it=it, sl=sl: s.dma_start(out=wbuf[sl][:, :],
                  in_=src_d.ap()[:, WTW * it:WTW * (it + 1)]),
                  reads=(f"wbufs{sl}",), writes=(f"wbufs{sl}",), dma=True)
                E("ve", lambda v, sl=sl: v.tensor_scalar(wbuf[sl][:, :], wbuf[sl][:, :], wstat[:, 19:20], MAGIC, AO.mult, AO.add),
                  reads=(f"wbufs{sl}", "wsbc" + tag), writes=(f"wbufs{sl}",))
                E("ve", lambda v, sl=sl: v.tensor_scalar(wbuf[sl][:, :], wbuf[sl][:, :], MAGIC, 1.0, AO.subtract, AO.min),
                  reads=(f"wbufs{sl}",), writes=(f"wbufs{sl}",))
                E("ve", lambda v, sl=sl: v.tensor_scalar(wqb[sl][:, :], wbuf[sl][:, :], -1.0, None, AO.max),
                  reads=(f"wbufs{sl}", f"wqbs{sl}"), writes=(f"wqbs{sl}",))
                E("ge", lambda g, it=it, sl=sl: g.dma_start(out=dst_d.ap()[:, WTW * it:WTW * (it + 1)], in_=wqb[sl][:, :]),
                  reads=(f"wqbs{sl}",), writes=(f"{tag}done",), dma=True)
            for it in range(nt):
                pieces.append(lambda it=it: quant_tile(it))
            return pieces

        wq_pieces = quant_weights(w1t_d, w1q_h, 8 * DH, "w1", 0)
        if dbg_h is not None:
            def dbg_wq():
                E("ge", lambda g: g.dma_start(out=dbg_h.ap()[:, 2064:2112], in_=wstat[:, 0:48]),
                  reads=("wstat", "wsbcw1"), writes=("dbg2",), dma=True)
                E("ge", lambda g: g.dma_start(out=dbg_h.ap()[0:32, 2112:2240], in_=wtrT[:, :]),
                  reads=("wtrT",), writes=("dbg2",), dma=True)
            wq_pieces.append(dbg_wq)
        wq_pieces += quant_weights(w2t_d, w2q_h, 32 * D, "w2", 1)
        _wq_cursor = [0]

        def emit_wq(npiece):
            hi = min(_wq_cursor[0] + npiece, len(wq_pieces))
            for pi in range(_wq_cursor[0], hi):
                wq_pieces[pi]()
            _wq_cursor[0] = hi

        bvhl = sb(pw, "bvhl", [1, 2 * D], BF16)
        bvf = sb(pw, "bvf", [1, D])
        kaug = [sb(pw, f"kaug{h}", [KC, S], BF16) for h in range(4)]
        vqt = sb(pw, "vqt", [128, 32 * 260], BF16)
        qaug = [[sb(pw, f"qa{h}_{g}", [KC, GN if g < G8 else NBANK], BF16)
                 for g in range(G8 + 1)] for h in range(4)]
        lnfw = [sb(pw, f"lnfw{s_}", [128, 8 * 512], BF16) for s_ in range(2)]
        lnow = [sb(pw, f"lnow{s_}", [128, 8 * GN], BF16) for s_ in range(2)]
        rkw = [sb(pw, f"rkw{s_}", [128, 2 * 512]) for s_ in range(2)]
        rqw = [sb(pw, f"rqw{s_}", [128, 2 * GN]) for s_ in range(2)]
        wqkv = sb(pw, "wqkv", [128, 256])
        wqkv_b = sb(pw, "wqkv_b", [128, 8 * 256 * 3], BF16)
        xsb = sb(pw, "xsb", [128, 3 * 512])
        sqb = sb(pw, "sqb", [128, 512], BF16)
        p4t = sb(pw, "p4t", [128, 3 * 4 * GN], BF16)
        pbank = sb(pw, "pbank", [128, 4 * KBANK * NBANK], BF16)
        oraw = sb(pw, "oraw", [128, 2 * NQ])
        drawt = sb(pw, "drawt", [128, NQ])
        dinvb = sb(pw, "dinvb", [128, NQ])
        obf = sb(pw, "obf", [128, 2 * NQ], BF16)
        nstat = sb(pw, "nstat", [128, 40])
        ntr = sb(pw, "ntr", [32, 32])
        ntrT = sb(pw, "ntrT", [32, 32])
        ropeT = sb(pw, "ropeT", [128, 512]); ropeT2 = sb(pw, "ropeT2", [128, 512])
        valr = sb(pw, "valr", [1, 8])

        E("ve", lambda v: v.memset(bass.AP(vqt, 64, [[260 * 32, 128], [65, 128], [1, 1]]), 1.0),
          writes=("vq_ones",))
        E("sy", lambda s: s.dma_start(out=bvf[:, :], in_=bvr_d.ap()), writes=("bvf",), dma=True)
        E("ve", lambda v: v.tensor_copy(bvhl[0:1, 0:D], bvf[0:1, :]), reads=("bvf",), writes=("bvhl_h",))
        E("ve", lambda v: v.tensor_sub(bvhl[0:1, D:2 * D], bvf[0:1, :], bvhl[0:1, 0:D]),
          reads=("bvf", "bvhl_h"), writes=("bvhl_l",))

        def emit_rope(xs, xs_c0, ctab, cc0, stab, sc0, n, dsts, rd):
            E("ve", lambda v: v.tensor_mul(ropeT[:, 0:n], xs[:, xs_c0:xs_c0 + n], ctab[:, cc0:cc0 + n]),
              reads=rd + ("ropeT",), writes=("ropeT",))
            for b_ in (0, 64):
                E("ve", lambda v, b_=b_: v.tensor_mul(ropeT2[b_:b_ + 32, 0:n],
                  xs[b_ + 32:b_ + 64, xs_c0:xs_c0 + n], stab[b_ + 32:b_ + 64, sc0:sc0 + n]),
                  reads=rd + ("ropeT2",), writes=("ropeT2",))
                E("ve", lambda v, b_=b_: v.tensor_mul(ropeT2[b_ + 32:b_ + 64, 0:n],
                  xs[b_:b_ + 32, xs_c0:xs_c0 + n], stab[b_:b_ + 32, sc0:sc0 + n]),
                  reads=rd + ("ropeT2",), writes=("ropeT2",))
            for i, (dt_, p0, c0, dbuf) in enumerate(dsts):
                b_ = 64 * i
                E("ve", lambda v, dt_=dt_, p0=p0, c0=c0, b_=b_: v.tensor_add(
                    dt_[p0:p0 + 64, c0:c0 + n], ropeT[b_:b_ + 64, 0:n], ropeT2[b_:b_ + 64, 0:n]),
                    reads=("ropeT", "ropeT2", dbuf), writes=(dbuf,))

        import os as _os
        _QORD = [int(c) for c in _os.environ.get("K_QORDER", "0123")]
        _QPH = int(_os.environ.get("K_QPHASES", "9"))
        _QW = _os.environ.get("K_QW", "1") == "1"
        _Q64 = _os.environ.get("K_Q64", "1") == "1"
        _QMS = _os.environ.get("K_QMS", "1") == "1"
        for _Qi, Q in enumerate(_QORD):
            emit_wq(34 if _Qi < 3 else len(wq_pieces))
            for ic in (range(8) if (_Qi == 0 or _QW) else ()):
                for wi, wd in enumerate((wq_d, wk_d, wv_d)):
                    E("sy", lambda s, ic=ic, wd=wd, Q=Q: s.dma_start(
                        out=wqkv[:, 0:256],
                        in_=wd.ap()[:, D * ic + 256 * Q: D * ic + 256 * Q + 256]),
                        reads=("wqkv",), writes=("wqkv",), dma=True)
                    E("ve", lambda v, ic=ic, wi=wi: v.tensor_copy(
                        wqkv_b[:, 768 * ic + 256 * wi: 768 * ic + 256 * wi + 256], wqkv[:, 0:256]),
                        reads=("wqkv", f"wb{wi}_{ic}"), writes=(f"wb{wi}_{ic}",))
            WBap = lambda wi, ic, j, w=128: wqkv_b[:, 768 * ic + 256 * wi + 128 * j: 768 * ic + 256 * wi + 128 * j + w]
            WBrd = lambda wi, ic: (f"wb{wi}_{ic}",)

            if _Qi == 0:
                for h in range(4):
                    E("ve", lambda v, h=h: v.memset(kaug[h][64:96, :], 0.0),
                      reads=(f"ka{h}", f"kc{h}"), writes=(f"kc{h}",))
                    E("sy", lambda s, h=h: s.dma_start(out=kaug[h][KB0:KB0 + NKILL, :], in_=wkill_d.ap()),
                      reads=(f"kkill{h}",), writes=(f"kkill{h}",), dma=True)
            if _Qi == 0 or _QMS:
                E("ve", lambda v: v.memset(nstat[:, 0:40], 0.0), reads=("nstat",), writes=("nstat",))
                E("ve", lambda v: v.memset(ntr[:, :], 0.0), reads=("ntr",), writes=("ntr",))
                E("ve", lambda v: v.memset(drawt[:, :], 1.0), reads=("draw",), writes=("draw",))

            for tg in (range(8) if (_Qi == 0 or _QPH >= 1) else ()):
                sl = tg % 2
                lf = lnfw[sl]
                for ic in range(8):
                    E("sy", lambda s, ic=ic, tg=tg, lf=lf: s.dma_start(
                        out=lf[:, 512 * ic:512 * (ic + 1)],
                        in_=ln1f_h.ap()[:, S * ic + 512 * tg: S * ic + 512 * (tg + 1)]),
                        reads=("lfdone", f"lnfw{sl}"), writes=(f"lnfw{sl}",), dma=True)
                rk = rkw[sl]
                E("sy", lambda s, tg=tg, rk=rk: s.dma_start(out=rk[:, 0:512],
                  in_=c2k_d.ap()[:, 512 * tg:512 * (tg + 1)]), reads=(f"rkw{sl}",), writes=(f"rkw{sl}",), dma=True)
                E("sy", lambda s, tg=tg, rk=rk: s.dma_start(out=rk[:, 512:1024],
                  in_=nsk_d.ap()[:, 512 * tg:512 * (tg + 1)]), reads=(f"rkw{sl}",), writes=(f"rkw{sl}",), dma=True)
                for j in range(2):
                    pj = PS[4 + sl]; pb = PSN[4 + sl]
                    for ic in range(8):
                        E("te", lambda t, ic=ic, j=j, lf=lf, pj=pj: t.matmul(
                            pj[:, 0:512], WBap(1, ic, j), lf[:, 512 * ic:512 * (ic + 1)],
                            start=(ic == 0), stop=(ic == 7)),
                            reads=(f"lnfw{sl}",) + WBrd(1, ic),
                            writes=(pb,) if ic in (0, 7) else (), inc=(ic == 7))
                    xcol = 512 * sl
                    E("se", lambda s, pj=pj, j=j, xcol=xcol, Q=Q: s.activation(
                        xsb[:, xcol:xcol + 512], pj[:, 0:512], AF.Identity,
                        bias=biases[:, 8 + 2 * Q + j:9 + 2 * Q + j]),
                        reads=(pb, "bias_k", f"xsb{sl}"), writes=(f"xsb{sl}",))
                    E("ve", lambda v, xcol=xcol: v.tensor_mul(sqb[:, :], xsb[:, xcol:xcol + 512],
                      xsb[:, xcol:xcol + 512]), reads=(f"xsb{sl}", "sqb"), writes=("sqb",))
                    E("te", lambda t, j=j: t.matmul(PS[6][0:4, 0:512], hs4[j][:, :], sqb[:, :],
                      start=True, stop=True), reads=("sqb", f"hs4{j}", "PS6"), writes=("PS6",))
                    E("ve", lambda v: v.tensor_reduce(nstat[0:4, 4:5], PS[6][0:4, 0:512], AX.X, AO.max),
                      reads=("PS6", "nstat"), writes=("nstat",), lazy=True)
                    E("ve", lambda v: v.tensor_max(nstat[0:4, 0:1], nstat[0:4, 0:1], nstat[0:4, 4:5]),
                      reads=("nstat",), writes=("nstat",), lazy=True)
                    emit_rope(xsb, xcol, rk, 0, rk, 512, 512,
                              [(kaug[2 * j], 0, 512 * tg, f"ka{2 * j}"),
                               (kaug[2 * j + 1], 0, 512 * tg, f"ka{2 * j + 1}")],
                              (f"xsb{sl}", f"rkw{sl}"))
                for q4 in range(4):
                    tt = 4 * tg + q4
                    pv_ = PS[4 + (1 - sl)]; pvb = PSN[4 + (1 - sl)]
                    for ic in range(8):
                        E("te", lambda t, ic=ic, q4=q4, lf=lf, pv_=pv_: t.matmul(
                            pv_[:, 0:256], lf[:, 512 * ic + 128 * q4:512 * ic + 128 * q4 + 128],
                            WBap(2, ic, 0, 256), start=(ic == 0), stop=False),
                            reads=(f"lnfw{sl}",) + WBrd(2, ic),
                            writes=(pvb,) if ic == 0 else (), inc=False)
                    E("te", lambda t, pv_=pv_, Q=Q: t.matmul(pv_[:, 0:256], onesbf[0:1, :],
                      bvhl[0:1, 256 * Q:256 * Q + 256], start=False, stop=False),
                      reads=("bvhl_h", "onesbf"), inc=False)
                    E("te", lambda t, pv_=pv_, Q=Q: t.matmul(pv_[:, 0:256], onesbf[0:1, :],
                      bvhl[0:1, D + 256 * Q:D + 256 * Q + 256], start=False, stop=True),
                      reads=("bvhl_l", "onesbf"), writes=(pvb,))
                    E("se", lambda s, tt=tt, pv_=pv_: s.copy(
                        bass.AP(vqt, 260 * tt, [[260 * 32, 128], [65, 4], [1, 64]]), pv_[:, 0:256]),
                        reads=(pvb, "vq_ones", f"vq{tt}"), writes=(f"vq{tt}",))

            for g in (range(G8 + 1) if (_Qi == 0 or _QPH >= 2) else ()):
                n = GN if g < G8 else NBANK
                sl = g % 2
                lo = lnow[sl]
                for ic in range(8):
                    E("sy", lambda s, ic=ic, g=g, n=n, lo=lo: s.dma_start(
                        out=lo[:, GN * ic:GN * ic + n],
                        in_=ln1o_h.ap()[:, NQ * ic + GN * g: NQ * ic + GN * g + n]),
                        reads=("lodone", f"lnow{sl}"), writes=(f"lnow{sl}",), dma=True)
                rq = rqw[sl]
                E("sy", lambda s, g=g, n=n, rq=rq: s.dma_start(out=rq[:, 0:n],
                  in_=c2q_d.ap()[:, GN * g:GN * g + n]), reads=(f"rqw{sl}",), writes=(f"rqw{sl}",), dma=True)
                E("sy", lambda s, g=g, n=n, rq=rq: s.dma_start(out=rq[:, GN:GN + n],
                  in_=nsq_d.ap()[:, GN * g:GN * g + n]), reads=(f"rqw{sl}",), writes=(f"rqw{sl}",), dma=True)
                for j in range(2):
                    pj = PS[4 + sl]; pb = PSN[4 + sl]
                    for ic in range(8):
                        E("te", lambda t, ic=ic, j=j, n=n, lo=lo, pj=pj: t.matmul(
                            pj[:, 0:n], WBap(0, ic, j), lo[:, GN * ic:GN * ic + n],
                            start=(ic == 0), stop=(ic == 7)),
                            reads=(f"lnow{sl}",) + WBrd(0, ic),
                            writes=(pb,) if ic in (0, 7) else (), inc=(ic == 7))
                    E("se", lambda s, pj=pj, j=j, n=n, Q=Q: s.activation(xsb[:, 1024:1024 + n], pj[:, 0:n], AF.Identity,
                      bias=biases[:, 2 * Q + j:2 * Q + j + 1]),
                      reads=(pb, "bias_q", "xsbq"), writes=("xsbq",))
                    E("ve", lambda v, n=n: v.tensor_mul(sqb[:, 0:n], xsb[:, 1024:1024 + n], xsb[:, 1024:1024 + n]),
                      reads=("xsbq", "sqb"), writes=("sqb",))
                    E("te", lambda t, j=j, n=n: t.matmul(PS[6][0:4, 0:n], hs4[j][:, :], sqb[:, 0:n],
                      start=True, stop=True), reads=("sqb", f"hs4{j}", "PS6"), writes=("PS6",))
                    E("ve", lambda v, n=n: v.tensor_reduce(nstat[0:4, 5:6], PS[6][0:4, 0:n], AX.X, AO.max),
                      reads=("PS6", "nstat"), writes=("nstat",), lazy=True)
                    E("ve", lambda v: v.tensor_max(nstat[0:4, 1:2], nstat[0:4, 1:2], nstat[0:4, 5:6]),
                      reads=("nstat",), writes=("nstat",), lazy=True)
                    emit_rope(xsb, 1024, rq, 0, rq, GN, n,
                              [(qaug[2 * j][g], 0, 0, f"qa{2 * j}_{g}"),
                               (qaug[2 * j + 1][g], 0, 0, f"qa{2 * j + 1}_{g}")],
                              ("xsbq", f"rqw{sl}"))
            if _Qi == 0:
                for h in range(4):
                    for g in range(G8 + 1):
                        n = GN if g < G8 else NBANK
                        qa = qaug[h][g]
                        E("ve", lambda v, qa=qa, n=n: v.memset(qa[64:96, 0:n], 0.0), writes=(f"qc1_{h}_{g}",))
                        E("ve", lambda v, qa=qa, n=n: v.memset(qa[64:65, 0:n], 1.0),
                          reads=(f"qc1_{h}_{g}",), writes=(f"qc1_{h}_{g}",))
                        E("sy", lambda s, qa=qa, g=g, n=n: s.dma_start(
                            out=qa[KB0:KB0 + NKILL, 0:n],
                            in_=qkill_d.ap()[:, GN * g:GN * g + n]),
                            writes=(f"qc2_{h}_{g}",), dma=True)

            if _Qi > 0 and not _Q64:
                continue
            E("ve", lambda v: v.tensor_add(nstat[0:4, 6:7], nstat[0:4, 0:1], nstat[0:4, 1:2]),
              reads=("nstat",), writes=("nstat",), lazy=True)
            E("ve", lambda v: v.tensor_scalar(nstat[0:4, 6:7], nstat[0:4, 6:7], -1.0 / 16.0, None, AO.mult),
              reads=("nstat",), writes=("nstat",), lazy=True)
            E("ve", lambda v: v.tensor_copy(ntr[0:32, 0:1], nstat[0:32, 6:7]),
              reads=("nstat", "ntr"), writes=("ntr",), lazy=True)
            vguard()
            E("ve", lambda v: v.transpose(ntrT[0:32, 0:32], ntr[0:32, 0:32]),
              reads=("ntr", "ntrT"), writes=("ntrT",), lazy=True)
            vguard()
            E("ve", lambda v: v.tensor_copy(valr[0:1, 0:4], ntrT[0:1, 0:4]), reads=("ntrT", "valr"), writes=("valr",), lazy=True)
            for h in range(4):
                for kc8 in range(8):
                    E("ve", lambda v, h=h, kc8=kc8: v.tensor_scalar(
                        kaug[h][64:65, 512 * kc8:512 * (kc8 + 1)], onesw[0:1, :],
                        valr[0:1, h:h + 1], None, AO.mult),
                        reads=("valr", "onesw", f"kc{h}"), writes=(f"kc{h}",))

            for g in (range(G8 + 1) if (_Qi == 0 or _QPH >= 3) else ()):
                n = GN if g < G8 else NBANK
                K = KSLOT[g] if g < G8 else KBANK
                is_bank = (g == G8)

                def emit_score(i, g=g, n=n, is_bank=is_bank):
                    rot = i % 3
                    bs = 2 * (i % 2)
                    for h in range(4):
                        psc = PS[(0 if bs == 0 else 4) + h // 2]
                        pscb = PSN[(0 if bs == 0 else 4) + h // 2]
                        coff = 256 * (h % 2)
                        oap = psc[:, coff:coff + n]
                        kap = kaug[h][0:KC, 128 * i:128 * (i + 1)]
                        qap = qaug[h][g][0:KC, 0:n]
                        E("te", lambda t, oap=oap, kap=kap, qap=qap: t.matmul(
                            oap, kap, qap, start=True, stop=True),
                            reads=(f"ka{h}", f"kc{h}", f"kkill{h}", f"qa{h}_{g}",
                                   f"qc1_{h}_{g}", f"qc2_{h}_{g}", pscb),
                            writes=(pscb,))
                    if not is_bank:
                        for hp in (0, 2):
                            psc = PS[(0 if bs == 0 else 4) + hp // 2]
                            pscb = PSN[(0 if bs == 0 else 4) + hp // 2]
                            iap = psc[:, 0:2 * n]
                            pcol = (4 * rot + hp) * GN
                            oap = p4t[:, pcol:pcol + 2 * n]
                            E("se", lambda s, oap=oap, iap=iap: s.activation(oap, iap, AF.Exp),
                              reads=(pscb, f"p4_{rot}_{hp}", f"p4_{rot}_{hp + 1}"),
                              writes=(f"p4_{rot}_{hp}", f"p4_{rot}_{hp + 1}"))
                    else:
                        for h in range(4):
                            psc = PS[(0 if bs == 0 else 4) + h // 2]
                            pscb = PSN[(0 if bs == 0 else 4) + h // 2]
                            coff = 256 * (h % 2)
                            iap = psc[:, coff:coff + n]
                            pcol = h * KBANK * NBANK + i * NBANK
                            oap = pbank[:, pcol:pcol + n]
                            E("se", lambda s, oap=oap, iap=iap: s.activation(oap, iap, AF.Exp),
                              reads=(pscb, f"pbk{h}"), writes=(f"pbk{h}",))
                    if not is_bank:
                        for dpos, mask, mbuf in (((4 * g, 4 * g + 1), m1_s, "m1_s"),
                                                 ((4 * g + 2, 4 * g + 3), m2_s, "m2_s")):
                            for l in range(2):
                                if i == dpos[l]:
                                    pap = bass.AP(p4t, (4 * rot) * GN + 128 * l,
                                                  [[3 * 4 * GN, 128], [GN, 4], [1, 128]])
                                    map_ = bass.AP(mask, 0, [[512, 128], [128, 4], [1, 128]])
                                    E("ve", lambda v, pap=pap, map_=map_: v.tensor_tensor(
                                        pap, pap, map_, AO.mult),
                                        reads=(f"p4_{rot}_0", f"p4_{rot}_1", f"p4_{rot}_2", f"p4_{rot}_3", mbuf),
                                        writes=(f"p4_{rot}_0", f"p4_{rot}_1", f"p4_{rot}_2", f"p4_{rot}_3"))

                def emit_pv(i, g=g, n=n, K=K, is_bank=is_bank):
                    rot = i % 3
                    for h in range(4):
                        obank = (2, 3, 6, 7)[h]
                        po = PS[obank]; pob = PSN[obank]
                        coff = 0
                        if not is_bank:
                            pcol = (4 * rot + h) * GN
                            prd = f"p4_{rot}_{h}"
                            src = p4t
                        else:
                            pcol = h * KBANK * NBANK + i * NBANK
                            prd = f"pbm{h}"
                            src = pbank
                        oap = po[0:65, coff:coff + n]
                        vap = bass.AP(vqt, 260 * i + 65 * h, [[260 * 32, 128], [1, 65]])
                        pap = src[:, pcol:pcol + n]
                        E("te", lambda t, oap=oap, vap=vap, pap=pap, i=i, K=K: t.matmul(
                            oap, vap, pap, start=(i == 0), stop=(i == K - 1)),
                            reads=(f"vq{i}", prd, "vq_ones"),
                            writes=(pob,) if i in (0, K - 1) else (), inc=(i == K - 1))

                if not is_bank:
                    for i in range(K):
                        emit_score(i)
                        if i >= 1:
                            emit_pv(i - 1)
                    emit_pv(K - 1)
                else:
                    for i in range(K):
                        emit_score(i)
                    for h in range(4):
                        E("ve", lambda v, h=h: v.tensor_tensor(
                            pbank[:, h * KBANK * NBANK:(h + 1) * KBANK * NBANK],
                            pbank[:, h * KBANK * NBANK:(h + 1) * KBANK * NBANK],
                            bmask_s[:, :], AO.mult),
                            reads=(f"pbk{h}", "bmask_s"), writes=(f"pbm{h}", f"pbk{h}"))
                    for i in range(K):
                        emit_pv(i)

                gc0 = GN * g
                for h in range(4):
                    obank = (2, 3, 6, 7)[h]
                    po = PS[obank]; pob = PSN[obank]
                    coff = 0
                    E("se", lambda s, h=h, po=po, coff=coff, gc0=gc0, n=n: s.copy(
                        oraw[64 * (h % 2):64 * (h % 2) + 64, NQ * (h // 2) + gc0: NQ * (h // 2) + gc0 + n],
                        po[0:64, coff:coff + n]), reads=(pob, "oraw"), writes=("oraw",))
                    E("se", lambda s, h=h, po=po, coff=coff, gc0=gc0, n=n: s.copy(
                        drawt[32 * h:32 * h + 1, gc0:gc0 + n], po[64:65, coff:coff + n]),
                        reads=(pob, "draw"), writes=("draw",))
                if dbg2_h is not None and is_bank:
                    E("ge", lambda g_: g_.dma_start(out=dbg2_h.ap()[:, 0:4 * KBANK * NBANK],
                      in_=pbank[:, :]), reads=tuple(f"pbm{h}" for h in range(4)),
                      writes=("dbg4",), dma=True)
                    E("ge", lambda g_: g_.dma_start(
                      out=dbg2_h.ap()[:, 4 * KBANK * NBANK:4 * KBANK * NBANK + 16],
                      in_=drawt[:, 2048:2064]), reads=("draw",), writes=("dbg4",), dma=True)
                    E("ge", lambda g_: g_.dma_start(
                      out=dbg2_h.ap()[:, 4 * KBANK * NBANK + 16:4 * KBANK * NBANK + 32],
                      in_=oraw[:, 2048:2064]), reads=("oraw",), writes=("dbg4",), dma=True)
                    E("ge", lambda g_: g_.dma_start(
                      out=dbg2_h.ap()[:, 4 * KBANK * NBANK + 32:4 * KBANK * NBANK + 48],
                      in_=oraw[:, NQ + 2048:NQ + 2064]), reads=("oraw",), writes=("dbg4",), dma=True)

            if _Qi > 0 and _QPH < 4:
                continue
            E("ve", lambda v: v.reciprocal(dinvb[:, :], drawt[:, :]), reads=("draw", "dinv"), writes=("dinv",), lazy=True)
            for jc in range(2):
                for t5 in range(5):
                    c5 = min(512 * t5, NQ - 512)
                    n5 = 512
                    E("te", lambda t, jc=jc, c5=c5, n5=n5: t.matmul(PS[4][:, 0:n5], selq[jc][:, :],
                      dinvb[:, c5:c5 + n5], start=True, stop=True),
                      reads=("dinv", f"selq{jc}", "PS4"), writes=("PS4",))
                    E("ve", lambda v, c5=c5, n5=n5: v.tensor_copy(drawt[:, c5:c5 + n5], PS[4][:, 0:n5]),
                      reads=("PS4", "draw"), writes=("draw2",))
                    E("ve", lambda v, jc=jc, c5=c5, n5=n5: v.tensor_mul(
                        obf[:, NQ * jc + c5: NQ * jc + c5 + n5],
                        oraw[:, NQ * jc + c5: NQ * jc + c5 + n5], drawt[:, c5:c5 + n5]),
                        reads=("draw2", "oraw", f"obf{jc}_{t5}"), writes=(f"obf{jc}_{t5}",))
                E("ge", lambda g_, jc=jc, Q=Q: g_.dma_start(
                    out=obf_h.ap()[:, NQ * (2 * Q + jc): NQ * (2 * Q + jc + 1)],
                    in_=obf[:, NQ * jc: NQ * (jc + 1)]),
                    reads=tuple(f"obf{jc}_{t5}" for t5 in range(5)), writes=(f"obfh{2 * Q + jc}",), dma=True)

    emit_wq(len(wq_pieces))
    sch.barrier()
    if _DBG_STOP:
        sch.ops["ge"].append(lambda h: h.wait_ge(SEMS["D:obfh1"], 16))
        return _finish(nc, sch, root)
    with contextlib.ExitStack() as p3:
        lt3 = [make_ln(p3), make_ln(p3)]
        wo_b = sb(p3, "wo_b", [128, 8 * D], BF16)
        wo_f = sb(p3, "wo_f", [128, 2048])
        obw = [sb(p3, f"obwb{i}", [128, 8 * 512], BF16) for i in range(2)]
        zbuf = [sb(p3, f"zbufb{i}", [128, 8 * 512]) for i in range(2)]
        xlw = [sb(p3, f"xlwb{i}", [128, 8 * 512]) for i in range(2)]
        xaw = [sb(p3, f"xawb{i}", [128, 8 * 512]) for i in range(2)]
        for half in range(4):
            E("sy", lambda s, half=half: s.dma_start(out=wo_f[:, :],
              in_=wo_d.ap()[:, 2048 * half:2048 * (half + 1)]), reads=("wo_f",), writes=("wo_f",), dma=True)
            E("ve", lambda v, half=half: v.tensor_copy(wo_b[:, 2048 * half:2048 * (half + 1)], wo_f[:, :]),
              reads=("wo_f",), writes=(f"wo_b{half}",))
        WOrd = tuple(f"wo_b{half}" for half in range(4))
        for g in range(5):
            n = 512
            c0 = min(512 * g, NQ - 512)
            sl = g % 2
            ob_ = obw[sl]; zb_ = zbuf[sl]; xl_ = xlw[sl]; xa_ = xaw[sl]
            mmb = 0 if sl == 0 else 4
            for ic in range(8):
                E("sy", lambda s, ic=ic, c0=c0, n=n, ob_=ob_, sl=sl: s.dma_start(
                  out=ob_[:, 512 * ic:512 * ic + n],
                  in_=obf_h.ap()[:, NQ * ic + c0: NQ * ic + c0 + n]),
                  reads=(f"obfh{ic}", f"obw{sl}"), writes=(f"obw{sl}",), dma=True)
                E("sy", lambda s, ic=ic, c0=c0, n=n, xl_=xl_, sl=sl: s.dma_start(
                  out=xl_[:, 512 * ic:512 * ic + n],
                  in_=xloc.ap()[:, NQ * ic + c0: NQ * ic + c0 + n]),
                  reads=("xlwf",), writes=("xlwf",), dma=True)
            for oc in range(8):
                pj = PS[mmb + oc % 2]; pb = PSN[mmb + oc % 2]
                for ic in range(8):
                    E("te", lambda t, oc=oc, ic=ic, n=n, pj=pj, ob_=ob_, sl=sl: t.matmul(
                        pj[:, 0:n], wo_b[:, D * ic + 128 * oc: D * ic + 128 * oc + 128],
                        ob_[:, 512 * ic:512 * ic + n], start=(ic == 0), stop=(ic == 7)),
                        reads=(f"obw{sl}",) + WOrd,
                        writes=(pb,) if ic in (0, 7) else (), inc=(ic == 7))
                E("se", lambda s, oc=oc, n=n, pj=pj, zb_=zb_, sl=sl: s.activation(
                  zb_[:, 512 * oc:512 * oc + n],
                  pj[:, 0:n], AF.Identity, bias=biases[:, 16 + oc:17 + oc]),
                  reads=(pb, "bias_o", f"zbuf{sl}"), writes=(f"zbuf{sl}",))
            emit_ln(lt3[sl], n,
                    src=lambda ic, zb_=zb_, sl=sl: (zb_[:, 512 * ic:512 * ic + n], (f"zbuf{sl}",)),
                    resid=lambda ic, xl_=xl_, sl=sl: (xl_[:, 512 * ic:512 * ic + n], ("xlwf",)),
                    res_out=lambda ic, xa_=xa_, sl=sl: (xa_[:, 512 * ic:512 * ic + n], f"xaw{ic}_{sl}"),
                    banks=(2, 3) if sl == 0 else (6, 7))
            for ic in range(8):
                E("ge", lambda g_, ic=ic, c0=c0, n=n, xa_=xa_, sl=sl: g_.dma_start(
                    out=xatt_h.ap()[:, NQ * ic + c0: NQ * ic + c0 + n],
                    in_=xa_[:, 512 * ic:512 * ic + n]),
                    reads=(f"xaw{ic}_{sl}",), writes=("xattdone",), dma=True)

    sch.barrier()
    with contextlib.ExitStack() as p4s:
        lt4 = make_ln(p4s)
        hslab = sb(p4s, "hslab", [128, 32 * SW])
        xattw = sb(p4s, "xattw", [128, 8 * SLABW])
        ln2f = sb(p4s, "ln2f", [128, 8 * SLABW])
        xqb = sb(p4s, "xqb", [128, 8 * SLABW], BF16)
        w1w = [sb(p4s, f"w1w{i}", [128, 8 * 128], BF16) for i in range(2)]
        w2w = [sb(p4s, f"w2w{i}", [128, 1024], BF16) for i in range(2)]
        tmpc = sb(p4s, "tmpc", [128, FSLAB + 2])
        fcols = sb(p4s, "fcols", [128, 32])
        amtr = sb(p4s, "amtr", [128, 544])
        amtc = sb(p4s, "amtc", [128, 544])
        st32 = sb(p4s, "st32", [32, 544])
        amT = sb(p4s, "amT", [32, 544])
        amC = sb(p4s, "amC", [32, 544])
        srow = sb(p4s, "srow", [1, 1100])
        sbc = sb(p4s, "sbc", [128, SLABW]); sxinv = sb(p4s, "sxinv", [128, SLABW])
        s2bc = sb(p4s, "s2bc", [128, FSLAB + 2]); s2inv = sb(p4s, "s2inv", [128, FSLAB + 2])
        h2qb = [sb(p4s, f"h2qb{i}", [128, 512], BF16) for i in range(2)]
        zb2 = sb(p4s, "zb2", [128, 8 * FSLAB])
        ybuf = sb(p4s, "ybuf", [128, 8 * FSLAB])
        E("ve", lambda v: v.memset(amC[:, :], 0.0), writes=("amC",))
        E("ve", lambda v: v.memset(amT[:, :], 0.0), writes=("amT",))
        E("ve", lambda v: v.memset(amtr[:, :], 0.0), writes=("amtr",))

        def absmax_rows(tag, src_fn, nsrc, n, dst_c0):
            for i in range(nsrc):
                api, rdi = src_fn(i)
                if i == 0:
                    E("se", lambda s, api=api: s.activation(amtr[:, 0:n], api, AF.Abs),
                      reads=rdi + ("amtr",), writes=("amtr",))
                else:
                    E("se", lambda s, api=api: s.activation(amtc[:, 0:n], api, AF.Abs),
                      reads=rdi + ("amtc",), writes=("amtc",))
                    E("ve", lambda v: v.tensor_max(amtr[:, 0:n], amtr[:, 0:n], amtc[:, 0:n]),
                      reads=("amtc", "amtr"), writes=("amtr",))
            E("ve", lambda v: v.tensor_copy(amtc[0:64, 0:n], amtr[64:128, 0:n]),
              reads=("amtr", "amtc"), writes=("amtc",))
            E("ve", lambda v: v.tensor_max(amtr[0:64, 0:n], amtr[0:64, 0:n], amtc[0:64, 0:n]),
              reads=("amtc", "amtr"), writes=("amtr",))
            E("ve", lambda v: v.tensor_copy(amtc[0:32, 0:n], amtr[32:64, 0:n]),
              reads=("amtr", "amtc"), writes=("amtc",))
            E("ve", lambda v: v.tensor_max(amtr[0:32, 0:n], amtr[0:32, 0:n], amtc[0:32, 0:n]),
              reads=("amtc", "amtr"), writes=("amtr",))
            nb = (n + 31) // 32
            vguard()
            for j in range(nb):
                E("ve", lambda v, j=j: v.transpose(amT[0:32, 32 * j:32 * j + 32],
                  amtr[0:32, 32 * j:32 * j + 32]), reads=("amtr", "amT"), writes=("amT",))
            vguard()
            for j in range(nb):
                E("ve", lambda v, j=j: v.tensor_reduce(amC[0:32, 32 * j:32 * j + 1],
                  amT[0:32, 32 * j:32 * j + 32], AX.X, AO.max), reads=("amT", "amC"), writes=("amC",))
            vguard()
            for j in range(nb):
                E("ve", lambda v, j=j: v.transpose(st32[0:32, 32 * j:32 * j + 32],
                  amC[0:32, 32 * j:32 * j + 32]), reads=("amC", "st32"), writes=("st32",))
            vguard()

        def bcast_row(src_c0, n, dst, dstbuf):
            for it in range((n + 511) // 512):
                c0 = min(512 * it, max(0, n - 512))
                w = min(512, n)
                E("te", lambda t, c0=c0, w=w: t.matmul(PS[5][:, 0:w], onesf[0:1, :],
                  srow[0:1, src_c0 + c0: src_c0 + c0 + w], start=True, stop=True),
                  reads=("srow", "onesf", "PS5"), writes=("PS5",))
                E("ve", lambda v, c0=c0, w=w: v.tensor_copy(dst[:, c0:c0 + w], PS[5][:, 0:w]),
                  reads=("PS5", dstbuf), writes=(dstbuf,))

        def quant_cols(srcT, src_rd_w, n, sb_, sbuf_n, dstT, dst_w, stride):
            for ic in range(8):
                E("ve", lambda v, ic=ic: v.tensor_tensor(srcT[:, stride * ic:stride * ic + n],
                  srcT[:, stride * ic:stride * ic + n], sb_[:, 0:n], AO.mult),
                  reads=(src_rd_w(ic), sbuf_n), writes=(src_rd_w(ic),))
                E("ve", lambda v, ic=ic: v.tensor_scalar(dstT[:, stride * ic:stride * ic + n],
                  srcT[:, stride * ic:stride * ic + n], MAGIC, MAGIC, AO.add, AO.subtract),
                  reads=(src_rd_w(ic), dst_w(ic)), writes=(dst_w(ic),))

        for sl_i in range(NSLAB):
            t0 = FSLAB * sl_i
            for ic in range(8):
                for (wo, so, nn) in ((1, t0, 256), (259, t0 + 256, 256)):
                    E("sy", lambda s, ic=ic, wo=wo, so=so, nn=nn: s.dma_start(
                        out=xattw[:, SLABW * ic + wo: SLABW * ic + wo + nn],
                        in_=xatt_h.ap()[:, NQ * ic + so: NQ * ic + so + nn]),
                        reads=("xattdone", "xattw"), writes=("xattw",), dma=True)
                E("sy", lambda s, ic=ic, sl_i=sl_i: s.dma_start(
                    out=fcols[:, 4 * ic: 4 * ic + 4],
                    in_=xatt_h.ap()[:, NQ * ic + NOWN + 4 * sl_i: NQ * ic + NOWN + 4 * sl_i + 4]),
                    reads=("xattdone", "fcols"), writes=("fcols",), dma=True)
            for (wo, fo, nn) in ((0, 0, 1), (257, 1, 2), (515, 3, 1)):
                E("ve", lambda v, wo=wo, fo=fo, nn=nn: v.tensor_copy(
                    bass.AP(xattw, wo, [[8 * SLABW, 128], [SLABW, 8], [1, nn]]),
                    bass.AP(fcols, fo, [[32, 128], [4, 8], [1, nn]])),
                    reads=("fcols", "xattw"), writes=("xattw",), lazy=True)
            for (lc0, lw) in ((0, 512), (SLABW - 512, 512)):
                emit_ln(lt4, lw,
                        src=lambda ic, lc0=lc0, lw=lw: (
                            xattw[:, SLABW * ic + lc0:SLABW * ic + lc0 + lw], ("xattw",)),
                        dst_f=lambda ic, lc0=lc0, lw=lw: (
                            ln2f[:, SLABW * ic + lc0:SLABW * ic + lc0 + lw], f"ln2f{ic}"))
            LN2RD = tuple(f"ln2f{ic}" for ic in range(8))
            absmax_rows("xq", lambda ic: (ln2f[:, SLABW * ic:SLABW * (ic + 1)], LN2RD), 8, SLABW, 0)
            E("ve", lambda v: v.tensor_scalar(srow[0:1, 550:550 + SLABW], st32[0:1, 0:SLABW], EPS, None, AO.max),
              reads=("srow", "st32"), writes=("srow",))
            E("ve", lambda v: v.reciprocal(srow[0:1, 0:SLABW], srow[0:1, 550:550 + SLABW]),
              reads=("srow",), writes=("srow",), lazy=True)
            E("ve", lambda v: v.tensor_scalar(srow[0:1, 0:SLABW], srow[0:1, 0:SLABW], 127.0, None, AO.mult),
              reads=("srow",), writes=("srow",))
            E("ve", lambda v: v.tensor_scalar(srow[0:1, 550:550 + SLABW], srow[0:1, 550:550 + SLABW],
              1.0 / 127.0, None, AO.mult), reads=("srow",), writes=("srow",))
            bcast_row(0, SLABW, sbc, "sbc")
            bcast_row(550, SLABW, sxinv, "sxinv")
            quant_cols(ln2f, lambda ic: f"ln2f{ic}", SLABW, sbc[:, 0:SLABW], "sbc",
                       xqb, lambda ic: f"xqb{ic}", SLABW)
            XQRD = tuple(f"xqb{ic}" for ic in range(8))
            for ic in range(8):
                E("sy", lambda s, ic=ic, sl_i=sl_i: s.dma_start(
                    out=ln2f[:, SLABW * ic: SLABW * ic + FSLAB],
                    in_=xatt_h.ap()[:, NQ * ic + FSLAB * sl_i: NQ * ic + FSLAB * (sl_i + 1)]),
                    reads=("xattdone", f"ln2f{ic}"), writes=(f"ln2f{ic}",), dma=True)
            for oc in range(32):
                wsl = oc % 2
                E("sy", lambda s, oc=oc, wsl=wsl: s.dma_start(out=w1w[wsl][:, :],
                  in_=bass.AP(w1q_h, 128 * oc, [[8 * DH, 128], [DH, 8], [1, 128]])),
                  reads=("w1done", f"w1w{wsl}"), writes=(f"w1w{wsl}",), dma=True)
                for (c0, nn) in ((0, 512), (SLABW - 128, 128)):
                    pj = PS[4 + oc % 2]; pb = PSN[4 + oc % 2]
                    for ic in range(8):
                        E("te", lambda t, ic=ic, c0=c0, nn=nn, wsl=wsl, pj=pj: t.matmul(
                            pj[:, 0:nn], w1w[wsl][:, 128 * ic:128 * (ic + 1)],
                            xqb[:, SLABW * ic + c0: SLABW * ic + c0 + nn],
                            start=(ic == 0), stop=(ic == 7)),
                            reads=XQRD + (f"w1w{wsl}",),
                            writes=(pb,) if ic in (0, 7) else (), inc=(ic == 7))
                    E("ve", lambda v, oc=oc, c0=c0, nn=nn, pj=pj: v.scalar_tensor_tensor(
                        hslab[:, SW * oc + c0: SW * oc + c0 + nn], pj[:, 0:nn], swinv[:, 0:1],
                        sxinv[:, c0:c0 + nn], AO.mult, AO.mult),
                        reads=(pb, "swinv0", "sxinv", f"hs{oc}", f"h2_{oc}"), writes=(f"hs{oc}", f"h2_{oc}"))
            HSRD = tuple(f"hs{oc}" for oc in range(32))
            if dbg_h is not None and sl_i == 0:
                E("ge", lambda g_: g_.dma_start(out=dbg_h.ap()[:, 1548:1558], in_=hslab[:, 506:516]),
                  reads=HSRD, writes=("dbg3",), dma=True)
            for (pos, extc) in ((0, 0), (3, SLABW - 1)):
                slot = 4 * sl_i + pos
                if slot not in (0, 15):
                    continue
                hcols = bass.AP(hslab, extc, [[32 * SW, 128], [SW, 32], [1, 1]])
                E("ve", lambda v, hcols=hcols, slot=slot: v.tensor_scalar(
                    hcols, hcols, cfil_s[:, slot:slot + 1], None, AO.mult),
                    reads=("cfil_s",) + HSRD, writes=HSRD, lazy=True)
            CVW = FSLAB + 2
            for oc in range(32):
                E("ve", lambda v, oc=oc: v.tensor_scalar(tmpc[:, 0:CVW],
                  hslab[:, SW * oc: SW * oc + CVW], cw_s[:, oc:oc + 1], None, AO.mult),
                  reads=(f"hs{oc}", "cw_s", "tmpc"), writes=("tmpc",))
                E("ve", lambda v, oc=oc: v.scalar_tensor_tensor(tmpc[:, 0:CVW],
                  hslab[:, SW * oc + 1: SW * oc + 1 + CVW], cw_s[:, 32 + oc:33 + oc], tmpc[:, 0:CVW],
                  AO.mult, AO.add), reads=(f"hs{oc}", "cw_s", "tmpc"), writes=("tmpc",))
                E("ve", lambda v, oc=oc: v.scalar_tensor_tensor(tmpc[:, 0:CVW],
                  hslab[:, SW * oc + 2: SW * oc + 2 + CVW], cw_s[:, 64 + oc:65 + oc], tmpc[:, 0:CVW],
                  AO.mult, AO.add), reads=(f"hs{oc}", "cw_s", "tmpc"), writes=("tmpc",))
                E("se", lambda s, oc=oc: s.activation(hslab[:, SW * oc + 1: SW * oc + 1 + CVW],
                  tmpc[:, 0:CVW], AF.Silu, bias=cb_s[:, oc:oc + 1]),
                  reads=("tmpc", "cb_s", f"hs{oc}"), writes=(f"h2_{oc}",))
            H2RD = tuple(f"h2_{oc}" for oc in range(32))
            absmax_rows("h2", lambda oc: (hslab[:, SW * oc + 1: SW * oc + 1 + CVW], H2RD), 32, CVW, 0)
            E("ve", lambda v: v.tensor_scalar(srow[0:1, 550:550 + CVW], st32[0:1, 0:CVW], EPS, None, AO.max),
              reads=("srow", "st32"), writes=("srow",))
            E("ve", lambda v: v.reciprocal(srow[0:1, 0:CVW], srow[0:1, 550:550 + CVW]),
              reads=("srow",), writes=("srow",), lazy=True)
            E("ve", lambda v: v.tensor_scalar(srow[0:1, 0:CVW], srow[0:1, 0:CVW], 127.0, None, AO.mult),
              reads=("srow",), writes=("srow",))
            E("ve", lambda v: v.tensor_scalar(srow[0:1, 550:550 + CVW], srow[0:1, 550:550 + CVW],
              1.0 / 127.0, None, AO.mult), reads=("srow",), writes=("srow",))
            bcast_row(0, CVW, s2bc, "s2bc")
            bcast_row(550, CVW, s2inv, "s2inv")
            for ch in range(32):
                wsl = ch % 2
                E("sy", lambda s, ch=ch, wsl=wsl: s.dma_start(out=w2w[wsl][:, :],
                  in_=w2q_h.ap()[:, 1024 * ch:1024 * (ch + 1)]),
                  reads=("w2done", f"w2w{wsl}"), writes=(f"w2w{wsl}",), dma=True)
                hq = h2qb[ch % 2]
                E("ve", lambda v, ch=ch: v.tensor_tensor(tmpc[:, 0:CVW],
                  hslab[:, SW * ch + 1: SW * ch + 1 + CVW], s2bc[:, 0:CVW], AO.mult),
                  reads=(f"h2_{ch}", "s2bc", "tmpc"), writes=("tmpc",))
                E("ve", lambda v, hq=hq: v.tensor_scalar(hq[:, 0:256], tmpc[:, 0:256], MAGIC, MAGIC,
                  AO.add, AO.subtract), reads=("tmpc", f"hq{ch % 2}"), writes=(f"hq{ch % 2}",))
                E("ve", lambda v, hq=hq: v.tensor_scalar(hq[:, 256:512], tmpc[:, 258:514], MAGIC, MAGIC,
                  AO.add, AO.subtract), reads=("tmpc", f"hq{ch % 2}"), writes=(f"hq{ch % 2}",))
                for oc in range(8):
                    pj = PS[oc]; pb = PSN[oc]
                    E("te", lambda t, ch=ch, oc=oc, wsl=wsl, hq=hq, pj=pj: t.matmul(
                        pj[:, 0:FSLAB], w2w[wsl][:, 128 * oc:128 * (oc + 1)], hq[:, :],
                        start=(ch == 0), stop=(ch == 31)),
                        reads=(f"hq{ch % 2}", f"w2w{wsl}"),
                        writes=(pb,) if ch in (0, 31) else (), inc=(ch == 31 or oc == 7))
            for oc in range(8):
                pj = PS[oc]; pb = PSN[oc]
                E("ve", lambda v, oc=oc, pj=pj: v.scalar_tensor_tensor(zb2[:, FSLAB * oc: FSLAB * oc + 256],
                  pj[:, 0:256], swinv[:, 1:2], s2inv[:, 0:256], AO.mult, AO.mult),
                  reads=(pb, "swinv1", "s2inv", "zb2"), writes=("zb2",))
                E("ve", lambda v, oc=oc, pj=pj: v.scalar_tensor_tensor(zb2[:, FSLAB * oc + 256: FSLAB * oc + 512],
                  pj[:, 256:512], swinv[:, 1:2], s2inv[:, 258:514], AO.mult, AO.mult),
                  reads=(pb, "swinv1", "s2inv", "zb2"), writes=("zb2",))
            emit_ln(lt4, FSLAB,
                    src=lambda ic: (zb2[:, FSLAB * ic:FSLAB * (ic + 1)], ("zb2",)),
                    resid=lambda ic: (ln2f[:, SLABW * ic:SLABW * ic + FSLAB], (f"ln2f{ic}",)),
                    res_out=lambda ic: (ybuf[:, FSLAB * ic:FSLAB * (ic + 1)], f"yb{ic}"))
            for ic in range(8):
                E("ge", lambda g_, ic=ic, sl_i=sl_i: g_.dma_start(
                    out=yout.ap()[:, NOWN * ic + FSLAB * sl_i: NOWN * ic + FSLAB * (sl_i + 1)],
                    in_=ybuf[:, FSLAB * ic:FSLAB * (ic + 1)]),
                    reads=(f"yb{ic}",), writes=("ydone",), dma=True)
            if dbg_h is not None and sl_i == 0:
                E("ge", lambda g_: g_.dma_start(out=dbg_h.ap()[:, 0:514], in_=xqb[:, 0:514]),
                  reads=tuple(f"xqb{i}" for i in range(8)), writes=("dbg3",), dma=True)
                E("ge", lambda g_: g_.dma_start(out=dbg_h.ap()[:, 514:1028], in_=hslab[:, 0:514]),
                  reads=tuple(f"h2_{o}" for o in range(32)), writes=("dbg3",), dma=True)
                E("ge", lambda g_: g_.dma_start(out=dbg_h.ap()[:, 1028:1540], in_=zb2[:, 0:512]),
                  reads=("zb2",), writes=("dbg3",), dma=True)
                E("ge", lambda g_: g_.dma_start(out=dbg_h.ap()[:, 1540:1544], in_=s2bc[:, 4:8]),
                  reads=("s2bc",), writes=("dbg3",), dma=True)
                E("ge", lambda g_: g_.dma_start(out=dbg_h.ap()[:, 1544:1548], in_=sbc[:, 4:8]),
                  reads=("sbc",), writes=("dbg3",), dma=True)
                E("ge", lambda g_: g_.dma_start(out=dbg_h.ap()[:, 1630:1640], in_=sxinv[:, 506:516]),
                  reads=("sxinv",), writes=("dbg3",), dma=True)
                E("ge", lambda g_: g_.dma_start(out=dbg_h.ap()[:, 1704:1714], in_=sbc[:, 506:516]),
                  reads=("sbc",), writes=("dbg3",), dma=True)

    ytot = 16 * sch.dfam["ydone"]
    sch.ops["ge"].append(lambda h: h.wait_ge(SEMS["D:ydone"], ytot))
    return _finish(nc, sch, root)


def _finish(nc, sch, root):
    with nc.Block() as block:
        @block.sync
        def _(eng):
            for f in sch.ops["sy"]:
                f(eng)

        @block.tensor
        def _(eng):
            for f in sch.ops["te"]:
                f(eng)

        @block.vector
        def _(eng):
            for f in sch.ops["ve"]:
                f(eng)

        @block.scalar
        def _(eng):
            for f in sch.ops["se"]:
                f(eng)

        @block.gpsimd
        def _(eng):
            for f in sch.ops["ge"]:
                f(eng)

    root.close()
    return nc


SPLIT = np.concatenate([np.arange(0, 64, 2), np.arange(1, 64, 2)])


def _chunkmaj(a):
    rows, cols = a.shape
    nch = rows // 128
    return np.ascontiguousarray(a.reshape(nch, 128, cols).transpose(1, 0, 2).reshape(128, nch * cols))


def _feat(a):
    return _chunkmaj(np.ascontiguousarray(a.T))


_CACHE = {}


def kernel(L0_attn_x, L0_attn_wq, L0_attn_bq, L0_attn_wk, L0_attn_bk,
           L0_attn_wv, L0_attn_bv, L0_attn_wo, L0_attn_bo,
           L0_ffn_w1, L0_ffn_conv_w, L0_ffn_conv_b, L0_ffn_w2):
    x = np.asarray(L0_attn_x, np.float32)
    wq = np.asarray(L0_attn_wq, np.float32); wk = np.asarray(L0_attn_wk, np.float32)
    wv = np.asarray(L0_attn_wv, np.float32); wo = np.asarray(L0_attn_wo, np.float32)
    bq = np.asarray(L0_attn_bq, np.float32); bk = np.asarray(L0_attn_bk, np.float32)
    bv = np.asarray(L0_attn_bv, np.float32); bo = np.asarray(L0_attn_bo, np.float32)
    w1 = np.asarray(L0_ffn_w1, np.float32); w2 = np.asarray(L0_ffn_w2, np.float32)
    cwt = np.asarray(L0_ffn_conv_w, np.float32); cbt = np.asarray(L0_ffn_conv_b, np.float32)

    if "nc" not in _CACHE:
        _CACHE["nc"] = build_program()
    nc = _CACHE["nc"]

    perm = np.concatenate([64 * h + SPLIT for h in range(H)])
    wq_p = wq[:, perm]; wk_p = wk[:, perm]
    bq_p = bq[perm]; bk_p = bk[perm]
    d_wq = _chunkmaj(wq_p); d_wk = _chunkmaj(wk_p)
    d_wv = _chunkmaj(wv); d_wo = _chunkmaj(wo)
    d_bq = np.ascontiguousarray(bq_p.reshape(8, 128).T)
    d_bk = np.ascontiguousarray(bk_p.reshape(8, 128).T)
    d_bo = np.ascontiguousarray(bo.reshape(8, 128).T)
    d_bvr = bv.reshape(1, D)
    d_w1t = _chunkmaj(np.ascontiguousarray(w1.T))
    d_w2t = _chunkmaj(np.ascontiguousarray(w2.T))
    d_cw = np.concatenate([cwt[:, 0, t].reshape(32, 128).T for t in range(3)], axis=1)
    d_cw = np.ascontiguousarray(d_cw)
    d_cb = np.ascontiguousarray(cbt.reshape(32, 128).T)
    d_idf = np.eye(128, dtype=np.float32)

    s8 = 1.0 / np.sqrt(np.sqrt(64.0))
    s8 = 8.0 ** -0.5
    invf = (10000.0 ** (-np.arange(0, 64, 2, dtype=np.float64) / 64.0))

    def rope_tables(pos):
        ang = pos[None, :].astype(np.float64) * invf[:, None]
        c = np.cos(ang) * s8; s = np.sin(ang) * s8
        slot_c = np.concatenate([c, c], axis=0)
        slot_s = np.concatenate([s, -s], axis=0)
        C2 = np.concatenate([slot_c, slot_c], axis=0).astype(np.float32)
        NS = np.concatenate([slot_s, slot_s], axis=0).astype(np.float32)
        return np.ascontiguousarray(C2), np.ascontiguousarray(NS)

    kpos = np.arange(S)
    d_c2k, d_nsk = rope_tables(kpos)

    tri = (np.arange(128)[:, None] <= np.arange(128)[None, :]).astype(ml_dtypes.bfloat16)
    ones128 = np.ones((128, 128), ml_dtypes.bfloat16)
    tri4 = np.ascontiguousarray(np.tile(tri, (1, 4)))
    ones4 = np.ascontiguousarray(np.tile(ones128, (1, 4)))

    per_side = {}
    for side in range(2):
        own_idx = own_idx_side(side)
        bpos = bank_pos_side(side)
        qpos = np.concatenate([own_idx, np.array(bpos)])
        c2q, nsq = rope_tables(qpos)
        wkill = np.zeros((NKILL, S), ml_dtypes.bfloat16)
        for g in range(G8):
            for l in range(2):
                thr = 128 * (4 * g + 2 * side + l + 1)
                wkill[2 * g + l, thr:] = ml_dtypes.bfloat16(1.0)
        qkill = np.zeros((NKILL, (G8 + 1) * GN), ml_dtypes.bfloat16)
        for g in range(G8):
            for l in range(2):
                qkill[2 * g + l, GN * g + 128 * l: GN * g + 128 * (l + 1)] = ml_dtypes.bfloat16(KILL)
        m1 = tri4 if side == 0 else ones4
        m2 = ones4 if side == 0 else tri4
        bm = np.zeros((128, KBANK * NBANK), ml_dtypes.bfloat16)
        for i in range(KBANK):
            for c in range(NBANK):
                valid = (128 * i + np.arange(128)) <= bpos[c]
                bm[:, NBANK * i + c] = valid.astype(ml_dtypes.bfloat16)
        cfil = np.ones((128, NBANK), np.float32)
        if side == 0:
            cfil[:, 0] = 0.0
        else:
            cfil[:, 15] = 0.0
        per_side[side] = dict(c2q=np.ascontiguousarray(c2q), nsq=np.ascontiguousarray(nsq),
                              wkill=np.ascontiguousarray(wkill), qkill=np.ascontiguousarray(qkill),
                              m1=np.ascontiguousarray(m1), m2=np.ascontiguousarray(m2),
                              bmask=np.ascontiguousarray(bm), cfil=cfil,
                              own_idx=own_idx, bpos=bpos)

    in_maps = []
    for core in range(8):
        b = core // 2; side = core % 2
        ps = per_side[side]
        xb = x[b]
        own = xb[ps["own_idx"]]
        bank = xb[ps["bpos"]]
        xloc = np.concatenate([own, bank], axis=0)
        in_maps.append({
            "xful": _feat(xb), "xloc": _feat(xloc),
            "wq": d_wq, "wk": d_wk, "wv": d_wv, "wo": d_wo,
            "bq": d_bq, "bk": d_bk, "bvr": d_bvr, "bo": d_bo,
            "w1t": d_w1t, "w2t": d_w2t, "cw": d_cw, "cb": d_cb,
            "c2q": ps["c2q"], "nsq": ps["nsq"], "c2k": d_c2k, "nsk": d_nsk,
            "wkill": ps["wkill"], "qkill": ps["qkill"], "m1": ps["m1"], "m2": ps["m2"],
            "bmask": ps["bmask"], "cfil": ps["cfil"], "idf": d_idf,
        })

    res = run_bass_kernel_spmd(nc, in_maps, core_ids=list(range(8)))

    out = np.empty((B, S, D), np.float32)
    for core in range(8):
        b = core // 2; side = core % 2
        y = res.results[core]["y"]
        yt = y.reshape(128, 8, NOWN).transpose(1, 0, 2).reshape(D, NOWN)
        out[b, per_side[side]["own_idx"]] = yt.T
    return out
